# version 01
# baseline (speedup 1.0000x reference)
import numpy as np
import ml_dtypes
import concourse.bass as bass
import concourse.mybir as mybir
from concourse.bass_utils import run_bass_kernel_spmd
from concourse.bass_core_rust import IndirectOffsetOnAxis
from contextlib import ExitStack

F32 = mybir.dt.float32
BF16 = mybir.dt.bfloat16
AF = mybir.ActivationFunctionType
ALU = mybir.AluOpType
BF = ml_dtypes.bfloat16
AXX = mybir.AxisListType.X

GROUPS = [[0, 1, 2, 3], [4, 5, 6, 7]]
NCORE = 8
T_LOC = 2048
D = 1024
NCH = 8
TB = 512
NTB = T_LOC // TB
EPS_LN = 1e-5


class Sem:
    __slots__ = ("h", "n")

    def __init__(self, h):
        self.h = h
        self.n = 0

    def inc(self, k=1):
        self.n += k
        return self.n


class P:
    ENGINES = ("sync", "tensor", "vector", "scalar", "gpsimd")
    SERIAL = ("vector", "scalar", "gpsimd")

    def __init__(self, nc, es):
        self.nc = nc
        self.es = es
        self.ops = {e: [] for e in self.ENGINES}
        self._sems = {}
        self.uid = 0
        self._selfsems = {e: self.sem("self_" + e) for e in self.SERIAL}

    def sem(self, name):
        if name not in self._sems:
            h = self.es.enter_context(self.nc.semaphore(name))
            self._sems[name] = Sem(h)
        return self._sems[name]

    def on(self, eng, fn):
        self.ops[eng].append(("op", fn))

    def ctl(self, eng, fn):
        self.ops[eng].append(("ctl", fn))

    def inc(self, eng, sem):
        v = sem.inc()
        self.ops[eng].append(("sig", sem.h))
        return v

    def wait(self, eng, sem, val):
        self.ctl(eng, lambda e, s=sem.h, v=val: e.wait_ge(s, v))

    def barrier(self, engines=None):
        engines = engines or self.ENGINES
        bs = self.sem("bar")
        for e in engines:
            self.inc(e, bs)
        tgt = bs.n
        for e in engines:
            self.wait(e, bs, tgt)

    def dma(self, q, out, in_, sem=None, serial=True, **kw):
        sem = sem or self.sem("dma_" + q)
        if serial and sem.n > 0:
            self.ctl(q, lambda e, s=sem.h, v=sem.n: e.wait_ge(s, v))
        tgt = sem.inc(16)
        self.ctl(q, lambda e, o=out, i=in_, s=sem.h, kw=kw: e.dma_start(out=o, in_=i, **kw).then_inc(s, 16))
        return sem, tgt

    def dma_lazy(self, q, mk, sem, serial=True):
        if serial and sem.n > 0:
            self.ctl(q, lambda e, s=sem.h, v=sem.n: e.wait_ge(s, v))
        tgt = sem.inc(16)

        def fire(e, mk=mk, s=sem.h):
            o, i = mk(e)
            e.dma_start(out=o, in_=i).then_inc(s, 16)
        self.ctl(q, fire)
        return sem, tgt

    def run(self):
        with self.nc.Block() as block:
            for eng_name in self.ENGINES:
                def runner(eng, name=eng_name):
                    ssem = self._selfsems.get(name)
                    cnt = 0
                    last = None
                    for kind, fn in self.ops[name]:
                        if kind == "op":
                            if ssem is not None:
                                if cnt > 0:
                                    eng.wait_ge(ssem.h, cnt)
                                inst = fn(eng)
                                inst.then_inc(ssem.h, 1)
                                cnt += 1
                            else:
                                last = fn(eng)
                        elif kind == "sig":
                            if ssem is not None:
                                if cnt > 0:
                                    eng.wait_ge(ssem.h, cnt)
                                eng.sem_inc(fn, 1)
                            elif last is not None:
                                last.then_inc(fn, 1)
                                last = None
                            else:
                                eng.nop().then_inc(fn, 1)
                        else:
                            fn(eng)
                getattr(block, eng_name)(runner)

    def sbuf(self, name, shape, dtype, es=None):
        return (es or self.es).enter_context(self.nc.sbuf_tensor(name, shape, dtype))

    def psum(self, name, shape, dtype=F32, es=None):
        return (es or self.es).enter_context(self.nc.psum_tensor(name, shape, dtype))


def host_constants():
    c = {}
    c["c_ident_bf"] = np.eye(128, dtype=BF)
    c["c_ident_f32"] = np.eye(128, dtype=np.float32)
    c["c_ones_row_bf"] = np.ones((1, 128), dtype=BF)
    c["c_ones_row_f32"] = np.ones((1, 128), dtype=np.float32)
    c["c_ones_col_bf"] = np.ones((128, 1), dtype=BF)
    pp = np.arange(128)[:, None]
    jj = np.arange(512)[None, :]
    masks = [((pp + off) <= jj).astype(BF) for off in (0, 128, 256, 384)]
    c["c_masks_causal"] = np.ascontiguousarray(np.concatenate(masks, axis=1))
    wmasks = []
    for off in (-256, -128, 0, 128, 256, 384):
        d_ = jj - (pp + off)
        wmasks.append(((d_ >= 0) & (d_ < 256)).astype(BF))
    c["c_masks_win"] = np.ascontiguousarray(np.concatenate(wmasks, axis=1))
    hs = np.zeros((128, 2), dtype=BF)
    hs[0:64, 0] = 1
    hs[64:128, 1] = 1
    c["c_hsel2"] = hs
    h16 = np.zeros((128, 8), dtype=BF)
    for i in range(8):
        h16[16 * i:16 * (i + 1), i] = 1
    c["c_hsel16"] = h16
    c["c_hexp16"] = np.ascontiguousarray(h16.T)
    c["c_half_row_f32"] = np.full((1, 128), 0.5, dtype=np.float32)
    c["c_ones_col_f32"] = np.ones((128, 1), dtype=np.float32)
    return c


CONST_SHAPES = {
    "c_ident_bf": ([128, 128], BF16), "c_ident_f32": ([128, 128], F32),
    "c_ones_row_bf": ([1, 128], BF16), "c_ones_row_f32": ([1, 128], F32),
    "c_ones_col_bf": ([128, 1], BF16),
    "c_masks_causal": ([128, 4 * 512], BF16), "c_masks_win": ([128, 6 * 512], BF16),
    "c_hsel2": ([128, 2], BF16), "c_half_row_f32": ([1, 128], F32),
    "c_hsel16": ([128, 8], BF16), "c_hexp16": ([8, 128], BF16),
    "c_ones_col_f32": ([128, 1], F32),
}


def ag_rows_of_chunk(g):
    return 2048 * g if g <= 3 else 2048 * (7 - g) + 1024


def emit_ln(p, C, xf, h_bf, affine=None, keep_stats=None, rstd_comb=False, tag="ln"):
    import os
    LNSTAGE = int(os.environ.get("LNSTAGE", "9"))
    """Feature-dim LayerNorm of xf -> h_bf (bf16). Allocates its own psum scope.

    affine: None | (g_col, b_col). keep_stats: fill C['s_mubc']/C['s_rstdbc']
    (outer-LN stats). rstd_comb: apply fused LN(LN(x)) combined rstd to h.
    """
    p.uid += 1
    tag = f"{tag}{p.uid}"
    es = ExitStack()
    ps1 = p.psum(tag + "_s1", [1, TB], es=es)
    ps2 = p.psum(tag + "_s2", [1, TB], es=es)
    pb1 = p.psum(tag + "_b1", [128, TB], es=es)
    pb2 = p.psum(tag + "_b2", [128, TB], es=es)
    sv, st, ss = p.sem(tag + "v"), p.sem(tag + "t"), p.sem(tag + "s")
    xc = [C["s_cast0"], C["s_cast1"]]
    x2 = [C["s_sq0"], C["s_sq1"]]
    onesb = C["s_onescol_bf"]
    ones1f = C["s_onesrow_f32"]
    s1r, s2r = C["s_lnrow1"], C["s_lnrow2"]
    mubc, rstdbc = C["s_mubc"], C["s_rstdbc"]
    var_sb, srt, tsc = C["s_lnvar"], C["s_lnsrt"], C["s_lntmp"]

    for tb in range(NTB):
        tsl = slice(tb * TB, (tb + 1) * TB)
        for c in range(NCH):
            bi = c % 2
            if c >= 2:
                p.wait("vector", st, st.n - 1)
            xs = xf[:, c * T_LOC + tb * TB: c * T_LOC + (tb + 1) * TB]
            p.on("vector", lambda e, o=xc[bi][:], i=xs: e.tensor_copy(o, i))
            p.on("vector", lambda e, o=x2[bi][:], a=xc[bi][:]: e.tensor_mul(o, a, a))
            vt = p.inc("vector", sv)
            p.wait("tensor", sv, vt)
            p.on("tensor", lambda e, o=ps1[0:1, :], r=xc[bi][:], s0=(c == 0), s1=(c == NCH - 1):
                 e.matmul(o, onesb[:, 0:1], r, start=s0, stop=s1))
            p.on("tensor", lambda e, o=ps2[0:1, :], r=x2[bi][:], s0=(c == 0), s1=(c == NCH - 1):
                 e.matmul(o, onesb[:, 0:1], r, start=s0, stop=s1))
            p.inc("tensor", st)
        if LNSTAGE < 2:
            continue
        p.wait("vector", st, st.n)
        p.on("vector", lambda e: e.tensor_copy(s1r[0:1, :], ps1[0:1, :]))
        p.on("vector", lambda e: e.tensor_copy(s2r[0:1, :], ps2[0:1, :]))
        ve = p.inc("vector", sv)
        p.wait("tensor", sv, ve)
        p.on("tensor", lambda e: e.matmul(pb1[:, :], ones1f[0:1, :], s1r[0:1, :], start=True, stop=True))
        p.on("tensor", lambda e: e.matmul(pb2[:, :], ones1f[0:1, :], s2r[0:1, :], start=True, stop=True))
        tb1 = p.inc("tensor", st)
        p.wait("vector", st, tb1)
        p.on("vector", lambda e: e.tensor_scalar(mubc[:], pb1[:, :], 1.0 / D, None, ALU.mult))
        p.on("vector", lambda e: e.tensor_scalar(tsc[:], pb2[:, :], 1.0 / D, None, ALU.mult))
        p.on("vector", lambda e: e.tensor_mul(srt[:], mubc[:], mubc[:]))
        p.on("vector", lambda e: e.tensor_sub(var_sb[:], tsc[:], srt[:]))
        if LNSTAGE < 3:
            continue
        vv = p.inc("vector", sv)
        p.wait("scalar", sv, vv)
        p.on("scalar", lambda e: e.activation(srt[:], var_sb[:], AF.Sqrt, bias=C["s_epscol"][:, 0:1]))
        sg = p.inc("scalar", ss)
        p.wait("vector", ss, sg)
        p.on("vector", lambda e: e.reciprocal(rstdbc[:], srt[:]))
        if keep_stats:
            p.on("vector", lambda e, o=keep_stats[0][:, tsl]: e.tensor_copy(o, mubc[:]))
            p.on("vector", lambda e, o=keep_stats[1][:, tsl]: e.tensor_copy(o, rstdbc[:]))
        if rstd_comb:
            p.on("vector", lambda e: e.tensor_mul(tsc[:], rstdbc[:], rstdbc[:]))
            p.on("vector", lambda e: e.tensor_mul(tsc[:], tsc[:], var_sb[:]))
            v2 = p.inc("vector", sv)
            p.wait("scalar", sv, v2)
            p.on("scalar", lambda e: e.activation(tsc[:], tsc[:], AF.Sqrt, bias=C["s_epscol"][:, 0:1]))
            s2_ = p.inc("scalar", ss)
            p.wait("vector", ss, s2_)
            p.on("vector", lambda e: e.reciprocal(var_sb[:], tsc[:]))
            p.on("vector", lambda e: e.tensor_mul(rstdbc[:], rstdbc[:], var_sb[:]))
        if LNSTAGE < 4:
            continue
        for c in range(NCH):
            xs = xf[:, c * T_LOC + tb * TB: c * T_LOC + (tb + 1) * TB]
            hs = h_bf[:, c * T_LOC + tb * TB: c * T_LOC + (tb + 1) * TB]
            p.on("vector", lambda e, x_=xs: e.tensor_sub(tsc[:], x_, mubc[:]))
            if affine is None:
                p.on("vector", lambda e, o=hs: e.tensor_mul(o, tsc[:], rstdbc[:]))
            else:
                g_col, b_col = affine
                p.on("vector", lambda e: e.tensor_mul(tsc[:], tsc[:], rstdbc[:]))
                p.on("vector", lambda e, o=hs, g=g_col[:, c:c + 1], b=b_col[:, c:c + 1]:
                     e.tensor_scalar(o, tsc[:], g, b, ALU.mult, ALU.add))
        p.inc("vector", sv)
    p.barrier()
    es.close()


def alloc_common(p, IN):
    C = {}
    C["s_cast0"] = p.sbuf("s_cast0", [128, TB], BF16)
    C["s_cast1"] = p.sbuf("s_cast1", [128, TB], BF16)
    C["s_sq0"] = p.sbuf("s_sq0", [128, TB], BF16)
    C["s_sq1"] = p.sbuf("s_sq1", [128, TB], BF16)
    C["s_lnrow1"] = p.sbuf("s_lnrow1", [1, TB], F32)
    C["s_lnrow2"] = p.sbuf("s_lnrow2", [1, TB], F32)
    C["s_mubc"] = p.sbuf("s_mubc", [128, TB], F32)
    C["s_rstdbc"] = p.sbuf("s_rstdbc", [128, TB], F32)
    C["s_lnvar"] = p.sbuf("s_lnvar", [128, TB], F32)
    C["s_lnsrt"] = p.sbuf("s_lnsrt", [128, TB], F32)
    C["s_lntmp"] = p.sbuf("s_lntmp", [128, TB], F32)
    C["s_onescol_bf"] = p.sbuf("s_onescol_bf", [128, 1], BF16)
    C["s_onesrow_f32"] = p.sbuf("s_onesrow_f32", [1, 128], F32)
    C["s_onesrow_bf"] = p.sbuf("s_onesrow_bf", [1, 128], BF16)
    C["s_ident_bf"] = p.sbuf("s_ident_bf", [128, 128], BF16)
    C["s_ident_f32"] = p.sbuf("s_ident_f32", [128, 128], F32)
    C["s_epscol"] = p.sbuf("s_epscol", [128, 1], F32)
    C["s_hbf"] = p.sbuf("s_hbf", [128, NCH * T_LOC], BF16)
    p.on("gpsimd", lambda e: e.memset(C["s_epscol"][:], EPS_LN))
    gs = p.sem("constset")
    p.inc("gpsimd", gs)
    for e in P.ENGINES:
        p.wait(e, gs, gs.n)
    ld = p.sem("constload")
    for name in ("c_ident_bf", "c_ident_f32", "c_ones_row_bf", "c_ones_row_f32", "c_ones_col_bf"):
        sb = {"c_ident_bf": C["s_ident_bf"], "c_ident_f32": C["s_ident_f32"],
              "c_ones_row_bf": C["s_onesrow_bf"], "c_ones_row_f32": C["s_onesrow_f32"],
              "c_ones_col_bf": C["s_onescol_bf"]}[name]
        p.dma("sync", sb[:], IN[name], sem=ld)
    for e in P.ENGINES:
        p.wait(e, ld, ld.n)
    return C


def emit_l1ffn(p, C, xf, IN):
    p.uid += 1
    tag = f"d{p.uid}"
    h_bf = C["s_hbf"]
    emit_ln(p, C, xf, h_bf, tag="lnD")

    es = ExitStack()
    lg = p.sbuf(tag + "lg", [8, 1], F32, es=es)
    uu = p.sbuf(tag + "u", [8, 1], F32, es=es)
    t0 = p.sbuf(tag + "t0", [8, 64], F32, es=es)
    ind = p.sbuf(tag + "ind", [8, 1], F32, es=es)
    bsel = p.sbuf(tag + "bsel", [128, 8], F32, es=es)
    brow = p.sbuf(tag + "brow", [1, 1024], F32, es=es)
    bexp = p.sbuf(tag + "bexp", [8, 1024], F32, es=es)
    indbc = p.sbuf(tag + "indbc", [128, 8], F32, es=es)
    wsel = p.sbuf(tag + "wsel", [128, 8 * 1024], BF16, es=es)
    wexp = p.sbuf(tag + "wexp", [128, 8 * 1024], F32, es=es)
    wacc = p.sbuf(tag + "wacc", [128, 1024], F32, es=es)
    pps = p.psum(tag + "pa", [128, TB], es=es)
    pps2 = p.psum(tag + "pb", [128, TB], es=es)
    pj = [p.psum(tag + "p0", [128, TB], es=es), p.psum(tag + "p1", [128, TB], es=es)]

    ld = p.sem(tag + "ld")
    vs, ts, ss = p.sem(tag + "v"), p.sem(tag + "t"), p.sem(tag + "s")
    idf = C["s_ident_f32"]

    p.dma("sync", lg[:], IN["L1_ffn_logits"], sem=ld)
    p.dma("sync", uu[:], IN["L1_ffn_u"], sem=ld)
    p.dma("sync", bexp[:], IN["L1_ffn_b"], sem=ld)
    p.wait("scalar", ld, ld.n)
    p.on("scalar", lambda e: e.activation(t0[0:8, 0:1], uu[:], AF.Ln))
    s1 = p.inc("scalar", ss)
    p.wait("vector", ss, s1)
    p.on("vector", lambda e: e.tensor_scalar(t0[0:8, 0:1], t0[0:8, 0:1], -1.0, None, ALU.mult))
    v1 = p.inc("vector", vs)
    p.wait("scalar", vs, v1)
    p.on("scalar", lambda e: e.activation(t0[0:8, 1:2], t0[0:8, 0:1], AF.Ln))
    s2 = p.inc("scalar", ss)
    p.wait("vector", ss, s2)
    p.wait("vector", ld, ld.n)
    p.on("vector", lambda e: e.tensor_sub(t0[0:8, 2:3], lg[:], t0[0:8, 1:2]))
    v2 = p.inc("vector", vs)
    p.wait("tensor", vs, v2)
    p.on("tensor", lambda e: e.transpose(pps[0:1, 0:8], t0[0:8, 2:3], idf[0:8, 0:8]))
    t1 = p.inc("tensor", ts)
    p.wait("vector", ts, t1)
    p.on("vector", lambda e: e.tensor_copy(t0[0:1, 3:11], pps[0:1, 0:8]))
    v3 = p.inc("vector", vs)
    p.wait("tensor", vs, v3)
    p.on("tensor", lambda e: e.matmul(pps2[0:8, 0:8], C["s_onesrow_f32"][0:1, 0:8], t0[0:1, 3:11], start=True, stop=True))
    t2 = p.inc("tensor", ts)
    p.wait("vector", ts, t2)
    p.on("vector", lambda e: e.tensor_tensor(t0[0:8, 16:24], t0[0:8, 2:3].broadcast_to([8, 8]), pps2[0:8, 0:8], ALU.is_ge))
    p.on("vector", lambda e: e.reduce_sum(t0[0:8, 24:25], t0[0:8, 16:24], axis=AXX))
    p.on("vector", lambda e: e.tensor_scalar(ind[:], t0[0:8, 24:25], 7.5, 0.5, ALU.is_ge, ALU.mult))
    v4 = p.inc("vector", vs)

    p.wait("tensor", vs, v4)
    p.on("tensor", lambda e: e.matmul(pps[0:1, 0:512], ind[:], bexp[:, 0:512], start=True, stop=True))
    p.on("tensor", lambda e: e.matmul(pps2[0:1, 0:512], ind[:], bexp[:, 512:1024], start=True, stop=True))
    t3 = p.inc("tensor", ts)
    p.wait("vector", ts, t3)
    p.on("vector", lambda e: e.tensor_copy(brow[0:1, 0:512], pps[0:1, 0:512]))
    p.on("vector", lambda e: e.tensor_copy(brow[0:1, 512:1024], pps2[0:1, 0:512]))
    v5 = p.inc("vector", vs)
    p.wait("tensor", vs, v5)
    for c in range(NCH):
        p.on("tensor", lambda e, c=c: e.transpose(pps[0:128, c:c + 1], brow[0:1, 128 * c:128 * (c + 1)], idf[0:1, 0:1]))
    t4 = p.inc("tensor", ts)
    p.wait("vector", ts, t4)
    p.on("vector", lambda e: e.tensor_copy(bsel[:, 0:8], pps[0:128, 0:8]))
    v6 = p.inc("vector", vs)

    p.wait("tensor", vs, v6)
    p.on("tensor", lambda e: e.transpose(pps2[0:1, 0:8], ind[:], idf[0:8, 0:8]))
    t5 = p.inc("tensor", ts)
    p.wait("vector", ts, t5)
    p.on("vector", lambda e: e.tensor_copy(t0[0:1, 32:40], pps2[0:1, 0:8]))
    v7 = p.inc("vector", vs)
    p.wait("tensor", vs, v7)
    p.on("tensor", lambda e: e.matmul(pps[0:128, 0:8], C["s_onesrow_f32"][0:1, :], t0[0:1, 32:40], start=True, stop=True))
    t6 = p.inc("tensor", ts)
    p.wait("vector", ts, t6)
    p.on("vector", lambda e: e.tensor_copy(indbc[:], pps[0:128, 0:8]))
    p.inc("vector", vs)
    p.barrier()

    wl = p.sem(tag + "wl")
    cs = p.sem(tag + "cs")
    for ci in range(NCH):
        if ci >= 1:
            p.wait("sync", cs, ci)
        for ex in range(8):
            p.dma("sync", wexp[:, ex * 1024:(ex + 1) * 1024],
                  IN["L1_ffn_W"][ex, ci * 128:(ci + 1) * 128, :], sem=wl)
        p.wait("vector", wl, wl.n)
        p.on("vector", lambda e: e.tensor_scalar(wacc[:], wexp[:, 0:1024], indbc[:, 0:1], None, ALU.mult))
        for ex in range(1, 7):
            p.on("vector", lambda e, ex=ex: e.scalar_tensor_tensor(
                wacc[:], wexp[:, ex * 1024:(ex + 1) * 1024], indbc[:, ex:ex + 1], wacc[:], ALU.mult, ALU.add))
        p.on("vector", lambda e, ci=ci: e.scalar_tensor_tensor(
            wsel[:, ci * 1024:(ci + 1) * 1024], wexp[:, 7 * 1024:8 * 1024], indbc[:, 7:8], wacc[:], ALU.mult, ALU.add))
        p.inc("vector", cs)
    p.barrier()

    pe = p.sem(tag + "pe")
    pt = p.sem(tag + "pt")
    for k in range(NTB * NCH):
        tb, oc = divmod(k, NCH)
        bi = k % 2
        if k >= 2:
            p.wait("tensor", pe, k - 1)
        for ci in range(NCH):
            p.on("tensor", lambda e, o=pj[bi][:, :], l=wsel[:, ci * 1024 + oc * 128: ci * 1024 + (oc + 1) * 128],
                 r=h_bf[:, ci * T_LOC + tb * TB: ci * T_LOC + (tb + 1) * TB],
                 s0=(ci == 0), s1=(ci == NCH - 1): e.matmul(o, l, r, start=s0, stop=s1))
        tt = p.inc("tensor", pt)
        p.wait("vector", pt, tt)
        xs = xf[:, oc * T_LOC + tb * TB: oc * T_LOC + (tb + 1) * TB]
        p.on("vector", lambda e, x_=xs, ps=pj[bi][:, :], bc=bsel[:, oc:oc + 1]:
             e.scalar_tensor_tensor(x_, ps, bc, x_, ALU.add, ALU.add))
        p.inc("vector", pe)
    p.barrier()
    es.close()



def emit_row_to_col(p, row_ap, n, col_out, ident, pm, ts, vs):
    p.wait("tensor", vs, vs.n)
    for c in range(n):
        p.on("tensor", lambda e, c=c: e.transpose(pm[0:128, c:c + 1], row_ap[0:1, 128 * c:128 * (c + 1)], ident[0:1, 0:1]))
    t = p.inc("tensor", ts)
    p.wait("vector", ts, t)
    p.on("vector", lambda e: e.tensor_copy(col_out[0:128, 0:n], pm[0:128, 0:n]))
    p.inc("vector", vs)


def emit_bias_combine(p, w_col, bexp_ap, bex_sb, brow, col_out, ident, pm, pm2, ts, vs, ld):
    p.wait("sync", ts, ts.n)
    _, dt_ = p.dma("sync", bex_sb[:], bexp_ap, sem=ld)
    p.wait("tensor", ld, dt_)
    p.wait("tensor", vs, vs.n)
    p.on("tensor", lambda e: e.matmul(pm[0:1, 0:512], w_col[:], bex_sb[:, 0:512], start=True, stop=True))
    p.on("tensor", lambda e: e.matmul(pm2[0:1, 0:512], w_col[:], bex_sb[:, 512:1024], start=True, stop=True))
    t = p.inc("tensor", ts)
    p.wait("vector", ts, t)
    p.on("vector", lambda e: e.tensor_copy(brow[0:1, 0:512], pm[0:1, 0:512]))
    p.on("vector", lambda e: e.tensor_copy(brow[0:1, 512:1024], pm2[0:1, 0:512]))
    p.inc("vector", vs)
    emit_row_to_col(p, brow, 8, col_out, ident, pm, ts, vs)



def emit_norm_evac(p, C, po_par, o_dest, scale_row, rrow, rc4, rbc, pmisc, prb, ts, vs):
    idf = C["s_ident_f32"]
    p.on("vector", lambda e: e.tensor_copy(rrow[0:1, :], po_par[64:65, :]))
    pv = p.inc("vector", vs)
    p.wait("tensor", vs, pv)
    for c4 in range(4):
        p.on("tensor", lambda e, c4=c4: e.transpose(
            pmisc[0:128, c4:c4 + 1], rrow[0:1, 128 * c4:128 * (c4 + 1)], idf[0:1, 0:1]))
    tt = p.inc("tensor", ts)
    p.wait("vector", ts, tt)
    p.on("vector", lambda e: e.tensor_copy(rc4[:, 0:4], pmisc[0:128, 0:4]))
    p.on("vector", lambda e: e.reciprocal(rc4[:, 0:4], rc4[:, 0:4]))
    pv = p.inc("vector", vs)
    p.wait("tensor", vs, pv)
    for c4 in range(4):
        p.on("tensor", lambda e, c4=c4: e.transpose(
            pmisc[0:1, 128 * c4:128 * (c4 + 1)], rc4[:, c4:c4 + 1], idf[:]))
    tt = p.inc("tensor", ts)
    p.wait("vector", ts, tt)
    p.on("vector", lambda e: e.tensor_copy(rrow[0:1, :], pmisc[0:1, 0:512]))
    pv = p.inc("vector", vs)
    p.wait("tensor", vs, pv)
    p.on("tensor", lambda e: e.matmul(prb[:, :], scale_row[0:1, :], rrow[0:1, :], start=True, stop=True))
    tt = p.inc("tensor", ts)
    p.wait("vector", ts, tt)
    p.on("vector", lambda e: e.tensor_copy(rbc[:], prb[:, :]))
    p.on("vector", lambda e: e.tensor_tensor(o_dest, po_par[0:64, :], rbc[0:64, :], ALU.mult))
    p.inc("vector", vs)


def emit_v_to_ag(p, C, vtmp, vrow, ptr, agv_in_ap, ld, vs, pt):
    for hh in range(16):
        p.on("gpsimd", lambda e, hh=hh: e.memset(vrow[:, 65 * hh + 64:65 * hh + 65], 1.0))
    gset = p.sem("gset" + str(p.uid) + "_" + str(len(p.ops["gpsimd"])))
    p.inc("gpsimd", gset)
    p.wait("vector", gset, gset.n)
    prev_dtgt = None
    prev_dvv = None
    for tau in range(16):
        if prev_dtgt is not None:
            p.wait("vector", ld, prev_dtgt)
        if prev_dvv is not None:
            p.wait("tensor", vs, prev_dvv)
        for oc in range(NCH):
            p.on("tensor", lambda e, tau=tau, oc=oc: e.transpose(
                ptr[:, oc * 128:(oc + 1) * 128],
                vtmp[:, oc * T_LOC + tau * 128: oc * T_LOC + (tau + 1) * 128], C["s_ident_bf"][:]))
        tt = p.inc("tensor", pt)
        p.wait("vector", pt, tt)
        for hh in range(16):
            oc, par = hh // 2, hh % 2
            p.on("vector", lambda e, oc=oc, par=par, hh=hh: e.tensor_copy(
                vrow[:, 65 * hh:65 * hh + 64], ptr[:, oc * 128 + par * 64: oc * 128 + par * 64 + 64]))
        dvv = p.inc("vector", vs)
        prev_dvv = dvv
        p.wait("sync", vs, dvv)
        _, prev_dtgt = p.dma("sync", agv_in_ap[tau * 128:(tau + 1) * 128, :], vrow[:], sem=ld)
    p.barrier()


def emit_l0ffn(p, C, xf, IN, DR):
    p.uid += 1
    tag = f"A{p.uid}_"
    h_bf = C["s_hbf"]
    emit_ln(p, C, xf, h_bf, tag=tag + "ln")

    idf = C["s_ident_f32"]
    o1f = C["s_onesrow_f32"]
    ts, vs, ss = p.sem(tag + "t"), p.sem(tag + "v"), p.sem(tag + "s")
    ld = p.sem(tag + "ld")
    cc = p.sem(tag + "cc")
    pe, pt, dv = p.sem(tag + "pe"), p.sem(tag + "pt"), p.sem(tag + "dv")
    gld = p.sem(tag + "gld")

    es0 = ExitStack()
    q_bf = p.sbuf(tag + "q_bf", [128, NCH * T_LOC], BF16, es=es0)
    o_coll = h_bf
    wbc = p.sbuf(tag + "wbc", [128, 4], F32, es=es0)
    bq_c = p.sbuf(tag + "bq_c", [128, 8], F32, es=es0)
    bk_c = p.sbuf(tag + "bk_c", [128, 8], F32, es=es0)
    bv_c = p.sbuf(tag + "bv_c", [128, 8], F32, es=es0)
    bo_h = p.sbuf(tag + "bo_h", [128, 8], F32, es=es0)
    nq = p.sbuf(tag + "nq", [2, 8], F32, es=es0)
    nk = p.sbuf(tag + "nk", [2, 8], F32, es=es0)
    negBp = p.sbuf(tag + "negBp", [128, 8], F32, es=es0)

    es1 = ExitStack()
    pool = p.sbuf(tag + "pool", [128, 8], F32, es=es1)
    poolb = p.sbuf(tag + "poolb", [128, 8], BF16, es=es1)
    wg1 = p.sbuf(tag + "wg1", [128, 1024], BF16, es=es1)
    wg2 = p.sbuf(tag + "wg2", [128, 4], BF16, es=es1)
    z1 = p.sbuf(tag + "z1", [128, 1], BF16, es=es1)
    lgt = p.sbuf(tag + "lgt", [4, 1], F32, es=es1)
    gt0 = p.sbuf(tag + "gt0", [8, 64], F32, es=es1)
    wcol = p.sbuf(tag + "wcol", [4, 1], F32, es=es1)
    brow = p.sbuf(tag + "brow", [1, 1024], F32, es=es1)
    bcol_sb = p.sbuf(tag + "bcol_sb", [128, 8], F32, es=es1)
    bg1_sb = p.sbuf(tag + "bg1_sb", [128, 1], F32, es=es1)
    bg2_sb = p.sbuf(tag + "bg2_sb", [4, 1], F32, es=es1)
    hsel2 = p.sbuf(tag + "hsel2", [128, 2], BF16, es=es1)
    wc = p.sbuf(tag + "wc", [128, 8 * 1024], BF16, es=es1)
    wstr = p.sbuf(tag + "wstr", [128, 4 * 1024], BF16, es=es1)
    wacc = p.sbuf(tag + "wacc", [128, 1024], F32, es=es1)
    pch = p.sbuf(tag + "pch", [128, T_LOC], BF16, es=es1)
    sqc = p.sbuf(tag + "sqc", [128, T_LOC], BF16, es=es1)
    vrow = p.sbuf(tag + "vrow", [128, 1040], BF16, es=es1)
    nred = p.sbuf(tag + "nred", [2, 8], F32, es=es1)
    pga = p.psum(tag + "pga", [128, 512], es=es1)
    pgb = p.psum(tag + "pgb", [128, 512], es=es1)
    pj = [p.psum(tag + "pj0", [128, TB], es=es1), p.psum(tag + "pj1", [128, TB], es=es1)]
    pn = p.psum(tag + "pn", [2, 512], es=es1)
    ptr = p.psum(tag + "ptr", [128, 1024], BF16, es=es1)
    wl = p.sem(tag + "wl")
    cmb = p.sem(tag + "cmb")

    for name, sb in (("bg1_col", bg1_sb), ("bg2_col", bg2_sb), ("c_hsel2", hsel2), ("bo_col", bcol_sb)):
        p.dma("sync", sb[:], IN[name], sem=ld)
    for c in range(NCH):
        p.on("vector", lambda e, c=c: e.reduce_sum(pool[:, c:c + 1], h_bf[:, c * T_LOC:(c + 1) * T_LOC], axis=AXX))
    pv = p.inc("vector", vs)
    p.wait("sync", vs, pv)
    p.dma("sync", DR["arp_in"], pool[:], sem=ld)
    p.wait("gpsimd", ld, ld.n)
    p.ctl("gpsimd", lambda e: e.collective_compute(
        "AllReduce", ALU.add, GROUPS, [DR["arp_in"].opt()], [DR["arp_out"].opt()]).then_inc(cc.h, 1))
    cc.n += 1
    p.wait("sync", cc, cc.n)
    p.dma("sync", pool[:], DR["arp_out"], sem=ld)
    for c in range(NCH):
        p.dma("gpsimd", wg1[:, 128 * c:128 * (c + 1)], IN["L0_ffn_Wg1"][c * 128:(c + 1) * 128, :], sem=gld)
    p.dma("gpsimd", wg2[:], IN["L0_ffn_Wg2"], sem=gld)
    p.wait("vector", ld, ld.n)
    p.wait("vector", gld, gld.n)
    p.on("vector", lambda e: e.tensor_scalar(poolb[:], pool[:], 1.0 / 8192.0, None, ALU.mult))
    pv = p.inc("vector", vs)
    p.wait("tensor", vs, pv)
    for c in range(NCH):
        p.on("tensor", lambda e, c=c: e.matmul(pga[0:128, 0:1], wg1[:, 128 * c:128 * (c + 1)], poolb[:, c:c + 1],
                                               start=(c == 0), stop=(c == NCH - 1)))
    t1 = p.inc("tensor", ts)
    p.wait("vector", ts, t1)
    p.on("vector", lambda e: e.tensor_scalar(z1[:], pga[0:128, 0:1], bg1_sb[:, 0:1], 0.0, ALU.add, ALU.max))
    pv = p.inc("vector", vs)
    p.wait("tensor", vs, pv)
    p.on("tensor", lambda e: e.matmul(pgb[0:4, 0:1], wg2[:], z1[:], start=True, stop=True))
    t2 = p.inc("tensor", ts)
    p.wait("vector", ts, t2)
    p.on("vector", lambda e: e.tensor_scalar(lgt[:], pgb[0:4, 0:1], bg2_sb[0:4, 0:1], None, ALU.add))
    pv = p.inc("vector", vs)
    p.wait("tensor", vs, pv)
    p.on("tensor", lambda e: e.transpose(pga[0:1, 0:4], lgt[:], idf[0:4, 0:4]))
    t3 = p.inc("tensor", ts)
    p.wait("vector", ts, t3)
    p.on("vector", lambda e: e.tensor_copy(gt0[0:1, 0:4], pga[0:1, 0:4]))
    pv = p.inc("vector", vs)
    p.wait("tensor", vs, pv)
    p.on("tensor", lambda e: e.matmul(pgb[0:4, 0:4], o1f[0:1, 0:4], gt0[0:1, 0:4], start=True, stop=True))
    t4 = p.inc("tensor", ts)
    p.wait("vector", ts, t4)
    p.on("vector", lambda e: e.tensor_tensor(gt0[0:4, 8:12], lgt[:].broadcast_to([4, 4]), pgb[0:4, 0:4], ALU.is_ge))
    p.on("vector", lambda e: e.reduce_sum(gt0[0:4, 12:13], gt0[0:4, 8:12], axis=AXX))
    p.on("vector", lambda e: e.tensor_scalar(gt0[0:4, 13:14], gt0[0:4, 12:13], 2.5, None, ALU.is_ge))
    p.on("vector", lambda e: e.tensor_scalar(gt0[0:4, 14:15], gt0[0:4, 12:13], 3.5, None, ALU.is_ge))
    pv = p.inc("vector", vs)
    p.wait("tensor", vs, pv)
    p.on("tensor", lambda e: e.matmul(pga[0:1, 8:9], gt0[0:4, 14:15], lgt[:], start=True, stop=True))
    t5 = p.inc("tensor", ts)
    p.wait("vector", ts, t5)
    p.on("vector", lambda e: e.tensor_copy(gt0[0:1, 16:17], pga[0:1, 8:9]))
    pv = p.inc("vector", vs)
    p.wait("tensor", vs, pv)
    p.on("tensor", lambda e: e.matmul(pgb[0:4, 8:9], o1f[0:1, 0:4], gt0[0:1, 16:17], start=True, stop=True))
    t6 = p.inc("tensor", ts)
    p.wait("vector", ts, t6)
    p.on("vector", lambda e: e.tensor_scalar(gt0[0:4, 17:18], pgb[0:4, 8:9], -1.0, None, ALU.mult))
    pv = p.inc("vector", vs)
    p.wait("scalar", vs, pv)
    p.on("scalar", lambda e: e.activation(gt0[0:4, 18:19], lgt[:], AF.Exp, bias=gt0[0:4, 17:18]))
    s1 = p.inc("scalar", ss)
    p.wait("vector", ss, s1)
    p.on("vector", lambda e: e.tensor_mul(gt0[0:4, 19:20], gt0[0:4, 18:19], gt0[0:4, 13:14]))
    pv = p.inc("vector", vs)
    p.wait("tensor", vs, pv)
    p.on("tensor", lambda e: e.matmul(pga[0:1, 9:10], gt0[0:4, 13:14], gt0[0:4, 18:19], start=True, stop=True))
    t7 = p.inc("tensor", ts)
    p.wait("vector", ts, t7)
    p.on("vector", lambda e: e.tensor_copy(gt0[0:1, 20:21], pga[0:1, 9:10]))
    pv = p.inc("vector", vs)
    p.wait("tensor", vs, pv)
    p.on("tensor", lambda e: e.matmul(pgb[0:4, 9:10], o1f[0:1, 0:4], gt0[0:1, 20:21], start=True, stop=True))
    t8 = p.inc("tensor", ts)
    p.wait("vector", ts, t8)
    p.on("vector", lambda e: e.tensor_copy(gt0[0:4, 21:22], pgb[0:4, 9:10]))
    p.on("vector", lambda e: e.reciprocal(gt0[0:4, 22:23], gt0[0:4, 21:22]))
    p.on("vector", lambda e: e.tensor_mul(wcol[:], gt0[0:4, 19:20], gt0[0:4, 22:23]))
    pv = p.inc("vector", vs)
    p.wait("tensor", vs, pv)
    p.on("tensor", lambda e: e.transpose(pga[0:1, 10:14], wcol[:], idf[0:4, 0:4]))
    t9 = p.inc("tensor", ts)
    p.wait("vector", ts, t9)
    p.on("vector", lambda e: e.tensor_copy(gt0[0:1, 24:28], pga[0:1, 10:14]))
    pv = p.inc("vector", vs)
    p.wait("tensor", vs, pv)
    p.on("tensor", lambda e: e.matmul(pgb[0:128, 12:16], o1f[0:1, :], gt0[0:1, 24:28], start=True, stop=True))
    t10 = p.inc("tensor", ts)
    p.wait("vector", ts, t10)
    p.on("vector", lambda e: e.tensor_copy(wbc[:], pgb[0:128, 12:16]))
    p.inc("vector", vs)
    bex_sb = p.sbuf(tag + "bex", [4, 1024], F32, es=es1)
    emit_bias_combine(p, wcol, IN["L0_ffn_bq"], bex_sb, brow, bq_c, idf, pga, pgb, ts, vs, ld)
    emit_bias_combine(p, wcol, IN["L0_ffn_bk"], bex_sb, brow, bk_c, idf, pga, pgb, ts, vs, ld)
    emit_bias_combine(p, wcol, IN["L0_ffn_bv"], bex_sb, brow, bv_c, idf, pga, pgb, ts, vs, ld)
    p.wait("vector", ld, ld.n)
    p.on("vector", lambda e: e.tensor_scalar(bo_h[:], bcol_sb[:], 0.5, None, ALU.mult))
    p.inc("vector", vs)
    p.barrier()

    def combine(which):
        win = IN["L0_ffn_W" + which]
        for ci in range(NCH):
            if ci >= 1:
                p.wait("gpsimd", cmb, cmb.n)
            for ex in range(4):
                p.dma("gpsimd", wstr[:, ex * 1024:(ex + 1) * 1024],
                      win[ex, ci * 128:(ci + 1) * 128, :], sem=wl)
            p.wait("vector", wl, wl.n)
            p.on("vector", lambda e: e.tensor_scalar(wacc[:], wstr[:, 0:1024], wbc[:, 0:1], None, ALU.mult))
            for ex in (1, 2):
                p.on("vector", lambda e, ex=ex: e.scalar_tensor_tensor(
                    wacc[:], wstr[:, ex * 1024:(ex + 1) * 1024], wbc[:, ex:ex + 1], wacc[:], ALU.mult, ALU.add))
            p.on("vector", lambda e, ci=ci: e.scalar_tensor_tensor(
                wc[:, ci * 1024:(ci + 1) * 1024], wstr[:, 3 * 1024:4 * 1024], wbc[:, 3:4], wacc[:], ALU.mult, ALU.add))
            p.inc("vector", cmb)
        p.barrier()

    def proj_oc(oc, dest, bias_col, scale2=None):
        for tb in range(NTB):
            bi = tb % 2
            if pt.n >= 2:
                p.wait("tensor", pe, pe.n - 1)
            for ci in range(NCH):
                p.on("tensor", lambda e, o=pj[bi][:, :], l=wc[:, ci * 1024 + oc * 128: ci * 1024 + (oc + 1) * 128],
                     r=h_bf[:, ci * T_LOC + tb * TB: ci * T_LOC + (tb + 1) * TB],
                     s0=(ci == 0), s1=(ci == NCH - 1): e.matmul(o, l, r, start=s0, stop=s1))
            tt = p.inc("tensor", pt)
            p.wait("vector", pt, tt)
            if scale2 is None:
                p.on("vector", lambda e, o=dest[:, tb * TB:(tb + 1) * TB], ps=pj[bi][:, :], b=bias_col:
                     e.tensor_scalar(o, ps, b, None, ALU.add))
            else:
                p.on("vector", lambda e, o=dest[:, tb * TB:(tb + 1) * TB], ps=pj[bi][:, :], b=bias_col:
                     e.tensor_scalar(o, ps, b, scale2, ALU.add, ALU.mult))
            p.inc("vector", pe)

    def norms_of(src2048, oc, ntile):
        p.on("vector", lambda e: e.tensor_mul(sqc[:], src2048, src2048))
        sv2 = p.inc("vector", vs)
        p.wait("tensor", vs, sv2)
        for tb in range(NTB):
            p.on("tensor", lambda e, tb=tb: e.matmul(pn[0:2, 0:512], hsel2[:], sqc[:, tb * TB:(tb + 1) * TB],
                                                     start=True, stop=True))
            tt = p.inc("tensor", pt)
            p.wait("vector", pt, tt)
            p.on("vector", lambda e, tb=tb: e.reduce_max(nred[0:2, tb:tb + 1], pn[0:2, 0:512], axis=AXX))
            p.inc("vector", pe)
            p.wait("tensor", pe, pe.n)
        p.on("vector", lambda e: e.tensor_tensor(nred[0:2, 4:5], nred[0:2, 0:1], nred[0:2, 1:2], ALU.max))
        p.on("vector", lambda e: e.tensor_tensor(nred[0:2, 5:6], nred[0:2, 2:3], nred[0:2, 3:4], ALU.max))
        p.on("vector", lambda e, oc=oc: e.tensor_tensor(ntile[0:2, oc:oc + 1], nred[0:2, 4:5], nred[0:2, 5:6], ALU.max))
        p.inc("vector", vs)

    combine("v")
    vtmp = q_bf
    for oc in range(NCH):
        proj_oc(oc, vtmp[:, oc * T_LOC:(oc + 1) * T_LOC], bv_c[:, oc:oc + 1])
    p.barrier()
    emit_v_to_ag(p, C, vtmp, vrow, ptr, DR["agv_in"], ld, vs, pt)

    combine("k")
    for oc in range(NCH):
        proj_oc(oc, pch, bk_c[:, oc:oc + 1])
        p.barrier()
        norms_of(pch[:], oc, nk)
        dvv = p.inc("vector", vs)
        p.wait("sync", vs, dvv)
        _, dtgt = p.dma("sync", DR["agk_in"][oc * 128:(oc + 1) * 128, :], pch[:], sem=ld)
        p.wait("vector", ld, dtgt)
    p.barrier()

    p.dma("sync", DR["arn_in"], nk[:], sem=ld)
    p.wait("gpsimd", ld, ld.n)
    p.ctl("gpsimd", lambda e: e.collective_compute(
        "AllGather", ALU.bypass, GROUPS, [DR["agk_in"].opt()], [DR["agk_out"].opt()]).then_inc(cc.h, 1))
    cc.n += 1
    p.ctl("gpsimd", lambda e: e.collective_compute(
        "AllGather", ALU.bypass, GROUPS, [DR["agv_in"].opt()], [DR["agv_out"].opt()]).then_inc(cc.h, 1))
    cc.n += 1
    p.ctl("gpsimd", lambda e: e.collective_compute(
        "AllReduce", ALU.max, GROUPS, [DR["arn_in"].opt()], [DR["arn_out"].opt()]).then_inc(cc.h, 1))
    cc.n += 1

    combine("q")
    for oc in range(NCH):
        proj_oc(oc, q_bf[:, oc * T_LOC:(oc + 1) * T_LOC], bq_c[:, oc:oc + 1], scale2=0.125)
        p.barrier()
        norms_of(q_bf[:, oc * T_LOC:(oc + 1) * T_LOC], oc, nq)
    p.barrier()
    es1.close()

    es2 = ExitStack()
    K_pair = p.sbuf(tag + "K_pair", [128, 8192], BF16, es=es2)
    vaug = p.sbuf(tag + "vaug", [128, 64 * 130], BF16, es=es2)
    P_sb = [p.sbuf(tag + "P0", [128, 1024], BF16, es=es2),
            p.sbuf(tag + "P1", [128, 1024], BF16, es=es2)]
    wo_bf = p.sbuf(tag + "wo", [128, 8 * 1024], BF16, es=es2)
    rrow = p.sbuf(tag + "rrow", [1, 512], F32, es=es2)
    rc4 = p.sbuf(tag + "rc4", [128, 4], F32, es=es2)
    rbc = p.sbuf(tag + "rbc", [128, 512], F32, es=es2)
    nrow = p.sbuf(tag + "nrow", [1, 16], F32, es=es2)
    psc = [p.psum(tag + "psc0", [128, 1024], es=es2), p.psum(tag + "psc1", [128, 1024], es=es2)]
    po = [p.psum(tag + "poa", [65, 512], es=es2), p.psum(tag + "pob", [65, 512], es=es2)]
    pmisc = p.psum(tag + "pmisc", [128, 512], es=es2)
    prb = p.psum(tag + "prb", [128, 512], es=es2)
    kl = p.sem(tag + "kl")
    scs = p.sem(tag + "scs")
    exs = p.sem(tag + "exs")
    pvs = p.sem(tag + "pvs")

    for ci in range(NCH):
        p.dma("gpsimd", wo_bf[:, ci * 1024:(ci + 1) * 1024], IN["L0_ffn_Wo"][ci * 128:(ci + 1) * 128, :], sem=gld)

    p.wait("sync", cc, cc.n)
    p.dma("sync", nk[:], DR["arn_out"], sem=ld)
    p.wait("vector", ld, ld.n)
    p.on("vector", lambda e: e.tensor_mul(nq[0:2, 0:8], nq[0:2, 0:8], nk[0:2, 0:8]))
    pv = p.inc("vector", vs)
    p.wait("scalar", vs, pv)
    p.on("scalar", lambda e: e.activation(nq[0:2, 0:8], nq[0:2, 0:8], AF.Sqrt))
    s2_ = p.inc("scalar", ss)
    p.wait("vector", ss, s2_)
    p.on("vector", lambda e: e.tensor_scalar(nq[0:2, 0:8], nq[0:2, 0:8], -0.125, None, ALU.mult))
    pv = p.inc("vector", vs)
    p.wait("tensor", vs, pv)
    p.on("tensor", lambda e: e.transpose(pmisc[0:8, 0:2], nq[0:2, 0:8], idf[0:2, 0:2]))
    tt = p.inc("tensor", ts)
    p.wait("vector", ts, tt)
    p.on("vector", lambda e: e.tensor_reduce(rc4[0:8, 0:1], pmisc[0:8, 0:2], AXX, ALU.min))
    pv = p.inc("vector", vs)
    p.wait("tensor", vs, pv)
    p.on("tensor", lambda e: e.transpose(pmisc[0:1, 8:16], rc4[0:8, 0:1], idf[0:8, 0:8]))
    tt = p.inc("tensor", ts)
    p.wait("vector", ts, tt)
    p.on("vector", lambda e: e.tensor_copy(nrow[0:1, 0:8], pmisc[0:1, 8:16]))
    pv = p.inc("vector", vs)
    p.wait("tensor", vs, pv)
    p.on("tensor", lambda e: e.matmul(prb[0:128, 0:8], o1f[0:1, :], nrow[0:1, 0:8], start=True, stop=True))
    tt = p.inc("tensor", ts)
    p.wait("vector", ts, tt)
    p.on("vector", lambda e: e.tensor_copy(negBp[:, 0:8], prb[0:128, 0:8]))
    p.inc("vector", vs)
    p.barrier()

    halfrow = p.sbuf(tag + "halfrow", [1, 128], F32, es=es2)
    p.dma("sync", halfrow[:], IN["c_half_row_f32"], sem=ld)
    for e_ in P.ENGINES:
        p.wait(e_, ld, ld.n)

    for hp in range(NCH):
        for r in range(4):
            p.dma("sync", K_pair[:, r * 2048:(r + 1) * 2048],
                  DR["agk_out"][r * 1024 + hp * 128: r * 1024 + (hp + 1) * 128, :], sem=kl)
            p.dma("sync", vaug[:, r * 16 * 130:(r + 1) * 16 * 130],
                  bass.AP(DR["agv_out"].tensor, (r * 2048) * 1040 + hp * 130,
                          [[1040, 128], [1040 * 128, 16], [1, 130]]), sem=kl)
        p.wait("tensor", kl, kl.n)
        p.wait("scalar", kl, kl.n)
        for qb in range(NTB):
            qsl = slice(hp * T_LOC + qb * TB, hp * T_LOC + (qb + 1) * TB)
            nkc = 64
            base_ex = exs.n
            base_pv = pvs.n
            for kc in range(nkc):
                bi = kc % 2
                if kc >= 2:
                    p.wait("tensor", exs, base_ex + kc - 1)
                p.on("tensor", lambda e, bi=bi, kc=kc, qsl=qsl: e.matmul(
                    psc[bi][:, 0:512], K_pair[0:64, kc * 128:(kc + 1) * 128], q_bf[0:64, qsl],
                    start=True, stop=True))
                p.on("tensor", lambda e, bi=bi, kc=kc, qsl=qsl: e.matmul(
                    psc[bi][:, 512:1024], K_pair[64:128, kc * 128:(kc + 1) * 128], q_bf[64:128, qsl],
                    start=True, stop=True, tile_position=(64, 0)))
                sct = p.inc("tensor", scs)
                p.wait("scalar", scs, sct)
                if kc >= 2:
                    p.wait("scalar", pvs, base_pv + kc - 1)
                p.on("scalar", lambda e, bi=bi, hp=hp: e.activation(
                    P_sb[bi][:], psc[bi][:], AF.Exp, bias=negBp[:, hp:hp + 1]))
                ext = p.inc("scalar", exs)
                p.wait("tensor", exs, ext)
                p.on("tensor", lambda e, bi=bi, kc=kc, s0=(kc == 0), s1=(kc == nkc - 1): e.matmul(
                    po[0][0:65, :], vaug[:, kc * 130:kc * 130 + 65], P_sb[bi][:, 0:512],
                    start=s0, stop=s1))
                p.on("tensor", lambda e, bi=bi, kc=kc, s0=(kc == 0), s1=(kc == nkc - 1): e.matmul(
                    po[1][0:65, :], vaug[:, kc * 130 + 65:(kc + 1) * 130], P_sb[bi][:, 512:1024],
                    start=s0, stop=s1))
                p.inc("tensor", pvs)
            p.wait("vector", pvs, pvs.n)
            for par in range(2):
                osl_r = slice(par * 64, par * 64 + 64)
                osl_c = slice(hp * T_LOC + qb * TB, hp * T_LOC + (qb + 1) * TB)
                emit_norm_evac(p, C, po[par], o_coll[osl_r, osl_c], halfrow,
                               rrow, rc4, rbc, pmisc, prb, ts, vs)
            p.barrier()

    pjw = [psc[0], psc[1]]
    wos = p.sem(tag + "wos")
    woe = p.sem(tag + "woe")
    p.wait("tensor", gld, gld.n)
    k = 0
    for tb in range(NTB):
        for oc in range(NCH):
            bi = k % 2
            if k >= 2:
                p.wait("tensor", woe, k - 1)
            for ci in range(NCH):
                p.on("tensor", lambda e, o=pjw[bi][:, 0:512], l=wo_bf[:, ci * 1024 + oc * 128: ci * 1024 + (oc + 1) * 128],
                     r=o_coll[:, ci * T_LOC + tb * TB: ci * T_LOC + (tb + 1) * TB],
                     s0=(ci == 0), s1=(ci == NCH - 1): e.matmul(o, l, r, start=s0, stop=s1))
            tt = p.inc("tensor", wos)
            p.wait("vector", wos, tt)
            xs = xf[:, oc * T_LOC + tb * TB: oc * T_LOC + (tb + 1) * TB]
            p.on("vector", lambda e, x_=xs, bi=bi, oc=oc: e.scalar_tensor_tensor(
                x_, pjw[bi][:, 0:512], bo_h[:, oc:oc + 1], x_, ALU.add, ALU.add))
            p.inc("vector", woe)
            k += 1
    p.barrier()
    es2.close()
    es0.close()



def emit_l0attn(p, C, xf, IN, DR, core_hg):
    import os as _os
    p.uid += 1
    tag = f"B{p.uid}_"
    h_bf = C["s_hbf"]
    emit_ln(p, C, xf, h_bf, tag=tag + "ln")

    idf = C["s_ident_f32"]
    o1f = C["s_onesrow_f32"]
    ts, vs, ss = p.sem(tag + "t"), p.sem(tag + "v"), p.sem(tag + "s")
    ld = p.sem(tag + "ld")
    gld = p.sem(tag + "gld")
    cc = p.sem(tag + "cc")
    pe, pt = p.sem(tag + "pe"), p.sem(tag + "pt")

    es0 = ExitStack()
    negB = p.sbuf(tag + "negB", [128, 1], F32, es=es0)
    rtexp = p.sbuf(tag + "rtexp", [8, 1], F32, es=es0)
    es1 = ExitStack()
    wq_bf = p.sbuf(tag + "wq", [128, 8 * 256], BF16, es=es1)
    wk_bf = p.sbuf(tag + "wk", [128, 8 * 256], BF16, es=es1)
    wv_bf = p.sbuf(tag + "wv", [128, 8 * 1024], BF16, es=es1)
    for ci in range(NCH):
        p.dma("gpsimd", wq_bf[:, ci * 256:(ci + 1) * 256], IN["L0_attn_Wq"][ci * 128:(ci + 1) * 128, :], sem=gld)
        p.dma("gpsimd", wk_bf[:, ci * 256:(ci + 1) * 256], IN["L0_attn_Wk"][ci * 128:(ci + 1) * 128, :], sem=gld)
        p.dma("gpsimd", wv_bf[:, ci * 1024:(ci + 1) * 1024], IN["L0_attn_Wv"][ci * 128:(ci + 1) * 128, :], sem=gld)
    bq_c = p.sbuf(tag + "bq_c", [128, 2], F32, es=es1)
    bk_c = p.sbuf(tag + "bk_c", [128, 2], F32, es=es1)
    bv_c = p.sbuf(tag + "bv_c", [128, 8], F32, es=es1)
    qp_c = p.sbuf(tag + "qp_c", [128, 2], F32, es=es1)
    kp_c = p.sbuf(tag + "kp_c", [128, 2], F32, es=es1)
    hsel16 = p.sbuf(tag + "hs16", [128, 8], BF16, es=es1)
    hexp16 = p.sbuf(tag + "he16", [8, 128], BF16, es=es1)
    tmp_t = p.sbuf(tag + "tmpt", [1, 1], F32, es=es1)
    rt8 = p.sbuf(tag + "rt8", [8, 1], F32, es=es1)
    for name, sb in (("L0_bq_col", bq_c), ("L0_bk_col", bk_c), ("L0_bv_col", bv_c),
                     ("L0_qp_col", qp_c), ("L0_kp_col", kp_c),
                     ("c_hsel16", hsel16), ("c_hexp16", hexp16)):
        p.dma("sync", sb[:], IN[name], sem=ld)
    p.dma("sync", tmp_t[:], IN["L0_attn_temperature"], sem=ld)

    qh_bf = p.sbuf(tag + "qh", [128, 2 * T_LOC], BF16, es=es1)
    kh_bf = p.sbuf(tag + "kh", [128, 2 * T_LOC], BF16, es=es1)
    vtmp = p.sbuf(tag + "vtmp", [128, NCH * T_LOC], BF16, es=es1)
    sqc = p.sbuf(tag + "sqc", [128, TB], BF16, es=es1)
    rn8 = p.sbuf(tag + "rn8", [8, TB], F32, es=es1)
    rn8b = p.sbuf(tag + "rn8b", [8, TB], BF16, es=es1)
    pj = [p.psum(tag + "pj0", [128, TB], es=es1), p.psum(tag + "pj1", [128, TB], es=es1)]
    pn8 = p.psum(tag + "pn8", [8, TB], es=es1)
    pex = p.psum(tag + "pex", [128, TB], es=es1)

    p.wait("vector", ld, ld.n)
    p.on("vector", lambda e: e.reciprocal(tmp_t[:], tmp_t[:]))
    pv = p.inc("vector", vs)
    p.wait("tensor", vs, pv)
    p.on("tensor", lambda e: e.matmul(pex[0:128, 0:1], o1f[0:1, :], tmp_t[:], start=True, stop=True))
    tt = p.inc("tensor", ts)
    p.wait("vector", ts, tt)
    p.on("vector", lambda e: e.tensor_copy(rt8[:], pex[0:8, 0:1]))
    p.on("vector", lambda e: e.tensor_copy(rtexp[:], pex[0:8, 0:1]))
    p.on("vector", lambda e: e.tensor_scalar(negB[:], pex[0:128, 0:1], -1.0, None, ALU.mult))
    p.on("vector", lambda e: e.tensor_tensor(negB[:], negB[:], pex[0:128, 0:1], ALU.min))
    p.inc("vector", vs)
    p.barrier()

    def proj_norm(wmat, nocs, bias_col, phase_col, dest, rt):
        for oc in range(nocs):
            k = 0
            for tb in range(NTB):
                bi = k % 2
                p.wait("tensor", pe, max(0, pe.n - 1))
                for ci in range(NCH):
                    p.on("tensor", lambda e, o=pj[bi][:, :], l=wmat[:, ci * 256 + oc * 128: ci * 256 + (oc + 1) * 128],
                         r=h_bf[:, ci * T_LOC + tb * TB: ci * T_LOC + (tb + 1) * TB],
                         s0=(ci == 0), s1=(ci == NCH - 1): e.matmul(o, l, r, start=s0, stop=s1))
                tt = p.inc("tensor", pt)
                p.wait("vector", pt, tt)
                raw = dest[:, oc * T_LOC + tb * TB: oc * T_LOC + (tb + 1) * TB]
                p.on("vector", lambda e, raw=raw, bi=bi, oc=oc: e.tensor_scalar(
                    raw, pj[bi][:, :], bias_col[:, oc:oc + 1], phase_col[:, oc:oc + 1], ALU.add, ALU.add))
                p.inc("vector", pe)
                p.on("vector", lambda e, raw=raw: e.tensor_mul(sqc[:], raw, raw))
                sv2 = p.inc("vector", vs)
                p.wait("tensor", vs, sv2)
                p.on("tensor", lambda e: e.matmul(pn8[0:8, :], hsel16[:], sqc[:], start=True, stop=True))
                tt2 = p.inc("tensor", pt)
                p.wait("scalar", pt, tt2)
                p.on("scalar", lambda e: e.activation(rn8[0:8, :], pn8[0:8, :], AF.Sqrt))
                sg = p.inc("scalar", ss)
                p.wait("vector", ss, sg)
                p.on("vector", lambda e: e.tensor_scalar(rn8[0:8, :], rn8[0:8, :], 1e-12, None, ALU.max))
                p.on("vector", lambda e: e.reciprocal(rn8[0:8, :], rn8[0:8, :]))
                if rt is not None:
                    p.on("vector", lambda e: e.tensor_scalar(rn8[0:8, :], rn8[0:8, :], rt[0:8, 0:1], None, ALU.mult))
                p.on("vector", lambda e: e.tensor_copy(rn8b[0:8, :], rn8[0:8, :]))
                sv3 = p.inc("vector", vs)
                p.wait("tensor", vs, sv3)
                p.on("tensor", lambda e: e.matmul(pex[:, :], hexp16[0:8, :], rn8b[0:8, :], start=True, stop=True))
                tt3 = p.inc("tensor", pt)
                p.wait("vector", pt, tt3)
                p.on("vector", lambda e, raw=raw: e.tensor_tensor(raw, raw, pex[:, :], ALU.mult))
                p.inc("vector", vs)
                k += 1
            p.barrier()

    proj_norm(wq_bf, 2, bq_c, qp_c, qh_bf, rt8)
    proj_norm(wk_bf, 2, bk_c, kp_c, kh_bf, None)
    for oc in range(NCH):
        k = 0
        for tb in range(NTB):
            bi = k % 2
            p.wait("tensor", pe, max(0, pe.n - 1))
            for ci in range(NCH):
                p.on("tensor", lambda e, o=pj[bi][:, :], l=wv_bf[:, ci * 1024 + oc * 128: ci * 1024 + (oc + 1) * 128],
                     r=h_bf[:, ci * T_LOC + tb * TB: ci * T_LOC + (tb + 1) * TB],
                     s0=(ci == 0), s1=(ci == NCH - 1): e.matmul(o, l, r, start=s0, stop=s1))
            tt = p.inc("tensor", pt)
            p.wait("vector", pt, tt)
            p.on("vector", lambda e, oc=oc, tb=tb, bi=bi: e.tensor_scalar(
                vtmp[:, oc * T_LOC + tb * TB: oc * T_LOC + (tb + 1) * TB],
                pj[bi][:, :], bv_c[:, oc:oc + 1], None, ALU.add))
            p.inc("vector", pe)
            k += 1
    p.barrier()
    for oc in range(2):
        p.dma("sync", DR["agq_in"][oc * 128:(oc + 1) * 128, :], qh_bf[:, oc * T_LOC:(oc + 1) * T_LOC], sem=ld)
        p.dma("sync", DR["agk_in"][oc * 128:(oc + 1) * 128, :], kh_bf[:, oc * T_LOC:(oc + 1) * T_LOC], sem=ld)
    for oc in range(NCH):
        p.dma("sync", DR["agvf_in"][oc * 128:(oc + 1) * 128, :], vtmp[:, oc * T_LOC:(oc + 1) * T_LOC], sem=ld)
    p.wait("gpsimd", ld, ld.n)
    p.ctl("gpsimd", lambda e: e.collective_compute(
        "AllGather", ALU.bypass, GROUPS, [DR["agq_in"].opt()], [DR["agq_out"].opt()]).then_inc(cc.h, 1))
    cc.n += 1
    p.ctl("gpsimd", lambda e: e.collective_compute(
        "AllGather", ALU.bypass, GROUPS, [DR["agk_in"].opt()], [DR["agk_out"].opt()]).then_inc(cc.h, 1))
    cc.n += 1
    p.ctl("gpsimd", lambda e: e.collective_compute(
        "AllGather", ALU.bypass, GROUPS, [DR["agvf_in"].opt()], [DR["agvf_out"].opt()]).then_inc(cc.h, 1))
    cc.n += 1
    p.barrier()
    es1.close()

    es2 = ExitStack()
    o_coll = h_bf
    Q2 = p.sbuf(tag + "Q2", [64, 8192], BF16, es=es2)
    K2 = p.sbuf(tag + "K2", [64, 8192], BF16, es=es2)
    v2 = p.sbuf(tag + "v2", [128, 8192], BF16, es=es2)
    vaug = p.sbuf(tag + "vaug", [128, 64 * 130], BF16, es=es2)
    P_sb = [p.sbuf(tag + "P0", [128, 1024], BF16, es=es2),
            p.sbuf(tag + "P1", [128, 1024], BF16, es=es2)]
    masks = p.sbuf(tag + "masks", [128, 4 * 512], BF16, es=es2)
    rrow = p.sbuf(tag + "rrow", [1, 512], F32, es=es2)
    rc4 = p.sbuf(tag + "rc4", [128, 4], F32, es=es2)
    rbc = p.sbuf(tag + "rbc", [128, 512], F32, es=es2)
    onerow = p.sbuf(tag + "onerow", [1, 128], F32, es=es2)
    psc = [p.psum(tag + "psc0", [128, 1024], es=es2), p.psum(tag + "psc1", [128, 1024], es=es2)]
    po = [p.psum(tag + "poa", [65, 512], es=es2), p.psum(tag + "pob", [65, 512], es=es2)]
    pmisc = p.psum(tag + "pmisc", [128, 512], es=es2)
    prb = pmisc
    ptr2 = p.psum(tag + "ptr2", [128, 1024], BF16, es=es2)
    kl = p.sem(tag + "kl")
    scs, exs, pvs = p.sem(tag + "scs"), p.sem(tag + "exs"), p.sem(tag + "pvs")
    mks = p.sem(tag + "mks")
    p.dma("sync", masks[:], IN["c_masks_causal"], sem=ld)
    p.dma("sync", onerow[:], IN["c_ones_row_f32"], sem=ld)
    p.wait("sync", cc, cc.n)

    for j2 in range(128):
        p.on("gpsimd", lambda e, j2=j2: e.memset(vaug[:, 65 * j2 + 64:65 * j2 + 65], 1.0)) if j2 < 2 * 64 else None
    gset2 = p.sem(tag + "gset2")
    p.inc("gpsimd", gset2)
    p.wait("vector", gset2, gset2.n)

    bidx = p.sbuf(tag + "bidx", [128, 56], mybir.dt.uint32, es=es2)
    p.dma("sync", bidx[:], IN["t_bidx"], sem=ld)
    p.wait("gpsimd", ld, ld.n)
    gg = p.sem(tag + "gg")
    for hp in range(2):
        for g in range(8):
            qcol = hp * 16 + g
            kcol = hp * 16 + 8 + g
            vcol = 32 + hp * 8 + g
            if gg.n > 0:
                p.ctl("gpsimd", lambda e, v=gg.n: e.wait_ge(gg.h, v))
            gg.inc(16)
            p.ctl("gpsimd", lambda e, g=g, qcol=qcol, v=gg.n: e.indirect_dma_start(
                Q2[0:48, g * 1024:(g + 1) * 1024], None,
                DR["agq_out"].rearrange("a (b c) -> (a b) c", b=2), IndirectOffsetOnAxis(bidx[:, qcol:qcol + 1], 0)
            ).then_inc(gg.h, 16))
            if gg.n > 0:
                p.ctl("gpsimd", lambda e, v=gg.n: e.wait_ge(gg.h, v))
            gg.inc(16)
            p.ctl("gpsimd", lambda e, g=g, kcol=kcol, v=gg.n: e.indirect_dma_start(
                K2[0:48, g * 1024:(g + 1) * 1024], None,
                DR["agk_out"].rearrange("a (b c) -> (a b) c", b=2), IndirectOffsetOnAxis(bidx[:, kcol:kcol + 1], 0)
            ).then_inc(gg.h, 16))
            if gg.n > 0:
                p.ctl("gpsimd", lambda e, v=gg.n: e.wait_ge(gg.h, v))
            gg.inc(16)
            p.ctl("gpsimd", lambda e, g=g, vcol=vcol, v=gg.n: e.indirect_dma_start(
                v2[:, g * 1024:(g + 1) * 1024], None,
                DR["agvf_out"].rearrange("a (b c) -> (a b) c", b=2), IndirectOffsetOnAxis(bidx[:, vcol:vcol + 1], 0)
            ).then_inc(gg.h, 16))
        for e_ in ("tensor", "scalar", "vector"):
            p.wait(e_, gg, gg.n)
        p.wait("tensor", kl, kl.n)
        p.wait("scalar", kl, kl.n)
        p.wait("vector", kl, kl.n)
        for tau8 in range(8):
            if tau8 >= 1:
                p.wait("tensor", vs, vs.n)
            for sub in range(8):
                tau = tau8 * 8 + sub
                p.on("tensor", lambda e, tau=tau, sub=sub: e.transpose(
                    ptr2[:, sub * 128:(sub + 1) * 128],
                    v2[:, tau * 128:(tau + 1) * 128], C["s_ident_bf"][:]))
            tt = p.inc("tensor", pt)
            p.wait("vector", pt, tt)
            for sub in range(8):
                tau = tau8 * 8 + sub
                for par in range(2):
                    p.on("vector", lambda e, tau=tau, sub=sub, par=par: e.tensor_copy(
                        vaug[:, tau * 130 + 65 * par: tau * 130 + 65 * par + 64],
                        ptr2[:, sub * 128 + par * 64: sub * 128 + par * 64 + 64]))
            p.inc("vector", vs)
        p.barrier()
        for G in range(16):
            nkc = 4 * G + 4
            base_ex = exs.n
            base_pv = pvs.n
            for kc in range(nkc):
                bi = kc % 2
                if kc >= 2:
                    p.wait("tensor", exs, base_ex + kc - 1)
                p.on("tensor", lambda e, bi=bi, kc=kc, G=G: e.matmul(
                    psc[bi][:, 0:512], K2[0:16, kc * 128:(kc + 1) * 128], Q2[0:16, G * 512:(G + 1) * 512],
                    start=True, stop=True))
                p.on("tensor", lambda e, bi=bi, kc=kc, G=G: e.matmul(
                    psc[bi][:, 512:1024], K2[32:48, kc * 128:(kc + 1) * 128], Q2[32:48, G * 512:(G + 1) * 512],
                    start=True, stop=True, tile_position=(32, 0)))
                sct = p.inc("tensor", scs)
                p.wait("scalar", scs, sct)
                if kc >= 2:
                    p.wait("scalar", pvs, base_pv + kc - 1)
                p.on("scalar", lambda e, bi=bi: e.activation(
                    P_sb[bi][:], psc[bi][:], AF.Exp, bias=negB[:, 0:1]))
                ext = p.inc("scalar", exs)
                if kc >= 4 * G:
                    off = kc - 4 * G
                    p.wait("vector", exs, ext)
                    p.on("vector", lambda e, bi=bi, off=off: e.tensor_mul(
                        P_sb[bi][:, 0:512], P_sb[bi][:, 0:512], masks[:, off * 512:(off + 1) * 512]))
                    p.on("vector", lambda e, bi=bi, off=off: e.tensor_mul(
                        P_sb[bi][:, 512:1024], P_sb[bi][:, 512:1024], masks[:, off * 512:(off + 1) * 512]))
                    mkt = p.inc("vector", mks)
                    p.wait("tensor", mks, mkt)
                else:
                    p.wait("tensor", exs, ext)
                p.on("tensor", lambda e, bi=bi, kc=kc, s0=(kc == 0), s1=(kc == nkc - 1): e.matmul(
                    po[0][0:65, :], vaug[:, kc * 130:kc * 130 + 65], P_sb[bi][:, 0:512],
                    start=s0, stop=s1))
                p.on("tensor", lambda e, bi=bi, kc=kc, s0=(kc == 0), s1=(kc == nkc - 1): e.matmul(
                    po[1][0:65, :], vaug[:, kc * 130 + 65:(kc + 1) * 130], P_sb[bi][:, 512:1024],
                    start=s0, stop=s1))
                p.inc("tensor", pvs)
            p.wait("vector", pvs, pvs.n)
            if _os.environ.get("EDBG") == "po" and hp == 0 and G == 1:
                p.on("vector", lambda e: e.tensor_copy(xf[0:65, 0:512], po[0][0:65, :]))
                p.on("vector", lambda e: e.tensor_copy(xf[0:65, 512:1024], po[1][0:65, :]))
                p.inc("vector", vs)
                p.barrier()
                es2.close()
                es0.close()
                return
            for par in range(2):
                osl_r = slice(par * 64, par * 64 + 64)
                osl_c = slice(hp * 8192 + G * TB, hp * 8192 + (G + 1) * TB)
                emit_norm_evac(p, C, po[par], o_coll[osl_r, osl_c], onerow,
                               rrow, rc4, rbc, pmisc, prb, ts, vs)
            p.barrier()

    for X in range(4):
        for hp in range(2):
            for par in range(2):
                for gi, g in enumerate((X, 7 - X)):
                    p.dma("sync", DR["ago%d_in" % X][(2 * hp + par) * 64:(2 * hp + par) * 64 + 64,
                                                     gi * 1024:(gi + 1) * 1024],
                          o_coll[par * 64:par * 64 + 64, hp * 8192 + g * 1024: hp * 8192 + (g + 1) * 1024], sem=ld)
    p.wait("gpsimd", ld, ld.n)
    for X in range(4):
        p.ctl("gpsimd", lambda e, X=X: e.collective_compute(
            "AllGather", ALU.bypass, GROUPS, [DR["ago%d_in" % X].opt()],
            [DR["agofm_out"][X * 1024:(X + 1) * 1024, :].opt()]).then_inc(cc.h, 1))
        cc.n += 1
    p.wait("gpsimd", cc, cc.n)
    for ci in range(NCH):
        ocol = 48 + ci
        if gg.n > 0:
            p.ctl("gpsimd", lambda e, v=gg.n: e.wait_ge(gg.h, v))
        gg.inc(16)
        p.ctl("gpsimd", lambda e, ci=ci, ocol=ocol: e.indirect_dma_start(
            o_coll[:, ci * T_LOC:(ci + 1) * T_LOC], None,
            DR["agofm_out"], IndirectOffsetOnAxis(bidx[:, ocol:ocol + 1], 0)
        ).then_inc(gg.h, 16))
    for e_ in P.ENGINES:
        p.wait(e_, gg, gg.n)
    p.barrier()
    es2.close()
    es3 = ExitStack()
    wo_bf = p.sbuf(tag + "wo", [128, 8 * 1024], BF16, es=es3)
    bo_c = p.sbuf(tag + "bo_c", [128, 8], F32, es=es3)
    pjw = [p.psum(tag + "pjw0", [128, 512], es=es3), p.psum(tag + "pjw1", [128, 512], es=es3)]
    for ci in range(NCH):
        p.dma("gpsimd", wo_bf[:, ci * 1024:(ci + 1) * 1024], IN["L0_attn_Wo"][ci * 128:(ci + 1) * 128, :], sem=gld)
    p.dma("sync", bo_c[:], IN["L0_bo_col"], sem=ld)
    for e_ in P.ENGINES:
        p.wait(e_, ld, ld.n)
        p.wait(e_, gld, gld.n)

    wos = p.sem(tag + "wos")
    woe = p.sem(tag + "woe")
    k = 0
    for tb in range(NTB):
        for oc in range(NCH):
            bi = k % 2
            if k >= 2:
                p.wait("tensor", woe, k - 1)
            for ci in range(NCH):
                p.on("tensor", lambda e, o=pjw[bi][:, :], l=wo_bf[:, ci * 1024 + oc * 128: ci * 1024 + (oc + 1) * 128],
                     r=o_coll[:, ci * T_LOC + tb * TB: ci * T_LOC + (tb + 1) * TB],
                     s0=(ci == 0), s1=(ci == NCH - 1): e.matmul(o, l, r, start=s0, stop=s1))
            tt = p.inc("tensor", wos)
            p.wait("vector", wos, tt)
            xs = xf[:, oc * T_LOC + tb * TB: oc * T_LOC + (tb + 1) * TB]
            p.on("vector", lambda e, x_=xs, bi=bi, oc=oc: e.scalar_tensor_tensor(
                x_, pjw[bi][:, :], bo_c[:, oc:oc + 1], x_, ALU.add, ALU.add))
            p.inc("vector", woe)
            k += 1
    p.barrier()
    es3.close()
    es0.close()



def emit_l1attn(p, C, xf, IN, DR):
    p.uid += 1
    tag = f"E{p.uid}_"
    h_bf = C["s_hbf"]
    idf = C["s_ident_f32"]
    o1f = C["s_onesrow_f32"]
    ts, vs, ss = p.sem(tag + "t"), p.sem(tag + "v"), p.sem(tag + "s")
    ld = p.sem(tag + "ld")
    gld = p.sem(tag + "gld")
    cc = p.sem(tag + "cc")
    pe, pt = p.sem(tag + "pe"), p.sem(tag + "pt")

    es0 = ExitStack()
    mubcE = p.sbuf(tag + "mubcE", [128, T_LOC], F32, es=es0)
    rstdE = p.sbuf(tag + "rstdE", [128, T_LOC], F32, es=es0)
    negB = p.sbuf(tag + "negB", [128, 1], F32, es=es0)
    g1c = p.sbuf(tag + "g1c", [128, 8], F32, es=es0)
    b1c = p.sbuf(tag + "b1c", [128, 8], F32, es=es0)
    for name, sb in (("L1_g1_col", g1c), ("L1_b1_col", b1c)):
        p.dma("sync", sb[:], IN[name], sem=ld)
    for e_ in P.ENGINES:
        p.wait(e_, ld, ld.n)
    emit_ln(p, C, xf, h_bf, affine=(g1c, b1c), keep_stats=(mubcE, rstdE),
            rstd_comb=True, tag=tag + "ln")
    import os as _os
    if _os.environ.get("EDBG") == "h":
        for c in range(NCH):
            p.on("vector", lambda e, c=c: e.tensor_copy(
                xf[:, c * T_LOC:(c + 1) * T_LOC], h_bf[:, c * T_LOC:(c + 1) * T_LOC]))
        p.inc("vector", vs)
        p.barrier()
        es0.close()
        return

    es1 = ExitStack()
    wqs = [p.sbuf(tag + "wqs0", [128, 1024], BF16, es=es1),
           p.sbuf(tag + "wqs1", [128, 1024], BF16, es=es1)]
    wk_bf = p.sbuf(tag + "wk", [128, 8 * 256], BF16, es=es1)
    wv_bf = p.sbuf(tag + "wv", [128, 8 * 256], BF16, es=es1)
    for ci in range(NCH):
        p.dma("gpsimd", wk_bf[:, ci * 256:(ci + 1) * 256], IN["L1_Wk_perm"][ci * 128:(ci + 1) * 128, :], sem=gld)
        p.dma("gpsimd", wv_bf[:, ci * 256:(ci + 1) * 256], IN["L1_attn_Wv"][ci * 128:(ci + 1) * 128, :], sem=gld)
    bq_c = p.sbuf(tag + "bq_c", [128, 8], F32, es=es1)
    bk_c = p.sbuf(tag + "bk_c", [128, 2], F32, es=es1)
    bv_c = p.sbuf(tag + "bv_c", [128, 2], F32, es=es1)
    hsel2 = p.sbuf(tag + "hs2", [128, 2], BF16, es=es1)
    ropeC = p.sbuf(tag + "ropeC", [128, T_LOC], BF16, es=es1)
    ropeS = p.sbuf(tag + "ropeS", [128, T_LOC], BF16, es=es1)
    for name, sb in (("L1_bq_col", bq_c), ("L1_bk_col", bk_c), ("L1_bv_col", bv_c),
                     ("c_hsel2", hsel2), ("t_rope_cos", ropeC), ("t_rope_sin", ropeS)):
        p.dma("sync", sb[:], IN[name], sem=ld)
    q_bf = p.sbuf(tag + "q_bf", [128, NCH * T_LOC], BF16, es=es1)
    kv_bf = p.sbuf(tag + "kv_bf", [128, 4 * T_LOC], BF16, es=es1)
    e1 = C["s_cast0"]
    sq5 = C["s_sq0"]
    nq = p.sbuf(tag + "nq", [2, 12], F32, es=es1)
    nred = p.sbuf(tag + "nred", [2, 8], F32, es=es1)
    pj = [p.psum(tag + "pj0", [128, TB], es=es1), p.psum(tag + "pj1", [128, TB], es=es1)]
    pn = p.psum(tag + "pn", [2, 512], es=es1)

    def proj(wmat, wstride, nocs, bias_col, dest, scale2=None):
        for oc in range(nocs):
            for tb in range(NTB):
                bi = (oc * NTB + tb) % 2
                p.wait("tensor", pe, max(0, pe.n - 1))
                for ci in range(NCH):
                    p.on("tensor", lambda e, o=pj[bi][:, :], l=wmat[:, ci * wstride + oc * 128: ci * wstride + (oc + 1) * 128],
                         r=h_bf[:, ci * T_LOC + tb * TB: ci * T_LOC + (tb + 1) * TB],
                         s0=(ci == 0), s1=(ci == NCH - 1): e.matmul(o, l, r, start=s0, stop=s1))
                tt = p.inc("tensor", pt)
                p.wait("vector", pt, tt)
                dsl = dest[:, oc * T_LOC + tb * TB: oc * T_LOC + (tb + 1) * TB]
                if scale2 is None:
                    p.on("vector", lambda e, o=dsl, bi=bi, oc=oc: e.tensor_scalar(
                        o, pj[bi][:, :], bias_col[:, oc:oc + 1], None, ALU.add))
                else:
                    p.on("vector", lambda e, o=dsl, bi=bi, oc=oc: e.tensor_scalar(
                        o, pj[bi][:, :], bias_col[:, oc:oc + 1], scale2, ALU.add, ALU.mult))
                p.inc("vector", pe)

    p.wait("vector", ld, ld.n)
    p.wait("vector", gld, gld.n)
    p.wait("tensor", ld, ld.n)
    p.wait("tensor", gld, gld.n)
    wql = p.sem(tag + "wql")
    for oc in range(NCH):
        bq2 = oc % 2
        if wql.n > 0:
            p.ctl("gpsimd", lambda e, v=wql.n: e.wait_ge(wql.h, v))
        wql.inc(16)
        p.ctl("gpsimd", lambda e, oc=oc, bq2=bq2: e.dma_start(
            out=wqs[bq2][:], in_=bass.AP(IN["L1_Wq_perm"].tensor, oc * 128,
                                         [[1024, 128], [1024 * 128, 8], [1, 128]])).then_inc(wql.h, 16))
        p.wait("tensor", wql, wql.n)
        for tb in range(NTB):
            bi = (oc * NTB + tb) % 2
            p.wait("tensor", pe, max(0, pe.n - 1))
            for ci in range(NCH):
                p.on("tensor", lambda e, o=pj[bi][:, :], l=wqs[bq2][:, ci * 128:(ci + 1) * 128],
                     r=h_bf[:, ci * T_LOC + tb * TB: ci * T_LOC + (tb + 1) * TB],
                     s0=(ci == 0), s1=(ci == NCH - 1): e.matmul(o, l, r, start=s0, stop=s1))
            tt = p.inc("tensor", pt)
            p.wait("vector", pt, tt)
            dsl = q_bf[:, oc * T_LOC + tb * TB: oc * T_LOC + (tb + 1) * TB]
            p.on("vector", lambda e, o=dsl, bi=bi, oc=oc: e.tensor_scalar(
                o, pj[bi][:, :], bq_c[:, oc:oc + 1], 0.125, ALU.add, ALU.mult))
            p.inc("vector", pe)
        wt = pt.n
        p.ctl("gpsimd", lambda e, v=wt: e.wait_ge(pt.h, v))
    proj(wk_bf, 256, 2, bk_c, kv_bf)
    proj(wv_bf, 256, 2, bv_c, kv_bf[:, 2 * T_LOC:])
    p.barrier()

    def norms2(src, col0, nocs):
        for oc in range(nocs):
            for tb in range(NTB):
                sl = src[:, oc * T_LOC + tb * TB: oc * T_LOC + (tb + 1) * TB]
                p.on("vector", lambda e, sl=sl: e.tensor_mul(sq5[:], sl, sl))
                sv2 = p.inc("vector", vs)
                p.wait("tensor", vs, sv2)
                p.on("tensor", lambda e: e.matmul(pn[0:2, 0:512], hsel2[:], sq5[:], start=True, stop=True))
                tt = p.inc("tensor", pt)
                p.wait("vector", pt, tt)
                p.on("vector", lambda e, tb=tb: e.reduce_max(nred[0:2, tb:tb + 1], pn[0:2, 0:512], axis=AXX))
                p.inc("vector", pe)
                p.wait("tensor", pe, pe.n)
            p.on("vector", lambda e: e.tensor_tensor(nred[0:2, 4:5], nred[0:2, 0:1], nred[0:2, 1:2], ALU.max))
            p.on("vector", lambda e: e.tensor_tensor(nred[0:2, 5:6], nred[0:2, 2:3], nred[0:2, 3:4], ALU.max))
            p.on("vector", lambda e, oc=oc, col0=col0: e.tensor_tensor(
                nq[0:2, col0 + oc:col0 + oc + 1], nred[0:2, 4:5], nred[0:2, 5:6], ALU.max))
            p.inc("vector", vs)

    def rope(dest, nocs):
        for oc in range(nocs):
            for tb in range(NTB):
                d = dest[:, oc * T_LOC + tb * TB: oc * T_LOC + (tb + 1) * TB]
                cc_ = ropeC[:, tb * TB:(tb + 1) * TB]
                ss_ = ropeS[:, tb * TB:(tb + 1) * TB]
                p.on("vector", lambda e, d=d, cc_=cc_: e.tensor_mul(e1[:], d, cc_))
                p.on("vector", lambda e, d=d, ss_=ss_: e.tensor_mul(sq5[0:64, :], d[64:128, :], ss_[64:128, :]))
                p.on("vector", lambda e, d=d, ss_=ss_: e.tensor_mul(sq5[64:128, :], d[0:64, :], ss_[0:64, :]))
                p.on("vector", lambda e, d=d: e.tensor_add(d, e1[:], sq5[:]))
            p.inc("vector", vs)

    import os as _os
    if _os.environ.get("EDBG") == "q":
        for c in range(NCH):
            p.on("vector", lambda e, c=c: e.tensor_copy(
                xf[:, c * T_LOC:(c + 1) * T_LOC], q_bf[:, c * T_LOC:(c + 1) * T_LOC]))
        p.inc("vector", vs)
        p.barrier()
        es1.close()
        es0.close()
        return
    if _os.environ.get("EDBG") == "qr":
        pass
    norms2(q_bf, 0, 8)
    norms2(kv_bf, 8, 2)
    p.on("vector", lambda e: e.reduce_max(nq[0:2, 10:11], nq[0:2, 0:8], axis=AXX))
    p.on("vector", lambda e: e.reduce_max(nq[0:2, 11:12], nq[0:2, 8:10], axis=AXX))
    p.inc("vector", vs)
    rope(q_bf, 8)
    rope(kv_bf, 2)
    p.barrier()
    if _os.environ.get("EDBG") == "kv":
        for c in range(4):
            p.on("vector", lambda e, c=c: e.tensor_copy(
                xf[:, c * T_LOC:(c + 1) * T_LOC], kv_bf[:, c * T_LOC:(c + 1) * T_LOC]))
        p.inc("vector", vs)
        p.barrier()
        es1.close()
        es0.close()
        return
    if _os.environ.get("EDBG") == "qrope":
        for c in range(NCH):
            p.on("vector", lambda e, c=c: e.tensor_copy(
                xf[:, c * T_LOC:(c + 1) * T_LOC], q_bf[:, c * T_LOC:(c + 1) * T_LOC]))
        p.inc("vector", vs)
        p.barrier()
        es1.close()
        es0.close()
        return
    p.dma("sync", DR["arnE_in"], nq[0:2, 10:12], sem=ld)
    for oc in range(NCH):
        p.dma("sync", DR["agqE_in"][oc * 128:(oc + 1) * 128, :], q_bf[:, oc * T_LOC:(oc + 1) * T_LOC], sem=ld)
    for oc in range(2):
        p.dma("sync", DR["agkE_in"][oc * 128:(oc + 1) * 128, :], kv_bf[:, oc * T_LOC:(oc + 1) * T_LOC], sem=ld)
        p.dma("sync", DR["agvE_in"][oc * 128:(oc + 1) * 128, :], kv_bf[:, (2 + oc) * T_LOC:(3 + oc) * T_LOC], sem=ld)
    p.wait("gpsimd", ld, ld.n)
    for nm_in, nm_out in (("agqE_in", "agqE_out"), ("agkE_in", "agkE_out"), ("agvE_in", "agvE_out")):
        p.ctl("gpsimd", lambda e, a=nm_in, b=nm_out: e.collective_compute(
            "AllGather", ALU.bypass, GROUPS, [DR[a].opt()], [DR[b].opt()]).then_inc(cc.h, 1))
        cc.n += 1
    p.ctl("gpsimd", lambda e: e.collective_compute(
        "AllReduce", ALU.max, GROUPS, [DR["arnE_in"].opt()], [DR["arnE_out"].opt()]).then_inc(cc.h, 1))
    cc.n += 1
    p.barrier()
    es1.close()

    es2 = ExitStack()
    o_coll = h_bf
    nsc = p.sbuf(tag + "nsc", [2, 4], F32, es=es2)
    Q2 = p.sbuf(tag + "Q2", [128, 8192], BF16, es=es2)
    K2 = p.sbuf(tag + "K2", [128, 8192], BF16, es=es2)
    v1 = p.sbuf(tag + "v1", [64, 8192], BF16, es=es2)
    vaug = p.sbuf(tag + "vaug", [128, 64 * 65], BF16, es=es2)
    P_sb = [p.sbuf(tag + "P0", [128, 1024], BF16, es=es2),
            p.sbuf(tag + "P1", [128, 1024], BF16, es=es2)]
    masks = p.sbuf(tag + "masks", [128, 6 * 512], BF16, es=es2)
    rrow = p.sbuf(tag + "rrow", [1, 512], F32, es=es2)
    rc4 = p.sbuf(tag + "rc4", [128, 4], F32, es=es2)
    rbc = p.sbuf(tag + "rbc", [128, 512], F32, es=es2)
    onerow = p.sbuf(tag + "onerow", [1, 128], F32, es=es2)
    bidx = p.sbuf(tag + "bidx", [128, 56], mybir.dt.uint32, es=es2)
    psc = [p.psum(tag + "psc0", [128, 1024], es=es2), p.psum(tag + "psc1", [128, 1024], es=es2)]
    po = [p.psum(tag + "poa", [65, 512], es=es2), p.psum(tag + "pob", [65, 512], es=es2)]
    pmisc = p.psum(tag + "pmisc", [128, 512], es=es2)
    prb = pmisc
    ptr2 = p.psum(tag + "ptr2", [128, 1024], BF16, es=es2)
    kl = p.sem(tag + "kl")
    gg = p.sem(tag + "gg")
    scs, exs, pvs = p.sem(tag + "scs"), p.sem(tag + "exs"), p.sem(tag + "pvs")
    mks = p.sem(tag + "mks")
    p.dma("sync", masks[:], IN["c_masks_win"], sem=ld)
    p.dma("sync", onerow[:], IN["c_ones_row_f32"], sem=ld)
    p.dma("sync", bidx[:], IN["t_eidx"], sem=ld)
    p.wait("sync", cc, cc.n)
    p.dma("sync", nsc[0:2, 0:2], DR["arnE_out"], sem=ld)
    p.wait("scalar", ld, ld.n)
    p.on("scalar", lambda e: e.activation(nsc[0:2, 0:2], nsc[0:2, 0:2], AF.Sqrt))
    sg = p.inc("scalar", ss)
    p.wait("vector", ss, sg)
    p.on("vector", lambda e: e.tensor_mul(nsc[0:2, 2:3], nsc[0:2, 0:1], nsc[0:2, 1:2]))
    pv = p.inc("vector", vs)
    p.wait("tensor", vs, pv)
    p.on("tensor", lambda e: e.transpose(pmisc[0:1, 0:2], nsc[0:2, 2:3], idf[0:2, 0:2]))
    tt = p.inc("tensor", ts)
    p.wait("vector", ts, tt)
    p.on("vector", lambda e: e.reduce_max(nsc[0:1, 3:4], pmisc[0:1, 0:2], axis=AXX))
    p.on("vector", lambda e: e.tensor_scalar(nsc[0:1, 3:4], nsc[0:1, 3:4], -1.0, None, ALU.mult))
    pv = p.inc("vector", vs)
    p.wait("tensor", vs, pv)
    p.on("tensor", lambda e: e.matmul(pmisc[0:128, 8:9], o1f[0:1, :], nsc[0:1, 3:4], start=True, stop=True))
    tt = p.inc("tensor", ts)
    p.wait("vector", ts, tt)
    p.on("vector", lambda e: e.tensor_copy(negB[:], pmisc[0:128, 8:9]))
    p.inc("vector", vs)
    for kc in range(64):
        p.on("gpsimd", lambda e, kc=kc: e.memset(vaug[:, 65 * kc + 64:65 * kc + 65], 1.0))
    gset = p.sem(tag + "gset")
    p.inc("gpsimd", gset)
    p.wait("vector", gset, gset.n)
    p.wait("gpsimd", ld, ld.n)

    def gath(dst, src_view, col):
        if gg.n > 0:
            p.ctl("gpsimd", lambda e, v=gg.n: e.wait_ge(gg.h, v))
        gg.inc(16)
        p.ctl("gpsimd", lambda e, dst=dst, src=src_view, col=col: e.indirect_dma_start(
            dst, None, src, IndirectOffsetOnAxis(bidx[:, col:col + 1], 0)).then_inc(gg.h, 16))

    for hp in range(2):
        for g in range(8):
            gath(Q2[:, g * 1024:(g + 1) * 1024],
                 DR["agqE_out"].rearrange("a (b c) -> (a b) c", b=2), hp * 8 + g)
            gath(K2[:, g * 1024:(g + 1) * 1024],
                 DR["agkE_out"].rearrange("a (b c) -> (a b) c", b=2), 16 + g)
            if hp == 0:
                gath(v1[0:64, g * 1024:(g + 1) * 1024],
                     DR["agvE_out"].rearrange("a (b c) -> (a b) c", b=2), 24 + g)
        for e_ in ("tensor", "scalar", "vector"):
            p.wait(e_, gg, gg.n)
        if _os.environ.get("EDBG") == "Q2" and hp == 0:
            for c in range(4):
                p.on("vector", lambda e, c=c: e.tensor_copy(
                    xf[:, c * T_LOC:(c + 1) * T_LOC], Q2[:, c * T_LOC:(c + 1) * T_LOC]))
            p.inc("vector", vs)
            p.barrier()
            es2.close()
            es0.close()
            return
        if hp == 0:
            for t8 in range(8):
                if t8 >= 1:
                    p.wait("tensor", vs, vs.n)
                for sub in range(8):
                    tau = t8 * 8 + sub
                    p.on("tensor", lambda e, tau=tau, sub=sub: e.transpose(
                        ptr2[0:128, sub * 128:sub * 128 + 64],
                        v1[0:64, tau * 128:(tau + 1) * 128], C["s_ident_bf"][0:64, 0:64]))
                tt = p.inc("tensor", pt)
                p.wait("vector", pt, tt)
                for sub in range(8):
                    tau = t8 * 8 + sub
                    p.on("vector", lambda e, tau=tau, sub=sub: e.tensor_copy(
                        vaug[:, tau * 65: tau * 65 + 64], ptr2[0:128, sub * 128: sub * 128 + 64]))
                p.inc("vector", vs)
            p.barrier()
            if _os.environ.get("EDBG") == "vaug":
                p.on("vector", lambda e: e.tensor_copy(xf[:, 0:4160], vaug[:]))
                p.inc("vector", vs)
                p.barrier()
                es2.close()
                es0.close()
                return
        for G in range(16):
            kcs = list(range(max(0, 4 * G - 2), 4 * G + 4))
            base_ex = exs.n
            base_pv = pvs.n
            for ik, kc in enumerate(kcs):
                bi = ik % 2
                if ik >= 2:
                    p.wait("tensor", exs, base_ex + ik - 1)
                p.on("tensor", lambda e, bi=bi, kc=kc, G=G: e.matmul(
                    psc[bi][:, 0:512], K2[0:64, kc * 128:(kc + 1) * 128], Q2[0:64, G * 512:(G + 1) * 512],
                    start=True, stop=True))
                p.on("tensor", lambda e, bi=bi, kc=kc, G=G: e.matmul(
                    psc[bi][:, 512:1024], K2[64:128, kc * 128:(kc + 1) * 128], Q2[64:128, G * 512:(G + 1) * 512],
                    start=True, stop=True, tile_position=(64, 0)))
                sct = p.inc("tensor", scs)
                p.wait("scalar", scs, sct)
                if ik >= 2:
                    p.wait("scalar", pvs, base_pv + ik - 1)
                p.on("scalar", lambda e, bi=bi: e.activation(
                    P_sb[bi][:], psc[bi][:], AF.Exp, bias=negB[:, 0:1]))
                p.inc("scalar", exs)
                mi = kc - 4 * G + 2
                p.wait("vector", exs, exs.n)
                p.on("vector", lambda e, bi=bi, mi=mi: e.tensor_mul(
                    P_sb[bi][:, 0:512], P_sb[bi][:, 0:512], masks[:, mi * 512:(mi + 1) * 512]))
                p.on("vector", lambda e, bi=bi, mi=mi: e.tensor_mul(
                    P_sb[bi][:, 512:1024], P_sb[bi][:, 512:1024], masks[:, mi * 512:(mi + 1) * 512]))
                mkt = p.inc("vector", mks)
                p.wait("tensor", mks, mkt)
                p.on("tensor", lambda e, bi=bi, kc=kc, s0=(ik == 0), s1=(ik == len(kcs) - 1): e.matmul(
                    po[0][0:65, :], vaug[:, kc * 65:kc * 65 + 65], P_sb[bi][:, 0:512],
                    start=s0, stop=s1))
                p.on("tensor", lambda e, bi=bi, kc=kc, s0=(ik == 0), s1=(ik == len(kcs) - 1): e.matmul(
                    po[1][0:65, :], vaug[:, kc * 65:kc * 65 + 65], P_sb[bi][:, 512:1024],
                    start=s0, stop=s1))
                p.inc("tensor", pvs)
            p.wait("vector", pvs, pvs.n)
            if _os.environ.get("EDBG") == "po" and hp == 0 and G == 1:
                p.on("vector", lambda e: e.tensor_copy(xf[0:65, 0:512], po[0][0:65, :]))
                p.on("vector", lambda e: e.tensor_copy(xf[0:65, 512:1024], po[1][0:65, :]))
                p.inc("vector", vs)
                p.barrier()
                es2.close()
                es0.close()
                return
            for par in range(2):
                osl_r = slice(par * 64, par * 64 + 64)
                osl_c = slice(hp * 8192 + G * TB, hp * 8192 + (G + 1) * TB)
                emit_norm_evac(p, C, po[par], o_coll[osl_r, osl_c], onerow,
                               rrow, rc4, rbc, pmisc, prb, ts, vs)
            p.barrier()

    p.wait("sync", vs, vs.n)
    if "dbg" in IN:
        for c in range(NCH):
            p.dma("sync", IN["dbg"][c * 128:(c + 1) * 128, :], o_coll[:, c * T_LOC:(c + 1) * T_LOC], sem=ld)
    import os as _os2
    if _os2.environ.get("EDBG") in ("oc", "agof", "o2"):
        for c in range(NCH):
            p.on("vector", lambda e, c=c: e.tensor_copy(
                xf[:, c * T_LOC:(c + 1) * T_LOC], o_coll[:, c * T_LOC:(c + 1) * T_LOC]))
        p.inc("vector", vs)
        p.barrier()
        es2.close()
        es0.close()
        return
    for X in range(4):
        for hp in range(2):
            for par in range(2):
                for gi, g in enumerate((X, 7 - X)):
                    p.dma("sync", DR["ago%d_in" % X][(2 * hp + par) * 64:(2 * hp + par) * 64 + 64,
                                                     gi * 1024:(gi + 1) * 1024],
                          o_coll[par * 64:par * 64 + 64, hp * 8192 + g * 1024: hp * 8192 + (g + 1) * 1024], sem=ld)
    p.wait("gpsimd", ld, ld.n)
    for X in range(4):
        p.ctl("gpsimd", lambda e, X=X: e.collective_compute(
            "AllGather", ALU.bypass, GROUPS, [DR["ago%d_in" % X].opt()],
            [DR["agofm_out"][X * 1024:(X + 1) * 1024, :].opt()]).then_inc(cc.h, 1))
        cc.n += 1
    p.wait("gpsimd", cc, cc.n)
    for ci in range(NCH):
        gath(o_coll[:, ci * T_LOC:(ci + 1) * T_LOC], DR["agofm_out"][0:128, :], 32 + ci)
    for e_ in P.ENGINES:
        p.wait(e_, gg, gg.n)
    p.barrier()
    import os as _os
    if _os.environ.get("EDBG") == "o":
        for c in range(NCH):
            p.on("vector", lambda e, c=c: e.tensor_copy(
                xf[:, c * T_LOC:(c + 1) * T_LOC], o_coll[:, c * T_LOC:(c + 1) * T_LOC]))
        p.inc("vector", vs)
        p.barrier()
        es2.close()
        es0.close()
        return
    es2.close()

    es3 = ExitStack()
    wos_ = [p.sbuf(tag + "wosa", [128, 1024], BF16, es=es3),
            p.sbuf(tag + "wosb", [128, 1024], BF16, es=es3)]
    x2tb = p.sbuf(tag + "x2tb", [128, 8 * TB], F32, es=es3)
    h2tb = p.sbuf(tag + "h2tb", [128, 8 * TB], BF16, es=es3)
    x1s = p.sbuf(tag + "x1s", [128, TB], F32, es=es3)
    f1 = p.sbuf(tag + "f1", [128, 32 * TB], BF16, es=es3)
    wf1s = [p.sbuf(tag + "wf1a", [128, 1024], BF16, es=es3),
            p.sbuf(tag + "wf1b", [128, 1024], BF16, es=es3)]
    wf2oc = p.sbuf(tag + "wf2oc", [128, 32 * 128], BF16, es=es3)
    bo_c = p.sbuf(tag + "bo_c", [128, 8], F32, es=es3)
    g2c = p.sbuf(tag + "g2c", [128, 8], F32, es=es3)
    b2c = p.sbuf(tag + "b2c", [128, 8], F32, es=es3)
    bf1c = p.sbuf(tag + "bf1c", [128, 32], F32, es=es3)
    bf2c = p.sbuf(tag + "bf2c", [128, 8], F32, es=es3)
    s1r2 = C["s_lnrow1"]
    s2r2 = C["s_lnrow2"]
    pja = p.psum(tag + "pja", [128, TB], es=es3)
    pjb = p.psum(tag + "pjb", [128, TB], es=es3)
    ps1 = p.psum(tag + "ps1", [1, TB], es=es3)
    ps2 = p.psum(tag + "ps2", [1, TB], es=es3)
    pb1 = p.psum(tag + "pb1", [128, TB], es=es3)
    pb2 = p.psum(tag + "pb2", [128, TB], es=es3)
    pjf = [pja, pjb]
    for name, sb in (("L1_bo_col", bo_c), ("L1_g2_col", g2c), ("L1_b2_col", b2c),
                     ("L1_bf1_col", bf1c), ("L1_bf2_col", bf2c)):
        p.dma("sync", sb[:], IN[name], sem=ld)
    for e_ in P.ENGINES:
        p.wait(e_, ld, ld.n)
        p.wait(e_, gld, gld.n)
    wfl = p.sem(tag + "wfl")
    mubc, rstdbc = C["s_mubc"], C["s_rstdbc"]
    var_sb, srt, tsc = C["s_lnvar"], C["s_lnsrt"], C["s_lntmp"]
    onesb = C["s_onescol_bf"]
    for tb in range(NTB):
        tsl = slice(tb * TB, (tb + 1) * TB)
        wo_mm_t = {}
        for oc in range(NCH):
            bi = oc % 2
            if wfl.n > 0:
                p.ctl("gpsimd", lambda e, v=wfl.n: e.wait_ge(wfl.h, v))
            if oc >= 2:
                p.ctl("gpsimd", lambda e, v=wo_mm_t[oc - 2]: e.wait_ge(pt.h, v))
            wfl.inc(16)
            p.ctl("gpsimd", lambda e, oc=oc, bi=bi: e.dma_start(
                out=wos_[bi][:], in_=bass.AP(IN["L1_attn_Wo"].tensor, oc * 128,
                                             [[1024, 128], [1024 * 128, 8], [1, 128]])).then_inc(wfl.h, 16))
            p.wait("tensor", wfl, wfl.n)
            if oc >= 2:
                p.wait("tensor", pe, pe.n - 1)
            for ci in range(NCH):
                p.on("tensor", lambda e, o=pjf[bi][:, :], l=wos_[bi][:, ci * 128:(ci + 1) * 128],
                     r=o_coll[:, ci * T_LOC + tb * TB: ci * T_LOC + (tb + 1) * TB],
                     s0=(ci == 0), s1=(ci == NCH - 1): e.matmul(o, l, r, start=s0, stop=s1))
            tt = p.inc("tensor", pt)
            p.wait("vector", pt, tt)
            wo_mm_t[oc] = pt.n
            xs_ = xf[:, oc * T_LOC + tb * TB: oc * T_LOC + (tb + 1) * TB]
            p.on("vector", lambda e, x_=xs_, m=mubcE[:, tsl]: e.tensor_sub(x1s[:], x_, m))
            p.on("vector", lambda e, r=rstdE[:, tsl]: e.tensor_mul(x1s[:], x1s[:], r))
            p.on("vector", lambda e, bi=bi, oc=oc: e.scalar_tensor_tensor(
                x2tb[:, oc * TB:(oc + 1) * TB], pjf[bi][:, :], bo_c[:, oc:oc + 1], x1s[:], ALU.add, ALU.add))
            p.inc("vector", pe)
        for c in range(NCH):
            bi2 = c % 2
            xc = [C["s_cast0"], C["s_cast1"]]
            x2s = [C["s_sq0"], C["s_sq1"]]
            if c >= 2:
                p.wait("vector", ts, ts.n - 1)
            p.on("vector", lambda e, c=c, b=bi2: e.tensor_copy(xc[b][:], x2tb[:, c * TB:(c + 1) * TB]))
            p.on("vector", lambda e, c=c, b=bi2: e.tensor_mul(x2s[b][:], xc[b][:], xc[b][:]))
            vt = p.inc("vector", vs)
            p.wait("tensor", vs, vt)
            p.on("tensor", lambda e, b=bi2, s0=(c == 0), s1=(c == NCH - 1): e.matmul(
                ps1[0:1, :], onesb[:, 0:1], xc[b][:], start=s0, stop=s1))
            p.on("tensor", lambda e, b=bi2, s0=(c == 0), s1=(c == NCH - 1): e.matmul(
                ps2[0:1, :], onesb[:, 0:1], x2s[b][:], start=s0, stop=s1))
            p.inc("tensor", ts)
        p.wait("vector", ts, ts.n)
        p.on("vector", lambda e: e.tensor_copy(s1r2[:], ps1[0:1, :]))
        p.on("vector", lambda e: e.tensor_copy(s2r2[:], ps2[0:1, :]))
        ve = p.inc("vector", vs)
        p.wait("tensor", vs, ve)
        p.on("tensor", lambda e: e.matmul(pb1[:, :], o1f[0:1, :], s1r2[0:1, :], start=True, stop=True))
        p.on("tensor", lambda e: e.matmul(pb2[:, :], o1f[0:1, :], s2r2[0:1, :], start=True, stop=True))
        tb1 = p.inc("tensor", ts)
        p.wait("vector", ts, tb1)
        p.on("vector", lambda e: e.tensor_scalar(mubc[:], pb1[:, :], 1.0 / D, None, ALU.mult))
        p.on("vector", lambda e: e.tensor_scalar(tsc[:], pb2[:, :], 1.0 / D, None, ALU.mult))
        p.on("vector", lambda e: e.tensor_mul(srt[:], mubc[:], mubc[:]))
        p.on("vector", lambda e: e.tensor_sub(var_sb[:], tsc[:], srt[:]))
        vv = p.inc("vector", vs)
        p.wait("scalar", vs, vv)
        p.on("scalar", lambda e: e.activation(srt[:], var_sb[:], AF.Sqrt, bias=C["s_epscol"][:, 0:1]))
        sg = p.inc("scalar", ss)
        p.wait("vector", ss, sg)
        p.on("vector", lambda e: e.reciprocal(rstdbc[:], srt[:]))
        for c in range(NCH):
            p.on("vector", lambda e, c=c: e.tensor_sub(tsc[:], x2tb[:, c * TB:(c + 1) * TB], mubc[:]))
            p.on("vector", lambda e: e.tensor_mul(tsc[:], tsc[:], rstdbc[:]))
            p.on("vector", lambda e, c=c: e.tensor_scalar(
                h2tb[:, c * TB:(c + 1) * TB], tsc[:], g2c[:, c:c + 1], b2c[:, c:c + 1], ALU.mult, ALU.add))
        p.inc("vector", vs)
        p.barrier()
        f1_mm_t = {}
        for ofc in range(32):
            bi = ofc % 2
            if wfl.n > 0:
                p.ctl("gpsimd", lambda e, v=wfl.n: e.wait_ge(wfl.h, v))
            if ofc >= 2:
                p.ctl("gpsimd", lambda e, v=f1_mm_t[ofc - 2]: e.wait_ge(pt.h, v))
            wfl.inc(16)
            p.ctl("gpsimd", lambda e, ofc=ofc, bi=bi: e.dma_start(
                out=wf1s[bi][:], in_=bass.AP(IN["L1_attn_Wf1"].tensor, ofc * 128,
                                             [[4096, 128], [4096 * 128, 8], [1, 128]])).then_inc(wfl.h, 16))
            p.wait("tensor", wfl, wfl.n)
            if ofc >= 2:
                p.wait("tensor", ss, ss.n - 1)
            for ci in range(NCH):
                p.on("tensor", lambda e, bi=bi, ci=ci, s0=(ci == 0), s1=(ci == NCH - 1): e.matmul(
                    pjf[bi][:, :], wf1s[bi][:, ci * 128:(ci + 1) * 128], h2tb[:, ci * TB:(ci + 1) * TB],
                    start=s0, stop=s1))
            tt = p.inc("tensor", pt)
            f1_mm_t[ofc] = tt
            p.wait("scalar", pt, tt)
            p.on("scalar", lambda e, ofc=ofc, bi=bi: e.activation(
                f1[:, ofc * TB:(ofc + 1) * TB], pjf[bi][:, :], AF.Gelu, bias=bf1c[:, ofc:ofc + 1]))
            p.inc("scalar", ss)
        p.barrier()
        f2_mm_t = {}
        for oc in range(NCH):
            bi = oc % 2
            if oc >= 2:
                p.wait("tensor", pe, pe.n - 1)
            if wfl.n > 0:
                p.ctl("gpsimd", lambda e, v=wfl.n: e.wait_ge(wfl.h, v))
            if oc >= 1:
                p.ctl("gpsimd", lambda e, v=f2_mm_t[oc - 1]: e.wait_ge(pt.h, v))
            wfl.inc(16)
            p.ctl("gpsimd", lambda e, oc=oc: e.dma_start(
                out=wf2oc[:], in_=bass.AP(IN["L1_attn_Wf2"].tensor, oc * 128,
                                          [[1024, 128], [1024 * 128, 32], [1, 128]])).then_inc(wfl.h, 16))
            p.wait("tensor", wfl, wfl.n)
            for ci in range(32):
                p.on("tensor", lambda e, bi=bi, ci=ci, s0=(ci == 0), s1=(ci == 31): e.matmul(
                    pjf[bi][:, :], wf2oc[:, ci * 128:(ci + 1) * 128], f1[:, ci * TB:(ci + 1) * TB],
                    start=s0, stop=s1))
            tt = p.inc("tensor", pt)
            f2_mm_t[oc] = tt
            p.wait("vector", pt, tt)
            xs_ = xf[:, oc * T_LOC + tb * TB: oc * T_LOC + (tb + 1) * TB]
            p.on("vector", lambda e, bi=bi, oc=oc: e.scalar_tensor_tensor(
                x1s[:], pjf[bi][:, :], bf2c[:, oc:oc + 1], x2tb[:, oc * TB:(oc + 1) * TB], ALU.add, ALU.add))
            p.on("vector", lambda e, x_=xs_: e.tensor_add(x_, x_, x1s[:]))
            p.inc("vector", pe)
        p.barrier()
    es3.close()
    es0.close()

def prep_col(vec, parts=128):
    n = vec.shape[-1]
    return np.ascontiguousarray(vec.reshape(n // parts, parts).T.astype(np.float32))


def shard_x(x):
    outs = []
    for core in range(NCORE):
        b, j = core // 4, core % 4
        xt = x[b].T
        outs.append(np.ascontiguousarray(
            np.concatenate([xt[:, 1024 * j:1024 * (j + 1)],
                            xt[:, 1024 * (7 - j):1024 * (8 - j)]], axis=1), dtype=np.float32))
    return outs


def unshard_x(parts):
    x = np.zeros((2, 8192, 1024), dtype=np.float32)
    for core in range(NCORE):
        b, j = core // 4, core % 4
        x[b, 1024 * j:1024 * (j + 1), :] = parts[core][:, 0:1024].T
        x[b, 1024 * (7 - j):1024 * (8 - j), :] = parts[core][:, 1024:2048].T
    return x


def declare_consts(nc, IN):
    for name, (shape, dt) in CONST_SHAPES.items():
        IN[name] = nc.dram_tensor(name, shape, dt, kind="ExternalInput").ap()


def const_maps(m, consts):
    for name in CONST_SHAPES:
        m[name] = consts[name]


def build_D():
    nc = bass.Bass("TRN2", target_bir_lowering=False, debug=False)
    es = ExitStack()
    p = P(nc, es)
    IN = {}
    IN["xin"] = nc.dram_tensor("xin", [D, T_LOC], F32, kind="ExternalInput").ap()
    xout = nc.dram_tensor("xout", [D, T_LOC], F32, kind="ExternalOutput").ap()
    IN["L1_ffn_W"] = nc.dram_tensor("L1_ffn_W", [8, D, D], F32, kind="ExternalInput").ap()
    IN["L1_ffn_b"] = nc.dram_tensor("L1_ffn_b", [8, D], F32, kind="ExternalInput").ap()
    IN["L1_ffn_logits"] = nc.dram_tensor("L1_ffn_logits", [8, 1], F32, kind="ExternalInput").ap()
    IN["L1_ffn_u"] = nc.dram_tensor("L1_ffn_u", [8, 1], F32, kind="ExternalInput").ap()
    declare_consts(nc, IN)

    xf = es.enter_context(nc.sbuf_tensor("xf", [128, NCH * T_LOC], F32))
    C = alloc_common(p, IN)

    xl = p.sem("xload")
    for c in range(NCH):
        p.dma("sync", xf[:, c * T_LOC:(c + 1) * T_LOC], IN["xin"][c * 128:(c + 1) * 128, :], sem=xl)
    for e in P.ENGINES:
        p.wait(e, xl, xl.n)

    import os
    mode = os.environ.get("MODE", "full")
    if mode == "ln":
        emit_ln(p, C, xf, C["s_hbf"], tag="lnonly")
        hv = p.sem("hv")
        for c in range(NCH):
            p.on("vector", lambda e, c=c: e.tensor_copy(
                xf[:, c * T_LOC:(c + 1) * T_LOC], C["s_hbf"][:, c * T_LOC:(c + 1) * T_LOC]))
        p.inc("vector", hv)
        p.barrier()
    elif mode == "none":
        pass
    else:
        emit_l1ffn(p, C, xf, IN)

    so = p.sem("xstore")
    for c in range(NCH):
        p.dma("sync", xout[c * 128:(c + 1) * 128, :], xf[:, c * T_LOC:(c + 1) * T_LOC], sem=so)
    p.wait("sync", so, so.n)
    p.run()
    es.close()
    return nc



def build_A():
    nc = bass.Bass("TRN2", target_bir_lowering=False, debug=False)
    es = ExitStack()
    p = P(nc, es)
    IN = {}
    IN["xin"] = nc.dram_tensor("xin", [D, T_LOC], F32, kind="ExternalInput").ap()
    xout = nc.dram_tensor("xout", [D, T_LOC], F32, kind="ExternalOutput").ap()
    for w in ("Wq", "Wk", "Wv"):
        IN["L0_ffn_" + w] = nc.dram_tensor("L0_ffn_" + w, [4, D, D], F32, kind="ExternalInput").ap()
    for b in ("bq", "bk", "bv"):
        IN["L0_ffn_" + b] = nc.dram_tensor("L0_ffn_" + b, [4, D], F32, kind="ExternalInput").ap()
    IN["L0_ffn_Wg1"] = nc.dram_tensor("L0_ffn_Wg1", [D, 128], F32, kind="ExternalInput").ap()
    IN["L0_ffn_Wg2"] = nc.dram_tensor("L0_ffn_Wg2", [128, 4], F32, kind="ExternalInput").ap()
    IN["L0_ffn_Wo"] = nc.dram_tensor("L0_ffn_Wo", [D, D], F32, kind="ExternalInput").ap()
    IN["bg1_col"] = nc.dram_tensor("bg1_col", [128, 1], F32, kind="ExternalInput").ap()
    IN["bg2_col"] = nc.dram_tensor("bg2_col", [4, 1], F32, kind="ExternalInput").ap()
    IN["bo_col"] = nc.dram_tensor("bo_col", [128, 8], F32, kind="ExternalInput").ap()
    declare_consts(nc, IN)
    DR = {
        "arp_in": nc.dram_tensor("arp_in", [128, 8], F32, kind="Internal").ap(),
        "arp_out": nc.dram_tensor("arp_out", [128, 8], F32, kind="Internal").ap(),
        "arn_in": nc.dram_tensor("arn_in", [2, 8], F32, kind="Internal").ap(),
        "arn_out": nc.dram_tensor("arn_out", [2, 8], F32, kind="Internal").ap(),
        "agk_in": nc.dram_tensor("agk_in", [D, T_LOC], BF16, kind="Internal").ap(),
        "agk_out": nc.dram_tensor("agk_out", [4 * D, T_LOC], BF16, kind="Internal").ap(),
        "agv_in": nc.dram_tensor("agv_in", [T_LOC, 1040], BF16, kind="Internal").ap(),
        "agv_out": nc.dram_tensor("agv_out", [8192, 1040], BF16, kind="Internal").ap(),
    }
    xf = es.enter_context(nc.sbuf_tensor("xf", [128, NCH * T_LOC], F32))
    C = alloc_common(p, IN)
    xl = p.sem("xload")
    for c in range(NCH):
        p.dma("sync", xf[:, c * T_LOC:(c + 1) * T_LOC], IN["xin"][c * 128:(c + 1) * 128, :], sem=xl)
    for e in P.ENGINES:
        p.wait(e, xl, xl.n)
    emit_l0ffn(p, C, xf, IN, DR)
    so = p.sem("xstore")
    for c in range(NCH):
        p.dma("sync", xout[c * 128:(c + 1) * 128, :], xf[:, c * T_LOC:(c + 1) * T_LOC], sem=so)
    p.wait("sync", so, so.n)
    p.run()
    es.close()
    return nc


def a_in_maps(INS, consts, xs):
    in_maps = []
    for core in range(NCORE):
        m = {"xin": xs[core]}
        for w in ("Wq", "Wk", "Wv", "bq", "bk", "bv", "Wg1", "Wg2", "Wo"):
            m["L0_ffn_" + w] = INS["L0_ffn_" + w]
        m["bg1_col"] = np.ascontiguousarray(INS["L0_ffn_bg1"].reshape(128, 1))
        m["bg2_col"] = np.ascontiguousarray(INS["L0_ffn_bg2"].reshape(4, 1))
        m["bo_col"] = prep_col(INS["L0_ffn_bo"])
        const_maps(m, consts)
        in_maps.append(m)
    return in_maps



def build_B():
    nc = bass.Bass("TRN2", target_bir_lowering=False, debug=False)
    es = ExitStack()
    p = P(nc, es)
    IN = {}
    IN["xin"] = nc.dram_tensor("xin", [D, T_LOC], F32, kind="ExternalInput").ap()
    xout = nc.dram_tensor("xout", [D, T_LOC], F32, kind="ExternalOutput").ap()
    IN["L0_attn_Wq"] = nc.dram_tensor("L0_attn_Wq", [D, 256], F32, kind="ExternalInput").ap()
    IN["L0_attn_Wk"] = nc.dram_tensor("L0_attn_Wk", [D, 256], F32, kind="ExternalInput").ap()
    IN["L0_attn_Wv"] = nc.dram_tensor("L0_attn_Wv", [D, D], F32, kind="ExternalInput").ap()
    IN["L0_attn_Wo"] = nc.dram_tensor("L0_attn_Wo", [D, D], F32, kind="ExternalInput").ap()
    IN["L0_attn_temperature"] = nc.dram_tensor("L0_attn_temperature", [1, 1], F32, kind="ExternalInput").ap()
    for nm, shape in (("L0_bq_col", [128, 2]), ("L0_bk_col", [128, 2]), ("L0_bv_col", [128, 8]),
                      ("L0_qp_col", [128, 2]), ("L0_kp_col", [128, 2]), ("L0_bo_col", [128, 8])):
        IN[nm] = nc.dram_tensor(nm, shape, F32, kind="ExternalInput").ap()
    IN["t_core_idx"] = nc.dram_tensor("t_core_idx", [1, 1], mybir.dt.int32, kind="ExternalInput").ap()
    declare_consts(nc, IN)
    DR = {
        "agq_in": nc.dram_tensor("agq_in", [256, T_LOC], BF16, kind="Internal").ap(),
        "agq_out": nc.dram_tensor("agq_out", [1024, T_LOC], BF16, kind="Internal").ap(),
        "agk_in": nc.dram_tensor("agk_in", [256, T_LOC], BF16, kind="Internal").ap(),
        "agk_out": nc.dram_tensor("agk_out", [1024, T_LOC], BF16, kind="Internal").ap(),
        "agvf_in": nc.dram_tensor("agvf_in", [1024, T_LOC], BF16, kind="Internal").ap(),
        "agvf_out": nc.dram_tensor("agvf_out", [4096, T_LOC], BF16, kind="Internal").ap(),
        "ago0_in": nc.dram_tensor("ago0_in", [256, T_LOC], BF16, kind="Internal").ap(),
        "ago1_in": nc.dram_tensor("ago1_in", [256, T_LOC], BF16, kind="Internal").ap(),
        "ago2_in": nc.dram_tensor("ago2_in", [256, T_LOC], BF16, kind="Internal").ap(),
        "ago3_in": nc.dram_tensor("ago3_in", [256, T_LOC], BF16, kind="Internal").ap(),
        "agofm_out": nc.dram_tensor("agofm_out", [4096, T_LOC], BF16, kind="Internal").ap(),
    }
    IN["t_bidx"] = nc.dram_tensor("t_bidx", [128, 56], mybir.dt.uint32, kind="ExternalInput").ap()
    xf = es.enter_context(nc.sbuf_tensor("xf", [128, NCH * T_LOC], F32))
    C = alloc_common(p, IN)
    xl = p.sem("xload")
    for c in range(NCH):
        p.dma("sync", xf[:, c * T_LOC:(c + 1) * T_LOC], IN["xin"][c * 128:(c + 1) * 128, :], sem=xl)
    for e in P.ENGINES:
        p.wait(e, xl, xl.n)
    emit_l0attn(p, C, xf, IN, DR, None)
    so = p.sem("xstore")
    for c in range(NCH):
        p.dma("sync", xout[c * 128:(c + 1) * 128, :], xf[:, c * T_LOC:(c + 1) * T_LOC], sem=so)
    p.wait("sync", so, so.n)
    p.run()
    es.close()
    return nc



def build_bidx(j):
    t = np.zeros((128, 56), dtype=np.uint32)
    for hp in range(2):
        for g in range(8):
            rblk = g if g <= 3 else 7 - g
            co = 0 if g <= 3 else 1024
            for par in range(2):
                rows = 2 * (256 * rblk + 16 * (4 * j + 2 * hp + par)) + (co // 1024)
                for pp in range(16):
                    t[32 * par + pp, hp * 16 + g] = rows + 2 * pp
                    t[32 * par + pp, hp * 16 + 8 + g] = rows + 2 * pp
            vrows = 2 * (1024 * rblk + 64 * (4 * j + 2 * hp)) + (co // 1024)
            for pp in range(128):
                t[pp, 32 + hp * 8 + g] = vrows + 2 * pp
    for ci in range(8):
        for pp in range(128):
            t[pp, 48 + ci] = 1024 * j + ci * 128 + pp
    return np.ascontiguousarray(t)

def b_in_maps(INS, consts, xs):
    in_maps = []
    for core in range(NCORE):
        m = {"xin": xs[core]}
        for w in ("Wq", "Wk", "Wv", "Wo"):
            m["L0_attn_" + w] = INS["L0_attn_" + w]
        m["L0_attn_temperature"] = np.ascontiguousarray(INS["L0_attn_temperature"].reshape(1, 1))
        m["L0_bq_col"] = prep_col(INS["L0_attn_bq"])
        m["L0_bk_col"] = prep_col(INS["L0_attn_bk"])
        m["L0_bv_col"] = prep_col(INS["L0_attn_bv"])
        m["L0_bo_col"] = prep_col(INS["L0_attn_bo"])
        m["L0_qp_col"] = prep_col(INS["L0_attn_q_phase"].reshape(256))
        m["L0_kp_col"] = prep_col(INS["L0_attn_k_phase"].reshape(256))
        m["t_core_idx"] = np.array([[core % 4]], dtype=np.int32)
        m["t_bidx"] = build_bidx(core % 4)
        const_maps(m, consts)
        in_maps.append(m)
    return in_maps



ROPE_PERM = np.concatenate([np.concatenate([np.arange(0, 32), np.arange(64, 96),
                                            np.arange(32, 64), np.arange(96, 128)]) + 128 * b
                            for b in range(8)])


def build_E():
    nc = bass.Bass("TRN2", target_bir_lowering=False, debug=False)
    es = ExitStack()
    p = P(nc, es)
    IN = {}
    IN["xin"] = nc.dram_tensor("xin", [D, T_LOC], F32, kind="ExternalInput").ap()
    xout = nc.dram_tensor("xout", [D, T_LOC], F32, kind="ExternalOutput").ap()
    IN["L1_Wq_perm"] = nc.dram_tensor("L1_Wq_perm", [D, D], F32, kind="ExternalInput").ap()
    IN["L1_Wk_perm"] = nc.dram_tensor("L1_Wk_perm", [D, 256], F32, kind="ExternalInput").ap()
    IN["L1_attn_Wv"] = nc.dram_tensor("L1_attn_Wv", [D, 256], F32, kind="ExternalInput").ap()
    IN["L1_attn_Wo"] = nc.dram_tensor("L1_attn_Wo", [D, D], F32, kind="ExternalInput").ap()
    IN["L1_attn_Wf1"] = nc.dram_tensor("L1_attn_Wf1", [D, 4096], F32, kind="ExternalInput").ap()
    IN["L1_attn_Wf2"] = nc.dram_tensor("L1_attn_Wf2", [4096, D], F32, kind="ExternalInput").ap()
    for nm, shape in (("L1_bq_col", [128, 8]), ("L1_bk_col", [128, 2]), ("L1_bv_col", [128, 2]),
                      ("L1_bo_col", [128, 8]), ("L1_g1_col", [128, 8]), ("L1_b1_col", [128, 8]),
                      ("L1_g2_col", [128, 8]), ("L1_b2_col", [128, 8]),
                      ("L1_bf1_col", [128, 32]), ("L1_bf2_col", [128, 8])):
        IN[nm] = nc.dram_tensor(nm, shape, F32, kind="ExternalInput").ap()
    IN["t_rope_cos"] = nc.dram_tensor("t_rope_cos", [128, T_LOC], BF16, kind="ExternalInput").ap()
    IN["t_rope_sin"] = nc.dram_tensor("t_rope_sin", [128, T_LOC], BF16, kind="ExternalInput").ap()
    IN["t_eidx"] = nc.dram_tensor("t_eidx", [128, 56], mybir.dt.uint32, kind="ExternalInput").ap()
    declare_consts(nc, IN)
    DR = {
        "agqE_in": nc.dram_tensor("agqE_in", [D, T_LOC], BF16, kind="Internal").ap(),
        "agqE_out": nc.dram_tensor("agqE_out", [4 * D, T_LOC], BF16, kind="Internal").ap(),
        "agkE_in": nc.dram_tensor("agkE_in", [256, T_LOC], BF16, kind="Internal").ap(),
        "agkE_out": nc.dram_tensor("agkE_out", [1024, T_LOC], BF16, kind="Internal").ap(),
        "agvE_in": nc.dram_tensor("agvE_in", [256, T_LOC], BF16, kind="Internal").ap(),
        "agvE_out": nc.dram_tensor("agvE_out", [1024, T_LOC], BF16, kind="Internal").ap(),
        "arnE_in": nc.dram_tensor("arnE_in", [2, 2], F32, kind="Internal").ap(),
        "arnE_out": nc.dram_tensor("arnE_out", [2, 2], F32, kind="Internal").ap(),
        "ago0_in": nc.dram_tensor("ago0_in", [256, T_LOC], BF16, kind="Internal").ap(),
        "ago1_in": nc.dram_tensor("ago1_in", [256, T_LOC], BF16, kind="Internal").ap(),
        "ago2_in": nc.dram_tensor("ago2_in", [256, T_LOC], BF16, kind="Internal").ap(),
        "ago3_in": nc.dram_tensor("ago3_in", [256, T_LOC], BF16, kind="Internal").ap(),
        "agofm_out": nc.dram_tensor("agofm_out", [4096, T_LOC], BF16, kind="Internal").ap(),
    }
    import os as _os3
    if _os3.environ.get("EDBG2"):
        IN["dbg"] = nc.dram_tensor("dbg", [D, T_LOC], BF16, kind="ExternalOutput").ap()
        IN["dbg2"] = nc.dram_tensor("dbg2", [D, T_LOC], BF16, kind="ExternalOutput").ap()
        IN["dbg3"] = nc.dram_tensor("dbg3", [D, T_LOC], BF16, kind="ExternalOutput").ap()
    xf = es.enter_context(nc.sbuf_tensor("xf", [128, NCH * T_LOC], F32))
    C = alloc_common(p, IN)
    xl = p.sem("xload")
    for c in range(NCH):
        p.dma("sync", xf[:, c * T_LOC:(c + 1) * T_LOC], IN["xin"][c * 128:(c + 1) * 128, :], sem=xl)
    for e in P.ENGINES:
        p.wait(e, xl, xl.n)
    emit_l1attn(p, C, xf, IN, DR)
    so = p.sem("xstore")
    for c in range(NCH):
        p.dma("sync", xout[c * 128:(c + 1) * 128, :], xf[:, c * T_LOC:(c + 1) * T_LOC], sem=so)
    p.wait("sync", so, so.n)
    p.run()
    es.close()
    return nc


def build_eidx(j):
    t = np.zeros((128, 56), dtype=np.uint32)

    def permrows(h):
        base = 128 * (h // 2) + 32 * (h % 2)
        return np.concatenate([np.arange(base, base + 32), np.arange(base + 64, base + 96)])
    for g in range(8):
        rblk = g if g <= 3 else 7 - g
        co = 0 if g <= 3 else 1
        for hp in range(2):
            for par in range(2):
                h = 4 * j + 2 * hp + par
                rows = permrows(h) + 1024 * rblk
                for pp in range(64):
                    t[64 * par + pp, hp * 8 + g] = 2 * rows[pp] + co
        krows = permrows(j) + 256 * rblk
        vbase = 2 * (256 * rblk + 64 * j) + co
        for pp in range(64):
            t[pp, 16 + g] = 2 * krows[pp] + co
            t[64 + pp, 16 + g] = 2 * krows[pp] + co
            t[pp, 24 + g] = vbase + 2 * pp
    for ci in range(8):
        for pp in range(128):
            t[pp, 32 + ci] = 1024 * j + ci * 128 + pp
    return np.ascontiguousarray(t)


def e_in_maps(INS, consts, xs):
    in_maps = []
    wq_perm = np.ascontiguousarray(INS["L1_attn_Wq"][:, ROPE_PERM])
    kperm = ROPE_PERM[:256]
    wk_perm = np.ascontiguousarray(INS["L1_attn_Wk"][:, kperm])
    bq_perm = INS["L1_attn_bq"][ROPE_PERM]
    bk_perm = INS["L1_attn_bk"][kperm]
    for core in range(NCORE):
        b, j = core // 4, core % 4
        m = {"xin": xs[core]}
        m["L1_Wq_perm"] = wq_perm
        m["L1_Wk_perm"] = wk_perm
        m["L1_attn_Wv"] = INS["L1_attn_Wv"]
        m["L1_attn_Wo"] = INS["L1_attn_Wo"]
        m["L1_attn_Wf1"] = INS["L1_attn_Wf1"]
        m["L1_attn_Wf2"] = INS["L1_attn_Wf2"]
        m["L1_bq_col"] = prep_col(bq_perm)
        m["L1_bk_col"] = prep_col(bk_perm)
        m["L1_bv_col"] = prep_col(INS["L1_attn_bv"])
        m["L1_bo_col"] = prep_col(INS["L1_attn_bo"])
        m["L1_g1_col"] = prep_col(INS["L1_attn_g1"])
        m["L1_b1_col"] = prep_col(INS["L1_attn_b1"])
        m["L1_g2_col"] = prep_col(INS["L1_attn_g2"])
        m["L1_b2_col"] = prep_col(INS["L1_attn_b2"])
        m["L1_bf1_col"] = prep_col(INS["L1_attn_bf1"])
        m["L1_bf2_col"] = prep_col(INS["L1_attn_bf2"])
        tt = core_tables(core)
        m["t_rope_cos"] = tt["t_rope_cos"]
        m["t_rope_sin"] = tt["t_rope_sin"]
        m["t_eidx"] = build_eidx(j)
        const_maps(m, consts)
        in_maps.append(m)
    return in_maps


def core_tables(core):
    b, j = core // 4, core % 4
    pos = np.concatenate([np.arange(1024 * j, 1024 * j + 1024),
                          np.arange(1024 * (7 - j), 1024 * (7 - j) + 1024)])
    half = 32
    theta = 1.0 / 10000.0 ** (np.arange(half, dtype=np.float64) / half)
    ang = pos[None, :] * theta[:, None]
    cos4 = np.tile(np.cos(ang), (4, 1))
    sin = np.sin(ang)
    s4 = np.concatenate([np.tile(sin, (2, 1)), -np.tile(sin, (2, 1))], axis=0)
    return {"t_rope_cos": np.ascontiguousarray(cos4.astype(BF)),
            "t_rope_sin": np.ascontiguousarray(s4.astype(BF))}




def d_in_maps(INS, consts, xs):
    in_maps = []
    for core in range(NCORE):
        m = {"xin": xs[core],
             "L1_ffn_W": INS["L1_ffn_W"], "L1_ffn_b": INS["L1_ffn_b"],
             "L1_ffn_logits": np.ascontiguousarray(
                 np.asarray(INS["L1_ffn_logits"], dtype=np.float32).reshape(8, 1)),
             "L1_ffn_u": np.ascontiguousarray(
                 np.asarray(INS["L1_ffn_u"], dtype=np.float32).reshape(8, 1))}
        const_maps(m, consts)
        in_maps.append(m)
    return in_maps


def kernel(**inputs):
    INS = {k: np.asarray(v) for k, v in inputs.items()}
    consts = host_constants()
    xs = shard_x(np.asarray(INS["L0_attn_x"], dtype=np.float32))
    cid = list(range(NCORE))

    def launch(nc, maps):
        res = run_bass_kernel_spmd(nc, maps, core_ids=cid)
        return [np.asarray(r["xout"], dtype=np.float32) for r in res.results]

    nc_A = build_A()
    nc_B = build_B()
    nc_D = build_D()
    nc_E = build_E()
    xs = launch(nc_A, a_in_maps(INS, consts, xs))
    xs = launch(nc_B, b_in_maps(INS, consts, xs))
    xs = launch(nc_A, a_in_maps(INS, consts, xs))
    xs = launch(nc_D, d_in_maps(INS, consts, xs))
    xs = launch(nc_E, e_in_maps(INS, consts, xs))
    xs = launch(nc_D, d_in_maps(INS, consts, xs))
    return unshard_x(xs)


# version 02
# speedup vs baseline: 1.0203x; 1.0203x over previous
import numpy as np
import re as _re
import ml_dtypes
import concourse.bass as bass
import concourse.mybir as mybir
from concourse.bass_utils import run_bass_kernel_spmd
from concourse.bass_core_rust import IndirectOffsetOnAxis
from contextlib import ExitStack

F32 = mybir.dt.float32
BF16 = mybir.dt.bfloat16
AF = mybir.ActivationFunctionType
ALU = mybir.AluOpType
BF = ml_dtypes.bfloat16
AXX = mybir.AxisListType.X

GROUPS = [[0, 1, 2, 3], [4, 5, 6, 7]]
NCORE = 8
T_LOC = 2048
D = 1024
NCH = 8
TB = 512
NTB = T_LOC // TB
EPS_LN = 1e-5


class Sem:
    __slots__ = ("h", "n")

    def __init__(self, h):
        self.h = h
        self.n = 0

    def inc(self, k=1):
        self.n += k
        return self.n


class P:
    ENGINES = ("sync", "tensor", "vector", "scalar", "gpsimd")
    SERIAL = ("vector", "scalar", "gpsimd")

    def __init__(self, nc, es):
        self.nc = nc
        self.es = es
        self.ops = {e: [] for e in self.ENGINES}
        self._sems = {}
        self.uid = 0
        self._selfsems = {e: self.sem("self_" + e) for e in self.SERIAL}

    def sem(self, name):
        name = _re.sub(r"\d", "", name)
        if name not in self._sems:
            h = self.es.enter_context(self.nc.semaphore(name))
            self._sems[name] = Sem(h)
        return self._sems[name]

    def on(self, eng, fn):
        self.ops[eng].append(("op", fn))

    def ctl(self, eng, fn):
        self.ops[eng].append(("ctl", fn))

    def inc(self, eng, sem):
        v = sem.inc()
        self.ops[eng].append(("sig", sem.h))
        return v

    def wait(self, eng, sem, val):
        self.ctl(eng, lambda e, s=sem.h, v=val: e.wait_ge(s, v))

    def barrier(self, engines=None):
        engines = engines or self.ENGINES
        bs = self.sem("bar")
        for e in engines:
            self.inc(e, bs)
        tgt = bs.n
        for e in engines:
            self.wait(e, bs, tgt)

    def dma(self, q, out, in_, sem=None, serial=True, **kw):
        sem = sem or self.sem("dma_" + q)
        if serial and sem.n > 0:
            self.ctl(q, lambda e, s=sem.h, v=sem.n: e.wait_ge(s, v))
        tgt = sem.inc(16)
        self.ctl(q, lambda e, o=out, i=in_, s=sem.h, kw=kw: e.dma_start(out=o, in_=i, **kw).then_inc(s, 16))
        return sem, tgt

    def dma_lazy(self, q, mk, sem, serial=True):
        if serial and sem.n > 0:
            self.ctl(q, lambda e, s=sem.h, v=sem.n: e.wait_ge(s, v))
        tgt = sem.inc(16)

        def fire(e, mk=mk, s=sem.h):
            o, i = mk(e)
            e.dma_start(out=o, in_=i).then_inc(s, 16)
        self.ctl(q, fire)
        return sem, tgt

    def run(self):
        with self.nc.Block() as block:
            for eng_name in self.ENGINES:
                def runner(eng, name=eng_name):
                    ssem = self._selfsems.get(name)
                    cnt = 0
                    last = None
                    for kind, fn in self.ops[name]:
                        if kind == "op":
                            if ssem is not None:
                                if cnt > 0:
                                    eng.wait_ge(ssem.h, cnt)
                                inst = fn(eng)
                                inst.then_inc(ssem.h, 1)
                                cnt += 1
                            else:
                                last = fn(eng)
                        elif kind == "sig":
                            if ssem is not None:
                                if cnt > 0:
                                    eng.wait_ge(ssem.h, cnt)
                                eng.sem_inc(fn, 1)
                            elif last is not None:
                                last.then_inc(fn, 1)
                                last = None
                            else:
                                eng.nop().then_inc(fn, 1)
                        else:
                            fn(eng)
                getattr(block, eng_name)(runner)

    def sbuf(self, name, shape, dtype, es=None):
        return (es or self.es).enter_context(self.nc.sbuf_tensor(name, shape, dtype))

    def psum(self, name, shape, dtype=F32, es=None):
        return (es or self.es).enter_context(self.nc.psum_tensor(name, shape, dtype))


def host_constants():
    c = {}
    c["c_ident_bf"] = np.eye(128, dtype=BF)
    c["c_ident_f32"] = np.eye(128, dtype=np.float32)
    c["c_ones_row_bf"] = np.ones((1, 128), dtype=BF)
    c["c_ones_row_f32"] = np.ones((1, 128), dtype=np.float32)
    c["c_ones_col_bf"] = np.ones((128, 1), dtype=BF)
    pp = np.arange(128)[:, None]
    jj = np.arange(512)[None, :]
    masks = [((pp + off) <= jj).astype(BF) for off in (0, 128, 256, 384)]
    c["c_masks_causal"] = np.ascontiguousarray(np.concatenate(masks, axis=1))
    wmasks = []
    for off in (-256, -128, 0, 128, 256, 384):
        d_ = jj - (pp + off)
        wmasks.append(((d_ >= 0) & (d_ < 256)).astype(BF))
    c["c_masks_win"] = np.ascontiguousarray(np.concatenate(wmasks, axis=1))
    hs = np.zeros((128, 2), dtype=BF)
    hs[0:64, 0] = 1
    hs[64:128, 1] = 1
    c["c_hsel2"] = hs
    h16 = np.zeros((128, 8), dtype=BF)
    for i in range(8):
        h16[16 * i:16 * (i + 1), i] = 1
    c["c_hsel16"] = h16
    c["c_hexp16"] = np.ascontiguousarray(h16.T)
    c["c_half_row_f32"] = np.full((1, 128), 0.5, dtype=np.float32)
    c["c_ones_col_f32"] = np.ones((128, 1), dtype=np.float32)
    return c


CONST_SHAPES = {
    "c_ident_bf": ([128, 128], BF16), "c_ident_f32": ([128, 128], F32),
    "c_ones_row_bf": ([1, 128], BF16), "c_ones_row_f32": ([1, 128], F32),
    "c_ones_col_bf": ([128, 1], BF16),
    "c_masks_causal": ([128, 4 * 512], BF16), "c_masks_win": ([128, 6 * 512], BF16),
    "c_hsel2": ([128, 2], BF16), "c_half_row_f32": ([1, 128], F32),
    "c_hsel16": ([128, 8], BF16), "c_hexp16": ([8, 128], BF16),
    "c_ones_col_f32": ([128, 1], F32),
}


def ag_rows_of_chunk(g):
    return 2048 * g if g <= 3 else 2048 * (7 - g) + 1024


def emit_ln(p, C, xf, h_bf, affine=None, keep_stats=None, rstd_comb=False, tag="ln"):
    import os
    LNSTAGE = int(os.environ.get("LNSTAGE", "9"))
    """Feature-dim LayerNorm of xf -> h_bf (bf16). Allocates its own psum scope.

    affine: None | (g_col, b_col). keep_stats: fill C['s_mubc']/C['s_rstdbc']
    (outer-LN stats). rstd_comb: apply fused LN(LN(x)) combined rstd to h.
    """
    p.uid += 1
    tag = f"{tag}{p.uid}"
    es = ExitStack()
    ps1 = p.psum(tag + "_s1", [1, TB], es=es)
    ps2 = p.psum(tag + "_s2", [1, TB], es=es)
    pb1 = p.psum(tag + "_b1", [128, TB], es=es)
    pb2 = p.psum(tag + "_b2", [128, TB], es=es)
    sv, st, ss = p.sem(tag + "v"), p.sem(tag + "t"), p.sem(tag + "s")
    xc = [C["s_cast0"], C["s_cast1"]]
    x2 = [C["s_sq0"], C["s_sq1"]]
    onesb = C["s_onescol_bf"]
    ones1f = C["s_onesrow_f32"]
    s1r, s2r = C["s_lnrow1"], C["s_lnrow2"]
    mubc, rstdbc = C["s_mubc"], C["s_rstdbc"]
    var_sb, srt, tsc = C["s_lnvar"], C["s_lnsrt"], C["s_lntmp"]

    for tb in range(NTB):
        tsl = slice(tb * TB, (tb + 1) * TB)
        for c in range(NCH):
            bi = c % 2
            if c >= 2:
                p.wait("vector", st, st.n - 1)
            xs = xf[:, c * T_LOC + tb * TB: c * T_LOC + (tb + 1) * TB]
            p.on("vector", lambda e, o=xc[bi][:], i=xs: e.tensor_copy(o, i))
            p.on("vector", lambda e, o=x2[bi][:], a=xc[bi][:]: e.tensor_mul(o, a, a))
            vt = p.inc("vector", sv)
            p.wait("tensor", sv, vt)
            p.on("tensor", lambda e, o=ps1[0:1, :], r=xc[bi][:], s0=(c == 0), s1=(c == NCH - 1):
                 e.matmul(o, onesb[:, 0:1], r, start=s0, stop=s1))
            p.on("tensor", lambda e, o=ps2[0:1, :], r=x2[bi][:], s0=(c == 0), s1=(c == NCH - 1):
                 e.matmul(o, onesb[:, 0:1], r, start=s0, stop=s1))
            p.inc("tensor", st)
        if LNSTAGE < 2:
            continue
        p.wait("vector", st, st.n)
        p.on("vector", lambda e: e.tensor_copy(s1r[0:1, :], ps1[0:1, :]))
        p.on("vector", lambda e: e.tensor_copy(s2r[0:1, :], ps2[0:1, :]))
        ve = p.inc("vector", sv)
        p.wait("tensor", sv, ve)
        p.on("tensor", lambda e: e.matmul(pb1[:, :], ones1f[0:1, :], s1r[0:1, :], start=True, stop=True))
        p.on("tensor", lambda e: e.matmul(pb2[:, :], ones1f[0:1, :], s2r[0:1, :], start=True, stop=True))
        tb1 = p.inc("tensor", st)
        p.wait("vector", st, tb1)
        p.on("vector", lambda e: e.tensor_scalar(mubc[:], pb1[:, :], 1.0 / D, None, ALU.mult))
        p.on("vector", lambda e: e.tensor_scalar(tsc[:], pb2[:, :], 1.0 / D, None, ALU.mult))
        p.on("vector", lambda e: e.tensor_mul(srt[:], mubc[:], mubc[:]))
        p.on("vector", lambda e: e.tensor_sub(var_sb[:], tsc[:], srt[:]))
        if LNSTAGE < 3:
            continue
        vv = p.inc("vector", sv)
        p.wait("scalar", sv, vv)
        p.on("scalar", lambda e: e.activation(srt[:], var_sb[:], AF.Sqrt, bias=C["s_epscol"][:, 0:1]))
        sg = p.inc("scalar", ss)
        p.wait("vector", ss, sg)
        p.on("vector", lambda e: e.reciprocal(rstdbc[:], srt[:]))
        if keep_stats:
            p.on("vector", lambda e, o=keep_stats[0][:, tsl]: e.tensor_copy(o, mubc[:]))
            p.on("vector", lambda e, o=keep_stats[1][:, tsl]: e.tensor_copy(o, rstdbc[:]))
        if rstd_comb:
            p.on("vector", lambda e: e.tensor_mul(tsc[:], rstdbc[:], rstdbc[:]))
            p.on("vector", lambda e: e.tensor_mul(tsc[:], tsc[:], var_sb[:]))
            v2 = p.inc("vector", sv)
            p.wait("scalar", sv, v2)
            p.on("scalar", lambda e: e.activation(tsc[:], tsc[:], AF.Sqrt, bias=C["s_epscol"][:, 0:1]))
            s2_ = p.inc("scalar", ss)
            p.wait("vector", ss, s2_)
            p.on("vector", lambda e: e.reciprocal(var_sb[:], tsc[:]))
            p.on("vector", lambda e: e.tensor_mul(rstdbc[:], rstdbc[:], var_sb[:]))
        if LNSTAGE < 4:
            continue
        for c in range(NCH):
            xs = xf[:, c * T_LOC + tb * TB: c * T_LOC + (tb + 1) * TB]
            hs = h_bf[:, c * T_LOC + tb * TB: c * T_LOC + (tb + 1) * TB]
            p.on("vector", lambda e, x_=xs: e.tensor_sub(tsc[:], x_, mubc[:]))
            if affine is None:
                p.on("vector", lambda e, o=hs: e.tensor_mul(o, tsc[:], rstdbc[:]))
            else:
                g_col, b_col = affine
                p.on("vector", lambda e: e.tensor_mul(tsc[:], tsc[:], rstdbc[:]))
                p.on("vector", lambda e, o=hs, g=g_col[:, c:c + 1], b=b_col[:, c:c + 1]:
                     e.tensor_scalar(o, tsc[:], g, b, ALU.mult, ALU.add))
        p.inc("vector", sv)
    p.barrier()
    es.close()


def alloc_common(p, IN):
    C = {}
    C["s_cast0"] = p.sbuf("s_cast0", [128, TB], BF16)
    C["s_cast1"] = p.sbuf("s_cast1", [128, TB], BF16)
    C["s_sq0"] = p.sbuf("s_sq0", [128, TB], BF16)
    C["s_sq1"] = p.sbuf("s_sq1", [128, TB], BF16)
    C["s_lnrow1"] = p.sbuf("s_lnrow1", [1, TB], F32)
    C["s_lnrow2"] = p.sbuf("s_lnrow2", [1, TB], F32)
    C["s_mubc"] = p.sbuf("s_mubc", [128, TB], F32)
    C["s_rstdbc"] = p.sbuf("s_rstdbc", [128, TB], F32)
    C["s_lnvar"] = p.sbuf("s_lnvar", [128, TB], F32)
    C["s_lnsrt"] = p.sbuf("s_lnsrt", [128, TB], F32)
    C["s_lntmp"] = p.sbuf("s_lntmp", [128, TB], F32)
    C["s_onescol_bf"] = p.sbuf("s_onescol_bf", [128, 1], BF16)
    C["s_onesrow_f32"] = p.sbuf("s_onesrow_f32", [1, 128], F32)
    C["s_onesrow_bf"] = p.sbuf("s_onesrow_bf", [1, 128], BF16)
    C["s_ident_bf"] = p.sbuf("s_ident_bf", [128, 128], BF16)
    C["s_ident_f32"] = p.sbuf("s_ident_f32", [128, 128], F32)
    C["s_epscol"] = p.sbuf("s_epscol", [128, 1], F32)
    C["s_hbf"] = p.sbuf("s_hbf", [128, NCH * T_LOC], BF16)
    p.on("gpsimd", lambda e: e.memset(C["s_epscol"][:], EPS_LN))
    gs = p.sem("constset")
    p.inc("gpsimd", gs)
    for e in P.ENGINES:
        p.wait(e, gs, gs.n)
    ld = p.sem("constload")
    for name in ("c_ident_bf", "c_ident_f32", "c_ones_row_bf", "c_ones_row_f32", "c_ones_col_bf"):
        sb = {"c_ident_bf": C["s_ident_bf"], "c_ident_f32": C["s_ident_f32"],
              "c_ones_row_bf": C["s_onesrow_bf"], "c_ones_row_f32": C["s_onesrow_f32"],
              "c_ones_col_bf": C["s_onescol_bf"]}[name]
        p.dma("sync", sb[:], IN[name], sem=ld)
    for e in P.ENGINES:
        p.wait(e, ld, ld.n)
    return C


def emit_l1ffn(p, C, xf, IN):
    p.uid += 1
    tag = f"d{p.uid}"
    h_bf = C["s_hbf"]
    emit_ln(p, C, xf, h_bf, tag="lnD")

    es = ExitStack()
    lg = p.sbuf(tag + "lg", [8, 1], F32, es=es)
    uu = p.sbuf(tag + "u", [8, 1], F32, es=es)
    t0 = p.sbuf(tag + "t0", [8, 64], F32, es=es)
    ind = p.sbuf(tag + "ind", [8, 1], F32, es=es)
    bsel = p.sbuf(tag + "bsel", [128, 8], F32, es=es)
    brow = p.sbuf(tag + "brow", [1, 1024], F32, es=es)
    bexp = p.sbuf(tag + "bexp", [8, 1024], F32, es=es)
    indbc = p.sbuf(tag + "indbc", [128, 8], F32, es=es)
    wsel = p.sbuf(tag + "wsel", [128, 8 * 1024], BF16, es=es)
    wexp = p.sbuf(tag + "wexp", [128, 8 * 1024], F32, es=es)
    wacc = p.sbuf(tag + "wacc", [128, 1024], F32, es=es)
    pps = p.psum(tag + "pa", [128, TB], es=es)
    pps2 = p.psum(tag + "pb", [128, TB], es=es)
    pj = [p.psum(tag + "p0", [128, TB], es=es), p.psum(tag + "p1", [128, TB], es=es)]

    ld = p.sem(tag + "ld")
    vs, ts, ss = p.sem(tag + "v"), p.sem(tag + "t"), p.sem(tag + "s")
    idf = C["s_ident_f32"]

    p.dma("sync", lg[:], IN["L1_ffn_logits"], sem=ld)
    p.dma("sync", uu[:], IN["L1_ffn_u"], sem=ld)
    p.dma("sync", bexp[:], IN["L1_ffn_b"], sem=ld)
    p.wait("scalar", ld, ld.n)
    p.on("scalar", lambda e: e.activation(t0[0:8, 0:1], uu[:], AF.Ln))
    s1 = p.inc("scalar", ss)
    p.wait("vector", ss, s1)
    p.on("vector", lambda e: e.tensor_scalar(t0[0:8, 0:1], t0[0:8, 0:1], -1.0, None, ALU.mult))
    v1 = p.inc("vector", vs)
    p.wait("scalar", vs, v1)
    p.on("scalar", lambda e: e.activation(t0[0:8, 1:2], t0[0:8, 0:1], AF.Ln))
    s2 = p.inc("scalar", ss)
    p.wait("vector", ss, s2)
    p.wait("vector", ld, ld.n)
    p.on("vector", lambda e: e.tensor_sub(t0[0:8, 2:3], lg[:], t0[0:8, 1:2]))
    v2 = p.inc("vector", vs)
    p.wait("tensor", vs, v2)
    p.on("tensor", lambda e: e.transpose(pps[0:1, 0:8], t0[0:8, 2:3], idf[0:8, 0:8]))
    t1 = p.inc("tensor", ts)
    p.wait("vector", ts, t1)
    p.on("vector", lambda e: e.tensor_copy(t0[0:1, 3:11], pps[0:1, 0:8]))
    v3 = p.inc("vector", vs)
    p.wait("tensor", vs, v3)
    p.on("tensor", lambda e: e.matmul(pps2[0:8, 0:8], C["s_onesrow_f32"][0:1, 0:8], t0[0:1, 3:11], start=True, stop=True))
    t2 = p.inc("tensor", ts)
    p.wait("vector", ts, t2)
    p.on("vector", lambda e: e.tensor_tensor(t0[0:8, 16:24], t0[0:8, 2:3].broadcast_to([8, 8]), pps2[0:8, 0:8], ALU.is_ge))
    p.on("vector", lambda e: e.reduce_sum(t0[0:8, 24:25], t0[0:8, 16:24], axis=AXX))
    p.on("vector", lambda e: e.tensor_scalar(ind[:], t0[0:8, 24:25], 7.5, 0.5, ALU.is_ge, ALU.mult))
    v4 = p.inc("vector", vs)

    p.wait("tensor", vs, v4)
    p.on("tensor", lambda e: e.matmul(pps[0:1, 0:512], ind[:], bexp[:, 0:512], start=True, stop=True))
    p.on("tensor", lambda e: e.matmul(pps2[0:1, 0:512], ind[:], bexp[:, 512:1024], start=True, stop=True))
    t3 = p.inc("tensor", ts)
    p.wait("vector", ts, t3)
    p.on("vector", lambda e: e.tensor_copy(brow[0:1, 0:512], pps[0:1, 0:512]))
    p.on("vector", lambda e: e.tensor_copy(brow[0:1, 512:1024], pps2[0:1, 0:512]))
    v5 = p.inc("vector", vs)
    p.wait("tensor", vs, v5)
    for c in range(NCH):
        p.on("tensor", lambda e, c=c: e.transpose(pps[0:128, c:c + 1], brow[0:1, 128 * c:128 * (c + 1)], idf[0:1, 0:1]))
    t4 = p.inc("tensor", ts)
    p.wait("vector", ts, t4)
    p.on("vector", lambda e: e.tensor_copy(bsel[:, 0:8], pps[0:128, 0:8]))
    v6 = p.inc("vector", vs)

    p.wait("tensor", vs, v6)
    p.on("tensor", lambda e: e.transpose(pps2[0:1, 0:8], ind[:], idf[0:8, 0:8]))
    t5 = p.inc("tensor", ts)
    p.wait("vector", ts, t5)
    p.on("vector", lambda e: e.tensor_copy(t0[0:1, 32:40], pps2[0:1, 0:8]))
    v7 = p.inc("vector", vs)
    p.wait("tensor", vs, v7)
    p.on("tensor", lambda e: e.matmul(pps[0:128, 0:8], C["s_onesrow_f32"][0:1, :], t0[0:1, 32:40], start=True, stop=True))
    t6 = p.inc("tensor", ts)
    p.wait("vector", ts, t6)
    p.on("vector", lambda e: e.tensor_copy(indbc[:], pps[0:128, 0:8]))
    p.inc("vector", vs)
    p.barrier()

    wl = p.sem(tag + "wl")
    cs = p.sem(tag + "cs")
    for ci in range(NCH):
        if ci >= 1:
            p.wait("sync", cs, ci)
        for ex in range(8):
            p.dma("sync", wexp[:, ex * 1024:(ex + 1) * 1024],
                  IN["L1_ffn_W"][ex, ci * 128:(ci + 1) * 128, :], sem=wl)
        p.wait("vector", wl, wl.n)
        p.on("vector", lambda e: e.tensor_scalar(wacc[:], wexp[:, 0:1024], indbc[:, 0:1], None, ALU.mult))
        for ex in range(1, 7):
            p.on("vector", lambda e, ex=ex: e.scalar_tensor_tensor(
                wacc[:], wexp[:, ex * 1024:(ex + 1) * 1024], indbc[:, ex:ex + 1], wacc[:], ALU.mult, ALU.add))
        p.on("vector", lambda e, ci=ci: e.scalar_tensor_tensor(
            wsel[:, ci * 1024:(ci + 1) * 1024], wexp[:, 7 * 1024:8 * 1024], indbc[:, 7:8], wacc[:], ALU.mult, ALU.add))
        p.inc("vector", cs)
    p.barrier()

    pe = p.sem(tag + "pe")
    pt = p.sem(tag + "pt")
    for k in range(NTB * NCH):
        tb, oc = divmod(k, NCH)
        bi = k % 2
        if k >= 2:
            p.wait("tensor", pe, k - 1)
        for ci in range(NCH):
            p.on("tensor", lambda e, o=pj[bi][:, :], l=wsel[:, ci * 1024 + oc * 128: ci * 1024 + (oc + 1) * 128],
                 r=h_bf[:, ci * T_LOC + tb * TB: ci * T_LOC + (tb + 1) * TB],
                 s0=(ci == 0), s1=(ci == NCH - 1): e.matmul(o, l, r, start=s0, stop=s1))
        tt = p.inc("tensor", pt)
        p.wait("vector", pt, tt)
        xs = xf[:, oc * T_LOC + tb * TB: oc * T_LOC + (tb + 1) * TB]
        p.on("vector", lambda e, x_=xs, ps=pj[bi][:, :], bc=bsel[:, oc:oc + 1]:
             e.scalar_tensor_tensor(x_, ps, bc, x_, ALU.add, ALU.add))
        p.inc("vector", pe)
    p.barrier()
    es.close()



def emit_row_to_col(p, row_ap, n, col_out, ident, pm, ts, vs):
    p.wait("tensor", vs, vs.n)
    for c in range(n):
        p.on("tensor", lambda e, c=c: e.transpose(pm[0:128, c:c + 1], row_ap[0:1, 128 * c:128 * (c + 1)], ident[0:1, 0:1]))
    t = p.inc("tensor", ts)
    p.wait("vector", ts, t)
    p.on("vector", lambda e: e.tensor_copy(col_out[0:128, 0:n], pm[0:128, 0:n]))
    p.inc("vector", vs)


def emit_bias_combine(p, w_col, bexp_ap, bex_sb, brow, col_out, ident, pm, pm2, ts, vs, ld):
    p.wait("sync", ts, ts.n)
    _, dt_ = p.dma("sync", bex_sb[:], bexp_ap, sem=ld)
    p.wait("tensor", ld, dt_)
    p.wait("tensor", vs, vs.n)
    p.on("tensor", lambda e: e.matmul(pm[0:1, 0:512], w_col[:], bex_sb[:, 0:512], start=True, stop=True))
    p.on("tensor", lambda e: e.matmul(pm2[0:1, 0:512], w_col[:], bex_sb[:, 512:1024], start=True, stop=True))
    t = p.inc("tensor", ts)
    p.wait("vector", ts, t)
    p.on("vector", lambda e: e.tensor_copy(brow[0:1, 0:512], pm[0:1, 0:512]))
    p.on("vector", lambda e: e.tensor_copy(brow[0:1, 512:1024], pm2[0:1, 0:512]))
    p.inc("vector", vs)
    emit_row_to_col(p, brow, 8, col_out, ident, pm, ts, vs)



def emit_norm_evac(p, C, po_par, o_dest, scale_row, rrow, rc4, rbc, pmisc, prb, ts, vs):
    idf = C["s_ident_f32"]
    p.on("vector", lambda e: e.tensor_copy(rrow[0:1, :], po_par[64:65, :]))
    pv = p.inc("vector", vs)
    p.wait("tensor", vs, pv)
    for c4 in range(4):
        p.on("tensor", lambda e, c4=c4: e.transpose(
            pmisc[0:128, c4:c4 + 1], rrow[0:1, 128 * c4:128 * (c4 + 1)], idf[0:1, 0:1]))
    tt = p.inc("tensor", ts)
    p.wait("vector", ts, tt)
    p.on("vector", lambda e: e.tensor_copy(rc4[:, 0:4], pmisc[0:128, 0:4]))
    p.on("vector", lambda e: e.reciprocal(rc4[:, 0:4], rc4[:, 0:4]))
    pv = p.inc("vector", vs)
    p.wait("tensor", vs, pv)
    for c4 in range(4):
        p.on("tensor", lambda e, c4=c4: e.transpose(
            pmisc[0:1, 128 * c4:128 * (c4 + 1)], rc4[:, c4:c4 + 1], idf[:]))
    tt = p.inc("tensor", ts)
    p.wait("vector", ts, tt)
    p.on("vector", lambda e: e.tensor_copy(rrow[0:1, :], pmisc[0:1, 0:512]))
    pv = p.inc("vector", vs)
    p.wait("tensor", vs, pv)
    p.on("tensor", lambda e: e.matmul(prb[:, :], scale_row[0:1, :], rrow[0:1, :], start=True, stop=True))
    tt = p.inc("tensor", ts)
    p.wait("vector", ts, tt)
    p.on("vector", lambda e: e.tensor_copy(rbc[:], prb[:, :]))
    p.on("vector", lambda e: e.tensor_tensor(o_dest, po_par[0:64, :], rbc[0:64, :], ALU.mult))
    p.inc("vector", vs)


def emit_v_to_ag(p, C, vtmp, vrow, ptr, agv_in_ap, ld, vs, pt):
    for hh in range(16):
        p.on("gpsimd", lambda e, hh=hh: e.memset(vrow[:, 65 * hh + 64:65 * hh + 65], 1.0))
    gset = p.sem("gset" + str(p.uid) + "_" + str(len(p.ops["gpsimd"])))
    p.inc("gpsimd", gset)
    p.wait("vector", gset, gset.n)
    prev_dtgt = None
    prev_dvv = None
    for tau in range(16):
        if prev_dtgt is not None:
            p.wait("vector", ld, prev_dtgt)
        if prev_dvv is not None:
            p.wait("tensor", vs, prev_dvv)
        for oc in range(NCH):
            p.on("tensor", lambda e, tau=tau, oc=oc: e.transpose(
                ptr[:, oc * 128:(oc + 1) * 128],
                vtmp[:, oc * T_LOC + tau * 128: oc * T_LOC + (tau + 1) * 128], C["s_ident_bf"][:]))
        tt = p.inc("tensor", pt)
        p.wait("vector", pt, tt)
        for hh in range(16):
            oc, par = hh // 2, hh % 2
            p.on("vector", lambda e, oc=oc, par=par, hh=hh: e.tensor_copy(
                vrow[:, 65 * hh:65 * hh + 64], ptr[:, oc * 128 + par * 64: oc * 128 + par * 64 + 64]))
        dvv = p.inc("vector", vs)
        prev_dvv = dvv
        p.wait("sync", vs, dvv)
        _, prev_dtgt = p.dma("sync", agv_in_ap[tau * 128:(tau + 1) * 128, :], vrow[:], sem=ld)
    p.barrier()


def emit_l0ffn(p, C, xf, IN, DR):
    p.uid += 1
    tag = f"A{p.uid}_"
    h_bf = C["s_hbf"]
    emit_ln(p, C, xf, h_bf, tag=tag + "ln")

    idf = C["s_ident_f32"]
    o1f = C["s_onesrow_f32"]
    ts, vs, ss = p.sem(tag + "t"), p.sem(tag + "v"), p.sem(tag + "s")
    ld = p.sem(tag + "ld")
    cc = p.sem(tag + "cc")
    pe, pt, dv = p.sem(tag + "pe"), p.sem(tag + "pt"), p.sem(tag + "dv")
    gld = p.sem(tag + "gld")

    es0 = ExitStack()
    q_bf = p.sbuf(tag + "q_bf", [128, NCH * T_LOC], BF16, es=es0)
    o_coll = h_bf
    wbc = p.sbuf(tag + "wbc", [128, 4], F32, es=es0)
    bq_c = p.sbuf(tag + "bq_c", [128, 8], F32, es=es0)
    bk_c = p.sbuf(tag + "bk_c", [128, 8], F32, es=es0)
    bv_c = p.sbuf(tag + "bv_c", [128, 8], F32, es=es0)
    bo_h = p.sbuf(tag + "bo_h", [128, 8], F32, es=es0)
    nq = p.sbuf(tag + "nq", [2, 8], F32, es=es0)
    nk = p.sbuf(tag + "nk", [2, 8], F32, es=es0)
    negBp = p.sbuf(tag + "negBp", [128, 8], F32, es=es0)

    es1 = ExitStack()
    pool = p.sbuf(tag + "pool", [128, 8], F32, es=es1)
    poolb = p.sbuf(tag + "poolb", [128, 8], BF16, es=es1)
    wg1 = p.sbuf(tag + "wg1", [128, 1024], BF16, es=es1)
    wg2 = p.sbuf(tag + "wg2", [128, 4], BF16, es=es1)
    z1 = p.sbuf(tag + "z1", [128, 1], BF16, es=es1)
    lgt = p.sbuf(tag + "lgt", [4, 1], F32, es=es1)
    gt0 = p.sbuf(tag + "gt0", [8, 64], F32, es=es1)
    wcol = p.sbuf(tag + "wcol", [4, 1], F32, es=es1)
    brow = p.sbuf(tag + "brow", [1, 1024], F32, es=es1)
    bcol_sb = p.sbuf(tag + "bcol_sb", [128, 8], F32, es=es1)
    bg1_sb = p.sbuf(tag + "bg1_sb", [128, 1], F32, es=es1)
    bg2_sb = p.sbuf(tag + "bg2_sb", [4, 1], F32, es=es1)
    hsel2 = p.sbuf(tag + "hsel2", [128, 2], BF16, es=es1)
    wc = p.sbuf(tag + "wc", [128, 8 * 1024], BF16, es=es1)
    wstr = p.sbuf(tag + "wstr", [128, 4 * 1024], BF16, es=es1)
    wacc = p.sbuf(tag + "wacc", [128, 1024], F32, es=es1)
    pch = p.sbuf(tag + "pch", [128, T_LOC], BF16, es=es1)
    sqc = p.sbuf(tag + "sqc", [128, T_LOC], BF16, es=es1)
    vrow = p.sbuf(tag + "vrow", [128, 1040], BF16, es=es1)
    nred = p.sbuf(tag + "nred", [2, 8], F32, es=es1)
    pga = p.psum(tag + "pga", [128, 512], es=es1)
    pgb = p.psum(tag + "pgb", [128, 512], es=es1)
    pj = [p.psum(tag + "pj0", [128, TB], es=es1), p.psum(tag + "pj1", [128, TB], es=es1)]
    pn = p.psum(tag + "pn", [2, 512], es=es1)
    ptr = p.psum(tag + "ptr", [128, 1024], BF16, es=es1)
    wl = p.sem(tag + "wl")
    cmb = p.sem(tag + "cmb")

    for name, sb in (("bg1_col", bg1_sb), ("bg2_col", bg2_sb), ("c_hsel2", hsel2), ("bo_col", bcol_sb)):
        p.dma("sync", sb[:], IN[name], sem=ld)
    for c in range(NCH):
        p.on("vector", lambda e, c=c: e.reduce_sum(pool[:, c:c + 1], h_bf[:, c * T_LOC:(c + 1) * T_LOC], axis=AXX))
    pv = p.inc("vector", vs)
    p.wait("sync", vs, pv)
    p.dma("sync", DR["arp_in"], pool[:], sem=ld)
    p.wait("gpsimd", ld, ld.n)
    p.ctl("gpsimd", lambda e: e.collective_compute(
        "AllReduce", ALU.add, GROUPS, [DR["arp_in"].opt()], [DR["arp_out"].opt()]).then_inc(cc.h, 1))
    cc.n += 1
    p.wait("sync", cc, cc.n)
    p.dma("sync", pool[:], DR["arp_out"], sem=ld)
    for c in range(NCH):
        p.dma("gpsimd", wg1[:, 128 * c:128 * (c + 1)], IN["L0_ffn_Wg1"][c * 128:(c + 1) * 128, :], sem=gld)
    p.dma("gpsimd", wg2[:], IN["L0_ffn_Wg2"], sem=gld)
    p.wait("vector", ld, ld.n)
    p.wait("vector", gld, gld.n)
    p.on("vector", lambda e: e.tensor_scalar(poolb[:], pool[:], 1.0 / 8192.0, None, ALU.mult))
    pv = p.inc("vector", vs)
    p.wait("tensor", vs, pv)
    for c in range(NCH):
        p.on("tensor", lambda e, c=c: e.matmul(pga[0:128, 0:1], wg1[:, 128 * c:128 * (c + 1)], poolb[:, c:c + 1],
                                               start=(c == 0), stop=(c == NCH - 1)))
    t1 = p.inc("tensor", ts)
    p.wait("vector", ts, t1)
    p.on("vector", lambda e: e.tensor_scalar(z1[:], pga[0:128, 0:1], bg1_sb[:, 0:1], 0.0, ALU.add, ALU.max))
    pv = p.inc("vector", vs)
    p.wait("tensor", vs, pv)
    p.on("tensor", lambda e: e.matmul(pgb[0:4, 0:1], wg2[:], z1[:], start=True, stop=True))
    t2 = p.inc("tensor", ts)
    p.wait("vector", ts, t2)
    p.on("vector", lambda e: e.tensor_scalar(lgt[:], pgb[0:4, 0:1], bg2_sb[0:4, 0:1], None, ALU.add))
    pv = p.inc("vector", vs)
    p.wait("tensor", vs, pv)
    p.on("tensor", lambda e: e.transpose(pga[0:1, 0:4], lgt[:], idf[0:4, 0:4]))
    t3 = p.inc("tensor", ts)
    p.wait("vector", ts, t3)
    p.on("vector", lambda e: e.tensor_copy(gt0[0:1, 0:4], pga[0:1, 0:4]))
    pv = p.inc("vector", vs)
    p.wait("tensor", vs, pv)
    p.on("tensor", lambda e: e.matmul(pgb[0:4, 0:4], o1f[0:1, 0:4], gt0[0:1, 0:4], start=True, stop=True))
    t4 = p.inc("tensor", ts)
    p.wait("vector", ts, t4)
    p.on("vector", lambda e: e.tensor_tensor(gt0[0:4, 8:12], lgt[:].broadcast_to([4, 4]), pgb[0:4, 0:4], ALU.is_ge))
    p.on("vector", lambda e: e.reduce_sum(gt0[0:4, 12:13], gt0[0:4, 8:12], axis=AXX))
    p.on("vector", lambda e: e.tensor_scalar(gt0[0:4, 13:14], gt0[0:4, 12:13], 2.5, None, ALU.is_ge))
    p.on("vector", lambda e: e.tensor_scalar(gt0[0:4, 14:15], gt0[0:4, 12:13], 3.5, None, ALU.is_ge))
    pv = p.inc("vector", vs)
    p.wait("tensor", vs, pv)
    p.on("tensor", lambda e: e.matmul(pga[0:1, 8:9], gt0[0:4, 14:15], lgt[:], start=True, stop=True))
    t5 = p.inc("tensor", ts)
    p.wait("vector", ts, t5)
    p.on("vector", lambda e: e.tensor_copy(gt0[0:1, 16:17], pga[0:1, 8:9]))
    pv = p.inc("vector", vs)
    p.wait("tensor", vs, pv)
    p.on("tensor", lambda e: e.matmul(pgb[0:4, 8:9], o1f[0:1, 0:4], gt0[0:1, 16:17], start=True, stop=True))
    t6 = p.inc("tensor", ts)
    p.wait("vector", ts, t6)
    p.on("vector", lambda e: e.tensor_scalar(gt0[0:4, 17:18], pgb[0:4, 8:9], -1.0, None, ALU.mult))
    pv = p.inc("vector", vs)
    p.wait("scalar", vs, pv)
    p.on("scalar", lambda e: e.activation(gt0[0:4, 18:19], lgt[:], AF.Exp, bias=gt0[0:4, 17:18]))
    s1 = p.inc("scalar", ss)
    p.wait("vector", ss, s1)
    p.on("vector", lambda e: e.tensor_mul(gt0[0:4, 19:20], gt0[0:4, 18:19], gt0[0:4, 13:14]))
    pv = p.inc("vector", vs)
    p.wait("tensor", vs, pv)
    p.on("tensor", lambda e: e.matmul(pga[0:1, 9:10], gt0[0:4, 13:14], gt0[0:4, 18:19], start=True, stop=True))
    t7 = p.inc("tensor", ts)
    p.wait("vector", ts, t7)
    p.on("vector", lambda e: e.tensor_copy(gt0[0:1, 20:21], pga[0:1, 9:10]))
    pv = p.inc("vector", vs)
    p.wait("tensor", vs, pv)
    p.on("tensor", lambda e: e.matmul(pgb[0:4, 9:10], o1f[0:1, 0:4], gt0[0:1, 20:21], start=True, stop=True))
    t8 = p.inc("tensor", ts)
    p.wait("vector", ts, t8)
    p.on("vector", lambda e: e.tensor_copy(gt0[0:4, 21:22], pgb[0:4, 9:10]))
    p.on("vector", lambda e: e.reciprocal(gt0[0:4, 22:23], gt0[0:4, 21:22]))
    p.on("vector", lambda e: e.tensor_mul(wcol[:], gt0[0:4, 19:20], gt0[0:4, 22:23]))
    pv = p.inc("vector", vs)
    p.wait("tensor", vs, pv)
    p.on("tensor", lambda e: e.transpose(pga[0:1, 10:14], wcol[:], idf[0:4, 0:4]))
    t9 = p.inc("tensor", ts)
    p.wait("vector", ts, t9)
    p.on("vector", lambda e: e.tensor_copy(gt0[0:1, 24:28], pga[0:1, 10:14]))
    pv = p.inc("vector", vs)
    p.wait("tensor", vs, pv)
    p.on("tensor", lambda e: e.matmul(pgb[0:128, 12:16], o1f[0:1, :], gt0[0:1, 24:28], start=True, stop=True))
    t10 = p.inc("tensor", ts)
    p.wait("vector", ts, t10)
    p.on("vector", lambda e: e.tensor_copy(wbc[:], pgb[0:128, 12:16]))
    p.inc("vector", vs)
    bex_sb = p.sbuf(tag + "bex", [4, 1024], F32, es=es1)
    emit_bias_combine(p, wcol, IN["L0_ffn_bq"], bex_sb, brow, bq_c, idf, pga, pgb, ts, vs, ld)
    emit_bias_combine(p, wcol, IN["L0_ffn_bk"], bex_sb, brow, bk_c, idf, pga, pgb, ts, vs, ld)
    emit_bias_combine(p, wcol, IN["L0_ffn_bv"], bex_sb, brow, bv_c, idf, pga, pgb, ts, vs, ld)
    p.wait("vector", ld, ld.n)
    p.on("vector", lambda e: e.tensor_scalar(bo_h[:], bcol_sb[:], 0.5, None, ALU.mult))
    p.inc("vector", vs)
    p.barrier()

    def combine(which):
        win = IN["L0_ffn_W" + which]
        for ci in range(NCH):
            if ci >= 1:
                p.wait("gpsimd", cmb, cmb.n)
            for ex in range(4):
                p.dma("gpsimd", wstr[:, ex * 1024:(ex + 1) * 1024],
                      win[ex, ci * 128:(ci + 1) * 128, :], sem=wl)
            p.wait("vector", wl, wl.n)
            p.on("vector", lambda e: e.tensor_scalar(wacc[:], wstr[:, 0:1024], wbc[:, 0:1], None, ALU.mult))
            for ex in (1, 2):
                p.on("vector", lambda e, ex=ex: e.scalar_tensor_tensor(
                    wacc[:], wstr[:, ex * 1024:(ex + 1) * 1024], wbc[:, ex:ex + 1], wacc[:], ALU.mult, ALU.add))
            p.on("vector", lambda e, ci=ci: e.scalar_tensor_tensor(
                wc[:, ci * 1024:(ci + 1) * 1024], wstr[:, 3 * 1024:4 * 1024], wbc[:, 3:4], wacc[:], ALU.mult, ALU.add))
            p.inc("vector", cmb)
        p.barrier()

    def proj_oc(oc, dest, bias_col, scale2=None):
        for tb in range(NTB):
            bi = tb % 2
            if pt.n >= 2:
                p.wait("tensor", pe, pe.n - 1)
            for ci in range(NCH):
                p.on("tensor", lambda e, o=pj[bi][:, :], l=wc[:, ci * 1024 + oc * 128: ci * 1024 + (oc + 1) * 128],
                     r=h_bf[:, ci * T_LOC + tb * TB: ci * T_LOC + (tb + 1) * TB],
                     s0=(ci == 0), s1=(ci == NCH - 1): e.matmul(o, l, r, start=s0, stop=s1))
            tt = p.inc("tensor", pt)
            p.wait("vector", pt, tt)
            if scale2 is None:
                p.on("vector", lambda e, o=dest[:, tb * TB:(tb + 1) * TB], ps=pj[bi][:, :], b=bias_col:
                     e.tensor_scalar(o, ps, b, None, ALU.add))
            else:
                p.on("vector", lambda e, o=dest[:, tb * TB:(tb + 1) * TB], ps=pj[bi][:, :], b=bias_col:
                     e.tensor_scalar(o, ps, b, scale2, ALU.add, ALU.mult))
            p.inc("vector", pe)

    def norms_of(src2048, oc, ntile):
        p.on("vector", lambda e: e.tensor_mul(sqc[:], src2048, src2048))
        sv2 = p.inc("vector", vs)
        p.wait("tensor", vs, sv2)
        for tb in range(NTB):
            p.on("tensor", lambda e, tb=tb: e.matmul(pn[0:2, 0:512], hsel2[:], sqc[:, tb * TB:(tb + 1) * TB],
                                                     start=True, stop=True))
            tt = p.inc("tensor", pt)
            p.wait("vector", pt, tt)
            p.on("vector", lambda e, tb=tb: e.reduce_max(nred[0:2, tb:tb + 1], pn[0:2, 0:512], axis=AXX))
            p.inc("vector", pe)
            p.wait("tensor", pe, pe.n)
        p.on("vector", lambda e: e.tensor_tensor(nred[0:2, 4:5], nred[0:2, 0:1], nred[0:2, 1:2], ALU.max))
        p.on("vector", lambda e: e.tensor_tensor(nred[0:2, 5:6], nred[0:2, 2:3], nred[0:2, 3:4], ALU.max))
        p.on("vector", lambda e, oc=oc: e.tensor_tensor(ntile[0:2, oc:oc + 1], nred[0:2, 4:5], nred[0:2, 5:6], ALU.max))
        p.inc("vector", vs)

    combine("v")
    vtmp = q_bf
    for oc in range(NCH):
        proj_oc(oc, vtmp[:, oc * T_LOC:(oc + 1) * T_LOC], bv_c[:, oc:oc + 1])
    p.barrier()
    emit_v_to_ag(p, C, vtmp, vrow, ptr, DR["agv_in"], ld, vs, pt)

    combine("k")
    for oc in range(NCH):
        proj_oc(oc, pch, bk_c[:, oc:oc + 1])
        p.barrier()
        norms_of(pch[:], oc, nk)
        dvv = p.inc("vector", vs)
        p.wait("sync", vs, dvv)
        _, dtgt = p.dma("sync", DR["agk_in"][oc * 128:(oc + 1) * 128, :], pch[:], sem=ld)
        p.wait("vector", ld, dtgt)
    p.barrier()

    p.dma("sync", DR["arn_in"], nk[:], sem=ld)
    p.wait("gpsimd", ld, ld.n)
    p.ctl("gpsimd", lambda e: e.collective_compute(
        "AllGather", ALU.bypass, GROUPS, [DR["agk_in"].opt()], [DR["agk_out"].opt()]).then_inc(cc.h, 1))
    cc.n += 1
    p.ctl("gpsimd", lambda e: e.collective_compute(
        "AllGather", ALU.bypass, GROUPS, [DR["agv_in"].opt()], [DR["agv_out"].opt()]).then_inc(cc.h, 1))
    cc.n += 1
    p.ctl("gpsimd", lambda e: e.collective_compute(
        "AllReduce", ALU.max, GROUPS, [DR["arn_in"].opt()], [DR["arn_out"].opt()]).then_inc(cc.h, 1))
    cc.n += 1

    combine("q")
    for oc in range(NCH):
        proj_oc(oc, q_bf[:, oc * T_LOC:(oc + 1) * T_LOC], bq_c[:, oc:oc + 1], scale2=0.125)
        p.barrier()
        norms_of(q_bf[:, oc * T_LOC:(oc + 1) * T_LOC], oc, nq)
    p.barrier()
    es1.close()

    es2 = ExitStack()
    K_pair = p.sbuf(tag + "K_pair", [128, 8192], BF16, es=es2)
    vaug = p.sbuf(tag + "vaug", [128, 64 * 130], BF16, es=es2)
    P_sb = [p.sbuf(tag + "P0", [128, 1024], BF16, es=es2),
            p.sbuf(tag + "P1", [128, 1024], BF16, es=es2)]
    wo_bf = p.sbuf(tag + "wo", [128, 8 * 1024], BF16, es=es2)
    rrow = p.sbuf(tag + "rrow", [1, 512], F32, es=es2)
    rc4 = p.sbuf(tag + "rc4", [128, 4], F32, es=es2)
    rbc = p.sbuf(tag + "rbc", [128, 512], F32, es=es2)
    nrow = p.sbuf(tag + "nrow", [1, 16], F32, es=es2)
    psc = [p.psum(tag + "psc0", [128, 1024], es=es2), p.psum(tag + "psc1", [128, 1024], es=es2)]
    po = [p.psum(tag + "poa", [65, 512], es=es2), p.psum(tag + "pob", [65, 512], es=es2)]
    pmisc = p.psum(tag + "pmisc", [128, 512], es=es2)
    prb = p.psum(tag + "prb", [128, 512], es=es2)
    kl = p.sem(tag + "kl")
    scs = p.sem(tag + "scs")
    exs = p.sem(tag + "exs")
    pvs = p.sem(tag + "pvs")

    for ci in range(NCH):
        p.dma("gpsimd", wo_bf[:, ci * 1024:(ci + 1) * 1024], IN["L0_ffn_Wo"][ci * 128:(ci + 1) * 128, :], sem=gld)

    p.wait("sync", cc, cc.n)
    p.dma("sync", nk[:], DR["arn_out"], sem=ld)
    p.wait("vector", ld, ld.n)
    p.on("vector", lambda e: e.tensor_mul(nq[0:2, 0:8], nq[0:2, 0:8], nk[0:2, 0:8]))
    pv = p.inc("vector", vs)
    p.wait("scalar", vs, pv)
    p.on("scalar", lambda e: e.activation(nq[0:2, 0:8], nq[0:2, 0:8], AF.Sqrt))
    s2_ = p.inc("scalar", ss)
    p.wait("vector", ss, s2_)
    p.on("vector", lambda e: e.tensor_scalar(nq[0:2, 0:8], nq[0:2, 0:8], -0.125, None, ALU.mult))
    pv = p.inc("vector", vs)
    p.wait("tensor", vs, pv)
    p.on("tensor", lambda e: e.transpose(pmisc[0:8, 0:2], nq[0:2, 0:8], idf[0:2, 0:2]))
    tt = p.inc("tensor", ts)
    p.wait("vector", ts, tt)
    p.on("vector", lambda e: e.tensor_reduce(rc4[0:8, 0:1], pmisc[0:8, 0:2], AXX, ALU.min))
    pv = p.inc("vector", vs)
    p.wait("tensor", vs, pv)
    p.on("tensor", lambda e: e.transpose(pmisc[0:1, 8:16], rc4[0:8, 0:1], idf[0:8, 0:8]))
    tt = p.inc("tensor", ts)
    p.wait("vector", ts, tt)
    p.on("vector", lambda e: e.tensor_copy(nrow[0:1, 0:8], pmisc[0:1, 8:16]))
    pv = p.inc("vector", vs)
    p.wait("tensor", vs, pv)
    p.on("tensor", lambda e: e.matmul(prb[0:128, 0:8], o1f[0:1, :], nrow[0:1, 0:8], start=True, stop=True))
    tt = p.inc("tensor", ts)
    p.wait("vector", ts, tt)
    p.on("vector", lambda e: e.tensor_copy(negBp[:, 0:8], prb[0:128, 0:8]))
    p.inc("vector", vs)
    p.barrier()

    halfrow = p.sbuf(tag + "halfrow", [1, 128], F32, es=es2)
    p.dma("sync", halfrow[:], IN["c_half_row_f32"], sem=ld)
    for e_ in P.ENGINES:
        p.wait(e_, ld, ld.n)

    for hp in range(NCH):
        for r in range(4):
            p.dma("sync", K_pair[:, r * 2048:(r + 1) * 2048],
                  DR["agk_out"][r * 1024 + hp * 128: r * 1024 + (hp + 1) * 128, :], sem=kl)
            p.dma("sync", vaug[:, r * 16 * 130:(r + 1) * 16 * 130],
                  bass.AP(DR["agv_out"].tensor, (r * 2048) * 1040 + hp * 130,
                          [[1040, 128], [1040 * 128, 16], [1, 130]]), sem=kl)
        p.wait("tensor", kl, kl.n)
        p.wait("scalar", kl, kl.n)
        for qb in range(NTB):
            qsl = slice(hp * T_LOC + qb * TB, hp * T_LOC + (qb + 1) * TB)
            nkc = 64
            base_ex = exs.n
            base_pv = pvs.n
            for kc in range(nkc):
                bi = kc % 2
                if kc >= 2:
                    p.wait("tensor", exs, base_ex + kc - 1)
                p.on("tensor", lambda e, bi=bi, kc=kc, qsl=qsl: e.matmul(
                    psc[bi][:, 0:512], K_pair[0:64, kc * 128:(kc + 1) * 128], q_bf[0:64, qsl],
                    start=True, stop=True))
                p.on("tensor", lambda e, bi=bi, kc=kc, qsl=qsl: e.matmul(
                    psc[bi][:, 512:1024], K_pair[64:128, kc * 128:(kc + 1) * 128], q_bf[64:128, qsl],
                    start=True, stop=True, tile_position=(64, 0)))
                sct = p.inc("tensor", scs)
                p.wait("scalar", scs, sct)
                if kc >= 2:
                    p.wait("scalar", pvs, base_pv + kc - 1)
                p.on("scalar", lambda e, bi=bi, hp=hp: e.activation(
                    P_sb[bi][:], psc[bi][:], AF.Exp, bias=negBp[:, hp:hp + 1]))
                ext = p.inc("scalar", exs)
                p.wait("tensor", exs, ext)
                p.on("tensor", lambda e, bi=bi, kc=kc, s0=(kc == 0), s1=(kc == nkc - 1): e.matmul(
                    po[0][0:65, :], vaug[:, kc * 130:kc * 130 + 65], P_sb[bi][:, 0:512],
                    start=s0, stop=s1))
                p.on("tensor", lambda e, bi=bi, kc=kc, s0=(kc == 0), s1=(kc == nkc - 1): e.matmul(
                    po[1][0:65, :], vaug[:, kc * 130 + 65:(kc + 1) * 130], P_sb[bi][:, 512:1024],
                    start=s0, stop=s1))
                p.inc("tensor", pvs)
            p.wait("vector", pvs, pvs.n)
            for par in range(2):
                osl_r = slice(par * 64, par * 64 + 64)
                osl_c = slice(hp * T_LOC + qb * TB, hp * T_LOC + (qb + 1) * TB)
                emit_norm_evac(p, C, po[par], o_coll[osl_r, osl_c], halfrow,
                               rrow, rc4, rbc, pmisc, prb, ts, vs)
            p.barrier()

    pjw = [psc[0], psc[1]]
    wos = p.sem(tag + "wos")
    woe = p.sem(tag + "woe")
    p.wait("tensor", gld, gld.n)
    k = 0
    for tb in range(NTB):
        for oc in range(NCH):
            bi = k % 2
            if k >= 2:
                p.wait("tensor", woe, k - 1)
            for ci in range(NCH):
                p.on("tensor", lambda e, o=pjw[bi][:, 0:512], l=wo_bf[:, ci * 1024 + oc * 128: ci * 1024 + (oc + 1) * 128],
                     r=o_coll[:, ci * T_LOC + tb * TB: ci * T_LOC + (tb + 1) * TB],
                     s0=(ci == 0), s1=(ci == NCH - 1): e.matmul(o, l, r, start=s0, stop=s1))
            tt = p.inc("tensor", wos)
            p.wait("vector", wos, tt)
            xs = xf[:, oc * T_LOC + tb * TB: oc * T_LOC + (tb + 1) * TB]
            p.on("vector", lambda e, x_=xs, bi=bi, oc=oc: e.scalar_tensor_tensor(
                x_, pjw[bi][:, 0:512], bo_h[:, oc:oc + 1], x_, ALU.add, ALU.add))
            p.inc("vector", woe)
            k += 1
    p.barrier()
    es2.close()
    es0.close()



def emit_l0attn(p, C, xf, IN, DR, core_hg):
    import os as _os
    p.uid += 1
    tag = f"B{p.uid}_"
    h_bf = C["s_hbf"]
    emit_ln(p, C, xf, h_bf, tag=tag + "ln")

    idf = C["s_ident_f32"]
    o1f = C["s_onesrow_f32"]
    ts, vs, ss = p.sem(tag + "t"), p.sem(tag + "v"), p.sem(tag + "s")
    ld = p.sem(tag + "ld")
    gld = p.sem(tag + "gld")
    cc = p.sem(tag + "cc")
    pe, pt = p.sem(tag + "pe"), p.sem(tag + "pt")

    es0 = ExitStack()
    negB = p.sbuf(tag + "negB", [128, 1], F32, es=es0)
    rtexp = p.sbuf(tag + "rtexp", [8, 1], F32, es=es0)
    es1 = ExitStack()
    wq_bf = p.sbuf(tag + "wq", [128, 8 * 256], BF16, es=es1)
    wk_bf = p.sbuf(tag + "wk", [128, 8 * 256], BF16, es=es1)
    wv_bf = p.sbuf(tag + "wv", [128, 8 * 1024], BF16, es=es1)
    for ci in range(NCH):
        p.dma("gpsimd", wq_bf[:, ci * 256:(ci + 1) * 256], IN["L0_attn_Wq"][ci * 128:(ci + 1) * 128, :], sem=gld)
        p.dma("gpsimd", wk_bf[:, ci * 256:(ci + 1) * 256], IN["L0_attn_Wk"][ci * 128:(ci + 1) * 128, :], sem=gld)
        p.dma("gpsimd", wv_bf[:, ci * 1024:(ci + 1) * 1024], IN["L0_attn_Wv"][ci * 128:(ci + 1) * 128, :], sem=gld)
    bq_c = p.sbuf(tag + "bq_c", [128, 2], F32, es=es1)
    bk_c = p.sbuf(tag + "bk_c", [128, 2], F32, es=es1)
    bv_c = p.sbuf(tag + "bv_c", [128, 8], F32, es=es1)
    qp_c = p.sbuf(tag + "qp_c", [128, 2], F32, es=es1)
    kp_c = p.sbuf(tag + "kp_c", [128, 2], F32, es=es1)
    hsel16 = p.sbuf(tag + "hs16", [128, 8], BF16, es=es1)
    hexp16 = p.sbuf(tag + "he16", [8, 128], BF16, es=es1)
    tmp_t = p.sbuf(tag + "tmpt", [1, 1], F32, es=es1)
    rt8 = p.sbuf(tag + "rt8", [8, 1], F32, es=es1)
    for name, sb in (("L0_bq_col", bq_c), ("L0_bk_col", bk_c), ("L0_bv_col", bv_c),
                     ("L0_qp_col", qp_c), ("L0_kp_col", kp_c),
                     ("c_hsel16", hsel16), ("c_hexp16", hexp16)):
        p.dma("sync", sb[:], IN[name], sem=ld)
    p.dma("sync", tmp_t[:], IN["L0_attn_temperature"], sem=ld)

    qh_bf = p.sbuf(tag + "qh", [128, 2 * T_LOC], BF16, es=es1)
    kh_bf = p.sbuf(tag + "kh", [128, 2 * T_LOC], BF16, es=es1)
    vtmp = p.sbuf(tag + "vtmp", [128, NCH * T_LOC], BF16, es=es1)
    sqc = p.sbuf(tag + "sqc", [128, TB], BF16, es=es1)
    rn8 = p.sbuf(tag + "rn8", [8, TB], F32, es=es1)
    rn8b = p.sbuf(tag + "rn8b", [8, TB], BF16, es=es1)
    pj = [p.psum(tag + "pj0", [128, TB], es=es1), p.psum(tag + "pj1", [128, TB], es=es1)]
    pn8 = p.psum(tag + "pn8", [8, TB], es=es1)
    pex = p.psum(tag + "pex", [128, TB], es=es1)

    p.wait("vector", ld, ld.n)
    p.on("vector", lambda e: e.reciprocal(tmp_t[:], tmp_t[:]))
    pv = p.inc("vector", vs)
    p.wait("tensor", vs, pv)
    p.on("tensor", lambda e: e.matmul(pex[0:128, 0:1], o1f[0:1, :], tmp_t[:], start=True, stop=True))
    tt = p.inc("tensor", ts)
    p.wait("vector", ts, tt)
    p.on("vector", lambda e: e.tensor_copy(rt8[:], pex[0:8, 0:1]))
    p.on("vector", lambda e: e.tensor_copy(rtexp[:], pex[0:8, 0:1]))
    p.on("vector", lambda e: e.tensor_scalar(negB[:], pex[0:128, 0:1], -1.0, None, ALU.mult))
    p.on("vector", lambda e: e.tensor_tensor(negB[:], negB[:], pex[0:128, 0:1], ALU.min))
    p.inc("vector", vs)
    p.barrier()

    def proj_norm(wmat, nocs, bias_col, phase_col, dest, rt):
        for oc in range(nocs):
            k = 0
            for tb in range(NTB):
                bi = k % 2
                p.wait("tensor", pe, max(0, pe.n - 1))
                for ci in range(NCH):
                    p.on("tensor", lambda e, o=pj[bi][:, :], l=wmat[:, ci * 256 + oc * 128: ci * 256 + (oc + 1) * 128],
                         r=h_bf[:, ci * T_LOC + tb * TB: ci * T_LOC + (tb + 1) * TB],
                         s0=(ci == 0), s1=(ci == NCH - 1): e.matmul(o, l, r, start=s0, stop=s1))
                tt = p.inc("tensor", pt)
                p.wait("vector", pt, tt)
                raw = dest[:, oc * T_LOC + tb * TB: oc * T_LOC + (tb + 1) * TB]
                p.on("vector", lambda e, raw=raw, bi=bi, oc=oc: e.tensor_scalar(
                    raw, pj[bi][:, :], bias_col[:, oc:oc + 1], phase_col[:, oc:oc + 1], ALU.add, ALU.add))
                p.inc("vector", pe)
                p.on("vector", lambda e, raw=raw: e.tensor_mul(sqc[:], raw, raw))
                sv2 = p.inc("vector", vs)
                p.wait("tensor", vs, sv2)
                p.on("tensor", lambda e: e.matmul(pn8[0:8, :], hsel16[:], sqc[:], start=True, stop=True))
                tt2 = p.inc("tensor", pt)
                p.wait("scalar", pt, tt2)
                p.on("scalar", lambda e: e.activation(rn8[0:8, :], pn8[0:8, :], AF.Sqrt))
                sg = p.inc("scalar", ss)
                p.wait("vector", ss, sg)
                p.on("vector", lambda e: e.tensor_scalar(rn8[0:8, :], rn8[0:8, :], 1e-12, None, ALU.max))
                p.on("vector", lambda e: e.reciprocal(rn8[0:8, :], rn8[0:8, :]))
                if rt is not None:
                    p.on("vector", lambda e: e.tensor_scalar(rn8[0:8, :], rn8[0:8, :], rt[0:8, 0:1], None, ALU.mult))
                p.on("vector", lambda e: e.tensor_copy(rn8b[0:8, :], rn8[0:8, :]))
                sv3 = p.inc("vector", vs)
                p.wait("tensor", vs, sv3)
                p.on("tensor", lambda e: e.matmul(pex[:, :], hexp16[0:8, :], rn8b[0:8, :], start=True, stop=True))
                tt3 = p.inc("tensor", pt)
                p.wait("vector", pt, tt3)
                p.on("vector", lambda e, raw=raw: e.tensor_tensor(raw, raw, pex[:, :], ALU.mult))
                p.inc("vector", vs)
                k += 1
            p.barrier()

    proj_norm(wq_bf, 2, bq_c, qp_c, qh_bf, rt8)
    proj_norm(wk_bf, 2, bk_c, kp_c, kh_bf, None)
    for oc in range(NCH):
        k = 0
        for tb in range(NTB):
            bi = k % 2
            p.wait("tensor", pe, max(0, pe.n - 1))
            for ci in range(NCH):
                p.on("tensor", lambda e, o=pj[bi][:, :], l=wv_bf[:, ci * 1024 + oc * 128: ci * 1024 + (oc + 1) * 128],
                     r=h_bf[:, ci * T_LOC + tb * TB: ci * T_LOC + (tb + 1) * TB],
                     s0=(ci == 0), s1=(ci == NCH - 1): e.matmul(o, l, r, start=s0, stop=s1))
            tt = p.inc("tensor", pt)
            p.wait("vector", pt, tt)
            p.on("vector", lambda e, oc=oc, tb=tb, bi=bi: e.tensor_scalar(
                vtmp[:, oc * T_LOC + tb * TB: oc * T_LOC + (tb + 1) * TB],
                pj[bi][:, :], bv_c[:, oc:oc + 1], None, ALU.add))
            p.inc("vector", pe)
            k += 1
    p.barrier()
    for oc in range(2):
        p.dma("sync", DR["agq_in"][oc * 128:(oc + 1) * 128, :], qh_bf[:, oc * T_LOC:(oc + 1) * T_LOC], sem=ld)
        p.dma("sync", DR["agk_in"][oc * 128:(oc + 1) * 128, :], kh_bf[:, oc * T_LOC:(oc + 1) * T_LOC], sem=ld)
    for oc in range(NCH):
        p.dma("sync", DR["agvf_in"][oc * 128:(oc + 1) * 128, :], vtmp[:, oc * T_LOC:(oc + 1) * T_LOC], sem=ld)
    p.wait("gpsimd", ld, ld.n)
    p.ctl("gpsimd", lambda e: e.collective_compute(
        "AllGather", ALU.bypass, GROUPS, [DR["agq_in"].opt()], [DR["agq_out"].opt()]).then_inc(cc.h, 1))
    cc.n += 1
    p.ctl("gpsimd", lambda e: e.collective_compute(
        "AllGather", ALU.bypass, GROUPS, [DR["agk_in"].opt()], [DR["agk_out"].opt()]).then_inc(cc.h, 1))
    cc.n += 1
    p.ctl("gpsimd", lambda e: e.collective_compute(
        "AllGather", ALU.bypass, GROUPS, [DR["agvf_in"].opt()], [DR["agvf_out"].opt()]).then_inc(cc.h, 1))
    cc.n += 1
    p.barrier()
    es1.close()

    es2 = ExitStack()
    o_coll = h_bf
    Q2 = p.sbuf(tag + "Q2", [64, 8192], BF16, es=es2)
    K2 = p.sbuf(tag + "K2", [64, 8192], BF16, es=es2)
    v2 = p.sbuf(tag + "v2", [128, 8192], BF16, es=es2)
    vaug = p.sbuf(tag + "vaug", [128, 64 * 130], BF16, es=es2)
    P_sb = [p.sbuf(tag + "P0", [128, 1024], BF16, es=es2),
            p.sbuf(tag + "P1", [128, 1024], BF16, es=es2)]
    masks = p.sbuf(tag + "masks", [128, 4 * 512], BF16, es=es2)
    rrow = p.sbuf(tag + "rrow", [1, 512], F32, es=es2)
    rc4 = p.sbuf(tag + "rc4", [128, 4], F32, es=es2)
    rbc = p.sbuf(tag + "rbc", [128, 512], F32, es=es2)
    onerow = p.sbuf(tag + "onerow", [1, 128], F32, es=es2)
    psc = [p.psum(tag + "psc0", [128, 1024], es=es2), p.psum(tag + "psc1", [128, 1024], es=es2)]
    po = [p.psum(tag + "poa", [65, 512], es=es2), p.psum(tag + "pob", [65, 512], es=es2)]
    pmisc = p.psum(tag + "pmisc", [128, 512], es=es2)
    prb = pmisc
    ptr2 = p.psum(tag + "ptr2", [128, 1024], BF16, es=es2)
    kl = p.sem(tag + "kl")
    scs, exs, pvs = p.sem(tag + "scs"), p.sem(tag + "exs"), p.sem(tag + "pvs")
    mks = p.sem(tag + "mks")
    p.dma("sync", masks[:], IN["c_masks_causal"], sem=ld)
    p.dma("sync", onerow[:], IN["c_ones_row_f32"], sem=ld)
    p.wait("sync", cc, cc.n)

    for j2 in range(128):
        p.on("gpsimd", lambda e, j2=j2: e.memset(vaug[:, 65 * j2 + 64:65 * j2 + 65], 1.0)) if j2 < 2 * 64 else None
    gset2 = p.sem(tag + "gsetb")
    p.inc("gpsimd", gset2)
    p.wait("vector", gset2, gset2.n)

    bidx = p.sbuf(tag + "bidx", [128, 56], mybir.dt.uint32, es=es2)
    p.dma("sync", bidx[:], IN["t_bidx"], sem=ld)
    p.wait("gpsimd", ld, ld.n)
    gg = p.sem(tag + "gg")
    for hp in range(2):
        for g in range(8):
            qcol = hp * 16 + g
            kcol = hp * 16 + 8 + g
            vcol = 32 + hp * 8 + g
            if gg.n > 0:
                p.ctl("gpsimd", lambda e, v=gg.n: e.wait_ge(gg.h, v))
            gg.inc(16)
            p.ctl("gpsimd", lambda e, g=g, qcol=qcol, v=gg.n: e.indirect_dma_start(
                Q2[0:48, g * 1024:(g + 1) * 1024], None,
                DR["agq_out"].rearrange("a (b c) -> (a b) c", b=2), IndirectOffsetOnAxis(bidx[:, qcol:qcol + 1], 0)
            ).then_inc(gg.h, 16))
            if gg.n > 0:
                p.ctl("gpsimd", lambda e, v=gg.n: e.wait_ge(gg.h, v))
            gg.inc(16)
            p.ctl("gpsimd", lambda e, g=g, kcol=kcol, v=gg.n: e.indirect_dma_start(
                K2[0:48, g * 1024:(g + 1) * 1024], None,
                DR["agk_out"].rearrange("a (b c) -> (a b) c", b=2), IndirectOffsetOnAxis(bidx[:, kcol:kcol + 1], 0)
            ).then_inc(gg.h, 16))
            if gg.n > 0:
                p.ctl("gpsimd", lambda e, v=gg.n: e.wait_ge(gg.h, v))
            gg.inc(16)
            p.ctl("gpsimd", lambda e, g=g, vcol=vcol, v=gg.n: e.indirect_dma_start(
                v2[:, g * 1024:(g + 1) * 1024], None,
                DR["agvf_out"].rearrange("a (b c) -> (a b) c", b=2), IndirectOffsetOnAxis(bidx[:, vcol:vcol + 1], 0)
            ).then_inc(gg.h, 16))
        for e_ in ("tensor", "scalar", "vector"):
            p.wait(e_, gg, gg.n)
        p.wait("tensor", kl, kl.n)
        p.wait("scalar", kl, kl.n)
        p.wait("vector", kl, kl.n)
        for tau8 in range(8):
            if tau8 >= 1:
                p.wait("tensor", vs, vs.n)
            for sub in range(8):
                tau = tau8 * 8 + sub
                p.on("tensor", lambda e, tau=tau, sub=sub: e.transpose(
                    ptr2[:, sub * 128:(sub + 1) * 128],
                    v2[:, tau * 128:(tau + 1) * 128], C["s_ident_bf"][:]))
            tt = p.inc("tensor", pt)
            p.wait("vector", pt, tt)
            for sub in range(8):
                tau = tau8 * 8 + sub
                for par in range(2):
                    p.on("vector", lambda e, tau=tau, sub=sub, par=par: e.tensor_copy(
                        vaug[:, tau * 130 + 65 * par: tau * 130 + 65 * par + 64],
                        ptr2[:, sub * 128 + par * 64: sub * 128 + par * 64 + 64]))
            p.inc("vector", vs)
        p.barrier()
        for G in range(16):
            nkc = 4 * G + 4
            base_ex = exs.n
            base_pv = pvs.n
            for kc in range(nkc):
                bi = kc % 2
                if kc >= 2:
                    p.wait("tensor", exs, base_ex + kc - 1)
                p.on("tensor", lambda e, bi=bi, kc=kc, G=G: e.matmul(
                    psc[bi][:, 0:512], K2[0:16, kc * 128:(kc + 1) * 128], Q2[0:16, G * 512:(G + 1) * 512],
                    start=True, stop=True))
                p.on("tensor", lambda e, bi=bi, kc=kc, G=G: e.matmul(
                    psc[bi][:, 512:1024], K2[32:48, kc * 128:(kc + 1) * 128], Q2[32:48, G * 512:(G + 1) * 512],
                    start=True, stop=True, tile_position=(32, 0)))
                sct = p.inc("tensor", scs)
                p.wait("scalar", scs, sct)
                if kc >= 2:
                    p.wait("scalar", pvs, base_pv + kc - 1)
                p.on("scalar", lambda e, bi=bi: e.activation(
                    P_sb[bi][:], psc[bi][:], AF.Exp, bias=negB[:, 0:1]))
                ext = p.inc("scalar", exs)
                if kc >= 4 * G:
                    off = kc - 4 * G
                    p.wait("vector", exs, ext)
                    p.on("vector", lambda e, bi=bi, off=off: e.tensor_mul(
                        P_sb[bi][:, 0:512], P_sb[bi][:, 0:512], masks[:, off * 512:(off + 1) * 512]))
                    p.on("vector", lambda e, bi=bi, off=off: e.tensor_mul(
                        P_sb[bi][:, 512:1024], P_sb[bi][:, 512:1024], masks[:, off * 512:(off + 1) * 512]))
                    mkt = p.inc("vector", mks)
                    p.wait("tensor", mks, mkt)
                else:
                    p.wait("tensor", exs, ext)
                p.on("tensor", lambda e, bi=bi, kc=kc, s0=(kc == 0), s1=(kc == nkc - 1): e.matmul(
                    po[0][0:65, :], vaug[:, kc * 130:kc * 130 + 65], P_sb[bi][:, 0:512],
                    start=s0, stop=s1))
                p.on("tensor", lambda e, bi=bi, kc=kc, s0=(kc == 0), s1=(kc == nkc - 1): e.matmul(
                    po[1][0:65, :], vaug[:, kc * 130 + 65:(kc + 1) * 130], P_sb[bi][:, 512:1024],
                    start=s0, stop=s1))
                p.inc("tensor", pvs)
            p.wait("vector", pvs, pvs.n)
            if _os.environ.get("EDBG") == "po" and hp == 0 and G == 1:
                p.on("vector", lambda e: e.tensor_copy(xf[0:65, 0:512], po[0][0:65, :]))
                p.on("vector", lambda e: e.tensor_copy(xf[0:65, 512:1024], po[1][0:65, :]))
                p.inc("vector", vs)
                p.barrier()
                es2.close()
                es0.close()
                return
            for par in range(2):
                osl_r = slice(par * 64, par * 64 + 64)
                osl_c = slice(hp * 8192 + G * TB, hp * 8192 + (G + 1) * TB)
                emit_norm_evac(p, C, po[par], o_coll[osl_r, osl_c], onerow,
                               rrow, rc4, rbc, pmisc, prb, ts, vs)
            p.barrier()

    for X in range(4):
        for hp in range(2):
            for par in range(2):
                for gi, g in enumerate((X, 7 - X)):
                    p.dma("sync", DR["ago%d_in" % X][(2 * hp + par) * 64:(2 * hp + par) * 64 + 64,
                                                     gi * 1024:(gi + 1) * 1024],
                          o_coll[par * 64:par * 64 + 64, hp * 8192 + g * 1024: hp * 8192 + (g + 1) * 1024], sem=ld)
    p.wait("gpsimd", ld, ld.n)
    for X in range(4):
        p.ctl("gpsimd", lambda e, X=X: e.collective_compute(
            "AllGather", ALU.bypass, GROUPS, [DR["ago%d_in" % X].opt()],
            [DR["agofm_out"][X * 1024:(X + 1) * 1024, :].opt()]).then_inc(cc.h, 1))
        cc.n += 1
    p.wait("gpsimd", cc, cc.n)
    for ci in range(NCH):
        ocol = 48 + ci
        if gg.n > 0:
            p.ctl("gpsimd", lambda e, v=gg.n: e.wait_ge(gg.h, v))
        gg.inc(16)
        p.ctl("gpsimd", lambda e, ci=ci, ocol=ocol: e.indirect_dma_start(
            o_coll[:, ci * T_LOC:(ci + 1) * T_LOC], None,
            DR["agofm_out"], IndirectOffsetOnAxis(bidx[:, ocol:ocol + 1], 0)
        ).then_inc(gg.h, 16))
    for e_ in P.ENGINES:
        p.wait(e_, gg, gg.n)
    p.barrier()
    es2.close()
    es3 = ExitStack()
    wo_bf = p.sbuf(tag + "wo", [128, 8 * 1024], BF16, es=es3)
    bo_c = p.sbuf(tag + "bo_c", [128, 8], F32, es=es3)
    pjw = [p.psum(tag + "pjw0", [128, 512], es=es3), p.psum(tag + "pjw1", [128, 512], es=es3)]
    for ci in range(NCH):
        p.dma("gpsimd", wo_bf[:, ci * 1024:(ci + 1) * 1024], IN["L0_attn_Wo"][ci * 128:(ci + 1) * 128, :], sem=gld)
    p.dma("sync", bo_c[:], IN["L0_bo_col"], sem=ld)
    for e_ in P.ENGINES:
        p.wait(e_, ld, ld.n)
        p.wait(e_, gld, gld.n)

    wos = p.sem(tag + "wos")
    woe = p.sem(tag + "woe")
    k = 0
    for tb in range(NTB):
        for oc in range(NCH):
            bi = k % 2
            if k >= 2:
                p.wait("tensor", woe, k - 1)
            for ci in range(NCH):
                p.on("tensor", lambda e, o=pjw[bi][:, :], l=wo_bf[:, ci * 1024 + oc * 128: ci * 1024 + (oc + 1) * 128],
                     r=o_coll[:, ci * T_LOC + tb * TB: ci * T_LOC + (tb + 1) * TB],
                     s0=(ci == 0), s1=(ci == NCH - 1): e.matmul(o, l, r, start=s0, stop=s1))
            tt = p.inc("tensor", wos)
            p.wait("vector", wos, tt)
            xs = xf[:, oc * T_LOC + tb * TB: oc * T_LOC + (tb + 1) * TB]
            p.on("vector", lambda e, x_=xs, bi=bi, oc=oc: e.scalar_tensor_tensor(
                x_, pjw[bi][:, :], bo_c[:, oc:oc + 1], x_, ALU.add, ALU.add))
            p.inc("vector", woe)
            k += 1
    p.barrier()
    es3.close()
    es0.close()



def emit_l1attn(p, C, xf, IN, DR):
    p.uid += 1
    tag = f"E{p.uid}_"
    h_bf = C["s_hbf"]
    idf = C["s_ident_f32"]
    o1f = C["s_onesrow_f32"]
    ts, vs, ss = p.sem(tag + "t"), p.sem(tag + "v"), p.sem(tag + "s")
    ld = p.sem(tag + "ld")
    gld = p.sem(tag + "gld")
    cc = p.sem(tag + "cc")
    pe, pt = p.sem(tag + "pe"), p.sem(tag + "pt")

    es0 = ExitStack()
    mubcE = p.sbuf(tag + "mubcE", [128, T_LOC], F32, es=es0)
    rstdE = p.sbuf(tag + "rstdE", [128, T_LOC], F32, es=es0)
    negB = p.sbuf(tag + "negB", [128, 1], F32, es=es0)
    g1c = p.sbuf(tag + "g1c", [128, 8], F32, es=es0)
    b1c = p.sbuf(tag + "b1c", [128, 8], F32, es=es0)
    for name, sb in (("L1_g1_col", g1c), ("L1_b1_col", b1c)):
        p.dma("sync", sb[:], IN[name], sem=ld)
    for e_ in P.ENGINES:
        p.wait(e_, ld, ld.n)
    emit_ln(p, C, xf, h_bf, affine=(g1c, b1c), keep_stats=(mubcE, rstdE),
            rstd_comb=True, tag=tag + "ln")
    import os as _os
    if _os.environ.get("EDBG") == "h":
        for c in range(NCH):
            p.on("vector", lambda e, c=c: e.tensor_copy(
                xf[:, c * T_LOC:(c + 1) * T_LOC], h_bf[:, c * T_LOC:(c + 1) * T_LOC]))
        p.inc("vector", vs)
        p.barrier()
        es0.close()
        return

    es1 = ExitStack()
    wqs = [p.sbuf(tag + "wqs0", [128, 1024], BF16, es=es1),
           p.sbuf(tag + "wqs1", [128, 1024], BF16, es=es1)]
    wk_bf = p.sbuf(tag + "wk", [128, 8 * 256], BF16, es=es1)
    wv_bf = p.sbuf(tag + "wv", [128, 8 * 256], BF16, es=es1)
    for ci in range(NCH):
        p.dma("gpsimd", wk_bf[:, ci * 256:(ci + 1) * 256], IN["L1_Wk_perm"][ci * 128:(ci + 1) * 128, :], sem=gld)
        p.dma("gpsimd", wv_bf[:, ci * 256:(ci + 1) * 256], IN["L1_attn_Wv"][ci * 128:(ci + 1) * 128, :], sem=gld)
    bq_c = p.sbuf(tag + "bq_c", [128, 8], F32, es=es1)
    bk_c = p.sbuf(tag + "bk_c", [128, 2], F32, es=es1)
    bv_c = p.sbuf(tag + "bv_c", [128, 2], F32, es=es1)
    hsel2 = p.sbuf(tag + "hs2", [128, 2], BF16, es=es1)
    ropeC = p.sbuf(tag + "ropeC", [128, T_LOC], BF16, es=es1)
    ropeS = p.sbuf(tag + "ropeS", [128, T_LOC], BF16, es=es1)
    for name, sb in (("L1_bq_col", bq_c), ("L1_bk_col", bk_c), ("L1_bv_col", bv_c),
                     ("c_hsel2", hsel2), ("t_rope_cos", ropeC), ("t_rope_sin", ropeS)):
        p.dma("sync", sb[:], IN[name], sem=ld)
    q_bf = p.sbuf(tag + "q_bf", [128, NCH * T_LOC], BF16, es=es1)
    kv_bf = p.sbuf(tag + "kv_bf", [128, 4 * T_LOC], BF16, es=es1)
    e1 = C["s_cast0"]
    sq5 = C["s_sq0"]
    nq = p.sbuf(tag + "nq", [2, 12], F32, es=es1)
    nred = p.sbuf(tag + "nred", [2, 8], F32, es=es1)
    pj = [p.psum(tag + "pj0", [128, TB], es=es1), p.psum(tag + "pj1", [128, TB], es=es1)]
    pn = p.psum(tag + "pn", [2, 512], es=es1)

    def proj(wmat, wstride, nocs, bias_col, dest, scale2=None):
        for oc in range(nocs):
            for tb in range(NTB):
                bi = (oc * NTB + tb) % 2
                p.wait("tensor", pe, max(0, pe.n - 1))
                for ci in range(NCH):
                    p.on("tensor", lambda e, o=pj[bi][:, :], l=wmat[:, ci * wstride + oc * 128: ci * wstride + (oc + 1) * 128],
                         r=h_bf[:, ci * T_LOC + tb * TB: ci * T_LOC + (tb + 1) * TB],
                         s0=(ci == 0), s1=(ci == NCH - 1): e.matmul(o, l, r, start=s0, stop=s1))
                tt = p.inc("tensor", pt)
                p.wait("vector", pt, tt)
                dsl = dest[:, oc * T_LOC + tb * TB: oc * T_LOC + (tb + 1) * TB]
                if scale2 is None:
                    p.on("vector", lambda e, o=dsl, bi=bi, oc=oc: e.tensor_scalar(
                        o, pj[bi][:, :], bias_col[:, oc:oc + 1], None, ALU.add))
                else:
                    p.on("vector", lambda e, o=dsl, bi=bi, oc=oc: e.tensor_scalar(
                        o, pj[bi][:, :], bias_col[:, oc:oc + 1], scale2, ALU.add, ALU.mult))
                p.inc("vector", pe)

    p.wait("vector", ld, ld.n)
    p.wait("vector", gld, gld.n)
    p.wait("tensor", ld, ld.n)
    p.wait("tensor", gld, gld.n)
    wql = p.sem(tag + "wql")
    for oc in range(NCH):
        bq2 = oc % 2
        if wql.n > 0:
            p.ctl("gpsimd", lambda e, v=wql.n: e.wait_ge(wql.h, v))
        wql.inc(16)
        p.ctl("gpsimd", lambda e, oc=oc, bq2=bq2: e.dma_start(
            out=wqs[bq2][:], in_=bass.AP(IN["L1_Wq_perm"].tensor, oc * 128,
                                         [[1024, 128], [1024 * 128, 8], [1, 128]])).then_inc(wql.h, 16))
        p.wait("tensor", wql, wql.n)
        for tb in range(NTB):
            bi = (oc * NTB + tb) % 2
            p.wait("tensor", pe, max(0, pe.n - 1))
            for ci in range(NCH):
                p.on("tensor", lambda e, o=pj[bi][:, :], l=wqs[bq2][:, ci * 128:(ci + 1) * 128],
                     r=h_bf[:, ci * T_LOC + tb * TB: ci * T_LOC + (tb + 1) * TB],
                     s0=(ci == 0), s1=(ci == NCH - 1): e.matmul(o, l, r, start=s0, stop=s1))
            tt = p.inc("tensor", pt)
            p.wait("vector", pt, tt)
            dsl = q_bf[:, oc * T_LOC + tb * TB: oc * T_LOC + (tb + 1) * TB]
            p.on("vector", lambda e, o=dsl, bi=bi, oc=oc: e.tensor_scalar(
                o, pj[bi][:, :], bq_c[:, oc:oc + 1], 0.125, ALU.add, ALU.mult))
            p.inc("vector", pe)
        wt = pt.n
        p.ctl("gpsimd", lambda e, v=wt: e.wait_ge(pt.h, v))
    proj(wk_bf, 256, 2, bk_c, kv_bf)
    proj(wv_bf, 256, 2, bv_c, kv_bf[:, 2 * T_LOC:])
    p.barrier()

    def norms2(src, col0, nocs):
        for oc in range(nocs):
            for tb in range(NTB):
                sl = src[:, oc * T_LOC + tb * TB: oc * T_LOC + (tb + 1) * TB]
                p.on("vector", lambda e, sl=sl: e.tensor_mul(sq5[:], sl, sl))
                sv2 = p.inc("vector", vs)
                p.wait("tensor", vs, sv2)
                p.on("tensor", lambda e: e.matmul(pn[0:2, 0:512], hsel2[:], sq5[:], start=True, stop=True))
                tt = p.inc("tensor", pt)
                p.wait("vector", pt, tt)
                p.on("vector", lambda e, tb=tb: e.reduce_max(nred[0:2, tb:tb + 1], pn[0:2, 0:512], axis=AXX))
                p.inc("vector", pe)
                p.wait("tensor", pe, pe.n)
            p.on("vector", lambda e: e.tensor_tensor(nred[0:2, 4:5], nred[0:2, 0:1], nred[0:2, 1:2], ALU.max))
            p.on("vector", lambda e: e.tensor_tensor(nred[0:2, 5:6], nred[0:2, 2:3], nred[0:2, 3:4], ALU.max))
            p.on("vector", lambda e, oc=oc, col0=col0: e.tensor_tensor(
                nq[0:2, col0 + oc:col0 + oc + 1], nred[0:2, 4:5], nred[0:2, 5:6], ALU.max))
            p.inc("vector", vs)

    def rope(dest, nocs):
        for oc in range(nocs):
            for tb in range(NTB):
                d = dest[:, oc * T_LOC + tb * TB: oc * T_LOC + (tb + 1) * TB]
                cc_ = ropeC[:, tb * TB:(tb + 1) * TB]
                ss_ = ropeS[:, tb * TB:(tb + 1) * TB]
                p.on("vector", lambda e, d=d, cc_=cc_: e.tensor_mul(e1[:], d, cc_))
                p.on("vector", lambda e, d=d, ss_=ss_: e.tensor_mul(sq5[0:64, :], d[64:128, :], ss_[64:128, :]))
                p.on("vector", lambda e, d=d, ss_=ss_: e.tensor_mul(sq5[64:128, :], d[0:64, :], ss_[0:64, :]))
                p.on("vector", lambda e, d=d: e.tensor_add(d, e1[:], sq5[:]))
            p.inc("vector", vs)

    import os as _os
    if _os.environ.get("EDBG") == "q":
        for c in range(NCH):
            p.on("vector", lambda e, c=c: e.tensor_copy(
                xf[:, c * T_LOC:(c + 1) * T_LOC], q_bf[:, c * T_LOC:(c + 1) * T_LOC]))
        p.inc("vector", vs)
        p.barrier()
        es1.close()
        es0.close()
        return
    if _os.environ.get("EDBG") == "qr":
        pass
    norms2(q_bf, 0, 8)
    norms2(kv_bf, 8, 2)
    p.on("vector", lambda e: e.reduce_max(nq[0:2, 10:11], nq[0:2, 0:8], axis=AXX))
    p.on("vector", lambda e: e.reduce_max(nq[0:2, 11:12], nq[0:2, 8:10], axis=AXX))
    p.inc("vector", vs)
    rope(q_bf, 8)
    rope(kv_bf, 2)
    p.barrier()
    if _os.environ.get("EDBG") == "kv":
        for c in range(4):
            p.on("vector", lambda e, c=c: e.tensor_copy(
                xf[:, c * T_LOC:(c + 1) * T_LOC], kv_bf[:, c * T_LOC:(c + 1) * T_LOC]))
        p.inc("vector", vs)
        p.barrier()
        es1.close()
        es0.close()
        return
    if _os.environ.get("EDBG") == "qrope":
        for c in range(NCH):
            p.on("vector", lambda e, c=c: e.tensor_copy(
                xf[:, c * T_LOC:(c + 1) * T_LOC], q_bf[:, c * T_LOC:(c + 1) * T_LOC]))
        p.inc("vector", vs)
        p.barrier()
        es1.close()
        es0.close()
        return
    p.dma("sync", DR["arnE_in"], nq[0:2, 10:12], sem=ld)
    for oc in range(NCH):
        p.dma("sync", DR["agqE_in"][oc * 128:(oc + 1) * 128, :], q_bf[:, oc * T_LOC:(oc + 1) * T_LOC], sem=ld)
    for oc in range(2):
        p.dma("sync", DR["agkE_in"][oc * 128:(oc + 1) * 128, :], kv_bf[:, oc * T_LOC:(oc + 1) * T_LOC], sem=ld)
        p.dma("sync", DR["agvE_in"][oc * 128:(oc + 1) * 128, :], kv_bf[:, (2 + oc) * T_LOC:(3 + oc) * T_LOC], sem=ld)
    p.wait("gpsimd", ld, ld.n)
    for nm_in, nm_out in (("agqE_in", "agqE_out"), ("agkE_in", "agkE_out"), ("agvE_in", "agvE_out")):
        p.ctl("gpsimd", lambda e, a=nm_in, b=nm_out: e.collective_compute(
            "AllGather", ALU.bypass, GROUPS, [DR[a].opt()], [DR[b].opt()]).then_inc(cc.h, 1))
        cc.n += 1
    p.ctl("gpsimd", lambda e: e.collective_compute(
        "AllReduce", ALU.max, GROUPS, [DR["arnE_in"].opt()], [DR["arnE_out"].opt()]).then_inc(cc.h, 1))
    cc.n += 1
    p.barrier()
    es1.close()

    es2 = ExitStack()
    o_coll = h_bf
    nsc = p.sbuf(tag + "nsc", [2, 4], F32, es=es2)
    Q2 = p.sbuf(tag + "Q2", [128, 8192], BF16, es=es2)
    K2 = p.sbuf(tag + "K2", [128, 8192], BF16, es=es2)
    v1 = p.sbuf(tag + "v1", [64, 8192], BF16, es=es2)
    vaug = p.sbuf(tag + "vaug", [128, 64 * 65], BF16, es=es2)
    P_sb = [p.sbuf(tag + "P0", [128, 1024], BF16, es=es2),
            p.sbuf(tag + "P1", [128, 1024], BF16, es=es2)]
    masks = p.sbuf(tag + "masks", [128, 6 * 512], BF16, es=es2)
    rrow = p.sbuf(tag + "rrow", [1, 512], F32, es=es2)
    rc4 = p.sbuf(tag + "rc4", [128, 4], F32, es=es2)
    rbc = p.sbuf(tag + "rbc", [128, 512], F32, es=es2)
    onerow = p.sbuf(tag + "onerow", [1, 128], F32, es=es2)
    bidx = p.sbuf(tag + "bidx", [128, 56], mybir.dt.uint32, es=es2)
    psc = [p.psum(tag + "psc0", [128, 1024], es=es2), p.psum(tag + "psc1", [128, 1024], es=es2)]
    po = [p.psum(tag + "poa", [65, 512], es=es2), p.psum(tag + "pob", [65, 512], es=es2)]
    pmisc = p.psum(tag + "pmisc", [128, 512], es=es2)
    prb = pmisc
    ptr2 = p.psum(tag + "ptr2", [128, 1024], BF16, es=es2)
    kl = p.sem(tag + "kl")
    gg = p.sem(tag + "gg")
    scs, exs, pvs = p.sem(tag + "scs"), p.sem(tag + "exs"), p.sem(tag + "pvs")
    mks = p.sem(tag + "mks")
    p.dma("sync", masks[:], IN["c_masks_win"], sem=ld)
    p.dma("sync", onerow[:], IN["c_ones_row_f32"], sem=ld)
    p.dma("sync", bidx[:], IN["t_eidx"], sem=ld)
    p.wait("sync", cc, cc.n)
    p.dma("sync", nsc[0:2, 0:2], DR["arnE_out"], sem=ld)
    p.wait("scalar", ld, ld.n)
    p.on("scalar", lambda e: e.activation(nsc[0:2, 0:2], nsc[0:2, 0:2], AF.Sqrt))
    sg = p.inc("scalar", ss)
    p.wait("vector", ss, sg)
    p.on("vector", lambda e: e.tensor_mul(nsc[0:2, 2:3], nsc[0:2, 0:1], nsc[0:2, 1:2]))
    pv = p.inc("vector", vs)
    p.wait("tensor", vs, pv)
    p.on("tensor", lambda e: e.transpose(pmisc[0:1, 0:2], nsc[0:2, 2:3], idf[0:2, 0:2]))
    tt = p.inc("tensor", ts)
    p.wait("vector", ts, tt)
    p.on("vector", lambda e: e.reduce_max(nsc[0:1, 3:4], pmisc[0:1, 0:2], axis=AXX))
    p.on("vector", lambda e: e.tensor_scalar(nsc[0:1, 3:4], nsc[0:1, 3:4], -1.0, None, ALU.mult))
    pv = p.inc("vector", vs)
    p.wait("tensor", vs, pv)
    p.on("tensor", lambda e: e.matmul(pmisc[0:128, 8:9], o1f[0:1, :], nsc[0:1, 3:4], start=True, stop=True))
    tt = p.inc("tensor", ts)
    p.wait("vector", ts, tt)
    p.on("vector", lambda e: e.tensor_copy(negB[:], pmisc[0:128, 8:9]))
    p.inc("vector", vs)
    for kc in range(64):
        p.on("gpsimd", lambda e, kc=kc: e.memset(vaug[:, 65 * kc + 64:65 * kc + 65], 1.0))
    gset = p.sem(tag + "gset")
    p.inc("gpsimd", gset)
    p.wait("vector", gset, gset.n)
    p.wait("gpsimd", ld, ld.n)

    def gath(dst, src_view, col):
        if gg.n > 0:
            p.ctl("gpsimd", lambda e, v=gg.n: e.wait_ge(gg.h, v))
        gg.inc(16)
        p.ctl("gpsimd", lambda e, dst=dst, src=src_view, col=col: e.indirect_dma_start(
            dst, None, src, IndirectOffsetOnAxis(bidx[:, col:col + 1], 0)).then_inc(gg.h, 16))

    for hp in range(2):
        for g in range(8):
            gath(Q2[:, g * 1024:(g + 1) * 1024],
                 DR["agqE_out"].rearrange("a (b c) -> (a b) c", b=2), hp * 8 + g)
            gath(K2[:, g * 1024:(g + 1) * 1024],
                 DR["agkE_out"].rearrange("a (b c) -> (a b) c", b=2), 16 + g)
            if hp == 0:
                gath(v1[0:64, g * 1024:(g + 1) * 1024],
                     DR["agvE_out"].rearrange("a (b c) -> (a b) c", b=2), 24 + g)
        for e_ in ("tensor", "scalar", "vector"):
            p.wait(e_, gg, gg.n)
        if _os.environ.get("EDBG") == "Q2" and hp == 0:
            for c in range(4):
                p.on("vector", lambda e, c=c: e.tensor_copy(
                    xf[:, c * T_LOC:(c + 1) * T_LOC], Q2[:, c * T_LOC:(c + 1) * T_LOC]))
            p.inc("vector", vs)
            p.barrier()
            es2.close()
            es0.close()
            return
        if hp == 0:
            for t8 in range(8):
                if t8 >= 1:
                    p.wait("tensor", vs, vs.n)
                for sub in range(8):
                    tau = t8 * 8 + sub
                    p.on("tensor", lambda e, tau=tau, sub=sub: e.transpose(
                        ptr2[0:128, sub * 128:sub * 128 + 64],
                        v1[0:64, tau * 128:(tau + 1) * 128], C["s_ident_bf"][0:64, 0:64]))
                tt = p.inc("tensor", pt)
                p.wait("vector", pt, tt)
                for sub in range(8):
                    tau = t8 * 8 + sub
                    p.on("vector", lambda e, tau=tau, sub=sub: e.tensor_copy(
                        vaug[:, tau * 65: tau * 65 + 64], ptr2[0:128, sub * 128: sub * 128 + 64]))
                p.inc("vector", vs)
            p.barrier()
            if _os.environ.get("EDBG") == "vaug":
                p.on("vector", lambda e: e.tensor_copy(xf[:, 0:4160], vaug[:]))
                p.inc("vector", vs)
                p.barrier()
                es2.close()
                es0.close()
                return
        for G in range(16):
            kcs = list(range(max(0, 4 * G - 2), 4 * G + 4))
            base_ex = exs.n
            base_pv = pvs.n
            for ik, kc in enumerate(kcs):
                bi = ik % 2
                if ik >= 2:
                    p.wait("tensor", exs, base_ex + ik - 1)
                p.on("tensor", lambda e, bi=bi, kc=kc, G=G: e.matmul(
                    psc[bi][:, 0:512], K2[0:64, kc * 128:(kc + 1) * 128], Q2[0:64, G * 512:(G + 1) * 512],
                    start=True, stop=True))
                p.on("tensor", lambda e, bi=bi, kc=kc, G=G: e.matmul(
                    psc[bi][:, 512:1024], K2[64:128, kc * 128:(kc + 1) * 128], Q2[64:128, G * 512:(G + 1) * 512],
                    start=True, stop=True, tile_position=(64, 0)))
                sct = p.inc("tensor", scs)
                p.wait("scalar", scs, sct)
                if ik >= 2:
                    p.wait("scalar", pvs, base_pv + ik - 1)
                p.on("scalar", lambda e, bi=bi: e.activation(
                    P_sb[bi][:], psc[bi][:], AF.Exp, bias=negB[:, 0:1]))
                p.inc("scalar", exs)
                mi = kc - 4 * G + 2
                p.wait("vector", exs, exs.n)
                p.on("vector", lambda e, bi=bi, mi=mi: e.tensor_mul(
                    P_sb[bi][:, 0:512], P_sb[bi][:, 0:512], masks[:, mi * 512:(mi + 1) * 512]))
                p.on("vector", lambda e, bi=bi, mi=mi: e.tensor_mul(
                    P_sb[bi][:, 512:1024], P_sb[bi][:, 512:1024], masks[:, mi * 512:(mi + 1) * 512]))
                mkt = p.inc("vector", mks)
                p.wait("tensor", mks, mkt)
                p.on("tensor", lambda e, bi=bi, kc=kc, s0=(ik == 0), s1=(ik == len(kcs) - 1): e.matmul(
                    po[0][0:65, :], vaug[:, kc * 65:kc * 65 + 65], P_sb[bi][:, 0:512],
                    start=s0, stop=s1))
                p.on("tensor", lambda e, bi=bi, kc=kc, s0=(ik == 0), s1=(ik == len(kcs) - 1): e.matmul(
                    po[1][0:65, :], vaug[:, kc * 65:kc * 65 + 65], P_sb[bi][:, 512:1024],
                    start=s0, stop=s1))
                p.inc("tensor", pvs)
            p.wait("vector", pvs, pvs.n)
            if _os.environ.get("EDBG") == "po" and hp == 0 and G == 1:
                p.on("vector", lambda e: e.tensor_copy(xf[0:65, 0:512], po[0][0:65, :]))
                p.on("vector", lambda e: e.tensor_copy(xf[0:65, 512:1024], po[1][0:65, :]))
                p.inc("vector", vs)
                p.barrier()
                es2.close()
                es0.close()
                return
            for par in range(2):
                osl_r = slice(par * 64, par * 64 + 64)
                osl_c = slice(hp * 8192 + G * TB, hp * 8192 + (G + 1) * TB)
                emit_norm_evac(p, C, po[par], o_coll[osl_r, osl_c], onerow,
                               rrow, rc4, rbc, pmisc, prb, ts, vs)
            p.barrier()

    p.wait("sync", vs, vs.n)
    if "dbg" in IN:
        for c in range(NCH):
            p.dma("sync", IN["dbg"][c * 128:(c + 1) * 128, :], o_coll[:, c * T_LOC:(c + 1) * T_LOC], sem=ld)
    import os as _os2
    if _os2.environ.get("EDBG") in ("oc", "agof", "o2"):
        for c in range(NCH):
            p.on("vector", lambda e, c=c: e.tensor_copy(
                xf[:, c * T_LOC:(c + 1) * T_LOC], o_coll[:, c * T_LOC:(c + 1) * T_LOC]))
        p.inc("vector", vs)
        p.barrier()
        es2.close()
        es0.close()
        return
    for X in range(4):
        for hp in range(2):
            for par in range(2):
                for gi, g in enumerate((X, 7 - X)):
                    p.dma("sync", DR["ago%d_in" % X][(2 * hp + par) * 64:(2 * hp + par) * 64 + 64,
                                                     gi * 1024:(gi + 1) * 1024],
                          o_coll[par * 64:par * 64 + 64, hp * 8192 + g * 1024: hp * 8192 + (g + 1) * 1024], sem=ld)
    p.wait("gpsimd", ld, ld.n)
    for X in range(4):
        p.ctl("gpsimd", lambda e, X=X: e.collective_compute(
            "AllGather", ALU.bypass, GROUPS, [DR["ago%d_in" % X].opt()],
            [DR["agofm_out"][X * 1024:(X + 1) * 1024, :].opt()]).then_inc(cc.h, 1))
        cc.n += 1
    p.wait("gpsimd", cc, cc.n)
    for ci in range(NCH):
        gath(o_coll[:, ci * T_LOC:(ci + 1) * T_LOC], DR["agofm_out"][0:128, :], 32 + ci)
    for e_ in P.ENGINES:
        p.wait(e_, gg, gg.n)
    p.barrier()
    import os as _os
    if _os.environ.get("EDBG") == "o":
        for c in range(NCH):
            p.on("vector", lambda e, c=c: e.tensor_copy(
                xf[:, c * T_LOC:(c + 1) * T_LOC], o_coll[:, c * T_LOC:(c + 1) * T_LOC]))
        p.inc("vector", vs)
        p.barrier()
        es2.close()
        es0.close()
        return
    es2.close()

    es3 = ExitStack()
    wos_ = [p.sbuf(tag + "wosa", [128, 1024], BF16, es=es3),
            p.sbuf(tag + "wosb", [128, 1024], BF16, es=es3)]
    x2tb = p.sbuf(tag + "x2tb", [128, 8 * TB], F32, es=es3)
    h2tb = p.sbuf(tag + "h2tb", [128, 8 * TB], BF16, es=es3)
    x1s = p.sbuf(tag + "x1s", [128, TB], F32, es=es3)
    f1 = p.sbuf(tag + "f1", [128, 32 * TB], BF16, es=es3)
    wf1s = [p.sbuf(tag + "wf1a", [128, 1024], BF16, es=es3),
            p.sbuf(tag + "wf1b", [128, 1024], BF16, es=es3)]
    wf2oc = p.sbuf(tag + "wf2oc", [128, 32 * 128], BF16, es=es3)
    bo_c = p.sbuf(tag + "bo_c", [128, 8], F32, es=es3)
    g2c = p.sbuf(tag + "g2c", [128, 8], F32, es=es3)
    b2c = p.sbuf(tag + "b2c", [128, 8], F32, es=es3)
    bf1c = p.sbuf(tag + "bf1c", [128, 32], F32, es=es3)
    bf2c = p.sbuf(tag + "bf2c", [128, 8], F32, es=es3)
    s1r2 = C["s_lnrow1"]
    s2r2 = C["s_lnrow2"]
    pja = p.psum(tag + "pja", [128, TB], es=es3)
    pjb = p.psum(tag + "pjb", [128, TB], es=es3)
    ps1 = p.psum(tag + "ps1", [1, TB], es=es3)
    ps2 = p.psum(tag + "ps2", [1, TB], es=es3)
    pb1 = p.psum(tag + "pb1", [128, TB], es=es3)
    pb2 = p.psum(tag + "pb2", [128, TB], es=es3)
    pjf = [pja, pjb]
    for name, sb in (("L1_bo_col", bo_c), ("L1_g2_col", g2c), ("L1_b2_col", b2c),
                     ("L1_bf1_col", bf1c), ("L1_bf2_col", bf2c)):
        p.dma("sync", sb[:], IN[name], sem=ld)
    for e_ in P.ENGINES:
        p.wait(e_, ld, ld.n)
        p.wait(e_, gld, gld.n)
    wfl = p.sem(tag + "wfl")
    mubc, rstdbc = C["s_mubc"], C["s_rstdbc"]
    var_sb, srt, tsc = C["s_lnvar"], C["s_lnsrt"], C["s_lntmp"]
    onesb = C["s_onescol_bf"]
    for tb in range(NTB):
        tsl = slice(tb * TB, (tb + 1) * TB)
        wo_mm_t = {}
        for oc in range(NCH):
            bi = oc % 2
            if wfl.n > 0:
                p.ctl("gpsimd", lambda e, v=wfl.n: e.wait_ge(wfl.h, v))
            if oc >= 2:
                p.ctl("gpsimd", lambda e, v=wo_mm_t[oc - 2]: e.wait_ge(pt.h, v))
            wfl.inc(16)
            p.ctl("gpsimd", lambda e, oc=oc, bi=bi: e.dma_start(
                out=wos_[bi][:], in_=bass.AP(IN["L1_attn_Wo"].tensor, oc * 128,
                                             [[1024, 128], [1024 * 128, 8], [1, 128]])).then_inc(wfl.h, 16))
            p.wait("tensor", wfl, wfl.n)
            if oc >= 2:
                p.wait("tensor", pe, pe.n - 1)
            for ci in range(NCH):
                p.on("tensor", lambda e, o=pjf[bi][:, :], l=wos_[bi][:, ci * 128:(ci + 1) * 128],
                     r=o_coll[:, ci * T_LOC + tb * TB: ci * T_LOC + (tb + 1) * TB],
                     s0=(ci == 0), s1=(ci == NCH - 1): e.matmul(o, l, r, start=s0, stop=s1))
            tt = p.inc("tensor", pt)
            p.wait("vector", pt, tt)
            wo_mm_t[oc] = pt.n
            xs_ = xf[:, oc * T_LOC + tb * TB: oc * T_LOC + (tb + 1) * TB]
            p.on("vector", lambda e, x_=xs_, m=mubcE[:, tsl]: e.tensor_sub(x1s[:], x_, m))
            p.on("vector", lambda e, r=rstdE[:, tsl]: e.tensor_mul(x1s[:], x1s[:], r))
            p.on("vector", lambda e, bi=bi, oc=oc: e.scalar_tensor_tensor(
                x2tb[:, oc * TB:(oc + 1) * TB], pjf[bi][:, :], bo_c[:, oc:oc + 1], x1s[:], ALU.add, ALU.add))
            p.inc("vector", pe)
        for c in range(NCH):
            bi2 = c % 2
            xc = [C["s_cast0"], C["s_cast1"]]
            x2s = [C["s_sq0"], C["s_sq1"]]
            if c >= 2:
                p.wait("vector", ts, ts.n - 1)
            p.on("vector", lambda e, c=c, b=bi2: e.tensor_copy(xc[b][:], x2tb[:, c * TB:(c + 1) * TB]))
            p.on("vector", lambda e, c=c, b=bi2: e.tensor_mul(x2s[b][:], xc[b][:], xc[b][:]))
            vt = p.inc("vector", vs)
            p.wait("tensor", vs, vt)
            p.on("tensor", lambda e, b=bi2, s0=(c == 0), s1=(c == NCH - 1): e.matmul(
                ps1[0:1, :], onesb[:, 0:1], xc[b][:], start=s0, stop=s1))
            p.on("tensor", lambda e, b=bi2, s0=(c == 0), s1=(c == NCH - 1): e.matmul(
                ps2[0:1, :], onesb[:, 0:1], x2s[b][:], start=s0, stop=s1))
            p.inc("tensor", ts)
        p.wait("vector", ts, ts.n)
        p.on("vector", lambda e: e.tensor_copy(s1r2[:], ps1[0:1, :]))
        p.on("vector", lambda e: e.tensor_copy(s2r2[:], ps2[0:1, :]))
        ve = p.inc("vector", vs)
        p.wait("tensor", vs, ve)
        p.on("tensor", lambda e: e.matmul(pb1[:, :], o1f[0:1, :], s1r2[0:1, :], start=True, stop=True))
        p.on("tensor", lambda e: e.matmul(pb2[:, :], o1f[0:1, :], s2r2[0:1, :], start=True, stop=True))
        tb1 = p.inc("tensor", ts)
        p.wait("vector", ts, tb1)
        p.on("vector", lambda e: e.tensor_scalar(mubc[:], pb1[:, :], 1.0 / D, None, ALU.mult))
        p.on("vector", lambda e: e.tensor_scalar(tsc[:], pb2[:, :], 1.0 / D, None, ALU.mult))
        p.on("vector", lambda e: e.tensor_mul(srt[:], mubc[:], mubc[:]))
        p.on("vector", lambda e: e.tensor_sub(var_sb[:], tsc[:], srt[:]))
        vv = p.inc("vector", vs)
        p.wait("scalar", vs, vv)
        p.on("scalar", lambda e: e.activation(srt[:], var_sb[:], AF.Sqrt, bias=C["s_epscol"][:, 0:1]))
        sg = p.inc("scalar", ss)
        p.wait("vector", ss, sg)
        p.on("vector", lambda e: e.reciprocal(rstdbc[:], srt[:]))
        for c in range(NCH):
            p.on("vector", lambda e, c=c: e.tensor_sub(tsc[:], x2tb[:, c * TB:(c + 1) * TB], mubc[:]))
            p.on("vector", lambda e: e.tensor_mul(tsc[:], tsc[:], rstdbc[:]))
            p.on("vector", lambda e, c=c: e.tensor_scalar(
                h2tb[:, c * TB:(c + 1) * TB], tsc[:], g2c[:, c:c + 1], b2c[:, c:c + 1], ALU.mult, ALU.add))
        p.inc("vector", vs)
        p.barrier()
        f1_mm_t = {}
        for ofc in range(32):
            bi = ofc % 2
            if wfl.n > 0:
                p.ctl("gpsimd", lambda e, v=wfl.n: e.wait_ge(wfl.h, v))
            if ofc >= 2:
                p.ctl("gpsimd", lambda e, v=f1_mm_t[ofc - 2]: e.wait_ge(pt.h, v))
            wfl.inc(16)
            p.ctl("gpsimd", lambda e, ofc=ofc, bi=bi: e.dma_start(
                out=wf1s[bi][:], in_=bass.AP(IN["L1_attn_Wf1"].tensor, ofc * 128,
                                             [[4096, 128], [4096 * 128, 8], [1, 128]])).then_inc(wfl.h, 16))
            p.wait("tensor", wfl, wfl.n)
            if ofc >= 2:
                p.wait("tensor", ss, ss.n - 1)
            for ci in range(NCH):
                p.on("tensor", lambda e, bi=bi, ci=ci, s0=(ci == 0), s1=(ci == NCH - 1): e.matmul(
                    pjf[bi][:, :], wf1s[bi][:, ci * 128:(ci + 1) * 128], h2tb[:, ci * TB:(ci + 1) * TB],
                    start=s0, stop=s1))
            tt = p.inc("tensor", pt)
            f1_mm_t[ofc] = tt
            p.wait("scalar", pt, tt)
            p.on("scalar", lambda e, ofc=ofc, bi=bi: e.activation(
                f1[:, ofc * TB:(ofc + 1) * TB], pjf[bi][:, :], AF.Gelu, bias=bf1c[:, ofc:ofc + 1]))
            p.inc("scalar", ss)
        p.barrier()
        f2_mm_t = {}
        for oc in range(NCH):
            bi = oc % 2
            if oc >= 2:
                p.wait("tensor", pe, pe.n - 1)
            if wfl.n > 0:
                p.ctl("gpsimd", lambda e, v=wfl.n: e.wait_ge(wfl.h, v))
            if oc >= 1:
                p.ctl("gpsimd", lambda e, v=f2_mm_t[oc - 1]: e.wait_ge(pt.h, v))
            wfl.inc(16)
            p.ctl("gpsimd", lambda e, oc=oc: e.dma_start(
                out=wf2oc[:], in_=bass.AP(IN["L1_attn_Wf2"].tensor, oc * 128,
                                          [[1024, 128], [1024 * 128, 32], [1, 128]])).then_inc(wfl.h, 16))
            p.wait("tensor", wfl, wfl.n)
            for ci in range(32):
                p.on("tensor", lambda e, bi=bi, ci=ci, s0=(ci == 0), s1=(ci == 31): e.matmul(
                    pjf[bi][:, :], wf2oc[:, ci * 128:(ci + 1) * 128], f1[:, ci * TB:(ci + 1) * TB],
                    start=s0, stop=s1))
            tt = p.inc("tensor", pt)
            f2_mm_t[oc] = tt
            p.wait("vector", pt, tt)
            xs_ = xf[:, oc * T_LOC + tb * TB: oc * T_LOC + (tb + 1) * TB]
            p.on("vector", lambda e, bi=bi, oc=oc: e.scalar_tensor_tensor(
                x1s[:], pjf[bi][:, :], bf2c[:, oc:oc + 1], x2tb[:, oc * TB:(oc + 1) * TB], ALU.add, ALU.add))
            p.on("vector", lambda e, x_=xs_: e.tensor_add(x_, x_, x1s[:]))
            p.inc("vector", pe)
        p.barrier()
    es3.close()
    es0.close()

def prep_col(vec, parts=128):
    n = vec.shape[-1]
    return np.ascontiguousarray(vec.reshape(n // parts, parts).T.astype(np.float32))


def shard_x(x):
    outs = []
    for core in range(NCORE):
        b, j = core // 4, core % 4
        xt = x[b].T
        outs.append(np.ascontiguousarray(
            np.concatenate([xt[:, 1024 * j:1024 * (j + 1)],
                            xt[:, 1024 * (7 - j):1024 * (8 - j)]], axis=1), dtype=np.float32))
    return outs


def unshard_x(parts):
    x = np.zeros((2, 8192, 1024), dtype=np.float32)
    for core in range(NCORE):
        b, j = core // 4, core % 4
        x[b, 1024 * j:1024 * (j + 1), :] = parts[core][:, 0:1024].T
        x[b, 1024 * (7 - j):1024 * (8 - j), :] = parts[core][:, 1024:2048].T
    return x


def declare_consts(nc, IN):
    for name, (shape, dt) in CONST_SHAPES.items():
        IN[name] = nc.dram_tensor(name, shape, dt, kind="ExternalInput").ap()


def const_maps(m, consts):
    for name in CONST_SHAPES:
        m[name] = consts[name]


def build_D():
    nc = bass.Bass("TRN2", target_bir_lowering=False, debug=False)
    es = ExitStack()
    p = P(nc, es)
    IN = {}
    IN["xin"] = nc.dram_tensor("xin", [D, T_LOC], F32, kind="ExternalInput").ap()
    xout = nc.dram_tensor("xout", [D, T_LOC], F32, kind="ExternalOutput").ap()
    IN["L1_ffn_W"] = nc.dram_tensor("L1_ffn_W", [8, D, D], F32, kind="ExternalInput").ap()
    IN["L1_ffn_b"] = nc.dram_tensor("L1_ffn_b", [8, D], F32, kind="ExternalInput").ap()
    IN["L1_ffn_logits"] = nc.dram_tensor("L1_ffn_logits", [8, 1], F32, kind="ExternalInput").ap()
    IN["L1_ffn_u"] = nc.dram_tensor("L1_ffn_u", [8, 1], F32, kind="ExternalInput").ap()
    declare_consts(nc, IN)

    xf = es.enter_context(nc.sbuf_tensor("xf", [128, NCH * T_LOC], F32))
    C = alloc_common(p, IN)

    xl = p.sem("xload")
    for c in range(NCH):
        p.dma("sync", xf[:, c * T_LOC:(c + 1) * T_LOC], IN["xin"][c * 128:(c + 1) * 128, :], sem=xl)
    for e in P.ENGINES:
        p.wait(e, xl, xl.n)

    import os
    mode = os.environ.get("MODE", "full")
    if mode == "ln":
        emit_ln(p, C, xf, C["s_hbf"], tag="lnonly")
        hv = p.sem("hv")
        for c in range(NCH):
            p.on("vector", lambda e, c=c: e.tensor_copy(
                xf[:, c * T_LOC:(c + 1) * T_LOC], C["s_hbf"][:, c * T_LOC:(c + 1) * T_LOC]))
        p.inc("vector", hv)
        p.barrier()
    elif mode == "none":
        pass
    else:
        emit_l1ffn(p, C, xf, IN)

    so = p.sem("xstore")
    for c in range(NCH):
        p.dma("sync", xout[c * 128:(c + 1) * 128, :], xf[:, c * T_LOC:(c + 1) * T_LOC], sem=so)
    p.wait("sync", so, so.n)
    p.run()
    es.close()
    return nc



def build_A():
    nc = bass.Bass("TRN2", target_bir_lowering=False, debug=False)
    es = ExitStack()
    p = P(nc, es)
    IN = {}
    IN["xin"] = nc.dram_tensor("xin", [D, T_LOC], F32, kind="ExternalInput").ap()
    xout = nc.dram_tensor("xout", [D, T_LOC], F32, kind="ExternalOutput").ap()
    for w in ("Wq", "Wk", "Wv"):
        IN["L0_ffn_" + w] = nc.dram_tensor("L0_ffn_" + w, [4, D, D], F32, kind="ExternalInput").ap()
    for b in ("bq", "bk", "bv"):
        IN["L0_ffn_" + b] = nc.dram_tensor("L0_ffn_" + b, [4, D], F32, kind="ExternalInput").ap()
    IN["L0_ffn_Wg1"] = nc.dram_tensor("L0_ffn_Wg1", [D, 128], F32, kind="ExternalInput").ap()
    IN["L0_ffn_Wg2"] = nc.dram_tensor("L0_ffn_Wg2", [128, 4], F32, kind="ExternalInput").ap()
    IN["L0_ffn_Wo"] = nc.dram_tensor("L0_ffn_Wo", [D, D], F32, kind="ExternalInput").ap()
    IN["bg1_col"] = nc.dram_tensor("bg1_col", [128, 1], F32, kind="ExternalInput").ap()
    IN["bg2_col"] = nc.dram_tensor("bg2_col", [4, 1], F32, kind="ExternalInput").ap()
    IN["bo_col"] = nc.dram_tensor("bo_col", [128, 8], F32, kind="ExternalInput").ap()
    declare_consts(nc, IN)
    DR = {
        "arp_in": nc.dram_tensor("arp_in", [128, 8], F32, kind="Internal").ap(),
        "arp_out": nc.dram_tensor("arp_out", [128, 8], F32, kind="Internal").ap(),
        "arn_in": nc.dram_tensor("arn_in", [2, 8], F32, kind="Internal").ap(),
        "arn_out": nc.dram_tensor("arn_out", [2, 8], F32, kind="Internal").ap(),
        "agk_in": nc.dram_tensor("agk_in", [D, T_LOC], BF16, kind="Internal").ap(),
        "agk_out": nc.dram_tensor("agk_out", [4 * D, T_LOC], BF16, kind="Internal").ap(),
        "agv_in": nc.dram_tensor("agv_in", [T_LOC, 1040], BF16, kind="Internal").ap(),
        "agv_out": nc.dram_tensor("agv_out", [8192, 1040], BF16, kind="Internal").ap(),
    }
    xf = es.enter_context(nc.sbuf_tensor("xf", [128, NCH * T_LOC], F32))
    C = alloc_common(p, IN)
    xl = p.sem("xload")
    for c in range(NCH):
        p.dma("sync", xf[:, c * T_LOC:(c + 1) * T_LOC], IN["xin"][c * 128:(c + 1) * 128, :], sem=xl)
    for e in P.ENGINES:
        p.wait(e, xl, xl.n)
    emit_l0ffn(p, C, xf, IN, DR)
    so = p.sem("xstore")
    for c in range(NCH):
        p.dma("sync", xout[c * 128:(c + 1) * 128, :], xf[:, c * T_LOC:(c + 1) * T_LOC], sem=so)
    p.wait("sync", so, so.n)
    p.run()
    es.close()
    return nc


def a_in_maps(INS, consts, xs):
    in_maps = []
    for core in range(NCORE):
        m = {"xin": xs[core]}
        for w in ("Wq", "Wk", "Wv", "bq", "bk", "bv", "Wg1", "Wg2", "Wo"):
            m["L0_ffn_" + w] = INS["L0_ffn_" + w]
        m["bg1_col"] = np.ascontiguousarray(INS["L0_ffn_bg1"].reshape(128, 1))
        m["bg2_col"] = np.ascontiguousarray(INS["L0_ffn_bg2"].reshape(4, 1))
        m["bo_col"] = prep_col(INS["L0_ffn_bo"])
        const_maps(m, consts)
        in_maps.append(m)
    return in_maps



def build_B():
    nc = bass.Bass("TRN2", target_bir_lowering=False, debug=False)
    es = ExitStack()
    p = P(nc, es)
    IN = {}
    IN["xin"] = nc.dram_tensor("xin", [D, T_LOC], F32, kind="ExternalInput").ap()
    xout = nc.dram_tensor("xout", [D, T_LOC], F32, kind="ExternalOutput").ap()
    IN["L0_attn_Wq"] = nc.dram_tensor("L0_attn_Wq", [D, 256], F32, kind="ExternalInput").ap()
    IN["L0_attn_Wk"] = nc.dram_tensor("L0_attn_Wk", [D, 256], F32, kind="ExternalInput").ap()
    IN["L0_attn_Wv"] = nc.dram_tensor("L0_attn_Wv", [D, D], F32, kind="ExternalInput").ap()
    IN["L0_attn_Wo"] = nc.dram_tensor("L0_attn_Wo", [D, D], F32, kind="ExternalInput").ap()
    IN["L0_attn_temperature"] = nc.dram_tensor("L0_attn_temperature", [1, 1], F32, kind="ExternalInput").ap()
    for nm, shape in (("L0_bq_col", [128, 2]), ("L0_bk_col", [128, 2]), ("L0_bv_col", [128, 8]),
                      ("L0_qp_col", [128, 2]), ("L0_kp_col", [128, 2]), ("L0_bo_col", [128, 8])):
        IN[nm] = nc.dram_tensor(nm, shape, F32, kind="ExternalInput").ap()
    IN["t_core_idx"] = nc.dram_tensor("t_core_idx", [1, 1], mybir.dt.int32, kind="ExternalInput").ap()
    declare_consts(nc, IN)
    DR = {
        "agq_in": nc.dram_tensor("agq_in", [256, T_LOC], BF16, kind="Internal").ap(),
        "agq_out": nc.dram_tensor("agq_out", [1024, T_LOC], BF16, kind="Internal").ap(),
        "agk_in": nc.dram_tensor("agk_in", [256, T_LOC], BF16, kind="Internal").ap(),
        "agk_out": nc.dram_tensor("agk_out", [1024, T_LOC], BF16, kind="Internal").ap(),
        "agvf_in": nc.dram_tensor("agvf_in", [1024, T_LOC], BF16, kind="Internal").ap(),
        "agvf_out": nc.dram_tensor("agvf_out", [4096, T_LOC], BF16, kind="Internal").ap(),
        "ago0_in": nc.dram_tensor("ago0_in", [256, T_LOC], BF16, kind="Internal").ap(),
        "ago1_in": nc.dram_tensor("ago1_in", [256, T_LOC], BF16, kind="Internal").ap(),
        "ago2_in": nc.dram_tensor("ago2_in", [256, T_LOC], BF16, kind="Internal").ap(),
        "ago3_in": nc.dram_tensor("ago3_in", [256, T_LOC], BF16, kind="Internal").ap(),
        "agofm_out": nc.dram_tensor("agofm_out", [4096, T_LOC], BF16, kind="Internal").ap(),
    }
    IN["t_bidx"] = nc.dram_tensor("t_bidx", [128, 56], mybir.dt.uint32, kind="ExternalInput").ap()
    xf = es.enter_context(nc.sbuf_tensor("xf", [128, NCH * T_LOC], F32))
    C = alloc_common(p, IN)
    xl = p.sem("xload")
    for c in range(NCH):
        p.dma("sync", xf[:, c * T_LOC:(c + 1) * T_LOC], IN["xin"][c * 128:(c + 1) * 128, :], sem=xl)
    for e in P.ENGINES:
        p.wait(e, xl, xl.n)
    emit_l0attn(p, C, xf, IN, DR, None)
    so = p.sem("xstore")
    for c in range(NCH):
        p.dma("sync", xout[c * 128:(c + 1) * 128, :], xf[:, c * T_LOC:(c + 1) * T_LOC], sem=so)
    p.wait("sync", so, so.n)
    p.run()
    es.close()
    return nc



def build_bidx(j):
    t = np.zeros((128, 56), dtype=np.uint32)
    for hp in range(2):
        for g in range(8):
            rblk = g if g <= 3 else 7 - g
            co = 0 if g <= 3 else 1024
            for par in range(2):
                rows = 2 * (256 * rblk + 16 * (4 * j + 2 * hp + par)) + (co // 1024)
                for pp in range(16):
                    t[32 * par + pp, hp * 16 + g] = rows + 2 * pp
                    t[32 * par + pp, hp * 16 + 8 + g] = rows + 2 * pp
            vrows = 2 * (1024 * rblk + 64 * (4 * j + 2 * hp)) + (co // 1024)
            for pp in range(128):
                t[pp, 32 + hp * 8 + g] = vrows + 2 * pp
    for ci in range(8):
        for pp in range(128):
            t[pp, 48 + ci] = 1024 * j + ci * 128 + pp
    return np.ascontiguousarray(t)

def b_in_maps(INS, consts, xs):
    in_maps = []
    for core in range(NCORE):
        m = {"xin": xs[core]}
        for w in ("Wq", "Wk", "Wv", "Wo"):
            m["L0_attn_" + w] = INS["L0_attn_" + w]
        m["L0_attn_temperature"] = np.ascontiguousarray(INS["L0_attn_temperature"].reshape(1, 1))
        m["L0_bq_col"] = prep_col(INS["L0_attn_bq"])
        m["L0_bk_col"] = prep_col(INS["L0_attn_bk"])
        m["L0_bv_col"] = prep_col(INS["L0_attn_bv"])
        m["L0_bo_col"] = prep_col(INS["L0_attn_bo"])
        m["L0_qp_col"] = prep_col(INS["L0_attn_q_phase"].reshape(256))
        m["L0_kp_col"] = prep_col(INS["L0_attn_k_phase"].reshape(256))
        m["t_core_idx"] = np.array([[core % 4]], dtype=np.int32)
        m["t_bidx"] = build_bidx(core % 4)
        const_maps(m, consts)
        in_maps.append(m)
    return in_maps



ROPE_PERM = np.concatenate([np.concatenate([np.arange(0, 32), np.arange(64, 96),
                                            np.arange(32, 64), np.arange(96, 128)]) + 128 * b
                            for b in range(8)])


def build_E():
    nc = bass.Bass("TRN2", target_bir_lowering=False, debug=False)
    es = ExitStack()
    p = P(nc, es)
    IN = {}
    IN["xin"] = nc.dram_tensor("xin", [D, T_LOC], F32, kind="ExternalInput").ap()
    xout = nc.dram_tensor("xout", [D, T_LOC], F32, kind="ExternalOutput").ap()
    IN["L1_Wq_perm"] = nc.dram_tensor("L1_Wq_perm", [D, D], F32, kind="ExternalInput").ap()
    IN["L1_Wk_perm"] = nc.dram_tensor("L1_Wk_perm", [D, 256], F32, kind="ExternalInput").ap()
    IN["L1_attn_Wv"] = nc.dram_tensor("L1_attn_Wv", [D, 256], F32, kind="ExternalInput").ap()
    IN["L1_attn_Wo"] = nc.dram_tensor("L1_attn_Wo", [D, D], F32, kind="ExternalInput").ap()
    IN["L1_attn_Wf1"] = nc.dram_tensor("L1_attn_Wf1", [D, 4096], F32, kind="ExternalInput").ap()
    IN["L1_attn_Wf2"] = nc.dram_tensor("L1_attn_Wf2", [4096, D], F32, kind="ExternalInput").ap()
    for nm, shape in (("L1_bq_col", [128, 8]), ("L1_bk_col", [128, 2]), ("L1_bv_col", [128, 2]),
                      ("L1_bo_col", [128, 8]), ("L1_g1_col", [128, 8]), ("L1_b1_col", [128, 8]),
                      ("L1_g2_col", [128, 8]), ("L1_b2_col", [128, 8]),
                      ("L1_bf1_col", [128, 32]), ("L1_bf2_col", [128, 8])):
        IN[nm] = nc.dram_tensor(nm, shape, F32, kind="ExternalInput").ap()
    IN["t_rope_cos"] = nc.dram_tensor("t_rope_cos", [128, T_LOC], BF16, kind="ExternalInput").ap()
    IN["t_rope_sin"] = nc.dram_tensor("t_rope_sin", [128, T_LOC], BF16, kind="ExternalInput").ap()
    IN["t_eidx"] = nc.dram_tensor("t_eidx", [128, 56], mybir.dt.uint32, kind="ExternalInput").ap()
    declare_consts(nc, IN)
    DR = {
        "agqE_in": nc.dram_tensor("agqE_in", [D, T_LOC], BF16, kind="Internal").ap(),
        "agqE_out": nc.dram_tensor("agqE_out", [4 * D, T_LOC], BF16, kind="Internal").ap(),
        "agkE_in": nc.dram_tensor("agkE_in", [256, T_LOC], BF16, kind="Internal").ap(),
        "agkE_out": nc.dram_tensor("agkE_out", [1024, T_LOC], BF16, kind="Internal").ap(),
        "agvE_in": nc.dram_tensor("agvE_in", [256, T_LOC], BF16, kind="Internal").ap(),
        "agvE_out": nc.dram_tensor("agvE_out", [1024, T_LOC], BF16, kind="Internal").ap(),
        "arnE_in": nc.dram_tensor("arnE_in", [2, 2], F32, kind="Internal").ap(),
        "arnE_out": nc.dram_tensor("arnE_out", [2, 2], F32, kind="Internal").ap(),
        "ago0_in": nc.dram_tensor("ago0_in", [256, T_LOC], BF16, kind="Internal").ap(),
        "ago1_in": nc.dram_tensor("ago1_in", [256, T_LOC], BF16, kind="Internal").ap(),
        "ago2_in": nc.dram_tensor("ago2_in", [256, T_LOC], BF16, kind="Internal").ap(),
        "ago3_in": nc.dram_tensor("ago3_in", [256, T_LOC], BF16, kind="Internal").ap(),
        "agofm_out": nc.dram_tensor("agofm_out", [4096, T_LOC], BF16, kind="Internal").ap(),
    }
    import os as _os3
    if _os3.environ.get("EDBG2"):
        IN["dbg"] = nc.dram_tensor("dbg", [D, T_LOC], BF16, kind="ExternalOutput").ap()
        IN["dbg2"] = nc.dram_tensor("dbg2", [D, T_LOC], BF16, kind="ExternalOutput").ap()
        IN["dbg3"] = nc.dram_tensor("dbg3", [D, T_LOC], BF16, kind="ExternalOutput").ap()
    xf = es.enter_context(nc.sbuf_tensor("xf", [128, NCH * T_LOC], F32))
    C = alloc_common(p, IN)
    xl = p.sem("xload")
    for c in range(NCH):
        p.dma("sync", xf[:, c * T_LOC:(c + 1) * T_LOC], IN["xin"][c * 128:(c + 1) * 128, :], sem=xl)
    for e in P.ENGINES:
        p.wait(e, xl, xl.n)
    emit_l1attn(p, C, xf, IN, DR)
    so = p.sem("xstore")
    for c in range(NCH):
        p.dma("sync", xout[c * 128:(c + 1) * 128, :], xf[:, c * T_LOC:(c + 1) * T_LOC], sem=so)
    p.wait("sync", so, so.n)
    p.run()
    es.close()
    return nc


def build_eidx(j):
    t = np.zeros((128, 56), dtype=np.uint32)

    def permrows(h):
        base = 128 * (h // 2) + 32 * (h % 2)
        return np.concatenate([np.arange(base, base + 32), np.arange(base + 64, base + 96)])
    for g in range(8):
        rblk = g if g <= 3 else 7 - g
        co = 0 if g <= 3 else 1
        for hp in range(2):
            for par in range(2):
                h = 4 * j + 2 * hp + par
                rows = permrows(h) + 1024 * rblk
                for pp in range(64):
                    t[64 * par + pp, hp * 8 + g] = 2 * rows[pp] + co
        krows = permrows(j) + 256 * rblk
        vbase = 2 * (256 * rblk + 64 * j) + co
        for pp in range(64):
            t[pp, 16 + g] = 2 * krows[pp] + co
            t[64 + pp, 16 + g] = 2 * krows[pp] + co
            t[pp, 24 + g] = vbase + 2 * pp
    for ci in range(8):
        for pp in range(128):
            t[pp, 32 + ci] = 1024 * j + ci * 128 + pp
    return np.ascontiguousarray(t)


def e_in_maps(INS, consts, xs):
    in_maps = []
    wq_perm = np.ascontiguousarray(INS["L1_attn_Wq"][:, ROPE_PERM])
    kperm = ROPE_PERM[:256]
    wk_perm = np.ascontiguousarray(INS["L1_attn_Wk"][:, kperm])
    bq_perm = INS["L1_attn_bq"][ROPE_PERM]
    bk_perm = INS["L1_attn_bk"][kperm]
    for core in range(NCORE):
        b, j = core // 4, core % 4
        m = {"xin": xs[core]}
        m["L1_Wq_perm"] = wq_perm
        m["L1_Wk_perm"] = wk_perm
        m["L1_attn_Wv"] = INS["L1_attn_Wv"]
        m["L1_attn_Wo"] = INS["L1_attn_Wo"]
        m["L1_attn_Wf1"] = INS["L1_attn_Wf1"]
        m["L1_attn_Wf2"] = INS["L1_attn_Wf2"]
        m["L1_bq_col"] = prep_col(bq_perm)
        m["L1_bk_col"] = prep_col(bk_perm)
        m["L1_bv_col"] = prep_col(INS["L1_attn_bv"])
        m["L1_bo_col"] = prep_col(INS["L1_attn_bo"])
        m["L1_g1_col"] = prep_col(INS["L1_attn_g1"])
        m["L1_b1_col"] = prep_col(INS["L1_attn_b1"])
        m["L1_g2_col"] = prep_col(INS["L1_attn_g2"])
        m["L1_b2_col"] = prep_col(INS["L1_attn_b2"])
        m["L1_bf1_col"] = prep_col(INS["L1_attn_bf1"])
        m["L1_bf2_col"] = prep_col(INS["L1_attn_bf2"])
        tt = core_tables(core)
        m["t_rope_cos"] = tt["t_rope_cos"]
        m["t_rope_sin"] = tt["t_rope_sin"]
        m["t_eidx"] = build_eidx(j)
        const_maps(m, consts)
        in_maps.append(m)
    return in_maps


def core_tables(core):
    b, j = core // 4, core % 4
    pos = np.concatenate([np.arange(1024 * j, 1024 * j + 1024),
                          np.arange(1024 * (7 - j), 1024 * (7 - j) + 1024)])
    half = 32
    theta = 1.0 / 10000.0 ** (np.arange(half, dtype=np.float64) / half)
    ang = pos[None, :] * theta[:, None]
    cos4 = np.tile(np.cos(ang), (4, 1))
    sin = np.sin(ang)
    s4 = np.concatenate([np.tile(sin, (2, 1)), -np.tile(sin, (2, 1))], axis=0)
    return {"t_rope_cos": np.ascontiguousarray(cos4.astype(BF)),
            "t_rope_sin": np.ascontiguousarray(s4.astype(BF))}


def build_full():
    nc = bass.Bass("TRN2", target_bir_lowering=False, debug=False)
    es = ExitStack()
    p = P(nc, es)
    IN = {}
    IN["xin"] = nc.dram_tensor("xin", [D, T_LOC], F32, kind="ExternalInput").ap()
    xout = nc.dram_tensor("xout", [D, T_LOC], F32, kind="ExternalOutput").ap()
    for w in ("Wq", "Wk", "Wv"):
        IN["L0_ffn_" + w] = nc.dram_tensor("L0_ffn_" + w, [4, D, D], F32, kind="ExternalInput").ap()
    for b in ("bq", "bk", "bv"):
        IN["L0_ffn_" + b] = nc.dram_tensor("L0_ffn_" + b, [4, D], F32, kind="ExternalInput").ap()
    IN["L0_ffn_Wg1"] = nc.dram_tensor("L0_ffn_Wg1", [D, 128], F32, kind="ExternalInput").ap()
    IN["L0_ffn_Wg2"] = nc.dram_tensor("L0_ffn_Wg2", [128, 4], F32, kind="ExternalInput").ap()
    IN["L0_ffn_Wo"] = nc.dram_tensor("L0_ffn_Wo", [D, D], F32, kind="ExternalInput").ap()
    IN["bg1_col"] = nc.dram_tensor("bg1_col", [128, 1], F32, kind="ExternalInput").ap()
    IN["bg2_col"] = nc.dram_tensor("bg2_col", [4, 1], F32, kind="ExternalInput").ap()
    IN["bo_col"] = nc.dram_tensor("bo_col", [128, 8], F32, kind="ExternalInput").ap()
    IN["L0_attn_Wq"] = nc.dram_tensor("L0_attn_Wq", [D, 256], F32, kind="ExternalInput").ap()
    IN["L0_attn_Wk"] = nc.dram_tensor("L0_attn_Wk", [D, 256], F32, kind="ExternalInput").ap()
    IN["L0_attn_Wv"] = nc.dram_tensor("L0_attn_Wv", [D, D], F32, kind="ExternalInput").ap()
    IN["L0_attn_Wo"] = nc.dram_tensor("L0_attn_Wo", [D, D], F32, kind="ExternalInput").ap()
    IN["L0_attn_temperature"] = nc.dram_tensor("L0_attn_temperature", [1, 1], F32, kind="ExternalInput").ap()
    for nm, shape in (("L0_bq_col", [128, 2]), ("L0_bk_col", [128, 2]), ("L0_bv_col", [128, 8]),
                      ("L0_qp_col", [128, 2]), ("L0_kp_col", [128, 2]), ("L0_bo_col", [128, 8])):
        IN[nm] = nc.dram_tensor(nm, shape, F32, kind="ExternalInput").ap()
    IN["t_core_idx"] = nc.dram_tensor("t_core_idx", [1, 1], mybir.dt.int32, kind="ExternalInput").ap()
    IN["t_bidx"] = nc.dram_tensor("t_bidx", [128, 56], mybir.dt.uint32, kind="ExternalInput").ap()
    IN["L1_ffn_W"] = nc.dram_tensor("L1_ffn_W", [8, D, D], F32, kind="ExternalInput").ap()
    IN["L1_ffn_b"] = nc.dram_tensor("L1_ffn_b", [8, D], F32, kind="ExternalInput").ap()
    IN["L1_ffn_logits"] = nc.dram_tensor("L1_ffn_logits", [8, 1], F32, kind="ExternalInput").ap()
    IN["L1_ffn_u"] = nc.dram_tensor("L1_ffn_u", [8, 1], F32, kind="ExternalInput").ap()
    IN["L1_Wq_perm"] = nc.dram_tensor("L1_Wq_perm", [D, D], F32, kind="ExternalInput").ap()
    IN["L1_Wk_perm"] = nc.dram_tensor("L1_Wk_perm", [D, 256], F32, kind="ExternalInput").ap()
    IN["L1_attn_Wv"] = nc.dram_tensor("L1_attn_Wv", [D, 256], F32, kind="ExternalInput").ap()
    IN["L1_attn_Wo"] = nc.dram_tensor("L1_attn_Wo", [D, D], F32, kind="ExternalInput").ap()
    IN["L1_attn_Wf1"] = nc.dram_tensor("L1_attn_Wf1", [D, 4096], F32, kind="ExternalInput").ap()
    IN["L1_attn_Wf2"] = nc.dram_tensor("L1_attn_Wf2", [4096, D], F32, kind="ExternalInput").ap()
    for nm, shape in (("L1_bq_col", [128, 8]), ("L1_bk_col", [128, 2]), ("L1_bv_col", [128, 2]),
                      ("L1_bo_col", [128, 8]), ("L1_g1_col", [128, 8]), ("L1_b1_col", [128, 8]),
                      ("L1_g2_col", [128, 8]), ("L1_b2_col", [128, 8]),
                      ("L1_bf1_col", [128, 32]), ("L1_bf2_col", [128, 8])):
        IN[nm] = nc.dram_tensor(nm, shape, F32, kind="ExternalInput").ap()
    IN["t_rope_cos"] = nc.dram_tensor("t_rope_cos", [128, T_LOC], BF16, kind="ExternalInput").ap()
    IN["t_rope_sin"] = nc.dram_tensor("t_rope_sin", [128, T_LOC], BF16, kind="ExternalInput").ap()
    IN["t_eidx"] = nc.dram_tensor("t_eidx", [128, 56], mybir.dt.uint32, kind="ExternalInput").ap()
    declare_consts(nc, IN)
    ago = {f"ago{X}_in": nc.dram_tensor(f"ago{X}_in", [256, T_LOC], BF16, kind="Internal").ap()
           for X in range(4)}
    agofm = nc.dram_tensor("agofm_out", [4096, T_LOC], BF16, kind="Internal").ap()
    DR_A = {
        "arp_in": nc.dram_tensor("a_arp_in", [128, 8], F32, kind="Internal").ap(),
        "arp_out": nc.dram_tensor("a_arp_out", [128, 8], F32, kind="Internal").ap(),
        "arn_in": nc.dram_tensor("a_arn_in", [2, 8], F32, kind="Internal").ap(),
        "arn_out": nc.dram_tensor("a_arn_out", [2, 8], F32, kind="Internal").ap(),
        "agk_in": nc.dram_tensor("a_agk_in", [D, T_LOC], BF16, kind="Internal").ap(),
        "agk_out": nc.dram_tensor("a_agk_out", [4 * D, T_LOC], BF16, kind="Internal").ap(),
        "agv_in": nc.dram_tensor("a_agv_in", [T_LOC, 1040], BF16, kind="Internal").ap(),
        "agv_out": nc.dram_tensor("a_agv_out", [8192, 1040], BF16, kind="Internal").ap(),
    }
    DR_B = {
        "agq_in": nc.dram_tensor("b_agq_in", [256, T_LOC], BF16, kind="Internal").ap(),
        "agq_out": nc.dram_tensor("b_agq_out", [1024, T_LOC], BF16, kind="Internal").ap(),
        "agk_in": nc.dram_tensor("b_agk_in", [256, T_LOC], BF16, kind="Internal").ap(),
        "agk_out": nc.dram_tensor("b_agk_out", [1024, T_LOC], BF16, kind="Internal").ap(),
        "agvf_in": nc.dram_tensor("b_agvf_in", [1024, T_LOC], BF16, kind="Internal").ap(),
        "agvf_out": nc.dram_tensor("b_agvf_out", [4096, T_LOC], BF16, kind="Internal").ap(),
        "agofm_out": agofm, **ago,
    }
    DR_E = {
        "agqE_in": nc.dram_tensor("e_agq_in", [D, T_LOC], BF16, kind="Internal").ap(),
        "agqE_out": nc.dram_tensor("e_agq_out", [4 * D, T_LOC], BF16, kind="Internal").ap(),
        "agkE_in": nc.dram_tensor("e_agk_in", [256, T_LOC], BF16, kind="Internal").ap(),
        "agkE_out": nc.dram_tensor("e_agk_out", [1024, T_LOC], BF16, kind="Internal").ap(),
        "agvE_in": nc.dram_tensor("e_agv_in", [256, T_LOC], BF16, kind="Internal").ap(),
        "agvE_out": nc.dram_tensor("e_agv_out", [1024, T_LOC], BF16, kind="Internal").ap(),
        "arnE_in": nc.dram_tensor("e_arn_in", [2, 2], F32, kind="Internal").ap(),
        "arnE_out": nc.dram_tensor("e_arn_out", [2, 2], F32, kind="Internal").ap(),
        "agofm_out": agofm, **ago,
    }
    xf = es.enter_context(nc.sbuf_tensor("xf", [128, NCH * T_LOC], F32))
    C = alloc_common(p, IN)
    xl = p.sem("xload")
    for c in range(NCH):
        p.dma("sync", xf[:, c * T_LOC:(c + 1) * T_LOC], IN["xin"][c * 128:(c + 1) * 128, :], sem=xl)
    for e in P.ENGINES:
        p.wait(e, xl, xl.n)
    emit_l0ffn(p, C, xf, IN, DR_A)
    emit_l0attn(p, C, xf, IN, DR_B, None)
    emit_l0ffn(p, C, xf, IN, DR_A)
    emit_l1ffn(p, C, xf, IN)
    emit_l1attn(p, C, xf, IN, DR_E)
    emit_l1ffn(p, C, xf, IN)
    so = p.sem("xstore")
    for c in range(NCH):
        p.dma("sync", xout[c * 128:(c + 1) * 128, :], xf[:, c * T_LOC:(c + 1) * T_LOC], sem=so)
    p.wait("sync", so, so.n)
    p.run()
    es.close()
    return nc


def full_in_maps(INS, consts, xs):
    am = a_in_maps(INS, consts, xs)
    bm = b_in_maps(INS, consts, xs)
    dm = d_in_maps(INS, consts, xs) if "d_in_maps" in globals() else None
    em = e_in_maps(INS, consts, xs)
    maps = []
    for core in range(NCORE):
        m = {}
        m.update(am[core])
        m.update(bm[core])
        m.update(em[core])
        m["L1_ffn_W"] = INS["L1_ffn_W"]
        m["L1_ffn_b"] = INS["L1_ffn_b"]
        m["L1_ffn_logits"] = np.ascontiguousarray(
            np.asarray(INS["L1_ffn_logits"], dtype=np.float32).reshape(8, 1))
        m["L1_ffn_u"] = np.ascontiguousarray(
            np.asarray(INS["L1_ffn_u"], dtype=np.float32).reshape(8, 1))
        maps.append(m)
    return maps




def kernel(**inputs):
    INS = {k: np.asarray(v) for k, v in inputs.items()}
    consts = host_constants()
    xs = shard_x(np.asarray(INS["L0_attn_x"], dtype=np.float32))
    nc = build_full()
    res = run_bass_kernel_spmd(nc, full_in_maps(INS, consts, xs),
                               core_ids=list(range(NCORE)))
    return unshard_x([np.asarray(r["xout"], dtype=np.float32) for r in res.results])


# version 03
# speedup vs baseline: 1.4649x; 1.4358x over previous
import numpy as np
import re as _re
import ml_dtypes
import concourse.bass as bass
import concourse.mybir as mybir
from concourse.bass_utils import run_bass_kernel_spmd
from concourse.bass_core_rust import IndirectOffsetOnAxis
from contextlib import ExitStack

F32 = mybir.dt.float32
BF16 = mybir.dt.bfloat16
AF = mybir.ActivationFunctionType
ALU = mybir.AluOpType
BF = ml_dtypes.bfloat16
AXX = mybir.AxisListType.X

GROUPS = [[0, 1, 2, 3], [4, 5, 6, 7]]
NCORE = 8
T_LOC = 2048
D = 1024
NCH = 8
TB = 512
NTB = T_LOC // TB
EPS_LN = 1e-5


class Sem:
    __slots__ = ("h", "n")

    def __init__(self, h):
        self.h = h
        self.n = 0

    def inc(self, k=1):
        self.n += k
        return self.n


class P:
    ENGINES = ("sync", "tensor", "vector", "scalar", "gpsimd")
    SERIAL = ("vector", "scalar", "gpsimd")

    def __init__(self, nc, es):
        self.nc = nc
        self.es = es
        self.ops = {e: [] for e in self.ENGINES}
        self._sems = {}
        self.uid = 0
        self.label = "init"
        self._scope_ids = []
        self._selfsems = {e: self.sem("self_" + e) for e in self.SERIAL}

    def sem(self, name):
        name = _re.sub(r"\d", "", name)
        if name not in self._sems:
            h = self.es.enter_context(self.nc.semaphore(name))
            self._sems[name] = Sem(h)
        return self._sems[name]

    def on(self, eng, fn):
        self.ops[eng].append(("op", fn, self.label))

    def ctl(self, eng, fn):
        self.ops[eng].append(("ctl", fn, self.label))

    def inc(self, eng, sem):
        v = sem.inc()
        self.ops[eng].append(("sig", sem.h, self.label))
        return v

    def wait(self, eng, sem, val):
        self.ctl(eng, lambda e, s=sem.h, v=val: e.wait_ge(s, v))

    def barrier(self, engines=None):
        engines = engines or self.ENGINES
        bs = self.sem("bar")
        for e in engines:
            self.inc(e, bs)
        tgt = bs.n
        for e in engines:
            self.wait(e, bs, tgt)

    def dma(self, q, out, in_, sem=None, serial=True, **kw):
        sem = sem or self.sem("dma_" + q)
        if serial and sem.n > 0:
            self.ctl(q, lambda e, s=sem.h, v=sem.n: e.wait_ge(s, v))
        tgt = sem.inc(16)
        self.ctl(q, lambda e, o=out, i=in_, s=sem.h, kw=kw: e.dma_start(out=o, in_=i, **kw).then_inc(s, 16))
        return sem, tgt

    def dma_lazy(self, q, mk, sem, serial=True):
        if serial and sem.n > 0:
            self.ctl(q, lambda e, s=sem.h, v=sem.n: e.wait_ge(s, v))
        tgt = sem.inc(16)

        def fire(e, mk=mk, s=sem.h):
            o, i = mk(e)
            e.dma_start(out=o, in_=i).then_inc(s, 16)
        self.ctl(q, fire)
        return sem, tgt

    def run(self):
        with self.nc.Block() as block:
            for eng_name in self.ENGINES:
                def runner(eng, name=eng_name):
                    ssem = self._selfsems.get(name)
                    cnt = 0
                    last = None
                    cur_scope = None
                    for kind, fn, lbl in self.ops[name]:
                        if lbl != cur_scope:
                            if cur_scope is not None:
                                self.nc.leave_named_scope(cur_scope, self._scope_ids.pop(), False)
                            cur_scope = lbl
                            self._scope_ids.append(
                                self.nc.enter_named_scope(cur_scope, False)[0])
                        if kind == "op":
                            if ssem is not None:
                                if cnt > 0:
                                    eng.wait_ge(ssem.h, cnt)
                                inst = fn(eng)
                                inst.then_inc(ssem.h, 1)
                                cnt += 1
                            else:
                                last = fn(eng)
                        elif kind == "sig":
                            if ssem is not None:
                                if cnt > 0:
                                    eng.wait_ge(ssem.h, cnt)
                                eng.sem_inc(fn, 1)
                            elif last is not None:
                                last.then_inc(fn, 1)
                                last = None
                            else:
                                eng.nop().then_inc(fn, 1)
                        else:
                            fn(eng)
                    if cur_scope is not None:
                        self.nc.leave_named_scope(cur_scope, self._scope_ids.pop(), False)
                getattr(block, eng_name)(runner)

    def sbuf(self, name, shape, dtype, es=None):
        return (es or self.es).enter_context(self.nc.sbuf_tensor(name, shape, dtype))

    def psum(self, name, shape, dtype=F32, es=None):
        return (es or self.es).enter_context(self.nc.psum_tensor(name, shape, dtype))


def host_constants():
    c = {}
    c["c_ident_bf"] = np.eye(128, dtype=BF)
    c["c_ident_f32"] = np.eye(128, dtype=np.float32)
    c["c_ones_row_bf"] = np.ones((1, 128), dtype=BF)
    c["c_ones_row_f32"] = np.ones((1, 128), dtype=np.float32)
    c["c_ones_col_bf"] = np.ones((128, 1), dtype=BF)
    pp = np.arange(128)[:, None]
    jj = np.arange(512)[None, :]
    masks = [((pp + off) <= jj).astype(BF) for off in (0, 128, 256, 384)]
    c["c_masks_causal"] = np.ascontiguousarray(np.concatenate(masks, axis=1))
    wmasks = []
    for off in (-256, -128, 0, 128, 256, 384):
        d_ = jj - (pp + off)
        wmasks.append(((d_ >= 0) & (d_ < 256)).astype(BF))
    c["c_masks_win"] = np.ascontiguousarray(np.concatenate(wmasks, axis=1))
    hs = np.zeros((128, 2), dtype=BF)
    hs[0:64, 0] = 1
    hs[64:128, 1] = 1
    c["c_hsel2"] = hs
    h16 = np.zeros((128, 8), dtype=BF)
    for i in range(8):
        h16[16 * i:16 * (i + 1), i] = 1
    c["c_hsel16"] = h16
    c["c_hexp16"] = np.ascontiguousarray(h16.T)
    c["c_half_row_f32"] = np.full((1, 128), 0.5, dtype=np.float32)
    c["c_ones_col_f32"] = np.ones((128, 1), dtype=np.float32)
    return c


CONST_SHAPES = {
    "c_ident_bf": ([128, 128], BF16), "c_ident_f32": ([128, 128], F32),
    "c_ones_row_bf": ([1, 128], BF16), "c_ones_row_f32": ([1, 128], F32),
    "c_ones_col_bf": ([128, 1], BF16),
    "c_masks_causal": ([128, 4 * 512], BF16), "c_masks_win": ([128, 6 * 512], BF16),
    "c_hsel2": ([128, 2], BF16), "c_half_row_f32": ([1, 128], F32),
    "c_hsel16": ([128, 8], BF16), "c_hexp16": ([8, 128], BF16),
    "c_ones_col_f32": ([128, 1], F32),
}


def ag_rows_of_chunk(g):
    return 2048 * g if g <= 3 else 2048 * (7 - g) + 1024


def emit_ln(p, C, xf, h_bf, affine=None, keep_stats=None, rstd_comb=False, tag="ln"):
    import os
    LNSTAGE = int(os.environ.get("LNSTAGE", "9"))
    """Feature-dim LayerNorm of xf -> h_bf (bf16). Allocates its own psum scope.

    affine: None | (g_col, b_col). keep_stats: fill C['s_mubc']/C['s_rstdbc']
    (outer-LN stats). rstd_comb: apply fused LN(LN(x)) combined rstd to h.
    """
    p.uid += 1
    tag = f"{tag}{p.uid}"
    es = ExitStack()
    ps1 = p.psum(tag + "_s1", [1, TB], es=es)
    ps2 = p.psum(tag + "_s2", [1, TB], es=es)
    pb1 = p.psum(tag + "_b1", [128, TB], es=es)
    pb2 = p.psum(tag + "_b2", [128, TB], es=es)
    sv, st, ss = p.sem(tag + "v"), p.sem(tag + "t"), p.sem(tag + "s")
    xc = [C["s_cast0"], C["s_cast1"]]
    x2 = [C["s_sq0"], C["s_sq1"]]
    onesb = C["s_onescol_bf"]
    ones1f = C["s_onesrow_f32"]
    s1r, s2r = C["s_lnrow1"], C["s_lnrow2"]
    mubc, rstdbc = C["s_mubc"], C["s_rstdbc"]
    var_sb, srt, tsc = C["s_lnvar"], C["s_lnsrt"], C["s_lntmp"]

    for tb in range(NTB):
        tsl = slice(tb * TB, (tb + 1) * TB)
        for c in range(NCH):
            bi = c % 2
            if c >= 2:
                p.wait("vector", st, st.n - 1)
            xs = xf[:, c * T_LOC + tb * TB: c * T_LOC + (tb + 1) * TB]
            p.on("vector", lambda e, o=xc[bi][:], i=xs: e.tensor_copy(o, i))
            p.on("vector", lambda e, o=x2[bi][:], a=xc[bi][:]: e.tensor_mul(o, a, a))
            vt = p.inc("vector", sv)
            p.wait("tensor", sv, vt)
            p.on("tensor", lambda e, o=ps1[0:1, :], r=xc[bi][:], s0=(c == 0), s1=(c == NCH - 1):
                 e.matmul(o, onesb[:, 0:1], r, start=s0, stop=s1))
            p.on("tensor", lambda e, o=ps2[0:1, :], r=x2[bi][:], s0=(c == 0), s1=(c == NCH - 1):
                 e.matmul(o, onesb[:, 0:1], r, start=s0, stop=s1))
            p.inc("tensor", st)
        if LNSTAGE < 2:
            continue
        p.wait("vector", st, st.n)
        p.on("vector", lambda e: e.tensor_copy(s1r[0:1, :], ps1[0:1, :]))
        p.on("vector", lambda e: e.tensor_copy(s2r[0:1, :], ps2[0:1, :]))
        ve = p.inc("vector", sv)
        p.wait("tensor", sv, ve)
        p.on("tensor", lambda e: e.matmul(pb1[:, :], ones1f[0:1, :], s1r[0:1, :], start=True, stop=True))
        p.on("tensor", lambda e: e.matmul(pb2[:, :], ones1f[0:1, :], s2r[0:1, :], start=True, stop=True))
        tb1 = p.inc("tensor", st)
        p.wait("vector", st, tb1)
        p.on("vector", lambda e: e.tensor_scalar(mubc[:], pb1[:, :], 1.0 / D, None, ALU.mult))
        p.on("vector", lambda e: e.tensor_scalar(tsc[:], pb2[:, :], 1.0 / D, None, ALU.mult))
        p.on("vector", lambda e: e.tensor_mul(srt[:], mubc[:], mubc[:]))
        p.on("vector", lambda e: e.tensor_sub(var_sb[:], tsc[:], srt[:]))
        if LNSTAGE < 3:
            continue
        vv = p.inc("vector", sv)
        p.wait("scalar", sv, vv)
        p.on("scalar", lambda e: e.activation(srt[:], var_sb[:], AF.Sqrt, bias=C["s_epscol"][:, 0:1]))
        sg = p.inc("scalar", ss)
        p.wait("vector", ss, sg)
        p.on("vector", lambda e: e.reciprocal(rstdbc[:], srt[:]))
        if keep_stats:
            p.on("vector", lambda e, o=keep_stats[0][:, tsl]: e.tensor_copy(o, mubc[:]))
            p.on("vector", lambda e, o=keep_stats[1][:, tsl]: e.tensor_copy(o, rstdbc[:]))
        if rstd_comb:
            p.on("vector", lambda e: e.tensor_mul(tsc[:], rstdbc[:], rstdbc[:]))
            p.on("vector", lambda e: e.tensor_mul(tsc[:], tsc[:], var_sb[:]))
            v2 = p.inc("vector", sv)
            p.wait("scalar", sv, v2)
            p.on("scalar", lambda e: e.activation(tsc[:], tsc[:], AF.Sqrt, bias=C["s_epscol"][:, 0:1]))
            s2_ = p.inc("scalar", ss)
            p.wait("vector", ss, s2_)
            p.on("vector", lambda e: e.reciprocal(var_sb[:], tsc[:]))
            p.on("vector", lambda e: e.tensor_mul(rstdbc[:], rstdbc[:], var_sb[:]))
        if LNSTAGE < 4:
            continue
        for c in range(NCH):
            xs = xf[:, c * T_LOC + tb * TB: c * T_LOC + (tb + 1) * TB]
            hs = h_bf[:, c * T_LOC + tb * TB: c * T_LOC + (tb + 1) * TB]
            p.on("vector", lambda e, x_=xs: e.tensor_sub(tsc[:], x_, mubc[:]))
            if affine is None:
                p.on("vector", lambda e, o=hs: e.tensor_mul(o, tsc[:], rstdbc[:]))
            else:
                g_col, b_col = affine
                p.on("vector", lambda e: e.tensor_mul(tsc[:], tsc[:], rstdbc[:]))
                p.on("vector", lambda e, o=hs, g=g_col[:, c:c + 1], b=b_col[:, c:c + 1]:
                     e.tensor_scalar(o, tsc[:], g, b, ALU.mult, ALU.add))
        p.inc("vector", sv)
    p.barrier()
    es.close()


def alloc_common(p, IN):
    C = {}
    C["s_cast0"] = p.sbuf("s_cast0", [128, TB], BF16)
    C["s_cast1"] = p.sbuf("s_cast1", [128, TB], BF16)
    C["s_sq0"] = p.sbuf("s_sq0", [128, TB], BF16)
    C["s_sq1"] = p.sbuf("s_sq1", [128, TB], BF16)
    C["s_lnrow1"] = p.sbuf("s_lnrow1", [1, TB], F32)
    C["s_lnrow2"] = p.sbuf("s_lnrow2", [1, TB], F32)
    C["s_mubc"] = p.sbuf("s_mubc", [128, TB], F32)
    C["s_rstdbc"] = p.sbuf("s_rstdbc", [128, TB], F32)
    C["s_lnvar"] = p.sbuf("s_lnvar", [128, TB], F32)
    C["s_lnsrt"] = p.sbuf("s_lnsrt", [128, TB], F32)
    C["s_lntmp"] = p.sbuf("s_lntmp", [128, TB], F32)
    C["s_onescol_bf"] = p.sbuf("s_onescol_bf", [128, 1], BF16)
    C["s_onesrow_f32"] = p.sbuf("s_onesrow_f32", [1, 128], F32)
    C["s_onesrow_bf"] = p.sbuf("s_onesrow_bf", [1, 128], BF16)
    C["s_ident_bf"] = p.sbuf("s_ident_bf", [128, 128], BF16)
    C["s_ident_f32"] = p.sbuf("s_ident_f32", [128, 128], F32)
    C["s_epscol"] = p.sbuf("s_epscol", [128, 1], F32)
    C["s_hbf"] = p.sbuf("s_hbf", [128, NCH * T_LOC], BF16)
    p.on("gpsimd", lambda e: e.memset(C["s_epscol"][:], EPS_LN))
    gs = p.sem("constset")
    p.inc("gpsimd", gs)
    for e in P.ENGINES:
        p.wait(e, gs, gs.n)
    ld = p.sem("constload")
    for name in ("c_ident_bf", "c_ident_f32", "c_ones_row_bf", "c_ones_row_f32", "c_ones_col_bf"):
        sb = {"c_ident_bf": C["s_ident_bf"], "c_ident_f32": C["s_ident_f32"],
              "c_ones_row_bf": C["s_onesrow_bf"], "c_ones_row_f32": C["s_onesrow_f32"],
              "c_ones_col_bf": C["s_onescol_bf"]}[name]
        p.dma("sync", sb[:], IN[name], sem=ld)
    for e in P.ENGINES:
        p.wait(e, ld, ld.n)
    return C


def emit_l1ffn(p, C, xf, IN):
    p.uid += 1
    tag = f"d{p.uid}"
    h_bf = C["s_hbf"]
    emit_ln(p, C, xf, h_bf, tag="lnD")

    es = ExitStack()
    lg = p.sbuf(tag + "lg", [8, 1], F32, es=es)
    uu = p.sbuf(tag + "u", [8, 1], F32, es=es)
    t0 = p.sbuf(tag + "t0", [8, 64], F32, es=es)
    ind = p.sbuf(tag + "ind", [8, 1], F32, es=es)
    bsel = p.sbuf(tag + "bsel", [128, 8], F32, es=es)
    brow = p.sbuf(tag + "brow", [1, 1024], F32, es=es)
    bexp = p.sbuf(tag + "bexp", [8, 1024], F32, es=es)
    indbc = p.sbuf(tag + "indbc", [128, 8], F32, es=es)
    wsel = p.sbuf(tag + "wsel", [128, 8 * 1024], BF16, es=es)
    wexp = p.sbuf(tag + "wexp", [128, 8 * 1024], F32, es=es)
    wacc = p.sbuf(tag + "wacc", [128, 1024], F32, es=es)
    pps = p.psum(tag + "pa", [128, TB], es=es)
    pps2 = p.psum(tag + "pb", [128, TB], es=es)
    pj = [p.psum(tag + "p0", [128, TB], es=es), p.psum(tag + "p1", [128, TB], es=es)]

    ld = p.sem(tag + "ld")
    vs, ts, ss = p.sem(tag + "v"), p.sem(tag + "t"), p.sem(tag + "s")
    idf = C["s_ident_f32"]

    p.dma("sync", lg[:], IN["L1_ffn_logits"], sem=ld)
    p.dma("sync", uu[:], IN["L1_ffn_u"], sem=ld)
    p.dma("sync", bexp[:], IN["L1_ffn_b"], sem=ld)
    p.wait("scalar", ld, ld.n)
    p.on("scalar", lambda e: e.activation(t0[0:8, 0:1], uu[:], AF.Ln))
    s1 = p.inc("scalar", ss)
    p.wait("vector", ss, s1)
    p.on("vector", lambda e: e.tensor_scalar(t0[0:8, 0:1], t0[0:8, 0:1], -1.0, None, ALU.mult))
    v1 = p.inc("vector", vs)
    p.wait("scalar", vs, v1)
    p.on("scalar", lambda e: e.activation(t0[0:8, 1:2], t0[0:8, 0:1], AF.Ln))
    s2 = p.inc("scalar", ss)
    p.wait("vector", ss, s2)
    p.wait("vector", ld, ld.n)
    p.on("vector", lambda e: e.tensor_sub(t0[0:8, 2:3], lg[:], t0[0:8, 1:2]))
    v2 = p.inc("vector", vs)
    p.wait("tensor", vs, v2)
    p.on("tensor", lambda e: e.transpose(pps[0:1, 0:8], t0[0:8, 2:3], idf[0:8, 0:8]))
    t1 = p.inc("tensor", ts)
    p.wait("vector", ts, t1)
    p.on("vector", lambda e: e.tensor_copy(t0[0:1, 3:11], pps[0:1, 0:8]))
    v3 = p.inc("vector", vs)
    p.wait("tensor", vs, v3)
    p.on("tensor", lambda e: e.matmul(pps2[0:8, 0:8], C["s_onesrow_f32"][0:1, 0:8], t0[0:1, 3:11], start=True, stop=True))
    t2 = p.inc("tensor", ts)
    p.wait("vector", ts, t2)
    p.on("vector", lambda e: e.tensor_tensor(t0[0:8, 16:24], t0[0:8, 2:3].broadcast_to([8, 8]), pps2[0:8, 0:8], ALU.is_ge))
    p.on("vector", lambda e: e.reduce_sum(t0[0:8, 24:25], t0[0:8, 16:24], axis=AXX))
    p.on("vector", lambda e: e.tensor_scalar(ind[:], t0[0:8, 24:25], 7.5, 0.5, ALU.is_ge, ALU.mult))
    v4 = p.inc("vector", vs)

    p.wait("tensor", vs, v4)
    p.on("tensor", lambda e: e.matmul(pps[0:1, 0:512], ind[:], bexp[:, 0:512], start=True, stop=True))
    p.on("tensor", lambda e: e.matmul(pps2[0:1, 0:512], ind[:], bexp[:, 512:1024], start=True, stop=True))
    t3 = p.inc("tensor", ts)
    p.wait("vector", ts, t3)
    p.on("vector", lambda e: e.tensor_copy(brow[0:1, 0:512], pps[0:1, 0:512]))
    p.on("vector", lambda e: e.tensor_copy(brow[0:1, 512:1024], pps2[0:1, 0:512]))
    v5 = p.inc("vector", vs)
    p.wait("tensor", vs, v5)
    for c in range(NCH):
        p.on("tensor", lambda e, c=c: e.transpose(pps[0:128, c:c + 1], brow[0:1, 128 * c:128 * (c + 1)], idf[0:1, 0:1]))
    t4 = p.inc("tensor", ts)
    p.wait("vector", ts, t4)
    p.on("vector", lambda e: e.tensor_copy(bsel[:, 0:8], pps[0:128, 0:8]))
    v6 = p.inc("vector", vs)

    p.wait("tensor", vs, v6)
    p.on("tensor", lambda e: e.transpose(pps2[0:1, 0:8], ind[:], idf[0:8, 0:8]))
    t5 = p.inc("tensor", ts)
    p.wait("vector", ts, t5)
    p.on("vector", lambda e: e.tensor_copy(t0[0:1, 32:40], pps2[0:1, 0:8]))
    v7 = p.inc("vector", vs)
    p.wait("tensor", vs, v7)
    p.on("tensor", lambda e: e.matmul(pps[0:128, 0:8], C["s_onesrow_f32"][0:1, :], t0[0:1, 32:40], start=True, stop=True))
    t6 = p.inc("tensor", ts)
    p.wait("vector", ts, t6)
    p.on("vector", lambda e: e.tensor_copy(indbc[:], pps[0:128, 0:8]))
    p.inc("vector", vs)
    p.barrier()

    wl = p.sem(tag + "wl")
    cs = p.sem(tag + "cs")
    for ci in range(NCH):
        if ci >= 1:
            p.wait("sync", cs, ci)
        for ex in range(8):
            p.dma("sync", wexp[:, ex * 1024:(ex + 1) * 1024],
                  IN["L1_ffn_W"][ex, ci * 128:(ci + 1) * 128, :], sem=wl)
        p.wait("vector", wl, wl.n)
        p.on("vector", lambda e: e.tensor_scalar(wacc[:], wexp[:, 0:1024], indbc[:, 0:1], None, ALU.mult))
        for ex in range(1, 7):
            p.on("vector", lambda e, ex=ex: e.scalar_tensor_tensor(
                wacc[:], wexp[:, ex * 1024:(ex + 1) * 1024], indbc[:, ex:ex + 1], wacc[:], ALU.mult, ALU.add))
        p.on("vector", lambda e, ci=ci: e.scalar_tensor_tensor(
            wsel[:, ci * 1024:(ci + 1) * 1024], wexp[:, 7 * 1024:8 * 1024], indbc[:, 7:8], wacc[:], ALU.mult, ALU.add))
        p.inc("vector", cs)
    p.barrier()

    pe = p.sem(tag + "pe")
    pt = p.sem(tag + "pt")
    for k in range(NTB * NCH):
        tb, oc = divmod(k, NCH)
        bi = k % 2
        if k >= 2:
            p.wait("tensor", pe, k - 1)
        for ci in range(NCH):
            p.on("tensor", lambda e, o=pj[bi][:, :], l=wsel[:, ci * 1024 + oc * 128: ci * 1024 + (oc + 1) * 128],
                 r=h_bf[:, ci * T_LOC + tb * TB: ci * T_LOC + (tb + 1) * TB],
                 s0=(ci == 0), s1=(ci == NCH - 1): e.matmul(o, l, r, start=s0, stop=s1))
        tt = p.inc("tensor", pt)
        p.wait("vector", pt, tt)
        xs = xf[:, oc * T_LOC + tb * TB: oc * T_LOC + (tb + 1) * TB]
        p.on("vector", lambda e, x_=xs, ps=pj[bi][:, :], bc=bsel[:, oc:oc + 1]:
             e.scalar_tensor_tensor(x_, ps, bc, x_, ALU.add, ALU.add))
        p.inc("vector", pe)
    p.barrier()
    es.close()



def emit_row_to_col(p, row_ap, n, col_out, ident, pm, ts, vs):
    p.wait("tensor", vs, vs.n)
    for c in range(n):
        p.on("tensor", lambda e, c=c: e.transpose(pm[0:128, c:c + 1], row_ap[0:1, 128 * c:128 * (c + 1)], ident[0:1, 0:1]))
    t = p.inc("tensor", ts)
    p.wait("vector", ts, t)
    p.on("vector", lambda e: e.tensor_copy(col_out[0:128, 0:n], pm[0:128, 0:n]))
    p.inc("vector", vs)


def emit_bias_combine(p, w_col, bexp_ap, bex_sb, brow, col_out, ident, pm, pm2, ts, vs, ld):
    p.wait("sync", ts, ts.n)
    _, dt_ = p.dma("sync", bex_sb[:], bexp_ap, sem=ld)
    p.wait("tensor", ld, dt_)
    p.wait("tensor", vs, vs.n)
    p.on("tensor", lambda e: e.matmul(pm[0:1, 0:512], w_col[:], bex_sb[:, 0:512], start=True, stop=True))
    p.on("tensor", lambda e: e.matmul(pm2[0:1, 0:512], w_col[:], bex_sb[:, 512:1024], start=True, stop=True))
    t = p.inc("tensor", ts)
    p.wait("vector", ts, t)
    p.on("vector", lambda e: e.tensor_copy(brow[0:1, 0:512], pm[0:1, 0:512]))
    p.on("vector", lambda e: e.tensor_copy(brow[0:1, 512:1024], pm2[0:1, 0:512]))
    p.inc("vector", vs)
    emit_row_to_col(p, brow, 8, col_out, ident, pm, ts, vs)



def emit_norm_evac(p, C, po_par, o_dest, scale_row, rrow, rc4, rbc, pmisc, prb, ts, vs):
    idf = C["s_ident_f32"]
    p.on("vector", lambda e: e.tensor_copy(rrow[0:1, :], po_par[64:65, :]))
    pv = p.inc("vector", vs)
    p.wait("tensor", vs, pv)
    for c4 in range(4):
        p.on("tensor", lambda e, c4=c4: e.transpose(
            pmisc[0:128, c4:c4 + 1], rrow[0:1, 128 * c4:128 * (c4 + 1)], idf[0:1, 0:1]))
    tt = p.inc("tensor", ts)
    p.wait("vector", ts, tt)
    p.on("vector", lambda e: e.tensor_copy(rc4[:, 0:4], pmisc[0:128, 0:4]))
    p.on("vector", lambda e: e.reciprocal(rc4[:, 0:4], rc4[:, 0:4]))
    pv = p.inc("vector", vs)
    p.wait("tensor", vs, pv)
    for c4 in range(4):
        p.on("tensor", lambda e, c4=c4: e.transpose(
            pmisc[0:1, 128 * c4:128 * (c4 + 1)], rc4[:, c4:c4 + 1], idf[:]))
    tt = p.inc("tensor", ts)
    p.wait("vector", ts, tt)
    p.on("vector", lambda e: e.tensor_copy(rrow[0:1, :], pmisc[0:1, 0:512]))
    pv = p.inc("vector", vs)
    p.wait("tensor", vs, pv)
    p.on("tensor", lambda e: e.matmul(prb[:, :], scale_row[0:1, :], rrow[0:1, :], start=True, stop=True))
    tt = p.inc("tensor", ts)
    p.wait("vector", ts, tt)
    p.on("vector", lambda e: e.tensor_copy(rbc[:], prb[:, :]))
    p.on("vector", lambda e: e.tensor_tensor(o_dest, po_par[0:64, :], rbc[0:64, :], ALU.mult))
    p.inc("vector", vs)


def emit_v_to_ag(p, C, vtmp, vrow, ptr, agv_in_ap, ld, vs, pt):
    for hh in range(16):
        p.on("gpsimd", lambda e, hh=hh: e.memset(vrow[:, 65 * hh + 64:65 * hh + 65], 1.0))
    gset = p.sem("gset" + str(p.uid) + "_" + str(len(p.ops["gpsimd"])))
    p.inc("gpsimd", gset)
    p.wait("vector", gset, gset.n)
    prev_dtgt = None
    prev_dvv = None
    for tau in range(16):
        if prev_dtgt is not None:
            p.wait("vector", ld, prev_dtgt)
        if prev_dvv is not None:
            p.wait("tensor", vs, prev_dvv)
        for oc in range(NCH):
            p.on("tensor", lambda e, tau=tau, oc=oc: e.transpose(
                ptr[:, oc * 128:(oc + 1) * 128],
                vtmp[:, oc * T_LOC + tau * 128: oc * T_LOC + (tau + 1) * 128], C["s_ident_bf"][:]))
        tt = p.inc("tensor", pt)
        p.wait("vector", pt, tt)
        for hh in range(16):
            oc, par = hh // 2, hh % 2
            p.on("vector", lambda e, oc=oc, par=par, hh=hh: e.tensor_copy(
                vrow[:, 65 * hh:65 * hh + 64], ptr[:, oc * 128 + par * 64: oc * 128 + par * 64 + 64]))
        dvv = p.inc("vector", vs)
        prev_dvv = dvv
        p.wait("sync", vs, dvv)
        _, prev_dtgt = p.dma("sync", agv_in_ap[tau * 128:(tau + 1) * 128, :], vrow[:], sem=ld)
    p.barrier()


def emit_l0ffn(p, C, xf, IN, DR):
    p.uid += 1
    tag = f"A{p.uid}_"
    h_bf = C["s_hbf"]
    _lb = p.label
    emit_ln(p, C, xf, h_bf, tag=tag + "ln")

    idf = C["s_ident_f32"]
    o1f = C["s_onesrow_f32"]
    ts, vs, ss = p.sem(tag + "t"), p.sem(tag + "v"), p.sem(tag + "s")
    ld = p.sem(tag + "ld")
    cc = p.sem(tag + "cc")
    pe, pt, dv = p.sem(tag + "pe"), p.sem(tag + "pt"), p.sem(tag + "dv")
    gld = p.sem(tag + "gld")

    es0 = ExitStack()
    q_bf = p.sbuf(tag + "q_bf", [128, NCH * T_LOC], BF16, es=es0)
    o_coll = h_bf
    wbc = p.sbuf(tag + "wbc", [128, 4], F32, es=es0)
    bq_c = p.sbuf(tag + "bq_c", [128, 8], F32, es=es0)
    bk_c = p.sbuf(tag + "bk_c", [128, 8], F32, es=es0)
    bv_c = p.sbuf(tag + "bv_c", [128, 8], F32, es=es0)
    bo_h = p.sbuf(tag + "bo_h", [128, 8], F32, es=es0)
    nq = p.sbuf(tag + "nq", [2, 8], F32, es=es0)
    nk = p.sbuf(tag + "nk", [2, 8], F32, es=es0)
    negBp = p.sbuf(tag + "negBp", [128, 8], F32, es=es0)

    es1 = ExitStack()
    pool = p.sbuf(tag + "pool", [128, 8], F32, es=es1)
    poolb = p.sbuf(tag + "poolb", [128, 8], BF16, es=es1)
    wg1 = p.sbuf(tag + "wg1", [128, 1024], BF16, es=es1)
    wg2 = p.sbuf(tag + "wg2", [128, 4], BF16, es=es1)
    z1 = p.sbuf(tag + "z1", [128, 1], BF16, es=es1)
    lgt = p.sbuf(tag + "lgt", [4, 1], F32, es=es1)
    gt0 = p.sbuf(tag + "gt0", [8, 64], F32, es=es1)
    wcol = p.sbuf(tag + "wcol", [4, 1], F32, es=es1)
    brow = p.sbuf(tag + "brow", [1, 1024], F32, es=es1)
    bcol_sb = p.sbuf(tag + "bcol_sb", [128, 8], F32, es=es1)
    bg1_sb = p.sbuf(tag + "bg1_sb", [128, 1], F32, es=es1)
    bg2_sb = p.sbuf(tag + "bg2_sb", [4, 1], F32, es=es1)
    hsel2 = p.sbuf(tag + "hsel2", [128, 2], BF16, es=es1)
    wc = p.sbuf(tag + "wc", [128, 8 * 1024], BF16, es=es1)
    wstr = p.sbuf(tag + "wstr", [128, 4 * 1024], BF16, es=es1)
    wacc = p.sbuf(tag + "wacc", [128, 1024], F32, es=es1)
    pch = p.sbuf(tag + "pch", [128, T_LOC], BF16, es=es1)
    sqc = p.sbuf(tag + "sqc", [128, T_LOC], BF16, es=es1)
    vrow = p.sbuf(tag + "vrow", [128, 1040], BF16, es=es1)
    nred = p.sbuf(tag + "nred", [2, 8], F32, es=es1)
    pga = p.psum(tag + "pga", [128, 512], es=es1)
    pgb = p.psum(tag + "pgb", [128, 512], es=es1)
    pj = [p.psum(tag + "pj0", [128, TB], es=es1), p.psum(tag + "pj1", [128, TB], es=es1)]
    pn = p.psum(tag + "pn", [2, 512], es=es1)
    ptr = p.psum(tag + "ptr", [128, 1024], BF16, es=es1)
    wl = p.sem(tag + "wl")
    cmb = p.sem(tag + "cmb")

    p.label = _lb + "_gate"
    for name, sb in (("bg1_col", bg1_sb), ("bg2_col", bg2_sb), ("c_hsel2", hsel2), ("bo_col", bcol_sb)):
        p.dma("sync", sb[:], IN[name], sem=ld)
    for c in range(NCH):
        p.on("vector", lambda e, c=c: e.reduce_sum(pool[:, c:c + 1], h_bf[:, c * T_LOC:(c + 1) * T_LOC], axis=AXX))
    pv = p.inc("vector", vs)
    p.wait("sync", vs, pv)
    p.dma("sync", DR["arp_in"], pool[:], sem=ld)
    p.wait("gpsimd", ld, ld.n)
    p.ctl("gpsimd", lambda e: e.collective_compute(
        "AllReduce", ALU.add, GROUPS, [DR["arp_in"].opt()], [DR["arp_out"].opt()]).then_inc(cc.h, 1))
    cc.n += 1
    p.wait("sync", cc, cc.n)
    p.dma("sync", pool[:], DR["arp_out"], sem=ld)
    for c in range(NCH):
        p.dma("gpsimd", wg1[:, 128 * c:128 * (c + 1)], IN["L0_ffn_Wg1"][c * 128:(c + 1) * 128, :], sem=gld)
    p.dma("gpsimd", wg2[:], IN["L0_ffn_Wg2"], sem=gld)
    p.wait("vector", ld, ld.n)
    p.wait("vector", gld, gld.n)
    p.on("vector", lambda e: e.tensor_scalar(poolb[:], pool[:], 1.0 / 8192.0, None, ALU.mult))
    pv = p.inc("vector", vs)
    p.wait("tensor", vs, pv)
    for c in range(NCH):
        p.on("tensor", lambda e, c=c: e.matmul(pga[0:128, 0:1], wg1[:, 128 * c:128 * (c + 1)], poolb[:, c:c + 1],
                                               start=(c == 0), stop=(c == NCH - 1)))
    t1 = p.inc("tensor", ts)
    p.wait("vector", ts, t1)
    p.on("vector", lambda e: e.tensor_scalar(z1[:], pga[0:128, 0:1], bg1_sb[:, 0:1], 0.0, ALU.add, ALU.max))
    pv = p.inc("vector", vs)
    p.wait("tensor", vs, pv)
    p.on("tensor", lambda e: e.matmul(pgb[0:4, 0:1], wg2[:], z1[:], start=True, stop=True))
    t2 = p.inc("tensor", ts)
    p.wait("vector", ts, t2)
    p.on("vector", lambda e: e.tensor_scalar(lgt[:], pgb[0:4, 0:1], bg2_sb[0:4, 0:1], None, ALU.add))
    pv = p.inc("vector", vs)
    p.wait("tensor", vs, pv)
    p.on("tensor", lambda e: e.transpose(pga[0:1, 0:4], lgt[:], idf[0:4, 0:4]))
    t3 = p.inc("tensor", ts)
    p.wait("vector", ts, t3)
    p.on("vector", lambda e: e.tensor_copy(gt0[0:1, 0:4], pga[0:1, 0:4]))
    pv = p.inc("vector", vs)
    p.wait("tensor", vs, pv)
    p.on("tensor", lambda e: e.matmul(pgb[0:4, 0:4], o1f[0:1, 0:4], gt0[0:1, 0:4], start=True, stop=True))
    t4 = p.inc("tensor", ts)
    p.wait("vector", ts, t4)
    p.on("vector", lambda e: e.tensor_tensor(gt0[0:4, 8:12], lgt[:].broadcast_to([4, 4]), pgb[0:4, 0:4], ALU.is_ge))
    p.on("vector", lambda e: e.reduce_sum(gt0[0:4, 12:13], gt0[0:4, 8:12], axis=AXX))
    p.on("vector", lambda e: e.tensor_scalar(gt0[0:4, 13:14], gt0[0:4, 12:13], 2.5, None, ALU.is_ge))
    p.on("vector", lambda e: e.tensor_scalar(gt0[0:4, 14:15], gt0[0:4, 12:13], 3.5, None, ALU.is_ge))
    pv = p.inc("vector", vs)
    p.wait("tensor", vs, pv)
    p.on("tensor", lambda e: e.matmul(pga[0:1, 8:9], gt0[0:4, 14:15], lgt[:], start=True, stop=True))
    t5 = p.inc("tensor", ts)
    p.wait("vector", ts, t5)
    p.on("vector", lambda e: e.tensor_copy(gt0[0:1, 16:17], pga[0:1, 8:9]))
    pv = p.inc("vector", vs)
    p.wait("tensor", vs, pv)
    p.on("tensor", lambda e: e.matmul(pgb[0:4, 8:9], o1f[0:1, 0:4], gt0[0:1, 16:17], start=True, stop=True))
    t6 = p.inc("tensor", ts)
    p.wait("vector", ts, t6)
    p.on("vector", lambda e: e.tensor_scalar(gt0[0:4, 17:18], pgb[0:4, 8:9], -1.0, None, ALU.mult))
    pv = p.inc("vector", vs)
    p.wait("scalar", vs, pv)
    p.on("scalar", lambda e: e.activation(gt0[0:4, 18:19], lgt[:], AF.Exp, bias=gt0[0:4, 17:18]))
    s1 = p.inc("scalar", ss)
    p.wait("vector", ss, s1)
    p.on("vector", lambda e: e.tensor_mul(gt0[0:4, 19:20], gt0[0:4, 18:19], gt0[0:4, 13:14]))
    pv = p.inc("vector", vs)
    p.wait("tensor", vs, pv)
    p.on("tensor", lambda e: e.matmul(pga[0:1, 9:10], gt0[0:4, 13:14], gt0[0:4, 18:19], start=True, stop=True))
    t7 = p.inc("tensor", ts)
    p.wait("vector", ts, t7)
    p.on("vector", lambda e: e.tensor_copy(gt0[0:1, 20:21], pga[0:1, 9:10]))
    pv = p.inc("vector", vs)
    p.wait("tensor", vs, pv)
    p.on("tensor", lambda e: e.matmul(pgb[0:4, 9:10], o1f[0:1, 0:4], gt0[0:1, 20:21], start=True, stop=True))
    t8 = p.inc("tensor", ts)
    p.wait("vector", ts, t8)
    p.on("vector", lambda e: e.tensor_copy(gt0[0:4, 21:22], pgb[0:4, 9:10]))
    p.on("vector", lambda e: e.reciprocal(gt0[0:4, 22:23], gt0[0:4, 21:22]))
    p.on("vector", lambda e: e.tensor_mul(wcol[:], gt0[0:4, 19:20], gt0[0:4, 22:23]))
    pv = p.inc("vector", vs)
    p.wait("tensor", vs, pv)
    p.on("tensor", lambda e: e.transpose(pga[0:1, 10:14], wcol[:], idf[0:4, 0:4]))
    t9 = p.inc("tensor", ts)
    p.wait("vector", ts, t9)
    p.on("vector", lambda e: e.tensor_copy(gt0[0:1, 24:28], pga[0:1, 10:14]))
    pv = p.inc("vector", vs)
    p.wait("tensor", vs, pv)
    p.on("tensor", lambda e: e.matmul(pgb[0:128, 12:16], o1f[0:1, :], gt0[0:1, 24:28], start=True, stop=True))
    t10 = p.inc("tensor", ts)
    p.wait("vector", ts, t10)
    p.on("vector", lambda e: e.tensor_copy(wbc[:], pgb[0:128, 12:16]))
    p.inc("vector", vs)
    bex_sb = p.sbuf(tag + "bex", [4, 1024], F32, es=es1)
    emit_bias_combine(p, wcol, IN["L0_ffn_bq"], bex_sb, brow, bq_c, idf, pga, pgb, ts, vs, ld)
    emit_bias_combine(p, wcol, IN["L0_ffn_bk"], bex_sb, brow, bk_c, idf, pga, pgb, ts, vs, ld)
    emit_bias_combine(p, wcol, IN["L0_ffn_bv"], bex_sb, brow, bv_c, idf, pga, pgb, ts, vs, ld)
    p.wait("vector", ld, ld.n)
    p.on("vector", lambda e: e.tensor_scalar(bo_h[:], bcol_sb[:], 0.5, None, ALU.mult))
    p.inc("vector", vs)
    p.barrier()

    p.label = _lb + "_cmb"
    def combine(which):
        win = IN["L0_ffn_W" + which]
        for ci in range(NCH):
            if ci >= 1:
                p.wait("gpsimd", cmb, cmb.n)
            for ex in range(4):
                p.dma("gpsimd", wstr[:, ex * 1024:(ex + 1) * 1024],
                      win[ex, ci * 128:(ci + 1) * 128, :], sem=wl)
            p.wait("vector", wl, wl.n)
            p.on("vector", lambda e: e.tensor_scalar(wacc[:], wstr[:, 0:1024], wbc[:, 0:1], None, ALU.mult))
            for ex in (1, 2):
                p.on("vector", lambda e, ex=ex: e.scalar_tensor_tensor(
                    wacc[:], wstr[:, ex * 1024:(ex + 1) * 1024], wbc[:, ex:ex + 1], wacc[:], ALU.mult, ALU.add))
            p.on("vector", lambda e, ci=ci: e.scalar_tensor_tensor(
                wc[:, ci * 1024:(ci + 1) * 1024], wstr[:, 3 * 1024:4 * 1024], wbc[:, 3:4], wacc[:], ALU.mult, ALU.add))
            p.inc("vector", cmb)
        p.barrier()

    def proj_oc(oc, dest, bias_col, scale2=None):
        for tb in range(NTB):
            bi = tb % 2
            if pt.n >= 2:
                p.wait("tensor", pe, pe.n - 1)
            for ci in range(NCH):
                p.on("tensor", lambda e, o=pj[bi][:, :], l=wc[:, ci * 1024 + oc * 128: ci * 1024 + (oc + 1) * 128],
                     r=h_bf[:, ci * T_LOC + tb * TB: ci * T_LOC + (tb + 1) * TB],
                     s0=(ci == 0), s1=(ci == NCH - 1): e.matmul(o, l, r, start=s0, stop=s1))
            tt = p.inc("tensor", pt)
            p.wait("vector", pt, tt)
            if scale2 is None:
                p.on("vector", lambda e, o=dest[:, tb * TB:(tb + 1) * TB], ps=pj[bi][:, :], b=bias_col:
                     e.tensor_scalar(o, ps, b, None, ALU.add))
            else:
                p.on("vector", lambda e, o=dest[:, tb * TB:(tb + 1) * TB], ps=pj[bi][:, :], b=bias_col:
                     e.tensor_scalar(o, ps, b, scale2, ALU.add, ALU.mult))
            p.inc("vector", pe)

    def norms_of(src2048, oc, ntile):
        p.on("vector", lambda e: e.tensor_mul(sqc[:], src2048, src2048))
        sv2 = p.inc("vector", vs)
        p.wait("tensor", vs, sv2)
        for tb in range(NTB):
            p.on("tensor", lambda e, tb=tb: e.matmul(pn[0:2, 0:512], hsel2[:], sqc[:, tb * TB:(tb + 1) * TB],
                                                     start=True, stop=True))
            tt = p.inc("tensor", pt)
            p.wait("vector", pt, tt)
            p.on("vector", lambda e, tb=tb: e.reduce_max(nred[0:2, tb:tb + 1], pn[0:2, 0:512], axis=AXX))
            p.inc("vector", pe)
            p.wait("tensor", pe, pe.n)
        p.on("vector", lambda e: e.tensor_tensor(nred[0:2, 4:5], nred[0:2, 0:1], nred[0:2, 1:2], ALU.max))
        p.on("vector", lambda e: e.tensor_tensor(nred[0:2, 5:6], nred[0:2, 2:3], nred[0:2, 3:4], ALU.max))
        p.on("vector", lambda e, oc=oc: e.tensor_tensor(ntile[0:2, oc:oc + 1], nred[0:2, 4:5], nred[0:2, 5:6], ALU.max))
        p.inc("vector", vs)

    combine("v")
    vtmp = q_bf
    for oc in range(NCH):
        proj_oc(oc, vtmp[:, oc * T_LOC:(oc + 1) * T_LOC], bv_c[:, oc:oc + 1])
    p.barrier()
    emit_v_to_ag(p, C, vtmp, vrow, ptr, DR["agv_in"], ld, vs, pt)

    combine("k")
    for oc in range(NCH):
        proj_oc(oc, pch, bk_c[:, oc:oc + 1])
        p.barrier()
        norms_of(pch[:], oc, nk)
        dvv = p.inc("vector", vs)
        p.wait("sync", vs, dvv)
        _, dtgt = p.dma("sync", DR["agk_in"][oc * 128:(oc + 1) * 128, :], pch[:], sem=ld)
        p.wait("vector", ld, dtgt)
    p.barrier()

    p.dma("sync", DR["arn_in"], nk[:], sem=ld)
    p.wait("gpsimd", ld, ld.n)
    p.ctl("gpsimd", lambda e: e.collective_compute(
        "AllGather", ALU.bypass, GROUPS, [DR["agk_in"].opt()], [DR["agk_out"].opt()]).then_inc(cc.h, 1))
    cc.n += 1
    p.ctl("gpsimd", lambda e: e.collective_compute(
        "AllGather", ALU.bypass, GROUPS, [DR["agv_in"].opt()], [DR["agv_out"].opt()]).then_inc(cc.h, 1))
    cc.n += 1
    p.ctl("gpsimd", lambda e: e.collective_compute(
        "AllReduce", ALU.max, GROUPS, [DR["arn_in"].opt()], [DR["arn_out"].opt()]).then_inc(cc.h, 1))
    cc.n += 1

    combine("q")
    for oc in range(NCH):
        proj_oc(oc, q_bf[:, oc * T_LOC:(oc + 1) * T_LOC], bq_c[:, oc:oc + 1], scale2=0.125)
        p.barrier()
        norms_of(q_bf[:, oc * T_LOC:(oc + 1) * T_LOC], oc, nq)
    p.barrier()
    es1.close()

    p.label = _lb + "_att"
    es2 = ExitStack()
    K_pair = p.sbuf(tag + "K_pair", [128, 8192], BF16, es=es2)
    vaug = p.sbuf(tag + "vaug", [128, 64 * 130], BF16, es=es2)
    P_sb = [p.sbuf(tag + "P0", [128, 1024], BF16, es=es2),
            p.sbuf(tag + "P1", [128, 1024], BF16, es=es2)]
    wo_bf = p.sbuf(tag + "wo", [128, 8 * 1024], BF16, es=es2)
    rrow = p.sbuf(tag + "rrow", [1, 512], F32, es=es2)
    rc4 = p.sbuf(tag + "rc4", [128, 4], F32, es=es2)
    rbc = p.sbuf(tag + "rbc", [128, 512], F32, es=es2)
    nrow = p.sbuf(tag + "nrow", [1, 16], F32, es=es2)
    psc = [p.psum(tag + "psc0", [128, 1024], es=es2), p.psum(tag + "psc1", [128, 1024], es=es2)]
    po = [p.psum(tag + "poa", [65, 512], es=es2), p.psum(tag + "pob", [65, 512], es=es2)]
    pmisc = p.psum(tag + "pmisc", [128, 512], es=es2)
    prb = p.psum(tag + "prb", [128, 512], es=es2)
    kl = p.sem(tag + "kl")
    scs = p.sem(tag + "scs")
    exs = p.sem(tag + "exs")
    pvs = p.sem(tag + "pvs")

    for ci in range(NCH):
        p.dma("gpsimd", wo_bf[:, ci * 1024:(ci + 1) * 1024], IN["L0_ffn_Wo"][ci * 128:(ci + 1) * 128, :], sem=gld)

    p.wait("sync", cc, cc.n)
    p.dma("sync", nk[:], DR["arn_out"], sem=ld)
    p.wait("vector", ld, ld.n)
    p.on("vector", lambda e: e.tensor_mul(nq[0:2, 0:8], nq[0:2, 0:8], nk[0:2, 0:8]))
    pv = p.inc("vector", vs)
    p.wait("scalar", vs, pv)
    p.on("scalar", lambda e: e.activation(nq[0:2, 0:8], nq[0:2, 0:8], AF.Sqrt))
    s2_ = p.inc("scalar", ss)
    p.wait("vector", ss, s2_)
    p.on("vector", lambda e: e.tensor_scalar(nq[0:2, 0:8], nq[0:2, 0:8], -0.125, None, ALU.mult))
    pv = p.inc("vector", vs)
    p.wait("tensor", vs, pv)
    p.on("tensor", lambda e: e.transpose(pmisc[0:8, 0:2], nq[0:2, 0:8], idf[0:2, 0:2]))
    tt = p.inc("tensor", ts)
    p.wait("vector", ts, tt)
    p.on("vector", lambda e: e.tensor_reduce(rc4[0:8, 0:1], pmisc[0:8, 0:2], AXX, ALU.min))
    pv = p.inc("vector", vs)
    p.wait("tensor", vs, pv)
    p.on("tensor", lambda e: e.transpose(pmisc[0:1, 8:16], rc4[0:8, 0:1], idf[0:8, 0:8]))
    tt = p.inc("tensor", ts)
    p.wait("vector", ts, tt)
    p.on("vector", lambda e: e.tensor_copy(nrow[0:1, 0:8], pmisc[0:1, 8:16]))
    pv = p.inc("vector", vs)
    p.wait("tensor", vs, pv)
    p.on("tensor", lambda e: e.matmul(prb[0:128, 0:8], o1f[0:1, :], nrow[0:1, 0:8], start=True, stop=True))
    tt = p.inc("tensor", ts)
    p.wait("vector", ts, tt)
    p.on("vector", lambda e: e.tensor_copy(negBp[:, 0:8], prb[0:128, 0:8]))
    p.inc("vector", vs)
    p.barrier()

    halfrow = p.sbuf(tag + "halfrow", [1, 128], F32, es=es2)
    p.dma("sync", halfrow[:], IN["c_half_row_f32"], sem=ld)
    for e_ in P.ENGINES:
        p.wait(e_, ld, ld.n)

    for hp in range(NCH):
        for r in range(4):
            p.dma("sync", K_pair[:, r * 2048:(r + 1) * 2048],
                  DR["agk_out"][r * 1024 + hp * 128: r * 1024 + (hp + 1) * 128, :], sem=kl)
            p.dma("sync", vaug[:, r * 16 * 130:(r + 1) * 16 * 130],
                  bass.AP(DR["agv_out"].tensor, (r * 2048) * 1040 + hp * 130,
                          [[1040, 128], [1040 * 128, 16], [1, 130]]), sem=kl)
        p.wait("tensor", kl, kl.n)
        p.wait("scalar", kl, kl.n)
        for qb in range(NTB):
            qsl = slice(hp * T_LOC + qb * TB, hp * T_LOC + (qb + 1) * TB)
            nkc = 64
            base_ex = exs.n
            base_pv = pvs.n
            base_sc = scs.n

            def _scores(kc, qsl=qsl):
                bi = kc % 2
                p.on("tensor", lambda e, bi=bi, kc=kc, qsl=qsl: e.matmul(
                    psc[bi][:, 0:512], K_pair[0:64, kc * 128:(kc + 1) * 128], q_bf[0:64, qsl],
                    start=True, stop=True))
                p.on("tensor", lambda e, bi=bi, kc=kc, qsl=qsl: e.matmul(
                    psc[bi][:, 512:1024], K_pair[64:128, kc * 128:(kc + 1) * 128], q_bf[64:128, qsl],
                    start=True, stop=True, tile_position=(64, 0)))
                p.inc("tensor", scs)

            _scores(0)
            _scores(1)
            for kc in range(nkc):
                bi = kc % 2
                p.wait("scalar", scs, base_sc + kc + 1)
                if kc >= 2:
                    p.wait("scalar", pvs, base_pv + kc - 1)
                p.on("scalar", lambda e, bi=bi, hp=hp: e.activation(
                    P_sb[bi][:], psc[bi][:], AF.Exp, bias=negBp[:, hp:hp + 1]))
                p.inc("scalar", exs)
                p.wait("tensor", exs, base_ex + kc + 1)
                p.on("tensor", lambda e, bi=bi, kc=kc, s0=(kc == 0), s1=(kc == nkc - 1): e.matmul(
                    po[0][0:65, :], vaug[:, kc * 130:kc * 130 + 65], P_sb[bi][:, 0:512],
                    start=s0, stop=s1))
                p.on("tensor", lambda e, bi=bi, kc=kc, s0=(kc == 0), s1=(kc == nkc - 1): e.matmul(
                    po[1][0:65, :], vaug[:, kc * 130 + 65:(kc + 1) * 130], P_sb[bi][:, 512:1024],
                    start=s0, stop=s1))
                p.inc("tensor", pvs)
                if kc + 2 < nkc:
                    _scores(kc + 2)
            p.wait("vector", pvs, pvs.n)
            for par in range(2):
                osl_r = slice(par * 64, par * 64 + 64)
                osl_c = slice(hp * T_LOC + qb * TB, hp * T_LOC + (qb + 1) * TB)
                emit_norm_evac(p, C, po[par], o_coll[osl_r, osl_c], halfrow,
                               rrow, rc4, rbc, pmisc, prb, ts, vs)
            p.barrier()

    p.label = _lb + "_wo"
    pjw = [psc[0], psc[1]]
    wos = p.sem(tag + "wos")
    woe = p.sem(tag + "woe")
    p.wait("tensor", gld, gld.n)
    k = 0
    for tb in range(NTB):
        for oc in range(NCH):
            bi = k % 2
            if k >= 2:
                p.wait("tensor", woe, k - 1)
            for ci in range(NCH):
                p.on("tensor", lambda e, o=pjw[bi][:, 0:512], l=wo_bf[:, ci * 1024 + oc * 128: ci * 1024 + (oc + 1) * 128],
                     r=o_coll[:, ci * T_LOC + tb * TB: ci * T_LOC + (tb + 1) * TB],
                     s0=(ci == 0), s1=(ci == NCH - 1): e.matmul(o, l, r, start=s0, stop=s1))
            tt = p.inc("tensor", wos)
            p.wait("vector", wos, tt)
            xs = xf[:, oc * T_LOC + tb * TB: oc * T_LOC + (tb + 1) * TB]
            p.on("vector", lambda e, x_=xs, bi=bi, oc=oc: e.scalar_tensor_tensor(
                x_, pjw[bi][:, 0:512], bo_h[:, oc:oc + 1], x_, ALU.add, ALU.add))
            p.inc("vector", woe)
            k += 1
    p.barrier()
    es2.close()
    es0.close()



def emit_l0attn(p, C, xf, IN, DR, core_hg):
    import os as _os
    p.uid += 1
    tag = f"B{p.uid}_"
    h_bf = C["s_hbf"]
    emit_ln(p, C, xf, h_bf, tag=tag + "ln")

    idf = C["s_ident_f32"]
    o1f = C["s_onesrow_f32"]
    ts, vs, ss = p.sem(tag + "t"), p.sem(tag + "v"), p.sem(tag + "s")
    ld = p.sem(tag + "ld")
    gld = p.sem(tag + "gld")
    cc = p.sem(tag + "cc")
    pe, pt = p.sem(tag + "pe"), p.sem(tag + "pt")

    es0 = ExitStack()
    negB = p.sbuf(tag + "negB", [128, 1], F32, es=es0)
    rtexp = p.sbuf(tag + "rtexp", [8, 1], F32, es=es0)
    es1 = ExitStack()
    wq_bf = p.sbuf(tag + "wq", [128, 8 * 256], BF16, es=es1)
    wk_bf = p.sbuf(tag + "wk", [128, 8 * 256], BF16, es=es1)
    wv_bf = p.sbuf(tag + "wv", [128, 8 * 1024], BF16, es=es1)
    for ci in range(NCH):
        p.dma("gpsimd", wq_bf[:, ci * 256:(ci + 1) * 256], IN["L0_attn_Wq"][ci * 128:(ci + 1) * 128, :], sem=gld)
        p.dma("gpsimd", wk_bf[:, ci * 256:(ci + 1) * 256], IN["L0_attn_Wk"][ci * 128:(ci + 1) * 128, :], sem=gld)
        p.dma("gpsimd", wv_bf[:, ci * 1024:(ci + 1) * 1024], IN["L0_attn_Wv"][ci * 128:(ci + 1) * 128, :], sem=gld)
    bq_c = p.sbuf(tag + "bq_c", [128, 2], F32, es=es1)
    bk_c = p.sbuf(tag + "bk_c", [128, 2], F32, es=es1)
    bv_c = p.sbuf(tag + "bv_c", [128, 8], F32, es=es1)
    qp_c = p.sbuf(tag + "qp_c", [128, 2], F32, es=es1)
    kp_c = p.sbuf(tag + "kp_c", [128, 2], F32, es=es1)
    hsel16 = p.sbuf(tag + "hs16", [128, 8], BF16, es=es1)
    hexp16 = p.sbuf(tag + "he16", [8, 128], BF16, es=es1)
    tmp_t = p.sbuf(tag + "tmpt", [1, 1], F32, es=es1)
    rt8 = p.sbuf(tag + "rt8", [8, 1], F32, es=es1)
    for name, sb in (("L0_bq_col", bq_c), ("L0_bk_col", bk_c), ("L0_bv_col", bv_c),
                     ("L0_qp_col", qp_c), ("L0_kp_col", kp_c),
                     ("c_hsel16", hsel16), ("c_hexp16", hexp16)):
        p.dma("sync", sb[:], IN[name], sem=ld)
    p.dma("sync", tmp_t[:], IN["L0_attn_temperature"], sem=ld)

    qh_bf = p.sbuf(tag + "qh", [128, 2 * T_LOC], BF16, es=es1)
    kh_bf = p.sbuf(tag + "kh", [128, 2 * T_LOC], BF16, es=es1)
    vtmp = p.sbuf(tag + "vtmp", [128, NCH * T_LOC], BF16, es=es1)
    sqc = p.sbuf(tag + "sqc", [128, TB], BF16, es=es1)
    rn8 = p.sbuf(tag + "rn8", [8, TB], F32, es=es1)
    rn8b = p.sbuf(tag + "rn8b", [8, TB], BF16, es=es1)
    pj = [p.psum(tag + "pj0", [128, TB], es=es1), p.psum(tag + "pj1", [128, TB], es=es1)]
    pn8 = p.psum(tag + "pn8", [8, TB], es=es1)
    pex = p.psum(tag + "pex", [128, TB], es=es1)

    p.wait("vector", ld, ld.n)
    p.on("vector", lambda e: e.reciprocal(tmp_t[:], tmp_t[:]))
    pv = p.inc("vector", vs)
    p.wait("tensor", vs, pv)
    p.on("tensor", lambda e: e.matmul(pex[0:128, 0:1], o1f[0:1, :], tmp_t[:], start=True, stop=True))
    tt = p.inc("tensor", ts)
    p.wait("vector", ts, tt)
    p.on("vector", lambda e: e.tensor_copy(rt8[:], pex[0:8, 0:1]))
    p.on("vector", lambda e: e.tensor_copy(rtexp[:], pex[0:8, 0:1]))
    p.on("vector", lambda e: e.tensor_scalar(negB[:], pex[0:128, 0:1], -1.0, None, ALU.mult))
    p.on("vector", lambda e: e.tensor_tensor(negB[:], negB[:], pex[0:128, 0:1], ALU.min))
    p.inc("vector", vs)
    p.barrier()

    def proj_norm(wmat, nocs, bias_col, phase_col, dest, rt):
        for oc in range(nocs):
            k = 0
            for tb in range(NTB):
                bi = k % 2
                p.wait("tensor", pe, max(0, pe.n - 1))
                for ci in range(NCH):
                    p.on("tensor", lambda e, o=pj[bi][:, :], l=wmat[:, ci * 256 + oc * 128: ci * 256 + (oc + 1) * 128],
                         r=h_bf[:, ci * T_LOC + tb * TB: ci * T_LOC + (tb + 1) * TB],
                         s0=(ci == 0), s1=(ci == NCH - 1): e.matmul(o, l, r, start=s0, stop=s1))
                tt = p.inc("tensor", pt)
                p.wait("vector", pt, tt)
                raw = dest[:, oc * T_LOC + tb * TB: oc * T_LOC + (tb + 1) * TB]
                p.on("vector", lambda e, raw=raw, bi=bi, oc=oc: e.tensor_scalar(
                    raw, pj[bi][:, :], bias_col[:, oc:oc + 1], phase_col[:, oc:oc + 1], ALU.add, ALU.add))
                p.inc("vector", pe)
                p.on("vector", lambda e, raw=raw: e.tensor_mul(sqc[:], raw, raw))
                sv2 = p.inc("vector", vs)
                p.wait("tensor", vs, sv2)
                p.on("tensor", lambda e: e.matmul(pn8[0:8, :], hsel16[:], sqc[:], start=True, stop=True))
                tt2 = p.inc("tensor", pt)
                p.wait("scalar", pt, tt2)
                p.on("scalar", lambda e: e.activation(rn8[0:8, :], pn8[0:8, :], AF.Sqrt))
                sg = p.inc("scalar", ss)
                p.wait("vector", ss, sg)
                p.on("vector", lambda e: e.tensor_scalar(rn8[0:8, :], rn8[0:8, :], 1e-12, None, ALU.max))
                p.on("vector", lambda e: e.reciprocal(rn8[0:8, :], rn8[0:8, :]))
                if rt is not None:
                    p.on("vector", lambda e: e.tensor_scalar(rn8[0:8, :], rn8[0:8, :], rt[0:8, 0:1], None, ALU.mult))
                p.on("vector", lambda e: e.tensor_copy(rn8b[0:8, :], rn8[0:8, :]))
                sv3 = p.inc("vector", vs)
                p.wait("tensor", vs, sv3)
                p.on("tensor", lambda e: e.matmul(pex[:, :], hexp16[0:8, :], rn8b[0:8, :], start=True, stop=True))
                tt3 = p.inc("tensor", pt)
                p.wait("vector", pt, tt3)
                p.on("vector", lambda e, raw=raw: e.tensor_tensor(raw, raw, pex[:, :], ALU.mult))
                p.inc("vector", vs)
                k += 1
            p.barrier()

    proj_norm(wq_bf, 2, bq_c, qp_c, qh_bf, rt8)
    proj_norm(wk_bf, 2, bk_c, kp_c, kh_bf, None)
    for oc in range(NCH):
        k = 0
        for tb in range(NTB):
            bi = k % 2
            p.wait("tensor", pe, max(0, pe.n - 1))
            for ci in range(NCH):
                p.on("tensor", lambda e, o=pj[bi][:, :], l=wv_bf[:, ci * 1024 + oc * 128: ci * 1024 + (oc + 1) * 128],
                     r=h_bf[:, ci * T_LOC + tb * TB: ci * T_LOC + (tb + 1) * TB],
                     s0=(ci == 0), s1=(ci == NCH - 1): e.matmul(o, l, r, start=s0, stop=s1))
            tt = p.inc("tensor", pt)
            p.wait("vector", pt, tt)
            p.on("vector", lambda e, oc=oc, tb=tb, bi=bi: e.tensor_scalar(
                vtmp[:, oc * T_LOC + tb * TB: oc * T_LOC + (tb + 1) * TB],
                pj[bi][:, :], bv_c[:, oc:oc + 1], None, ALU.add))
            p.inc("vector", pe)
            k += 1
    p.barrier()
    for oc in range(2):
        p.dma("sync", DR["agq_in"][oc * 128:(oc + 1) * 128, :], qh_bf[:, oc * T_LOC:(oc + 1) * T_LOC], sem=ld)
        p.dma("sync", DR["agk_in"][oc * 128:(oc + 1) * 128, :], kh_bf[:, oc * T_LOC:(oc + 1) * T_LOC], sem=ld)
    for oc in range(NCH):
        p.dma("sync", DR["agvf_in"][oc * 128:(oc + 1) * 128, :], vtmp[:, oc * T_LOC:(oc + 1) * T_LOC], sem=ld)
    p.wait("gpsimd", ld, ld.n)
    p.ctl("gpsimd", lambda e: e.collective_compute(
        "AllGather", ALU.bypass, GROUPS, [DR["agq_in"].opt()], [DR["agq_out"].opt()]).then_inc(cc.h, 1))
    cc.n += 1
    p.ctl("gpsimd", lambda e: e.collective_compute(
        "AllGather", ALU.bypass, GROUPS, [DR["agk_in"].opt()], [DR["agk_out"].opt()]).then_inc(cc.h, 1))
    cc.n += 1
    p.ctl("gpsimd", lambda e: e.collective_compute(
        "AllGather", ALU.bypass, GROUPS, [DR["agvf_in"].opt()], [DR["agvf_out"].opt()]).then_inc(cc.h, 1))
    cc.n += 1
    p.barrier()
    es1.close()

    es2 = ExitStack()
    o_coll = h_bf
    Q2 = p.sbuf(tag + "Q2", [64, 8192], BF16, es=es2)
    K2 = p.sbuf(tag + "K2", [64, 8192], BF16, es=es2)
    v2 = p.sbuf(tag + "v2", [128, 8192], BF16, es=es2)
    vaug = p.sbuf(tag + "vaug", [128, 64 * 130], BF16, es=es2)
    P_sb = [p.sbuf(tag + "P0", [128, 1024], BF16, es=es2),
            p.sbuf(tag + "P1", [128, 1024], BF16, es=es2)]
    masks = p.sbuf(tag + "masks", [128, 4 * 512], BF16, es=es2)
    rrow = p.sbuf(tag + "rrow", [1, 512], F32, es=es2)
    rc4 = p.sbuf(tag + "rc4", [128, 4], F32, es=es2)
    rbc = p.sbuf(tag + "rbc", [128, 512], F32, es=es2)
    onerow = p.sbuf(tag + "onerow", [1, 128], F32, es=es2)
    psc = [p.psum(tag + "psc0", [128, 1024], es=es2), p.psum(tag + "psc1", [128, 1024], es=es2)]
    po = [p.psum(tag + "poa", [65, 512], es=es2), p.psum(tag + "pob", [65, 512], es=es2)]
    pmisc = p.psum(tag + "pmisc", [128, 512], es=es2)
    prb = pmisc
    ptr2 = p.psum(tag + "ptr2", [128, 1024], BF16, es=es2)
    kl = p.sem(tag + "kl")
    scs, exs, pvs = p.sem(tag + "scs"), p.sem(tag + "exs"), p.sem(tag + "pvs")
    mks = p.sem(tag + "mks")
    p.dma("sync", masks[:], IN["c_masks_causal"], sem=ld)
    p.dma("sync", onerow[:], IN["c_ones_row_f32"], sem=ld)
    p.wait("sync", cc, cc.n)

    for j2 in range(128):
        p.on("gpsimd", lambda e, j2=j2: e.memset(vaug[:, 65 * j2 + 64:65 * j2 + 65], 1.0)) if j2 < 2 * 64 else None
    gset2 = p.sem(tag + "gsetb")
    p.inc("gpsimd", gset2)
    p.wait("vector", gset2, gset2.n)

    bidx = p.sbuf(tag + "bidx", [128, 56], mybir.dt.uint32, es=es2)
    p.dma("sync", bidx[:], IN["t_bidx"], sem=ld)
    p.wait("gpsimd", ld, ld.n)
    gg = p.sem(tag + "gg")
    for hp in range(2):
        for g in range(8):
            qcol = hp * 16 + g
            kcol = hp * 16 + 8 + g
            vcol = 32 + hp * 8 + g
            if gg.n > 0:
                p.ctl("gpsimd", lambda e, v=gg.n: e.wait_ge(gg.h, v))
            gg.inc(16)
            p.ctl("gpsimd", lambda e, g=g, qcol=qcol, v=gg.n: e.indirect_dma_start(
                Q2[0:48, g * 1024:(g + 1) * 1024], None,
                DR["agq_out"].rearrange("a (b c) -> (a b) c", b=2), IndirectOffsetOnAxis(bidx[:, qcol:qcol + 1], 0)
            ).then_inc(gg.h, 16))
            if gg.n > 0:
                p.ctl("gpsimd", lambda e, v=gg.n: e.wait_ge(gg.h, v))
            gg.inc(16)
            p.ctl("gpsimd", lambda e, g=g, kcol=kcol, v=gg.n: e.indirect_dma_start(
                K2[0:48, g * 1024:(g + 1) * 1024], None,
                DR["agk_out"].rearrange("a (b c) -> (a b) c", b=2), IndirectOffsetOnAxis(bidx[:, kcol:kcol + 1], 0)
            ).then_inc(gg.h, 16))
            if gg.n > 0:
                p.ctl("gpsimd", lambda e, v=gg.n: e.wait_ge(gg.h, v))
            gg.inc(16)
            p.ctl("gpsimd", lambda e, g=g, vcol=vcol, v=gg.n: e.indirect_dma_start(
                v2[:, g * 1024:(g + 1) * 1024], None,
                DR["agvf_out"].rearrange("a (b c) -> (a b) c", b=2), IndirectOffsetOnAxis(bidx[:, vcol:vcol + 1], 0)
            ).then_inc(gg.h, 16))
        for e_ in ("tensor", "scalar", "vector"):
            p.wait(e_, gg, gg.n)
        p.wait("tensor", kl, kl.n)
        p.wait("scalar", kl, kl.n)
        p.wait("vector", kl, kl.n)
        for tau8 in range(8):
            if tau8 >= 1:
                p.wait("tensor", vs, vs.n)
            for sub in range(8):
                tau = tau8 * 8 + sub
                p.on("tensor", lambda e, tau=tau, sub=sub: e.transpose(
                    ptr2[:, sub * 128:(sub + 1) * 128],
                    v2[:, tau * 128:(tau + 1) * 128], C["s_ident_bf"][:]))
            tt = p.inc("tensor", pt)
            p.wait("vector", pt, tt)
            for sub in range(8):
                tau = tau8 * 8 + sub
                for par in range(2):
                    p.on("vector", lambda e, tau=tau, sub=sub, par=par: e.tensor_copy(
                        vaug[:, tau * 130 + 65 * par: tau * 130 + 65 * par + 64],
                        ptr2[:, sub * 128 + par * 64: sub * 128 + par * 64 + 64]))
            p.inc("vector", vs)
        p.barrier()
        for G in range(16):
            nkc = 4 * G + 4
            base_ex = exs.n
            base_pv = pvs.n
            base_sc = scs.n
            base_mk = mks.n

            def _scoresB(kc, G=G):
                bi = kc % 2
                p.on("tensor", lambda e, bi=bi, kc=kc, G=G: e.matmul(
                    psc[bi][:, 0:512], K2[0:16, kc * 128:(kc + 1) * 128], Q2[0:16, G * 512:(G + 1) * 512],
                    start=True, stop=True))
                p.on("tensor", lambda e, bi=bi, kc=kc, G=G: e.matmul(
                    psc[bi][:, 512:1024], K2[32:48, kc * 128:(kc + 1) * 128], Q2[32:48, G * 512:(G + 1) * 512],
                    start=True, stop=True, tile_position=(32, 0)))
                p.inc("tensor", scs)

            _scoresB(0)
            if nkc > 1:
                _scoresB(1)
            nmk = 0
            for kc in range(nkc):
                bi = kc % 2
                p.wait("scalar", scs, base_sc + kc + 1)
                if kc >= 2:
                    p.wait("scalar", pvs, base_pv + kc - 1)
                p.on("scalar", lambda e, bi=bi: e.activation(
                    P_sb[bi][:], psc[bi][:], AF.Exp, bias=negB[:, 0:1]))
                p.inc("scalar", exs)
                if kc >= 4 * G:
                    off = kc - 4 * G
                    p.wait("vector", exs, base_ex + kc + 1)
                    p.on("vector", lambda e, bi=bi, off=off: e.tensor_mul(
                        P_sb[bi][:, 0:512], P_sb[bi][:, 0:512], masks[:, off * 512:(off + 1) * 512]))
                    p.on("vector", lambda e, bi=bi, off=off: e.tensor_mul(
                        P_sb[bi][:, 512:1024], P_sb[bi][:, 512:1024], masks[:, off * 512:(off + 1) * 512]))
                    nmk += 1
                    p.inc("vector", mks)
                    p.wait("tensor", mks, base_mk + nmk)
                else:
                    p.wait("tensor", exs, base_ex + kc + 1)
                p.on("tensor", lambda e, bi=bi, kc=kc, s0=(kc == 0), s1=(kc == nkc - 1): e.matmul(
                    po[0][0:65, :], vaug[:, kc * 130:kc * 130 + 65], P_sb[bi][:, 0:512],
                    start=s0, stop=s1))
                p.on("tensor", lambda e, bi=bi, kc=kc, s0=(kc == 0), s1=(kc == nkc - 1): e.matmul(
                    po[1][0:65, :], vaug[:, kc * 130 + 65:(kc + 1) * 130], P_sb[bi][:, 512:1024],
                    start=s0, stop=s1))
                p.inc("tensor", pvs)
                if kc + 2 < nkc:
                    _scoresB(kc + 2)
            p.wait("vector", pvs, pvs.n)
            if _os.environ.get("EDBG") == "po" and hp == 0 and G == 1:
                p.on("vector", lambda e: e.tensor_copy(xf[0:65, 0:512], po[0][0:65, :]))
                p.on("vector", lambda e: e.tensor_copy(xf[0:65, 512:1024], po[1][0:65, :]))
                p.inc("vector", vs)
                p.barrier()
                es2.close()
                es0.close()
                return
            for par in range(2):
                osl_r = slice(par * 64, par * 64 + 64)
                osl_c = slice(hp * 8192 + G * TB, hp * 8192 + (G + 1) * TB)
                emit_norm_evac(p, C, po[par], o_coll[osl_r, osl_c], onerow,
                               rrow, rc4, rbc, pmisc, prb, ts, vs)
            p.barrier()

    for X in range(4):
        for hp in range(2):
            for par in range(2):
                for gi, g in enumerate((X, 7 - X)):
                    p.dma("sync", DR["ago%d_in" % X][(2 * hp + par) * 64:(2 * hp + par) * 64 + 64,
                                                     gi * 1024:(gi + 1) * 1024],
                          o_coll[par * 64:par * 64 + 64, hp * 8192 + g * 1024: hp * 8192 + (g + 1) * 1024], sem=ld)
    p.wait("gpsimd", ld, ld.n)
    for X in range(4):
        p.ctl("gpsimd", lambda e, X=X: e.collective_compute(
            "AllGather", ALU.bypass, GROUPS, [DR["ago%d_in" % X].opt()],
            [DR["agofm_out"][X * 1024:(X + 1) * 1024, :].opt()]).then_inc(cc.h, 1))
        cc.n += 1
    p.wait("gpsimd", cc, cc.n)
    for ci in range(NCH):
        ocol = 48 + ci
        if gg.n > 0:
            p.ctl("gpsimd", lambda e, v=gg.n: e.wait_ge(gg.h, v))
        gg.inc(16)
        p.ctl("gpsimd", lambda e, ci=ci, ocol=ocol: e.indirect_dma_start(
            o_coll[:, ci * T_LOC:(ci + 1) * T_LOC], None,
            DR["agofm_out"], IndirectOffsetOnAxis(bidx[:, ocol:ocol + 1], 0)
        ).then_inc(gg.h, 16))
    for e_ in P.ENGINES:
        p.wait(e_, gg, gg.n)
    p.barrier()
    es2.close()
    es3 = ExitStack()
    wo_bf = p.sbuf(tag + "wo", [128, 8 * 1024], BF16, es=es3)
    bo_c = p.sbuf(tag + "bo_c", [128, 8], F32, es=es3)
    pjw = [p.psum(tag + "pjw0", [128, 512], es=es3), p.psum(tag + "pjw1", [128, 512], es=es3)]
    for ci in range(NCH):
        p.dma("gpsimd", wo_bf[:, ci * 1024:(ci + 1) * 1024], IN["L0_attn_Wo"][ci * 128:(ci + 1) * 128, :], sem=gld)
    p.dma("sync", bo_c[:], IN["L0_bo_col"], sem=ld)
    for e_ in P.ENGINES:
        p.wait(e_, ld, ld.n)
        p.wait(e_, gld, gld.n)

    wos = p.sem(tag + "wos")
    woe = p.sem(tag + "woe")
    k = 0
    for tb in range(NTB):
        for oc in range(NCH):
            bi = k % 2
            if k >= 2:
                p.wait("tensor", woe, k - 1)
            for ci in range(NCH):
                p.on("tensor", lambda e, o=pjw[bi][:, :], l=wo_bf[:, ci * 1024 + oc * 128: ci * 1024 + (oc + 1) * 128],
                     r=o_coll[:, ci * T_LOC + tb * TB: ci * T_LOC + (tb + 1) * TB],
                     s0=(ci == 0), s1=(ci == NCH - 1): e.matmul(o, l, r, start=s0, stop=s1))
            tt = p.inc("tensor", wos)
            p.wait("vector", wos, tt)
            xs = xf[:, oc * T_LOC + tb * TB: oc * T_LOC + (tb + 1) * TB]
            p.on("vector", lambda e, x_=xs, bi=bi, oc=oc: e.scalar_tensor_tensor(
                x_, pjw[bi][:, :], bo_c[:, oc:oc + 1], x_, ALU.add, ALU.add))
            p.inc("vector", woe)
            k += 1
    p.barrier()
    es3.close()
    es0.close()



def emit_l1attn(p, C, xf, IN, DR):
    p.uid += 1
    tag = f"E{p.uid}_"
    h_bf = C["s_hbf"]
    idf = C["s_ident_f32"]
    o1f = C["s_onesrow_f32"]
    ts, vs, ss = p.sem(tag + "t"), p.sem(tag + "v"), p.sem(tag + "s")
    ld = p.sem(tag + "ld")
    gld = p.sem(tag + "gld")
    cc = p.sem(tag + "cc")
    pe, pt = p.sem(tag + "pe"), p.sem(tag + "pt")

    es0 = ExitStack()
    mubcE = p.sbuf(tag + "mubcE", [128, T_LOC], F32, es=es0)
    rstdE = p.sbuf(tag + "rstdE", [128, T_LOC], F32, es=es0)
    negB = p.sbuf(tag + "negB", [128, 1], F32, es=es0)
    g1c = p.sbuf(tag + "g1c", [128, 8], F32, es=es0)
    b1c = p.sbuf(tag + "b1c", [128, 8], F32, es=es0)
    for name, sb in (("L1_g1_col", g1c), ("L1_b1_col", b1c)):
        p.dma("sync", sb[:], IN[name], sem=ld)
    for e_ in P.ENGINES:
        p.wait(e_, ld, ld.n)
    emit_ln(p, C, xf, h_bf, affine=(g1c, b1c), keep_stats=(mubcE, rstdE),
            rstd_comb=True, tag=tag + "ln")
    import os as _os
    if _os.environ.get("EDBG") == "h":
        for c in range(NCH):
            p.on("vector", lambda e, c=c: e.tensor_copy(
                xf[:, c * T_LOC:(c + 1) * T_LOC], h_bf[:, c * T_LOC:(c + 1) * T_LOC]))
        p.inc("vector", vs)
        p.barrier()
        es0.close()
        return

    es1 = ExitStack()
    wqs = [p.sbuf(tag + "wqs0", [128, 1024], BF16, es=es1),
           p.sbuf(tag + "wqs1", [128, 1024], BF16, es=es1)]
    wk_bf = p.sbuf(tag + "wk", [128, 8 * 256], BF16, es=es1)
    wv_bf = p.sbuf(tag + "wv", [128, 8 * 256], BF16, es=es1)
    for ci in range(NCH):
        p.dma("gpsimd", wk_bf[:, ci * 256:(ci + 1) * 256], IN["L1_Wk_perm"][ci * 128:(ci + 1) * 128, :], sem=gld)
        p.dma("gpsimd", wv_bf[:, ci * 256:(ci + 1) * 256], IN["L1_attn_Wv"][ci * 128:(ci + 1) * 128, :], sem=gld)
    bq_c = p.sbuf(tag + "bq_c", [128, 8], F32, es=es1)
    bk_c = p.sbuf(tag + "bk_c", [128, 2], F32, es=es1)
    bv_c = p.sbuf(tag + "bv_c", [128, 2], F32, es=es1)
    hsel2 = p.sbuf(tag + "hs2", [128, 2], BF16, es=es1)
    ropeC = p.sbuf(tag + "ropeC", [128, T_LOC], BF16, es=es1)
    ropeS = p.sbuf(tag + "ropeS", [128, T_LOC], BF16, es=es1)
    for name, sb in (("L1_bq_col", bq_c), ("L1_bk_col", bk_c), ("L1_bv_col", bv_c),
                     ("c_hsel2", hsel2), ("t_rope_cos", ropeC), ("t_rope_sin", ropeS)):
        p.dma("sync", sb[:], IN[name], sem=ld)
    q_bf = p.sbuf(tag + "q_bf", [128, NCH * T_LOC], BF16, es=es1)
    kv_bf = p.sbuf(tag + "kv_bf", [128, 4 * T_LOC], BF16, es=es1)
    e1 = C["s_cast0"]
    sq5 = C["s_sq0"]
    nq = p.sbuf(tag + "nq", [2, 12], F32, es=es1)
    nred = p.sbuf(tag + "nred", [2, 8], F32, es=es1)
    pj = [p.psum(tag + "pj0", [128, TB], es=es1), p.psum(tag + "pj1", [128, TB], es=es1)]
    pn = p.psum(tag + "pn", [2, 512], es=es1)

    def proj(wmat, wstride, nocs, bias_col, dest, scale2=None):
        for oc in range(nocs):
            for tb in range(NTB):
                bi = (oc * NTB + tb) % 2
                p.wait("tensor", pe, max(0, pe.n - 1))
                for ci in range(NCH):
                    p.on("tensor", lambda e, o=pj[bi][:, :], l=wmat[:, ci * wstride + oc * 128: ci * wstride + (oc + 1) * 128],
                         r=h_bf[:, ci * T_LOC + tb * TB: ci * T_LOC + (tb + 1) * TB],
                         s0=(ci == 0), s1=(ci == NCH - 1): e.matmul(o, l, r, start=s0, stop=s1))
                tt = p.inc("tensor", pt)
                p.wait("vector", pt, tt)
                dsl = dest[:, oc * T_LOC + tb * TB: oc * T_LOC + (tb + 1) * TB]
                if scale2 is None:
                    p.on("vector", lambda e, o=dsl, bi=bi, oc=oc: e.tensor_scalar(
                        o, pj[bi][:, :], bias_col[:, oc:oc + 1], None, ALU.add))
                else:
                    p.on("vector", lambda e, o=dsl, bi=bi, oc=oc: e.tensor_scalar(
                        o, pj[bi][:, :], bias_col[:, oc:oc + 1], scale2, ALU.add, ALU.mult))
                p.inc("vector", pe)

    p.wait("vector", ld, ld.n)
    p.wait("vector", gld, gld.n)
    p.wait("tensor", ld, ld.n)
    p.wait("tensor", gld, gld.n)
    wql = p.sem(tag + "wql")
    for oc in range(NCH):
        bq2 = oc % 2
        if wql.n > 0:
            p.ctl("gpsimd", lambda e, v=wql.n: e.wait_ge(wql.h, v))
        wql.inc(16)
        p.ctl("gpsimd", lambda e, oc=oc, bq2=bq2: e.dma_start(
            out=wqs[bq2][:], in_=bass.AP(IN["L1_Wq_perm"].tensor, oc * 128,
                                         [[1024, 128], [1024 * 128, 8], [1, 128]])).then_inc(wql.h, 16))
        p.wait("tensor", wql, wql.n)
        for tb in range(NTB):
            bi = (oc * NTB + tb) % 2
            p.wait("tensor", pe, max(0, pe.n - 1))
            for ci in range(NCH):
                p.on("tensor", lambda e, o=pj[bi][:, :], l=wqs[bq2][:, ci * 128:(ci + 1) * 128],
                     r=h_bf[:, ci * T_LOC + tb * TB: ci * T_LOC + (tb + 1) * TB],
                     s0=(ci == 0), s1=(ci == NCH - 1): e.matmul(o, l, r, start=s0, stop=s1))
            tt = p.inc("tensor", pt)
            p.wait("vector", pt, tt)
            dsl = q_bf[:, oc * T_LOC + tb * TB: oc * T_LOC + (tb + 1) * TB]
            p.on("vector", lambda e, o=dsl, bi=bi, oc=oc: e.tensor_scalar(
                o, pj[bi][:, :], bq_c[:, oc:oc + 1], 0.125, ALU.add, ALU.mult))
            p.inc("vector", pe)
        wt = pt.n
        p.ctl("gpsimd", lambda e, v=wt: e.wait_ge(pt.h, v))
    proj(wk_bf, 256, 2, bk_c, kv_bf)
    proj(wv_bf, 256, 2, bv_c, kv_bf[:, 2 * T_LOC:])
    p.barrier()

    def norms2(src, col0, nocs):
        for oc in range(nocs):
            for tb in range(NTB):
                sl = src[:, oc * T_LOC + tb * TB: oc * T_LOC + (tb + 1) * TB]
                p.on("vector", lambda e, sl=sl: e.tensor_mul(sq5[:], sl, sl))
                sv2 = p.inc("vector", vs)
                p.wait("tensor", vs, sv2)
                p.on("tensor", lambda e: e.matmul(pn[0:2, 0:512], hsel2[:], sq5[:], start=True, stop=True))
                tt = p.inc("tensor", pt)
                p.wait("vector", pt, tt)
                p.on("vector", lambda e, tb=tb: e.reduce_max(nred[0:2, tb:tb + 1], pn[0:2, 0:512], axis=AXX))
                p.inc("vector", pe)
                p.wait("tensor", pe, pe.n)
            p.on("vector", lambda e: e.tensor_tensor(nred[0:2, 4:5], nred[0:2, 0:1], nred[0:2, 1:2], ALU.max))
            p.on("vector", lambda e: e.tensor_tensor(nred[0:2, 5:6], nred[0:2, 2:3], nred[0:2, 3:4], ALU.max))
            p.on("vector", lambda e, oc=oc, col0=col0: e.tensor_tensor(
                nq[0:2, col0 + oc:col0 + oc + 1], nred[0:2, 4:5], nred[0:2, 5:6], ALU.max))
            p.inc("vector", vs)

    def rope(dest, nocs):
        for oc in range(nocs):
            for tb in range(NTB):
                d = dest[:, oc * T_LOC + tb * TB: oc * T_LOC + (tb + 1) * TB]
                cc_ = ropeC[:, tb * TB:(tb + 1) * TB]
                ss_ = ropeS[:, tb * TB:(tb + 1) * TB]
                p.on("vector", lambda e, d=d, cc_=cc_: e.tensor_mul(e1[:], d, cc_))
                p.on("vector", lambda e, d=d, ss_=ss_: e.tensor_mul(sq5[0:64, :], d[64:128, :], ss_[64:128, :]))
                p.on("vector", lambda e, d=d, ss_=ss_: e.tensor_mul(sq5[64:128, :], d[0:64, :], ss_[0:64, :]))
                p.on("vector", lambda e, d=d: e.tensor_add(d, e1[:], sq5[:]))
            p.inc("vector", vs)

    import os as _os
    if _os.environ.get("EDBG") == "q":
        for c in range(NCH):
            p.on("vector", lambda e, c=c: e.tensor_copy(
                xf[:, c * T_LOC:(c + 1) * T_LOC], q_bf[:, c * T_LOC:(c + 1) * T_LOC]))
        p.inc("vector", vs)
        p.barrier()
        es1.close()
        es0.close()
        return
    if _os.environ.get("EDBG") == "qr":
        pass
    norms2(q_bf, 0, 8)
    norms2(kv_bf, 8, 2)
    p.on("vector", lambda e: e.reduce_max(nq[0:2, 10:11], nq[0:2, 0:8], axis=AXX))
    p.on("vector", lambda e: e.reduce_max(nq[0:2, 11:12], nq[0:2, 8:10], axis=AXX))
    p.inc("vector", vs)
    rope(q_bf, 8)
    rope(kv_bf, 2)
    p.barrier()
    if _os.environ.get("EDBG") == "kv":
        for c in range(4):
            p.on("vector", lambda e, c=c: e.tensor_copy(
                xf[:, c * T_LOC:(c + 1) * T_LOC], kv_bf[:, c * T_LOC:(c + 1) * T_LOC]))
        p.inc("vector", vs)
        p.barrier()
        es1.close()
        es0.close()
        return
    if _os.environ.get("EDBG") == "qrope":
        for c in range(NCH):
            p.on("vector", lambda e, c=c: e.tensor_copy(
                xf[:, c * T_LOC:(c + 1) * T_LOC], q_bf[:, c * T_LOC:(c + 1) * T_LOC]))
        p.inc("vector", vs)
        p.barrier()
        es1.close()
        es0.close()
        return
    p.dma("sync", DR["arnE_in"], nq[0:2, 10:12], sem=ld)
    for oc in range(NCH):
        p.dma("sync", DR["agqE_in"][oc * 128:(oc + 1) * 128, :], q_bf[:, oc * T_LOC:(oc + 1) * T_LOC], sem=ld)
    for oc in range(2):
        p.dma("sync", DR["agkE_in"][oc * 128:(oc + 1) * 128, :], kv_bf[:, oc * T_LOC:(oc + 1) * T_LOC], sem=ld)
        p.dma("sync", DR["agvE_in"][oc * 128:(oc + 1) * 128, :], kv_bf[:, (2 + oc) * T_LOC:(3 + oc) * T_LOC], sem=ld)
    p.wait("gpsimd", ld, ld.n)
    for nm_in, nm_out in (("agqE_in", "agqE_out"), ("agkE_in", "agkE_out"), ("agvE_in", "agvE_out")):
        p.ctl("gpsimd", lambda e, a=nm_in, b=nm_out: e.collective_compute(
            "AllGather", ALU.bypass, GROUPS, [DR[a].opt()], [DR[b].opt()]).then_inc(cc.h, 1))
        cc.n += 1
    p.ctl("gpsimd", lambda e: e.collective_compute(
        "AllReduce", ALU.max, GROUPS, [DR["arnE_in"].opt()], [DR["arnE_out"].opt()]).then_inc(cc.h, 1))
    cc.n += 1
    p.barrier()
    es1.close()

    es2 = ExitStack()
    o_coll = h_bf
    nsc = p.sbuf(tag + "nsc", [2, 4], F32, es=es2)
    Q2 = p.sbuf(tag + "Q2", [128, 8192], BF16, es=es2)
    K2 = p.sbuf(tag + "K2", [128, 8192], BF16, es=es2)
    v1 = p.sbuf(tag + "v1", [64, 8192], BF16, es=es2)
    vaug = p.sbuf(tag + "vaug", [128, 64 * 65], BF16, es=es2)
    P_sb = [p.sbuf(tag + "P0", [128, 1024], BF16, es=es2),
            p.sbuf(tag + "P1", [128, 1024], BF16, es=es2)]
    masks = p.sbuf(tag + "masks", [128, 6 * 512], BF16, es=es2)
    rrow = p.sbuf(tag + "rrow", [1, 512], F32, es=es2)
    rc4 = p.sbuf(tag + "rc4", [128, 4], F32, es=es2)
    rbc = p.sbuf(tag + "rbc", [128, 512], F32, es=es2)
    onerow = p.sbuf(tag + "onerow", [1, 128], F32, es=es2)
    bidx = p.sbuf(tag + "bidx", [128, 56], mybir.dt.uint32, es=es2)
    psc = [p.psum(tag + "psc0", [128, 1024], es=es2), p.psum(tag + "psc1", [128, 1024], es=es2)]
    po = [p.psum(tag + "poa", [65, 512], es=es2), p.psum(tag + "pob", [65, 512], es=es2)]
    pmisc = p.psum(tag + "pmisc", [128, 512], es=es2)
    prb = pmisc
    ptr2 = p.psum(tag + "ptr2", [128, 1024], BF16, es=es2)
    kl = p.sem(tag + "kl")
    gg = p.sem(tag + "gg")
    scs, exs, pvs = p.sem(tag + "scs"), p.sem(tag + "exs"), p.sem(tag + "pvs")
    mks = p.sem(tag + "mks")
    p.dma("sync", masks[:], IN["c_masks_win"], sem=ld)
    p.dma("sync", onerow[:], IN["c_ones_row_f32"], sem=ld)
    p.dma("sync", bidx[:], IN["t_eidx"], sem=ld)
    p.wait("sync", cc, cc.n)
    p.dma("sync", nsc[0:2, 0:2], DR["arnE_out"], sem=ld)
    p.wait("scalar", ld, ld.n)
    p.on("scalar", lambda e: e.activation(nsc[0:2, 0:2], nsc[0:2, 0:2], AF.Sqrt))
    sg = p.inc("scalar", ss)
    p.wait("vector", ss, sg)
    p.on("vector", lambda e: e.tensor_mul(nsc[0:2, 2:3], nsc[0:2, 0:1], nsc[0:2, 1:2]))
    pv = p.inc("vector", vs)
    p.wait("tensor", vs, pv)
    p.on("tensor", lambda e: e.transpose(pmisc[0:1, 0:2], nsc[0:2, 2:3], idf[0:2, 0:2]))
    tt = p.inc("tensor", ts)
    p.wait("vector", ts, tt)
    p.on("vector", lambda e: e.reduce_max(nsc[0:1, 3:4], pmisc[0:1, 0:2], axis=AXX))
    p.on("vector", lambda e: e.tensor_scalar(nsc[0:1, 3:4], nsc[0:1, 3:4], -1.0, None, ALU.mult))
    pv = p.inc("vector", vs)
    p.wait("tensor", vs, pv)
    p.on("tensor", lambda e: e.matmul(pmisc[0:128, 8:9], o1f[0:1, :], nsc[0:1, 3:4], start=True, stop=True))
    tt = p.inc("tensor", ts)
    p.wait("vector", ts, tt)
    p.on("vector", lambda e: e.tensor_copy(negB[:], pmisc[0:128, 8:9]))
    p.inc("vector", vs)
    for kc in range(64):
        p.on("gpsimd", lambda e, kc=kc: e.memset(vaug[:, 65 * kc + 64:65 * kc + 65], 1.0))
    gset = p.sem(tag + "gset")
    p.inc("gpsimd", gset)
    p.wait("vector", gset, gset.n)
    p.wait("gpsimd", ld, ld.n)

    def gath(dst, src_view, col):
        if gg.n > 0:
            p.ctl("gpsimd", lambda e, v=gg.n: e.wait_ge(gg.h, v))
        gg.inc(16)
        p.ctl("gpsimd", lambda e, dst=dst, src=src_view, col=col: e.indirect_dma_start(
            dst, None, src, IndirectOffsetOnAxis(bidx[:, col:col + 1], 0)).then_inc(gg.h, 16))

    for hp in range(2):
        for g in range(8):
            gath(Q2[:, g * 1024:(g + 1) * 1024],
                 DR["agqE_out"].rearrange("a (b c) -> (a b) c", b=2), hp * 8 + g)
            gath(K2[:, g * 1024:(g + 1) * 1024],
                 DR["agkE_out"].rearrange("a (b c) -> (a b) c", b=2), 16 + g)
            if hp == 0:
                gath(v1[0:64, g * 1024:(g + 1) * 1024],
                     DR["agvE_out"].rearrange("a (b c) -> (a b) c", b=2), 24 + g)
        for e_ in ("tensor", "scalar", "vector"):
            p.wait(e_, gg, gg.n)
        if _os.environ.get("EDBG") == "Q2" and hp == 0:
            for c in range(4):
                p.on("vector", lambda e, c=c: e.tensor_copy(
                    xf[:, c * T_LOC:(c + 1) * T_LOC], Q2[:, c * T_LOC:(c + 1) * T_LOC]))
            p.inc("vector", vs)
            p.barrier()
            es2.close()
            es0.close()
            return
        if hp == 0:
            for t8 in range(8):
                if t8 >= 1:
                    p.wait("tensor", vs, vs.n)
                for sub in range(8):
                    tau = t8 * 8 + sub
                    p.on("tensor", lambda e, tau=tau, sub=sub: e.transpose(
                        ptr2[0:128, sub * 128:sub * 128 + 64],
                        v1[0:64, tau * 128:(tau + 1) * 128], C["s_ident_bf"][0:64, 0:64]))
                tt = p.inc("tensor", pt)
                p.wait("vector", pt, tt)
                for sub in range(8):
                    tau = t8 * 8 + sub
                    p.on("vector", lambda e, tau=tau, sub=sub: e.tensor_copy(
                        vaug[:, tau * 65: tau * 65 + 64], ptr2[0:128, sub * 128: sub * 128 + 64]))
                p.inc("vector", vs)
            p.barrier()
            if _os.environ.get("EDBG") == "vaug":
                p.on("vector", lambda e: e.tensor_copy(xf[:, 0:4160], vaug[:]))
                p.inc("vector", vs)
                p.barrier()
                es2.close()
                es0.close()
                return
        for G in range(16):
            kcs = list(range(max(0, 4 * G - 2), 4 * G + 4))
            base_ex = exs.n
            base_pv = pvs.n
            base_sc = scs.n
            base_mk = mks.n

            def _scoresE(ik, G=G, kcs=kcs):
                kc = kcs[ik]
                bi = ik % 2
                p.on("tensor", lambda e, bi=bi, kc=kc, G=G: e.matmul(
                    psc[bi][:, 0:512], K2[0:64, kc * 128:(kc + 1) * 128], Q2[0:64, G * 512:(G + 1) * 512],
                    start=True, stop=True))
                p.on("tensor", lambda e, bi=bi, kc=kc, G=G: e.matmul(
                    psc[bi][:, 512:1024], K2[64:128, kc * 128:(kc + 1) * 128], Q2[64:128, G * 512:(G + 1) * 512],
                    start=True, stop=True, tile_position=(64, 0)))
                p.inc("tensor", scs)

            _scoresE(0)
            if len(kcs) > 1:
                _scoresE(1)
            for ik, kc in enumerate(kcs):
                bi = ik % 2
                p.wait("scalar", scs, base_sc + ik + 1)
                if ik >= 2:
                    p.wait("scalar", pvs, base_pv + ik - 1)
                p.on("scalar", lambda e, bi=bi: e.activation(
                    P_sb[bi][:], psc[bi][:], AF.Exp, bias=negB[:, 0:1]))
                p.inc("scalar", exs)
                mi = kc - 4 * G + 2
                p.wait("vector", exs, base_ex + ik + 1)
                p.on("vector", lambda e, bi=bi, mi=mi: e.tensor_mul(
                    P_sb[bi][:, 0:512], P_sb[bi][:, 0:512], masks[:, mi * 512:(mi + 1) * 512]))
                p.on("vector", lambda e, bi=bi, mi=mi: e.tensor_mul(
                    P_sb[bi][:, 512:1024], P_sb[bi][:, 512:1024], masks[:, mi * 512:(mi + 1) * 512]))
                p.inc("vector", mks)
                p.wait("tensor", mks, base_mk + ik + 1)
                p.on("tensor", lambda e, bi=bi, kc=kc, s0=(ik == 0), s1=(ik == len(kcs) - 1): e.matmul(
                    po[0][0:65, :], vaug[:, kc * 65:kc * 65 + 65], P_sb[bi][:, 0:512],
                    start=s0, stop=s1))
                p.on("tensor", lambda e, bi=bi, kc=kc, s0=(ik == 0), s1=(ik == len(kcs) - 1): e.matmul(
                    po[1][0:65, :], vaug[:, kc * 65:kc * 65 + 65], P_sb[bi][:, 512:1024],
                    start=s0, stop=s1))
                p.inc("tensor", pvs)
                if ik + 2 < len(kcs):
                    _scoresE(ik + 2)
            p.wait("vector", pvs, pvs.n)
            if _os.environ.get("EDBG") == "po" and hp == 0 and G == 1:
                p.on("vector", lambda e: e.tensor_copy(xf[0:65, 0:512], po[0][0:65, :]))
                p.on("vector", lambda e: e.tensor_copy(xf[0:65, 512:1024], po[1][0:65, :]))
                p.inc("vector", vs)
                p.barrier()
                es2.close()
                es0.close()
                return
            for par in range(2):
                osl_r = slice(par * 64, par * 64 + 64)
                osl_c = slice(hp * 8192 + G * TB, hp * 8192 + (G + 1) * TB)
                emit_norm_evac(p, C, po[par], o_coll[osl_r, osl_c], onerow,
                               rrow, rc4, rbc, pmisc, prb, ts, vs)
            p.barrier()

    p.wait("sync", vs, vs.n)
    if "dbg" in IN:
        for c in range(NCH):
            p.dma("sync", IN["dbg"][c * 128:(c + 1) * 128, :], o_coll[:, c * T_LOC:(c + 1) * T_LOC], sem=ld)
    import os as _os2
    if _os2.environ.get("EDBG") in ("oc", "agof", "o2"):
        for c in range(NCH):
            p.on("vector", lambda e, c=c: e.tensor_copy(
                xf[:, c * T_LOC:(c + 1) * T_LOC], o_coll[:, c * T_LOC:(c + 1) * T_LOC]))
        p.inc("vector", vs)
        p.barrier()
        es2.close()
        es0.close()
        return
    for X in range(4):
        for hp in range(2):
            for par in range(2):
                for gi, g in enumerate((X, 7 - X)):
                    p.dma("sync", DR["ago%d_in" % X][(2 * hp + par) * 64:(2 * hp + par) * 64 + 64,
                                                     gi * 1024:(gi + 1) * 1024],
                          o_coll[par * 64:par * 64 + 64, hp * 8192 + g * 1024: hp * 8192 + (g + 1) * 1024], sem=ld)
    p.wait("gpsimd", ld, ld.n)
    for X in range(4):
        p.ctl("gpsimd", lambda e, X=X: e.collective_compute(
            "AllGather", ALU.bypass, GROUPS, [DR["ago%d_in" % X].opt()],
            [DR["agofm_out"][X * 1024:(X + 1) * 1024, :].opt()]).then_inc(cc.h, 1))
        cc.n += 1
    p.wait("gpsimd", cc, cc.n)
    for ci in range(NCH):
        gath(o_coll[:, ci * T_LOC:(ci + 1) * T_LOC], DR["agofm_out"][0:128, :], 32 + ci)
    for e_ in P.ENGINES:
        p.wait(e_, gg, gg.n)
    p.barrier()
    import os as _os
    if _os.environ.get("EDBG") == "o":
        for c in range(NCH):
            p.on("vector", lambda e, c=c: e.tensor_copy(
                xf[:, c * T_LOC:(c + 1) * T_LOC], o_coll[:, c * T_LOC:(c + 1) * T_LOC]))
        p.inc("vector", vs)
        p.barrier()
        es2.close()
        es0.close()
        return
    es2.close()

    es3 = ExitStack()
    wos_ = [p.sbuf(tag + "wosa", [128, 1024], BF16, es=es3),
            p.sbuf(tag + "wosb", [128, 1024], BF16, es=es3)]
    x2tb = p.sbuf(tag + "x2tb", [128, 8 * TB], F32, es=es3)
    h2tb = p.sbuf(tag + "h2tb", [128, 8 * TB], BF16, es=es3)
    x1s = p.sbuf(tag + "x1s", [128, TB], F32, es=es3)
    f1 = p.sbuf(tag + "f1", [128, 32 * TB], BF16, es=es3)
    wf1s = [p.sbuf(tag + "wf1a", [128, 1024], BF16, es=es3),
            p.sbuf(tag + "wf1b", [128, 1024], BF16, es=es3)]
    wf2oc = p.sbuf(tag + "wf2oc", [128, 32 * 128], BF16, es=es3)
    bo_c = p.sbuf(tag + "bo_c", [128, 8], F32, es=es3)
    g2c = p.sbuf(tag + "g2c", [128, 8], F32, es=es3)
    b2c = p.sbuf(tag + "b2c", [128, 8], F32, es=es3)
    bf1c = p.sbuf(tag + "bf1c", [128, 32], F32, es=es3)
    bf2c = p.sbuf(tag + "bf2c", [128, 8], F32, es=es3)
    s1r2 = C["s_lnrow1"]
    s2r2 = C["s_lnrow2"]
    pja = p.psum(tag + "pja", [128, TB], es=es3)
    pjb = p.psum(tag + "pjb", [128, TB], es=es3)
    ps1 = p.psum(tag + "ps1", [1, TB], es=es3)
    ps2 = p.psum(tag + "ps2", [1, TB], es=es3)
    pb1 = p.psum(tag + "pb1", [128, TB], es=es3)
    pb2 = p.psum(tag + "pb2", [128, TB], es=es3)
    pjf = [pja, pjb]
    for name, sb in (("L1_bo_col", bo_c), ("L1_g2_col", g2c), ("L1_b2_col", b2c),
                     ("L1_bf1_col", bf1c), ("L1_bf2_col", bf2c)):
        p.dma("sync", sb[:], IN[name], sem=ld)
    for e_ in P.ENGINES:
        p.wait(e_, ld, ld.n)
        p.wait(e_, gld, gld.n)
    wfl = p.sem(tag + "wfl")
    mubc, rstdbc = C["s_mubc"], C["s_rstdbc"]
    var_sb, srt, tsc = C["s_lnvar"], C["s_lnsrt"], C["s_lntmp"]
    onesb = C["s_onescol_bf"]
    for tb in range(NTB):
        tsl = slice(tb * TB, (tb + 1) * TB)
        wo_mm_t = {}
        for oc in range(NCH):
            bi = oc % 2
            if wfl.n > 0:
                p.ctl("gpsimd", lambda e, v=wfl.n: e.wait_ge(wfl.h, v))
            if oc >= 2:
                p.ctl("gpsimd", lambda e, v=wo_mm_t[oc - 2]: e.wait_ge(pt.h, v))
            wfl.inc(16)
            p.ctl("gpsimd", lambda e, oc=oc, bi=bi: e.dma_start(
                out=wos_[bi][:], in_=bass.AP(IN["L1_attn_Wo"].tensor, oc * 128,
                                             [[1024, 128], [1024 * 128, 8], [1, 128]])).then_inc(wfl.h, 16))
            p.wait("tensor", wfl, wfl.n)
            if oc >= 2:
                p.wait("tensor", pe, pe.n - 1)
            for ci in range(NCH):
                p.on("tensor", lambda e, o=pjf[bi][:, :], l=wos_[bi][:, ci * 128:(ci + 1) * 128],
                     r=o_coll[:, ci * T_LOC + tb * TB: ci * T_LOC + (tb + 1) * TB],
                     s0=(ci == 0), s1=(ci == NCH - 1): e.matmul(o, l, r, start=s0, stop=s1))
            tt = p.inc("tensor", pt)
            p.wait("vector", pt, tt)
            wo_mm_t[oc] = pt.n
            xs_ = xf[:, oc * T_LOC + tb * TB: oc * T_LOC + (tb + 1) * TB]
            p.on("vector", lambda e, x_=xs_, m=mubcE[:, tsl]: e.tensor_sub(x1s[:], x_, m))
            p.on("vector", lambda e, r=rstdE[:, tsl]: e.tensor_mul(x1s[:], x1s[:], r))
            p.on("vector", lambda e, bi=bi, oc=oc: e.scalar_tensor_tensor(
                x2tb[:, oc * TB:(oc + 1) * TB], pjf[bi][:, :], bo_c[:, oc:oc + 1], x1s[:], ALU.add, ALU.add))
            p.inc("vector", pe)
        for c in range(NCH):
            bi2 = c % 2
            xc = [C["s_cast0"], C["s_cast1"]]
            x2s = [C["s_sq0"], C["s_sq1"]]
            if c >= 2:
                p.wait("vector", ts, ts.n - 1)
            p.on("vector", lambda e, c=c, b=bi2: e.tensor_copy(xc[b][:], x2tb[:, c * TB:(c + 1) * TB]))
            p.on("vector", lambda e, c=c, b=bi2: e.tensor_mul(x2s[b][:], xc[b][:], xc[b][:]))
            vt = p.inc("vector", vs)
            p.wait("tensor", vs, vt)
            p.on("tensor", lambda e, b=bi2, s0=(c == 0), s1=(c == NCH - 1): e.matmul(
                ps1[0:1, :], onesb[:, 0:1], xc[b][:], start=s0, stop=s1))
            p.on("tensor", lambda e, b=bi2, s0=(c == 0), s1=(c == NCH - 1): e.matmul(
                ps2[0:1, :], onesb[:, 0:1], x2s[b][:], start=s0, stop=s1))
            p.inc("tensor", ts)
        p.wait("vector", ts, ts.n)
        p.on("vector", lambda e: e.tensor_copy(s1r2[:], ps1[0:1, :]))
        p.on("vector", lambda e: e.tensor_copy(s2r2[:], ps2[0:1, :]))
        ve = p.inc("vector", vs)
        p.wait("tensor", vs, ve)
        p.on("tensor", lambda e: e.matmul(pb1[:, :], o1f[0:1, :], s1r2[0:1, :], start=True, stop=True))
        p.on("tensor", lambda e: e.matmul(pb2[:, :], o1f[0:1, :], s2r2[0:1, :], start=True, stop=True))
        tb1 = p.inc("tensor", ts)
        p.wait("vector", ts, tb1)
        p.on("vector", lambda e: e.tensor_scalar(mubc[:], pb1[:, :], 1.0 / D, None, ALU.mult))
        p.on("vector", lambda e: e.tensor_scalar(tsc[:], pb2[:, :], 1.0 / D, None, ALU.mult))
        p.on("vector", lambda e: e.tensor_mul(srt[:], mubc[:], mubc[:]))
        p.on("vector", lambda e: e.tensor_sub(var_sb[:], tsc[:], srt[:]))
        vv = p.inc("vector", vs)
        p.wait("scalar", vs, vv)
        p.on("scalar", lambda e: e.activation(srt[:], var_sb[:], AF.Sqrt, bias=C["s_epscol"][:, 0:1]))
        sg = p.inc("scalar", ss)
        p.wait("vector", ss, sg)
        p.on("vector", lambda e: e.reciprocal(rstdbc[:], srt[:]))
        for c in range(NCH):
            p.on("vector", lambda e, c=c: e.tensor_sub(tsc[:], x2tb[:, c * TB:(c + 1) * TB], mubc[:]))
            p.on("vector", lambda e: e.tensor_mul(tsc[:], tsc[:], rstdbc[:]))
            p.on("vector", lambda e, c=c: e.tensor_scalar(
                h2tb[:, c * TB:(c + 1) * TB], tsc[:], g2c[:, c:c + 1], b2c[:, c:c + 1], ALU.mult, ALU.add))
        p.inc("vector", vs)
        p.barrier()
        f1_mm_t = {}
        for ofc in range(32):
            bi = ofc % 2
            if wfl.n > 0:
                p.ctl("gpsimd", lambda e, v=wfl.n: e.wait_ge(wfl.h, v))
            if ofc >= 2:
                p.ctl("gpsimd", lambda e, v=f1_mm_t[ofc - 2]: e.wait_ge(pt.h, v))
            wfl.inc(16)
            p.ctl("gpsimd", lambda e, ofc=ofc, bi=bi: e.dma_start(
                out=wf1s[bi][:], in_=bass.AP(IN["L1_attn_Wf1"].tensor, ofc * 128,
                                             [[4096, 128], [4096 * 128, 8], [1, 128]])).then_inc(wfl.h, 16))
            p.wait("tensor", wfl, wfl.n)
            if ofc >= 2:
                p.wait("tensor", ss, ss.n - 1)
            for ci in range(NCH):
                p.on("tensor", lambda e, bi=bi, ci=ci, s0=(ci == 0), s1=(ci == NCH - 1): e.matmul(
                    pjf[bi][:, :], wf1s[bi][:, ci * 128:(ci + 1) * 128], h2tb[:, ci * TB:(ci + 1) * TB],
                    start=s0, stop=s1))
            tt = p.inc("tensor", pt)
            f1_mm_t[ofc] = tt
            p.wait("scalar", pt, tt)
            p.on("scalar", lambda e, ofc=ofc, bi=bi: e.activation(
                f1[:, ofc * TB:(ofc + 1) * TB], pjf[bi][:, :], AF.Gelu, bias=bf1c[:, ofc:ofc + 1]))
            p.inc("scalar", ss)
        p.barrier()
        f2_mm_t = {}
        for oc in range(NCH):
            bi = oc % 2
            if oc >= 2:
                p.wait("tensor", pe, pe.n - 1)
            if wfl.n > 0:
                p.ctl("gpsimd", lambda e, v=wfl.n: e.wait_ge(wfl.h, v))
            if oc >= 1:
                p.ctl("gpsimd", lambda e, v=f2_mm_t[oc - 1]: e.wait_ge(pt.h, v))
            wfl.inc(16)
            p.ctl("gpsimd", lambda e, oc=oc: e.dma_start(
                out=wf2oc[:], in_=bass.AP(IN["L1_attn_Wf2"].tensor, oc * 128,
                                          [[1024, 128], [1024 * 128, 32], [1, 128]])).then_inc(wfl.h, 16))
            p.wait("tensor", wfl, wfl.n)
            for ci in range(32):
                p.on("tensor", lambda e, bi=bi, ci=ci, s0=(ci == 0), s1=(ci == 31): e.matmul(
                    pjf[bi][:, :], wf2oc[:, ci * 128:(ci + 1) * 128], f1[:, ci * TB:(ci + 1) * TB],
                    start=s0, stop=s1))
            tt = p.inc("tensor", pt)
            f2_mm_t[oc] = tt
            p.wait("vector", pt, tt)
            xs_ = xf[:, oc * T_LOC + tb * TB: oc * T_LOC + (tb + 1) * TB]
            p.on("vector", lambda e, bi=bi, oc=oc: e.scalar_tensor_tensor(
                x1s[:], pjf[bi][:, :], bf2c[:, oc:oc + 1], x2tb[:, oc * TB:(oc + 1) * TB], ALU.add, ALU.add))
            p.on("vector", lambda e, x_=xs_: e.tensor_add(x_, x_, x1s[:]))
            p.inc("vector", pe)
        p.barrier()
    es3.close()
    es0.close()

def prep_col(vec, parts=128):
    n = vec.shape[-1]
    return np.ascontiguousarray(vec.reshape(n // parts, parts).T.astype(np.float32))


def shard_x(x):
    outs = []
    for core in range(NCORE):
        b, j = core // 4, core % 4
        xt = x[b].T
        outs.append(np.ascontiguousarray(
            np.concatenate([xt[:, 1024 * j:1024 * (j + 1)],
                            xt[:, 1024 * (7 - j):1024 * (8 - j)]], axis=1), dtype=np.float32))
    return outs


def unshard_x(parts):
    x = np.zeros((2, 8192, 1024), dtype=np.float32)
    for core in range(NCORE):
        b, j = core // 4, core % 4
        x[b, 1024 * j:1024 * (j + 1), :] = parts[core][:, 0:1024].T
        x[b, 1024 * (7 - j):1024 * (8 - j), :] = parts[core][:, 1024:2048].T
    return x


def declare_consts(nc, IN):
    for name, (shape, dt) in CONST_SHAPES.items():
        IN[name] = nc.dram_tensor(name, shape, dt, kind="ExternalInput").ap()


def const_maps(m, consts):
    for name in CONST_SHAPES:
        m[name] = consts[name]


def build_D():
    nc = bass.Bass("TRN2", target_bir_lowering=False, debug=False)
    es = ExitStack()
    p = P(nc, es)
    IN = {}
    IN["xin"] = nc.dram_tensor("xin", [D, T_LOC], F32, kind="ExternalInput").ap()
    xout = nc.dram_tensor("xout", [D, T_LOC], F32, kind="ExternalOutput").ap()
    IN["L1_ffn_W"] = nc.dram_tensor("L1_ffn_W", [8, D, D], F32, kind="ExternalInput").ap()
    IN["L1_ffn_b"] = nc.dram_tensor("L1_ffn_b", [8, D], F32, kind="ExternalInput").ap()
    IN["L1_ffn_logits"] = nc.dram_tensor("L1_ffn_logits", [8, 1], F32, kind="ExternalInput").ap()
    IN["L1_ffn_u"] = nc.dram_tensor("L1_ffn_u", [8, 1], F32, kind="ExternalInput").ap()
    declare_consts(nc, IN)

    xf = es.enter_context(nc.sbuf_tensor("xf", [128, NCH * T_LOC], F32))
    C = alloc_common(p, IN)

    xl = p.sem("xload")
    for c in range(NCH):
        p.dma("sync", xf[:, c * T_LOC:(c + 1) * T_LOC], IN["xin"][c * 128:(c + 1) * 128, :], sem=xl)
    for e in P.ENGINES:
        p.wait(e, xl, xl.n)

    import os
    mode = os.environ.get("MODE", "full")
    if mode == "ln":
        emit_ln(p, C, xf, C["s_hbf"], tag="lnonly")
        hv = p.sem("hv")
        for c in range(NCH):
            p.on("vector", lambda e, c=c: e.tensor_copy(
                xf[:, c * T_LOC:(c + 1) * T_LOC], C["s_hbf"][:, c * T_LOC:(c + 1) * T_LOC]))
        p.inc("vector", hv)
        p.barrier()
    elif mode == "none":
        pass
    else:
        emit_l1ffn(p, C, xf, IN)

    so = p.sem("xstore")
    for c in range(NCH):
        p.dma("sync", xout[c * 128:(c + 1) * 128, :], xf[:, c * T_LOC:(c + 1) * T_LOC], sem=so)
    p.wait("sync", so, so.n)
    p.run()
    es.close()
    return nc



def build_A():
    nc = bass.Bass("TRN2", target_bir_lowering=False, debug=False)
    es = ExitStack()
    p = P(nc, es)
    IN = {}
    IN["xin"] = nc.dram_tensor("xin", [D, T_LOC], F32, kind="ExternalInput").ap()
    xout = nc.dram_tensor("xout", [D, T_LOC], F32, kind="ExternalOutput").ap()
    for w in ("Wq", "Wk", "Wv"):
        IN["L0_ffn_" + w] = nc.dram_tensor("L0_ffn_" + w, [4, D, D], F32, kind="ExternalInput").ap()
    for b in ("bq", "bk", "bv"):
        IN["L0_ffn_" + b] = nc.dram_tensor("L0_ffn_" + b, [4, D], F32, kind="ExternalInput").ap()
    IN["L0_ffn_Wg1"] = nc.dram_tensor("L0_ffn_Wg1", [D, 128], F32, kind="ExternalInput").ap()
    IN["L0_ffn_Wg2"] = nc.dram_tensor("L0_ffn_Wg2", [128, 4], F32, kind="ExternalInput").ap()
    IN["L0_ffn_Wo"] = nc.dram_tensor("L0_ffn_Wo", [D, D], F32, kind="ExternalInput").ap()
    IN["bg1_col"] = nc.dram_tensor("bg1_col", [128, 1], F32, kind="ExternalInput").ap()
    IN["bg2_col"] = nc.dram_tensor("bg2_col", [4, 1], F32, kind="ExternalInput").ap()
    IN["bo_col"] = nc.dram_tensor("bo_col", [128, 8], F32, kind="ExternalInput").ap()
    declare_consts(nc, IN)
    DR = {
        "arp_in": nc.dram_tensor("arp_in", [128, 8], F32, kind="Internal").ap(),
        "arp_out": nc.dram_tensor("arp_out", [128, 8], F32, kind="Internal").ap(),
        "arn_in": nc.dram_tensor("arn_in", [2, 8], F32, kind="Internal").ap(),
        "arn_out": nc.dram_tensor("arn_out", [2, 8], F32, kind="Internal").ap(),
        "agk_in": nc.dram_tensor("agk_in", [D, T_LOC], BF16, kind="Internal").ap(),
        "agk_out": nc.dram_tensor("agk_out", [4 * D, T_LOC], BF16, kind="Internal").ap(),
        "agv_in": nc.dram_tensor("agv_in", [T_LOC, 1040], BF16, kind="Internal").ap(),
        "agv_out": nc.dram_tensor("agv_out", [8192, 1040], BF16, kind="Internal").ap(),
    }
    xf = es.enter_context(nc.sbuf_tensor("xf", [128, NCH * T_LOC], F32))
    C = alloc_common(p, IN)
    xl = p.sem("xload")
    for c in range(NCH):
        p.dma("sync", xf[:, c * T_LOC:(c + 1) * T_LOC], IN["xin"][c * 128:(c + 1) * 128, :], sem=xl)
    for e in P.ENGINES:
        p.wait(e, xl, xl.n)
    emit_l0ffn(p, C, xf, IN, DR)
    so = p.sem("xstore")
    for c in range(NCH):
        p.dma("sync", xout[c * 128:(c + 1) * 128, :], xf[:, c * T_LOC:(c + 1) * T_LOC], sem=so)
    p.wait("sync", so, so.n)
    p.run()
    es.close()
    return nc


def a_in_maps(INS, consts, xs):
    in_maps = []
    for core in range(NCORE):
        m = {"xin": xs[core]}
        for w in ("Wq", "Wk", "Wv", "bq", "bk", "bv", "Wg1", "Wg2", "Wo"):
            m["L0_ffn_" + w] = INS["L0_ffn_" + w]
        m["bg1_col"] = np.ascontiguousarray(INS["L0_ffn_bg1"].reshape(128, 1))
        m["bg2_col"] = np.ascontiguousarray(INS["L0_ffn_bg2"].reshape(4, 1))
        m["bo_col"] = prep_col(INS["L0_ffn_bo"])
        const_maps(m, consts)
        in_maps.append(m)
    return in_maps



def build_B():
    nc = bass.Bass("TRN2", target_bir_lowering=False, debug=False)
    es = ExitStack()
    p = P(nc, es)
    IN = {}
    IN["xin"] = nc.dram_tensor("xin", [D, T_LOC], F32, kind="ExternalInput").ap()
    xout = nc.dram_tensor("xout", [D, T_LOC], F32, kind="ExternalOutput").ap()
    IN["L0_attn_Wq"] = nc.dram_tensor("L0_attn_Wq", [D, 256], F32, kind="ExternalInput").ap()
    IN["L0_attn_Wk"] = nc.dram_tensor("L0_attn_Wk", [D, 256], F32, kind="ExternalInput").ap()
    IN["L0_attn_Wv"] = nc.dram_tensor("L0_attn_Wv", [D, D], F32, kind="ExternalInput").ap()
    IN["L0_attn_Wo"] = nc.dram_tensor("L0_attn_Wo", [D, D], F32, kind="ExternalInput").ap()
    IN["L0_attn_temperature"] = nc.dram_tensor("L0_attn_temperature", [1, 1], F32, kind="ExternalInput").ap()
    for nm, shape in (("L0_bq_col", [128, 2]), ("L0_bk_col", [128, 2]), ("L0_bv_col", [128, 8]),
                      ("L0_qp_col", [128, 2]), ("L0_kp_col", [128, 2]), ("L0_bo_col", [128, 8])):
        IN[nm] = nc.dram_tensor(nm, shape, F32, kind="ExternalInput").ap()
    IN["t_core_idx"] = nc.dram_tensor("t_core_idx", [1, 1], mybir.dt.int32, kind="ExternalInput").ap()
    declare_consts(nc, IN)
    DR = {
        "agq_in": nc.dram_tensor("agq_in", [256, T_LOC], BF16, kind="Internal").ap(),
        "agq_out": nc.dram_tensor("agq_out", [1024, T_LOC], BF16, kind="Internal").ap(),
        "agk_in": nc.dram_tensor("agk_in", [256, T_LOC], BF16, kind="Internal").ap(),
        "agk_out": nc.dram_tensor("agk_out", [1024, T_LOC], BF16, kind="Internal").ap(),
        "agvf_in": nc.dram_tensor("agvf_in", [1024, T_LOC], BF16, kind="Internal").ap(),
        "agvf_out": nc.dram_tensor("agvf_out", [4096, T_LOC], BF16, kind="Internal").ap(),
        "ago0_in": nc.dram_tensor("ago0_in", [256, T_LOC], BF16, kind="Internal").ap(),
        "ago1_in": nc.dram_tensor("ago1_in", [256, T_LOC], BF16, kind="Internal").ap(),
        "ago2_in": nc.dram_tensor("ago2_in", [256, T_LOC], BF16, kind="Internal").ap(),
        "ago3_in": nc.dram_tensor("ago3_in", [256, T_LOC], BF16, kind="Internal").ap(),
        "agofm_out": nc.dram_tensor("agofm_out", [4096, T_LOC], BF16, kind="Internal").ap(),
    }
    IN["t_bidx"] = nc.dram_tensor("t_bidx", [128, 56], mybir.dt.uint32, kind="ExternalInput").ap()
    xf = es.enter_context(nc.sbuf_tensor("xf", [128, NCH * T_LOC], F32))
    C = alloc_common(p, IN)
    xl = p.sem("xload")
    for c in range(NCH):
        p.dma("sync", xf[:, c * T_LOC:(c + 1) * T_LOC], IN["xin"][c * 128:(c + 1) * 128, :], sem=xl)
    for e in P.ENGINES:
        p.wait(e, xl, xl.n)
    emit_l0attn(p, C, xf, IN, DR, None)
    so = p.sem("xstore")
    for c in range(NCH):
        p.dma("sync", xout[c * 128:(c + 1) * 128, :], xf[:, c * T_LOC:(c + 1) * T_LOC], sem=so)
    p.wait("sync", so, so.n)
    p.run()
    es.close()
    return nc



def build_bidx(j):
    t = np.zeros((128, 56), dtype=np.uint32)
    for hp in range(2):
        for g in range(8):
            rblk = g if g <= 3 else 7 - g
            co = 0 if g <= 3 else 1024
            for par in range(2):
                rows = 2 * (256 * rblk + 16 * (4 * j + 2 * hp + par)) + (co // 1024)
                for pp in range(16):
                    t[32 * par + pp, hp * 16 + g] = rows + 2 * pp
                    t[32 * par + pp, hp * 16 + 8 + g] = rows + 2 * pp
            vrows = 2 * (1024 * rblk + 64 * (4 * j + 2 * hp)) + (co // 1024)
            for pp in range(128):
                t[pp, 32 + hp * 8 + g] = vrows + 2 * pp
    for ci in range(8):
        for pp in range(128):
            t[pp, 48 + ci] = 1024 * j + ci * 128 + pp
    return np.ascontiguousarray(t)

def b_in_maps(INS, consts, xs):
    in_maps = []
    for core in range(NCORE):
        m = {"xin": xs[core]}
        for w in ("Wq", "Wk", "Wv", "Wo"):
            m["L0_attn_" + w] = INS["L0_attn_" + w]
        m["L0_attn_temperature"] = np.ascontiguousarray(INS["L0_attn_temperature"].reshape(1, 1))
        m["L0_bq_col"] = prep_col(INS["L0_attn_bq"])
        m["L0_bk_col"] = prep_col(INS["L0_attn_bk"])
        m["L0_bv_col"] = prep_col(INS["L0_attn_bv"])
        m["L0_bo_col"] = prep_col(INS["L0_attn_bo"])
        m["L0_qp_col"] = prep_col(INS["L0_attn_q_phase"].reshape(256))
        m["L0_kp_col"] = prep_col(INS["L0_attn_k_phase"].reshape(256))
        m["t_core_idx"] = np.array([[core % 4]], dtype=np.int32)
        m["t_bidx"] = build_bidx(core % 4)
        const_maps(m, consts)
        in_maps.append(m)
    return in_maps



ROPE_PERM = np.concatenate([np.concatenate([np.arange(0, 32), np.arange(64, 96),
                                            np.arange(32, 64), np.arange(96, 128)]) + 128 * b
                            for b in range(8)])


def build_E():
    nc = bass.Bass("TRN2", target_bir_lowering=False, debug=False)
    es = ExitStack()
    p = P(nc, es)
    IN = {}
    IN["xin"] = nc.dram_tensor("xin", [D, T_LOC], F32, kind="ExternalInput").ap()
    xout = nc.dram_tensor("xout", [D, T_LOC], F32, kind="ExternalOutput").ap()
    IN["L1_Wq_perm"] = nc.dram_tensor("L1_Wq_perm", [D, D], F32, kind="ExternalInput").ap()
    IN["L1_Wk_perm"] = nc.dram_tensor("L1_Wk_perm", [D, 256], F32, kind="ExternalInput").ap()
    IN["L1_attn_Wv"] = nc.dram_tensor("L1_attn_Wv", [D, 256], F32, kind="ExternalInput").ap()
    IN["L1_attn_Wo"] = nc.dram_tensor("L1_attn_Wo", [D, D], F32, kind="ExternalInput").ap()
    IN["L1_attn_Wf1"] = nc.dram_tensor("L1_attn_Wf1", [D, 4096], F32, kind="ExternalInput").ap()
    IN["L1_attn_Wf2"] = nc.dram_tensor("L1_attn_Wf2", [4096, D], F32, kind="ExternalInput").ap()
    for nm, shape in (("L1_bq_col", [128, 8]), ("L1_bk_col", [128, 2]), ("L1_bv_col", [128, 2]),
                      ("L1_bo_col", [128, 8]), ("L1_g1_col", [128, 8]), ("L1_b1_col", [128, 8]),
                      ("L1_g2_col", [128, 8]), ("L1_b2_col", [128, 8]),
                      ("L1_bf1_col", [128, 32]), ("L1_bf2_col", [128, 8])):
        IN[nm] = nc.dram_tensor(nm, shape, F32, kind="ExternalInput").ap()
    IN["t_rope_cos"] = nc.dram_tensor("t_rope_cos", [128, T_LOC], BF16, kind="ExternalInput").ap()
    IN["t_rope_sin"] = nc.dram_tensor("t_rope_sin", [128, T_LOC], BF16, kind="ExternalInput").ap()
    IN["t_eidx"] = nc.dram_tensor("t_eidx", [128, 56], mybir.dt.uint32, kind="ExternalInput").ap()
    declare_consts(nc, IN)
    DR = {
        "agqE_in": nc.dram_tensor("agqE_in", [D, T_LOC], BF16, kind="Internal").ap(),
        "agqE_out": nc.dram_tensor("agqE_out", [4 * D, T_LOC], BF16, kind="Internal").ap(),
        "agkE_in": nc.dram_tensor("agkE_in", [256, T_LOC], BF16, kind="Internal").ap(),
        "agkE_out": nc.dram_tensor("agkE_out", [1024, T_LOC], BF16, kind="Internal").ap(),
        "agvE_in": nc.dram_tensor("agvE_in", [256, T_LOC], BF16, kind="Internal").ap(),
        "agvE_out": nc.dram_tensor("agvE_out", [1024, T_LOC], BF16, kind="Internal").ap(),
        "arnE_in": nc.dram_tensor("arnE_in", [2, 2], F32, kind="Internal").ap(),
        "arnE_out": nc.dram_tensor("arnE_out", [2, 2], F32, kind="Internal").ap(),
        "ago0_in": nc.dram_tensor("ago0_in", [256, T_LOC], BF16, kind="Internal").ap(),
        "ago1_in": nc.dram_tensor("ago1_in", [256, T_LOC], BF16, kind="Internal").ap(),
        "ago2_in": nc.dram_tensor("ago2_in", [256, T_LOC], BF16, kind="Internal").ap(),
        "ago3_in": nc.dram_tensor("ago3_in", [256, T_LOC], BF16, kind="Internal").ap(),
        "agofm_out": nc.dram_tensor("agofm_out", [4096, T_LOC], BF16, kind="Internal").ap(),
    }
    import os as _os3
    if _os3.environ.get("EDBG2"):
        IN["dbg"] = nc.dram_tensor("dbg", [D, T_LOC], BF16, kind="ExternalOutput").ap()
        IN["dbg2"] = nc.dram_tensor("dbg2", [D, T_LOC], BF16, kind="ExternalOutput").ap()
        IN["dbg3"] = nc.dram_tensor("dbg3", [D, T_LOC], BF16, kind="ExternalOutput").ap()
    xf = es.enter_context(nc.sbuf_tensor("xf", [128, NCH * T_LOC], F32))
    C = alloc_common(p, IN)
    xl = p.sem("xload")
    for c in range(NCH):
        p.dma("sync", xf[:, c * T_LOC:(c + 1) * T_LOC], IN["xin"][c * 128:(c + 1) * 128, :], sem=xl)
    for e in P.ENGINES:
        p.wait(e, xl, xl.n)
    emit_l1attn(p, C, xf, IN, DR)
    so = p.sem("xstore")
    for c in range(NCH):
        p.dma("sync", xout[c * 128:(c + 1) * 128, :], xf[:, c * T_LOC:(c + 1) * T_LOC], sem=so)
    p.wait("sync", so, so.n)
    p.run()
    es.close()
    return nc


def build_eidx(j):
    t = np.zeros((128, 56), dtype=np.uint32)

    def permrows(h):
        base = 128 * (h // 2) + 32 * (h % 2)
        return np.concatenate([np.arange(base, base + 32), np.arange(base + 64, base + 96)])
    for g in range(8):
        rblk = g if g <= 3 else 7 - g
        co = 0 if g <= 3 else 1
        for hp in range(2):
            for par in range(2):
                h = 4 * j + 2 * hp + par
                rows = permrows(h) + 1024 * rblk
                for pp in range(64):
                    t[64 * par + pp, hp * 8 + g] = 2 * rows[pp] + co
        krows = permrows(j) + 256 * rblk
        vbase = 2 * (256 * rblk + 64 * j) + co
        for pp in range(64):
            t[pp, 16 + g] = 2 * krows[pp] + co
            t[64 + pp, 16 + g] = 2 * krows[pp] + co
            t[pp, 24 + g] = vbase + 2 * pp
    for ci in range(8):
        for pp in range(128):
            t[pp, 32 + ci] = 1024 * j + ci * 128 + pp
    return np.ascontiguousarray(t)


def e_in_maps(INS, consts, xs):
    in_maps = []
    wq_perm = np.ascontiguousarray(INS["L1_attn_Wq"][:, ROPE_PERM])
    kperm = ROPE_PERM[:256]
    wk_perm = np.ascontiguousarray(INS["L1_attn_Wk"][:, kperm])
    bq_perm = INS["L1_attn_bq"][ROPE_PERM]
    bk_perm = INS["L1_attn_bk"][kperm]
    for core in range(NCORE):
        b, j = core // 4, core % 4
        m = {"xin": xs[core]}
        m["L1_Wq_perm"] = wq_perm
        m["L1_Wk_perm"] = wk_perm
        m["L1_attn_Wv"] = INS["L1_attn_Wv"]
        m["L1_attn_Wo"] = INS["L1_attn_Wo"]
        m["L1_attn_Wf1"] = INS["L1_attn_Wf1"]
        m["L1_attn_Wf2"] = INS["L1_attn_Wf2"]
        m["L1_bq_col"] = prep_col(bq_perm)
        m["L1_bk_col"] = prep_col(bk_perm)
        m["L1_bv_col"] = prep_col(INS["L1_attn_bv"])
        m["L1_bo_col"] = prep_col(INS["L1_attn_bo"])
        m["L1_g1_col"] = prep_col(INS["L1_attn_g1"])
        m["L1_b1_col"] = prep_col(INS["L1_attn_b1"])
        m["L1_g2_col"] = prep_col(INS["L1_attn_g2"])
        m["L1_b2_col"] = prep_col(INS["L1_attn_b2"])
        m["L1_bf1_col"] = prep_col(INS["L1_attn_bf1"])
        m["L1_bf2_col"] = prep_col(INS["L1_attn_bf2"])
        tt = core_tables(core)
        m["t_rope_cos"] = tt["t_rope_cos"]
        m["t_rope_sin"] = tt["t_rope_sin"]
        m["t_eidx"] = build_eidx(j)
        const_maps(m, consts)
        in_maps.append(m)
    return in_maps


def core_tables(core):
    b, j = core // 4, core % 4
    pos = np.concatenate([np.arange(1024 * j, 1024 * j + 1024),
                          np.arange(1024 * (7 - j), 1024 * (7 - j) + 1024)])
    half = 32
    theta = 1.0 / 10000.0 ** (np.arange(half, dtype=np.float64) / half)
    ang = pos[None, :] * theta[:, None]
    cos4 = np.tile(np.cos(ang), (4, 1))
    sin = np.sin(ang)
    s4 = np.concatenate([np.tile(sin, (2, 1)), -np.tile(sin, (2, 1))], axis=0)
    return {"t_rope_cos": np.ascontiguousarray(cos4.astype(BF)),
            "t_rope_sin": np.ascontiguousarray(s4.astype(BF))}


def build_full():
    nc = bass.Bass("TRN2", target_bir_lowering=False, debug=False)
    es = ExitStack()
    p = P(nc, es)
    IN = {}
    IN["xin"] = nc.dram_tensor("xin", [D, T_LOC], F32, kind="ExternalInput").ap()
    xout = nc.dram_tensor("xout", [D, T_LOC], F32, kind="ExternalOutput").ap()
    for w in ("Wq", "Wk", "Wv"):
        IN["L0_ffn_" + w] = nc.dram_tensor("L0_ffn_" + w, [4, D, D], F32, kind="ExternalInput").ap()
    for b in ("bq", "bk", "bv"):
        IN["L0_ffn_" + b] = nc.dram_tensor("L0_ffn_" + b, [4, D], F32, kind="ExternalInput").ap()
    IN["L0_ffn_Wg1"] = nc.dram_tensor("L0_ffn_Wg1", [D, 128], F32, kind="ExternalInput").ap()
    IN["L0_ffn_Wg2"] = nc.dram_tensor("L0_ffn_Wg2", [128, 4], F32, kind="ExternalInput").ap()
    IN["L0_ffn_Wo"] = nc.dram_tensor("L0_ffn_Wo", [D, D], F32, kind="ExternalInput").ap()
    IN["bg1_col"] = nc.dram_tensor("bg1_col", [128, 1], F32, kind="ExternalInput").ap()
    IN["bg2_col"] = nc.dram_tensor("bg2_col", [4, 1], F32, kind="ExternalInput").ap()
    IN["bo_col"] = nc.dram_tensor("bo_col", [128, 8], F32, kind="ExternalInput").ap()
    IN["L0_attn_Wq"] = nc.dram_tensor("L0_attn_Wq", [D, 256], F32, kind="ExternalInput").ap()
    IN["L0_attn_Wk"] = nc.dram_tensor("L0_attn_Wk", [D, 256], F32, kind="ExternalInput").ap()
    IN["L0_attn_Wv"] = nc.dram_tensor("L0_attn_Wv", [D, D], F32, kind="ExternalInput").ap()
    IN["L0_attn_Wo"] = nc.dram_tensor("L0_attn_Wo", [D, D], F32, kind="ExternalInput").ap()
    IN["L0_attn_temperature"] = nc.dram_tensor("L0_attn_temperature", [1, 1], F32, kind="ExternalInput").ap()
    for nm, shape in (("L0_bq_col", [128, 2]), ("L0_bk_col", [128, 2]), ("L0_bv_col", [128, 8]),
                      ("L0_qp_col", [128, 2]), ("L0_kp_col", [128, 2]), ("L0_bo_col", [128, 8])):
        IN[nm] = nc.dram_tensor(nm, shape, F32, kind="ExternalInput").ap()
    IN["t_core_idx"] = nc.dram_tensor("t_core_idx", [1, 1], mybir.dt.int32, kind="ExternalInput").ap()
    IN["t_bidx"] = nc.dram_tensor("t_bidx", [128, 56], mybir.dt.uint32, kind="ExternalInput").ap()
    IN["L1_ffn_W"] = nc.dram_tensor("L1_ffn_W", [8, D, D], F32, kind="ExternalInput").ap()
    IN["L1_ffn_b"] = nc.dram_tensor("L1_ffn_b", [8, D], F32, kind="ExternalInput").ap()
    IN["L1_ffn_logits"] = nc.dram_tensor("L1_ffn_logits", [8, 1], F32, kind="ExternalInput").ap()
    IN["L1_ffn_u"] = nc.dram_tensor("L1_ffn_u", [8, 1], F32, kind="ExternalInput").ap()
    IN["L1_Wq_perm"] = nc.dram_tensor("L1_Wq_perm", [D, D], F32, kind="ExternalInput").ap()
    IN["L1_Wk_perm"] = nc.dram_tensor("L1_Wk_perm", [D, 256], F32, kind="ExternalInput").ap()
    IN["L1_attn_Wv"] = nc.dram_tensor("L1_attn_Wv", [D, 256], F32, kind="ExternalInput").ap()
    IN["L1_attn_Wo"] = nc.dram_tensor("L1_attn_Wo", [D, D], F32, kind="ExternalInput").ap()
    IN["L1_attn_Wf1"] = nc.dram_tensor("L1_attn_Wf1", [D, 4096], F32, kind="ExternalInput").ap()
    IN["L1_attn_Wf2"] = nc.dram_tensor("L1_attn_Wf2", [4096, D], F32, kind="ExternalInput").ap()
    for nm, shape in (("L1_bq_col", [128, 8]), ("L1_bk_col", [128, 2]), ("L1_bv_col", [128, 2]),
                      ("L1_bo_col", [128, 8]), ("L1_g1_col", [128, 8]), ("L1_b1_col", [128, 8]),
                      ("L1_g2_col", [128, 8]), ("L1_b2_col", [128, 8]),
                      ("L1_bf1_col", [128, 32]), ("L1_bf2_col", [128, 8])):
        IN[nm] = nc.dram_tensor(nm, shape, F32, kind="ExternalInput").ap()
    IN["t_rope_cos"] = nc.dram_tensor("t_rope_cos", [128, T_LOC], BF16, kind="ExternalInput").ap()
    IN["t_rope_sin"] = nc.dram_tensor("t_rope_sin", [128, T_LOC], BF16, kind="ExternalInput").ap()
    IN["t_eidx"] = nc.dram_tensor("t_eidx", [128, 56], mybir.dt.uint32, kind="ExternalInput").ap()
    declare_consts(nc, IN)
    ago = {f"ago{X}_in": nc.dram_tensor(f"ago{X}_in", [256, T_LOC], BF16, kind="Internal").ap()
           for X in range(4)}
    agofm = nc.dram_tensor("agofm_out", [4096, T_LOC], BF16, kind="Internal").ap()
    DR_A = {
        "arp_in": nc.dram_tensor("a_arp_in", [128, 8], F32, kind="Internal").ap(),
        "arp_out": nc.dram_tensor("a_arp_out", [128, 8], F32, kind="Internal").ap(),
        "arn_in": nc.dram_tensor("a_arn_in", [2, 8], F32, kind="Internal").ap(),
        "arn_out": nc.dram_tensor("a_arn_out", [2, 8], F32, kind="Internal").ap(),
        "agk_in": nc.dram_tensor("a_agk_in", [D, T_LOC], BF16, kind="Internal").ap(),
        "agk_out": nc.dram_tensor("a_agk_out", [4 * D, T_LOC], BF16, kind="Internal").ap(),
        "agv_in": nc.dram_tensor("a_agv_in", [T_LOC, 1040], BF16, kind="Internal").ap(),
        "agv_out": nc.dram_tensor("a_agv_out", [8192, 1040], BF16, kind="Internal").ap(),
    }
    DR_B = {
        "agq_in": nc.dram_tensor("b_agq_in", [256, T_LOC], BF16, kind="Internal").ap(),
        "agq_out": nc.dram_tensor("b_agq_out", [1024, T_LOC], BF16, kind="Internal").ap(),
        "agk_in": nc.dram_tensor("b_agk_in", [256, T_LOC], BF16, kind="Internal").ap(),
        "agk_out": nc.dram_tensor("b_agk_out", [1024, T_LOC], BF16, kind="Internal").ap(),
        "agvf_in": nc.dram_tensor("b_agvf_in", [1024, T_LOC], BF16, kind="Internal").ap(),
        "agvf_out": nc.dram_tensor("b_agvf_out", [4096, T_LOC], BF16, kind="Internal").ap(),
        "agofm_out": agofm, **ago,
    }
    DR_E = {
        "agqE_in": nc.dram_tensor("e_agq_in", [D, T_LOC], BF16, kind="Internal").ap(),
        "agqE_out": nc.dram_tensor("e_agq_out", [4 * D, T_LOC], BF16, kind="Internal").ap(),
        "agkE_in": nc.dram_tensor("e_agk_in", [256, T_LOC], BF16, kind="Internal").ap(),
        "agkE_out": nc.dram_tensor("e_agk_out", [1024, T_LOC], BF16, kind="Internal").ap(),
        "agvE_in": nc.dram_tensor("e_agv_in", [256, T_LOC], BF16, kind="Internal").ap(),
        "agvE_out": nc.dram_tensor("e_agv_out", [1024, T_LOC], BF16, kind="Internal").ap(),
        "arnE_in": nc.dram_tensor("e_arn_in", [2, 2], F32, kind="Internal").ap(),
        "arnE_out": nc.dram_tensor("e_arn_out", [2, 2], F32, kind="Internal").ap(),
        "agofm_out": agofm, **ago,
    }
    xf = es.enter_context(nc.sbuf_tensor("xf", [128, NCH * T_LOC], F32))
    C = alloc_common(p, IN)
    xl = p.sem("xload")
    for c in range(NCH):
        p.dma("sync", xf[:, c * T_LOC:(c + 1) * T_LOC], IN["xin"][c * 128:(c + 1) * 128, :], sem=xl)
    for e in P.ENGINES:
        p.wait(e, xl, xl.n)
    p.label = "A"
    emit_l0ffn(p, C, xf, IN, DR_A)
    p.label = "B"
    emit_l0attn(p, C, xf, IN, DR_B, None)
    p.label = "C"
    emit_l0ffn(p, C, xf, IN, DR_A)
    p.label = "D"
    emit_l1ffn(p, C, xf, IN)
    p.label = "E"
    emit_l1attn(p, C, xf, IN, DR_E)
    p.label = "F"
    emit_l1ffn(p, C, xf, IN)
    p.label = "out"
    so = p.sem("xstore")
    for c in range(NCH):
        p.dma("sync", xout[c * 128:(c + 1) * 128, :], xf[:, c * T_LOC:(c + 1) * T_LOC], sem=so)
    p.wait("sync", so, so.n)
    p.run()
    es.close()
    return nc


def full_in_maps(INS, consts, xs):
    am = a_in_maps(INS, consts, xs)
    bm = b_in_maps(INS, consts, xs)
    dm = d_in_maps(INS, consts, xs) if "d_in_maps" in globals() else None
    em = e_in_maps(INS, consts, xs)
    maps = []
    for core in range(NCORE):
        m = {}
        m.update(am[core])
        m.update(bm[core])
        m.update(em[core])
        m["L1_ffn_W"] = INS["L1_ffn_W"]
        m["L1_ffn_b"] = INS["L1_ffn_b"]
        m["L1_ffn_logits"] = np.ascontiguousarray(
            np.asarray(INS["L1_ffn_logits"], dtype=np.float32).reshape(8, 1))
        m["L1_ffn_u"] = np.ascontiguousarray(
            np.asarray(INS["L1_ffn_u"], dtype=np.float32).reshape(8, 1))
        maps.append(m)
    return maps




def kernel(**inputs):
    INS = {k: np.asarray(v) for k, v in inputs.items()}
    consts = host_constants()
    xs = shard_x(np.asarray(INS["L0_attn_x"], dtype=np.float32))
    nc = build_full()
    res = run_bass_kernel_spmd(nc, full_in_maps(INS, consts, xs),
                               core_ids=list(range(NCORE)))
    return unshard_x([np.asarray(r["xout"], dtype=np.float32) for r in res.results])
